# version 14
# baseline (speedup 1.0000x reference)
import contextlib
import numpy as np
import concourse.bass as bass
import concourse.mybir as mybir
from concourse.bass_utils import run_bass_kernel_spmd

F32 = mybir.dt.float32
BF16 = mybir.dt.bfloat16
U8 = mybir.dt.uint8
ALU = mybir.AluOpType
AF = mybir.ActivationFunctionType
AXX = mybir.AxisListType.X

D = 1024
KC = 8
H = 16
TPC = 2048
NT = 16
G = 64
IF = 2816
FF = 4096
EPS = 1e-5
XT = TPC + G

FP4_TH_A = (0.25, 0.75, 1.25, 1.75)
FP4_TH_B = (2.5, 3.5, 5.0, 5.0)
SPIKE_B = (0.2441942809699397, 0.46396913384288535, 0.6861859295255305,
           0.9059607823984762, 1.083978413225562)
SOFTMAX_C = 24.0

NSL = [(0, 512), (512, 1024), (1024, 1536), (1536, 2048), (2048, 2560), (2560, 2816)]
FSL = [(s, s + 512) for s in range(0, FF, 512)]
TGROUPS = [(0, 512), (512, 1024), (1024, 1536), (1536, 2048), (2048, 2112)]

STAGE = 7
SH_GATE = 0
SH_UP = 360448
SH_DOWN = 720896
SH_FCT = 1081344
SH_PJT = 1605632
SH_WQ = 2129920
SH_WK = 2260992
SH_WV = 2392064
SH_WO = 2523136
SHN = 2654208

SBUF_CAP = 212800

O_R = 0
O_RG = 65536
O_IDENT = 69632
O_ONES = 69888
O_SEL = 70144
O_BOUT = 74240
O_KV = 76288
O_STATS = 78400
O_CB = 78592
O_BD = 78656
O_ZST = 80704
O_KG = 80960
O_VG = 81984
PERS_END = 84032
O_R1 = 84032
O_R2 = 117824
O_R3 = 150592
O_R5 = 184384
R5_END = O_R5 + 28416
assert R5_END <= SBUF_CAP, R5_END

O_E1 = O_R5
O_E2 = O_R5 + 2048
O_XNF = O_R5
O_BNST = O_R5 + 4096
O_MV = O_R5 + 4160
O_XNBF = O_R5 + 4224
O_QT = O_R5 + 6272
O_QA = O_R5 + 10368
O_QM = O_R5 + 14464
O_QS = O_R5 + 14720
O_QR = O_R5 + 14976
O_QSB = O_R5 + 15232
O_QCA = O_R5 + 15424
O_QCB = O_R5 + 17472
O_QSG = O_R5 + 19520
O_KBF = O_R5 + 6272
O_VEX = O_R5 + 8320
O_WQQ = O_R5 + 10432
O_ATT = O_R5 + 10432
O_DEN = O_R5
O_RCB = O_R5 + 2048
O_HG = O_R5 + 21568
O_HF = O_R5 + 23616
O_HQ = O_R5 + 25664
O_HQF = O_R5 + 26688
O_PROBS = O_R5 + 6272
O_PT = O_R5 + 7040
O_AVF = O_R5 + 8064
O_ZB = O_R5 + 10112
O_ZR = O_R5 + 10176
O_VH = O_R5 + 10432
O_H2 = O_R5 + 10432
O_SPA = O_R5 + 18688
O_BHF = O_R5 + 6272
O_BHG = O_R5 + 10368
O_QTMP = O_R5 + 21568
O_BHQ = O_R5 + 23616
O_HQF2 = O_R5 + 25664



def build_program():
    nc = bass.Bass("TRN2", target_bir_lowering=False)

    x_d = nc.dram_tensor("x", [TPC, D], F32, kind="ExternalInput")
    xg_d = nc.dram_tensor("xg", [G, D], F32, kind="ExternalInput")
    wqkv_d = nc.dram_tensor("wqkv", [D, 3 * D], F32, kind="ExternalInput")
    wout_d = nc.dram_tensor("wout", [D, D], F32, kind="ExternalInput")
    bout_d = nc.dram_tensor("bout", [1, D], F32, kind="ExternalInput")
    wgate_d = nc.dram_tensor("wgate", [IF, D], F32, kind="ExternalInput")
    wup_d = nc.dram_tensor("wup", [IF, D], F32, kind="ExternalInput")
    wdown_d = nc.dram_tensor("wdown", [D, IF], F32, kind="ExternalInput")
    wq_d = nc.dram_tensor("wq", [D, D], F32, kind="ExternalInput")
    wk_d = nc.dram_tensor("wk", [D, D], F32, kind="ExternalInput")
    wv_d = nc.dram_tensor("wv", [D, D], F32, kind="ExternalInput")
    wo_d = nc.dram_tensor("wo", [D, D], F32, kind="ExternalInput")
    wfc_d = nc.dram_tensor("wfc", [FF, D], F32, kind="ExternalInput")
    wproj_d = nc.dram_tensor("wproj", [D, FF], F32, kind="ExternalInput")
    ident_d = nc.dram_tensor("ident", [128, 128], F32, kind="ExternalInput")
    sel_d = nc.dram_tensor("sel", [16, KC * 128], F32, kind="ExternalInput")
    out_d = nc.dram_tensor("out", [TPC, D], F32, kind="ExternalOutput")

    wg_sh = nc.dram_tensor("wg_sh", [IF, 128], F32, kind="ExternalInput")
    wu_sh = nc.dram_tensor("wu_sh", [IF, 128], F32, kind="ExternalInput")
    wd_sh = nc.dram_tensor("wd_sh", [128, IF], F32, kind="ExternalInput")
    wfc_sh = nc.dram_tensor("wfc_sh", [FF, 128], F32, kind="ExternalInput")
    wpj_sh = nc.dram_tensor("wpj_sh", [D, 512], F32, kind="ExternalInput")
    w1_sh = {k: nc.dram_tensor(k + "_sh", [D, 128], F32, kind="ExternalInput")
             for k in ("wq", "wk", "wv", "wo")}

    cc_in = nc.dram_tensor("cc_in", [128, 520], F32)
    cc_out = nc.dram_tensor("cc_out", [128, 520], F32)
    wfc_shs = nc.dram_tensor("wfc_shs", [FF, 128], BF16)
    wpj_shs = nc.dram_tensor("wpj_shs", [D, 512], BF16)
    w1_shs = {k: nc.dram_tensor(k + "_shs", [D, 128], BF16) for k in ("wq", "wk", "wv", "wo")}
    wsh = nc.dram_tensor("wsh", [SHN], BF16)
    wsh_g = nc.dram_tensor("wsh_g", [8 * SHN], BF16)

    with contextlib.ExitStack() as es:
        E = es.enter_context
        sb = E(nc.sbuf_tensor("arena", [128, SBUF_CAP], U8))
        s_in = Sem(E(nc.semaphore("s_in")))
        s_tr = Sem(E(nc.semaphore("s_tr")))
        s_st = Sem(E(nc.semaphore("s_st")))
        s_wg = Sem(E(nc.semaphore("s_wg")))
        s_g = Sem(E(nc.semaphore("s_g")))
        s_cc = Sem(E(nc.semaphore("s_cc")))
        s_t = Sem(E(nc.semaphore("s_t")))
        s_v = Sem(E(nc.semaphore("s_v")))
        s_s = Sem(E(nc.semaphore("s_s")))
        s_gc = Sem(E(nc.semaphore("s_gc")))
        PS = [E(nc.psum_tensor(f"ps{_i}", [128, 512], F32)) for _i in range(8)]
        block = E(nc.Block())

        bf_base = sb.ap().bitcast(BF16)
        f32_base = sb.ap().bitcast(F32)
        PB = [p.ap().bitcast(BF16) for p in PS]

        def bfv(off, cols, np_=128):
            assert off % 2 == 0
            return bass.AP(bf_base.tensor, bf_base.offset + off // 2,
                           [[bf_base.ap[0][0], np_], [1, cols]])

        def f32v(off, cols, np_=128):
            assert off % 4 == 0
            return bass.AP(f32_base.tensor, f32_base.offset + off // 4,
                           [[f32_base.ap[0][0], np_], [1, cols]])

        def bfap(off, dims, np_=128):
            assert off % 2 == 0
            return bass.AP(bf_base.tensor, bf_base.offset + off // 2,
                           [[bf_base.ap[0][0], np_]] + [list(d) for d in dims])

        def f32ap(off, dims, np_=128):
            assert off % 4 == 0
            return bass.AP(f32_base.tensor, f32_base.offset + off // 4,
                           [[f32_base.ap[0][0], np_]] + [list(d) for d in dims])

        def psap(i, dims, np_=128):
            a = PS[i].ap()
            return bass.AP(a.tensor, a.offset,
                           [[a.ap[0][0], np_]] + [list(d) for d in dims])

        def pbap(i, dims, np_=128):
            a = PB[i]
            return bass.AP(a.tensor, a.offset, [[a.ap[0][0], np_]] + [list(d) for d in dims])

        def sec(name, fn):
            getattr(block, name)(fn)

        _sbuf = []

        def ssec(fn, incs):
            s_s.n += incs
            _sbuf.append(fn)

        def sflush():
            if not _sbuf:
                return
            fns = list(_sbuf)
            del _sbuf[:]

            def _run(s):
                for f in fns:
                    f(s)
            sec("scalar", _run)

        ident = bfv(O_IDENT, 128)
        ones = bfv(O_ONES, 128)
        sel = f32v(O_SEL, KC * 128)
        boutb = bfv(O_BOUT, D)
        kvsb = f32v(O_KV, 520)
        stats = f32v(O_STATS, 34)
        cbc = f32v(O_CB, 16)
        bd = bfv(O_BD, KC * 128)
        zst = bfv(O_ZST, KC * 16)

        def tile_np(i):
            return 128 if i < NT else G

        def rtile(i):
            if i < NT:
                return f32v(O_R + 4096 * i, D)
            return f32v(O_RG, D, np_=G)

        def tile_colbase(i):
            return 128 * i if i < NT else TPC

        def early_finish():
            sflush()
            vv = s_v.n

            def fstall(sy):
                sy.wait_ge(s_v.h, vv)
                sy.wait_ge(s_s.h, s_s.n)
                sy.wait_ge(s_t.h, s_t.n)
                if STAGE >= 4:
                    sy.wait_ge(s_g.h, s_g.n)
                for i in range(NT):
                    sy.dma_start(out=out_d[128 * i: 128 * (i + 1), :],
                                 in_=f32v(O_R + 4096 * i, D)).then_inc(s_st.h, 16); s_st.n += 16
                sy.wait_ge(s_st.h, s_st.n)
            sec("sync", fstall)

        def _loads(sy):
            for i in range(NT):
                sy.dma_start(out=f32v(O_R + 4096 * i, D), in_=x_d[128 * i: 128 * (i + 1), :])\
                  .then_inc(s_in.h, 16); s_in.n += 16
            sy.dma_start(out=f32v(O_RG, D, np_=G), in_=xg_d[:, :]).then_inc(s_in.h, 16); s_in.n += 16
            sy.dma_start(out=bass.AP(f32_base.tensor, f32_base.offset + O_SEL // 4,
                                     [[f32_base.ap[0][0], 16], [1, KC * 128]]),
                         in_=sel_d[:, :]).then_inc(s_in.h, 16); s_in.n += 16
        sec("sync", _loads)
        in_x = s_in.n

        wqkv_q = bfv(O_WQQ, KC * D)
        wqkv_kv = bfv(O_R2, KC * 2048)

        def _gload(g):
            g.dma_start(out=ident[:, :], in_=ident_d[:, :]).then_inc(s_wg.h, 16); s_wg.n += 16
            for c in range(KC):
                g.dma_start(out=bfv(O_WQQ + 2048 * c, D),
                            in_=wqkv_d[128 * c: 128 * (c + 1), 0:D]).then_inc(s_wg.h, 16); s_wg.n += 16
            for c in range(KC):
                g.dma_start(out=bfv(O_R2 + 4096 * c, 2048),
                            in_=wqkv_d[128 * c: 128 * (c + 1), D:3 * D]).then_inc(s_wg.h, 16); s_wg.n += 16
            g.dma_start(out=bass.AP(bf_base.tensor, bf_base.offset + O_BOUT // 2,
                                    [[bf_base.ap[0][0], 1], [1, D]]),
                        in_=bout_d[:, :]).then_inc(s_wg.h, 16); s_wg.n += 16
            g.dma_start(out=wfc_shs[:, :], in_=wfc_sh[:, :]).then_inc(s_wg.h, 16); s_wg.n += 16
            g.dma_start(out=wpj_shs[:, :], in_=wpj_sh[:, :]).then_inc(s_wg.h, 16); s_wg.n += 16
            for k in ("wq", "wk", "wv", "wo"):
                g.dma_start(out=w1_shs[k][:, :], in_=w1_sh[k][:, :]).then_inc(s_wg.h, 16); s_wg.n += 16
        sec("gpsimd", _gload)
        wg_ident = wg_wq = wg_wkv = wg_bout = s_wg.n

        def _consts(v):
            v.memset(ones[:, :], 1.0).then_inc(s_v.h, 1); s_v.n += 1
            v.memset(cbc[:, 0:1], -SOFTMAX_C).then_inc(s_v.h, 1); s_v.n += 1
            v.memset(bd[:, :], 0.0).then_inc(s_v.h, 1); s_v.n += 1
            v.memset(zst[:, :], 0.0).then_inc(s_v.h, 1); s_v.n += 1
        sec("vector", _consts)
        v_consts = s_v.n

        W_STG = O_R3
        W_QB = O_R3 + 11264
        wcoop_done = [0]

        def emit_wcoop():
            for wi, (shd, outd) in enumerate(((wg_sh, None), (wu_sh, None),
                                              (wd_sh, None))):
                in0 = s_in.n
                vg = s_v.n
                tg = s_t.n

                def fld(sy, wi=wi, shd=shd, in0=in0, vg=vg, tg=tg):
                    sy.wait_ge(s_in.h, in0)
                    if vg:
                        sy.wait_ge(s_v.h, vg)
                    if tg:
                        sy.wait_ge(s_t.h, tg)
                    if wi < 2:
                        for j in range(22):
                            sy.dma_start(out=f32v(W_STG + 4 * 128 * j, 128),
                                         in_=shd[128 * j: 128 * (j + 1), :]).then_inc(s_in.h, 16); s_in.n += 16
                    else:
                        sy.dma_start(out=f32v(W_STG, IF), in_=shd[:, :]).then_inc(s_in.h, 16); s_in.n += 16
                sec("sync", fld)
                dma_done = s_in.n

                def fwt(v, dma_done=dma_done):
                    v.wait_ge(s_in.h, dma_done)
                sec("vector", fwt)
                for (c0, c1) in ((0, 1024), (1024, 2048), (2048, 2816)):
                    emit_quant(f32v(W_STG + 4 * c0, c1 - c0),
                               bfv(W_QB + 2 * c0, c1 - c0), 128, c1 - c0, (c1 - c0) // 16,
                               base=O_R3 + 22528)
                v_q = s_v.n

                evacs = []
                for g3 in range(3):
                    jlist = range(8 * g3, min(22, 8 * g3 + 8))
                    pb = 6 + (g3 % 2)
                    te = tr_evac.get(pb, 0)

                    def ftw(t, jlist=jlist, pb=pb, v_q=v_q, te=te):
                        t.wait_ge(s_v.h, max(v_q, te))
                        t.wait_ge(s_wg.h, wg_ident)
                        for jj, j in enumerate(jlist):
                            t.transpose(bass.AP(PB[pb].tensor, PB[pb].offset + 128 * jj,
                                                [[PB[pb].ap[0][0], 128], [1, 128]]),
                                        bfv(W_QB + 2 * 128 * j, 128),
                                        ident[:, :]).then_inc(s_t.h, 1); s_t.n += 1
                    sec("tensor", ftw)
                    td = s_t.n

                    def fte(v, g3=g3, pb=pb, td=td, nj=len(list(jlist))):
                        v.wait_ge(s_t.h, td)
                        srcap = pbap(pb, [[128, nj], [1, 128]])
                        dstap = bfap(W_QB + 11264 * 0 + 2 * (IF + 128 * 8 * g3) if False else 0, [])
                        dstap = bfap(W_QB + 2 * IF + 2 * 128 * 8 * g3, [[128, nj], [1, 128]])
                        v.tensor_copy(dstap, srcap).then_inc(s_v.h, 1); s_v.n += 1
                    sec("vector", fte)
                    tr_evac[pb] = s_v.n
                v_T = s_v.n

                st0 = s_in.n

                def fst2(sy, wi=wi, outd=outd, v_T=v_T, st0=st0):
                    sy.wait_ge(s_in.h, st0)
                    sy.wait_ge(s_v.h, v_T)
                    if wi < 2:
                        dst = bass.AP(wsh.ap().tensor, SH_GATE if wi == 0 else SH_UP,
                                      [[IF, 128], [1, IF]])
                        sy.dma_start(out=dst, in_=bfv(W_QB + 2 * IF, IF)).then_inc(s_in.h, 16); s_in.n += 16
                    else:
                        srcap = bfap(W_QB + 2 * IF, [[128, 22], [1, 128]])
                        dstap = bass.AP(wsh.ap().tensor, SH_DOWN,
                                        [[128, 128], [128 * 128, 22], [1, 128]])
                        sy.dma_start(out=dstap, in_=srcap).then_inc(s_in.h, 16); s_in.n += 16
                sec("sync", fst2)

            tr0 = s_tr.n
            stg_t = bfv(W_STG, 4096)
            vg2 = s_v.n
            tg2 = s_t.n

            def ftr1(sy, tr0=tr0, vg2=vg2, tg2=tg2):
                sy.wait_ge(s_tr.h, tr0)
                sy.wait_ge(s_v.h, vg2)
                sy.wait_ge(s_t.h, tg2)
                sy.wait_ge(s_wg.h, wg_bout)
                sy.dma_start_transpose(stg_t[:, :], wfc_shs[0:FF, 0:128]).then_inc(s_tr.h, 16); s_tr.n += 16
            sec("sync", ftr1)
            tr1 = s_tr.n
            st1 = s_in.n

            def fsv1(sy, tr1=tr1, st1=st1):
                sy.wait_ge(s_in.h, st1)
                sy.wait_ge(s_tr.h, tr1)
                sy.dma_start(out=bass.AP(wsh.ap().tensor, SH_FCT, [[FF, 128], [1, FF]]),
                             in_=stg_t[:, :]).then_inc(s_in.h, 16); s_in.n += 16
            sec("sync", fsv1)

            tr0 = s_tr.n
            st1b = s_in.n

            def ftr2(sy, tr0=tr0, st1b=st1b):
                sy.wait_ge(s_tr.h, tr0)
                sy.wait_ge(s_in.h, st1b)
                for t4 in range(4):
                    sy.dma_start_transpose(bfv(W_STG + 2048 * t4, 1024),
                                           wpj_shs[0:D, 128 * t4: 128 * (t4 + 1)]).then_inc(s_tr.h, 16); s_tr.n += 16
            sec("sync", ftr2)
            tr2 = s_tr.n
            st2 = s_in.n

            def fsv2(sy, tr2=tr2, st2=st2):
                sy.wait_ge(s_in.h, st2)
                sy.wait_ge(s_tr.h, tr2)
                srcap = bfap(W_STG, [[1024, 4], [1, 1024]])
                dstap = bass.AP(wsh.ap().tensor, SH_PJT,
                                [[1024, 128], [128 * 1024, 4], [1, 1024]])
                sy.dma_start(out=dstap, in_=srcap).then_inc(s_in.h, 16); s_in.n += 16
            sec("sync", fsv2)

            tr0 = s_tr.n
            st3 = s_in.n

            def ftr3(sy, tr0=tr0, st3=st3):
                sy.wait_ge(s_tr.h, tr0)
                sy.wait_ge(s_in.h, st3)
                for ki, k in enumerate(("wq", "wk", "wv", "wo")):
                    sy.dma_start_transpose(bfv(W_STG + 2048 * ki, 1024),
                                           w1_shs[k][0:D, 0:128]).then_inc(s_tr.h, 16); s_tr.n += 16
            sec("sync", ftr3)
            tr3 = s_tr.n
            st4 = s_in.n

            def fsv3(sy, tr3=tr3, st4=st4):
                sy.wait_ge(s_in.h, st4)
                sy.wait_ge(s_tr.h, tr3)
                for ki in range(4):
                    sy.dma_start(out=bass.AP(wsh.ap().tensor, SH_WQ + 131072 * ki,
                                             [[1024, 128], [1, 1024]]),
                                 in_=bfv(W_STG + 2048 * ki, 1024)).then_inc(s_in.h, 16); s_in.n += 16
            sec("sync", fsv3)
            wcoop_done[0] = s_in.n

        xnf = f32v(O_XNF, D)
        bnst = f32v(O_BNST, 12)
        mvv = f32v(O_MV, 2)
        vep = f32v(O_MV + 8, 1)
        rstd = f32v(O_MV + 12, 1)
        xnbf = bfv(O_XNBF, D)
        elu1 = f32v(O_E1, 512)
        elu2 = f32v(O_E2, 512)

        vepal = f32v(O_E1, 17)

        def emit_ln_stats():
            def fv(v):
                v.wait_ge(s_in.h, in_x)
                v.wait_ge(s_v.h, s_v.n)
                v.wait_ge(s_s.h, s_s.n)
                for i in range(17):
                    np_ = tile_np(i)
                    srct = rtile(i)
                    v.bn_stats(bnst[0:np_, 0:6], srct[0:np_, 0:512]).then_inc(s_v.h, 1); s_v.n += 1
                    v.bn_stats(bnst[0:np_, 6:12], srct[0:np_, 512:1024]).then_inc(s_v.h, 1); s_v.n += 1
                    v.wait_ge(s_v.h, s_v.n)
                    v.bn_aggr(mvv[0:np_, :], bnst[0:np_, :]).then_inc(s_v.h, 1); s_v.n += 1
                    v.wait_ge(s_v.h, s_v.n)
                    v.tensor_copy(stats[0:np_, 2 * i: 2 * i + 1], mvv[0:np_, 0:1]).then_inc(s_v.h, 1); s_v.n += 1
                    v.tensor_scalar(vepal[0:np_, i:i + 1], mvv[0:np_, 1:2], EPS, None,
                                    op0=ALU.add).then_inc(s_v.h, 1); s_v.n += 1
            sec("vector", fv)
            v1 = s_v.n
            mid = s_s.n + 17

            def fs(s, v1=v1, mid=mid):
                s.wait_ge(s_v.h, v1)
                for i in range(17):
                    np_ = tile_np(i)
                    s.activation(stats[0:np_, 2 * i + 1: 2 * i + 2], vepal[0:np_, i:i + 1],
                                 AF.Ln).then_inc(s_s.h, 1)
                s.wait_ge(s_s.h, mid)
                for i in range(17):
                    np_ = tile_np(i)
                    s.activation(stats[0:np_, 2 * i + 1: 2 * i + 2],
                                 stats[0:np_, 2 * i + 1: 2 * i + 2], AF.Exp,
                                 scale=-0.5).then_inc(s_s.h, 1)
            ssec(fs, 34)

        def emit_ln(i, save_stats):
            np_ = tile_np(i)
            srct = rtile(i)
            s1 = s_s.n

            def fv2(v):
                v.wait_ge(s_s.h, s1)
                v.wait_ge(s_v.h, s_v.n)
                v.wait_ge(s_t.h, s_t.n)
                v.tensor_scalar(xnbf[0:np_, :], srct[0:np_, :], stats[0:np_, 2 * i: 2 * i + 1],
                                stats[0:np_, 2 * i + 1: 2 * i + 2], op0=ALU.subtract,
                                op1=ALU.mult).then_inc(s_v.h, 1); s_v.n += 1
            sec("vector", fv2)

        tr_evac = {}

        def emit_tm_to_fm(src_bf, dst_off, colbase, np_, pbank):
            v_src = s_v.n

            def ft(t):
                t.wait_ge(s_v.h, v_src)
                t.wait_ge(s_wg.h, wg_ident)
                if pbank in tr_evac:
                    t.wait_ge(s_v.h, tr_evac[pbank])
                for c in range(KC):
                    t.transpose(bass.AP(PB[pbank].tensor, PB[pbank].offset + 128 * c,
                                        [[PB[pbank].ap[0][0], 128], [1, np_]]),
                                src_bf[0:np_, 128 * c: 128 * (c + 1)],
                                ident[0:np_, 0:np_]).then_inc(s_t.h, 1)
                    s_t.n += 1
            sec("tensor", ft)
            t1 = s_t.n

            def fe(v):
                v.wait_ge(s_t.h, t1)
                srcap = pbap(pbank, [[128, KC], [1, np_]])
                dstap = bfap(dst_off + 2 * colbase, [[XT, KC], [1, np_]])
                v.tensor_copy(dstap, srcap).then_inc(s_v.h, 1); s_v.n += 1
            sec("vector", fe)
            tr_evac[pbank] = s_v.n

        def emit_quant(src_f32, dst_bf, np_, F, blocks, base=None, tguard=0, destructive=False,
                       geng=False):
            if base is None:
                offs = (O_QT, O_QM, O_QS, O_QR, O_QSB, O_QCA, O_QCB, O_QSG, O_QTMP)
            else:
                assert not geng
                offs = (base, base + 4096, base + 4352, base + 4608, base + 4864,
                        base + 4992, base + 7040, base + 9088, None)
            t_f = src_f32 if destructive else f32v(offs[0], F)
            m_f = f32v(offs[1], blocks)
            sc_f = f32v(offs[2], blocks)
            r_f = f32v(offs[3], blocks)
            s_b = bfv(offs[4], blocks)
            ca = bfv(offs[5], F)
            cbch = bfv(offs[6], F)
            sg = bfv(offs[7], F)
            tmp = bfv(offs[8], F) if offs[8] is not None else None

            if geng:
                def fvr(v, tguard=tguard):
                    v.wait_ge(s_v.h, s_v.n)
                    v.wait_ge(s_s.h, s_s.n)
                    if tguard:
                        v.wait_ge(s_t.h, tguard)
                    src3 = bass.AP(src_f32.tensor, src_f32.offset,
                                   [[src_f32.ap[0][0], np_], [16, blocks], [1, 16]])
                    v.tensor_reduce(m_f[0:np_, :], src3, axis=AXX, op=ALU.max,
                                    apply_absolute_value=True).then_inc(s_v.h, 1); s_v.n += 1
                    v.wait_ge(s_v.h, s_v.n)
                    v.tensor_scalar(sc_f[0:np_, :], m_f[0:np_, :], 1.0 / 6.0, 1e-12,
                                    op0=ALU.mult, op1=ALU.max).then_inc(s_v.h, 1); s_v.n += 1
                    v.wait_ge(s_v.h, s_v.n)
                    v.reciprocal(r_f[0:np_, :], sc_f[0:np_, :]).then_inc(s_v.h, 1); s_v.n += 1
                    v.tensor_copy(s_b[0:np_, :], sc_f[0:np_, :]).then_inc(s_v.h, 1); s_v.n += 1
                sec("vector", fvr)
                vr = s_v.n

                def fg2(g, vr=vr):
                    g.wait_ge(s_v.h, vr)
                    g.wait_ge(s_gc.h, s_gc.n)
                    rb = bass.AP(r_f.tensor, r_f.offset,
                                 [[r_f.ap[0][0], np_], [1, blocks], [0, 16]])
                    g.tensor_tensor(t_f[0:np_, :], src_f32[0:np_, 0:F], rb,
                                    op=ALU.mult).then_inc(s_gc.h, 1); s_gc.n += 1
                sec("gpsimd", fg2)
                g2 = s_gc.n
                smid = s_s.n + 1

                def fsgg(s, g2=g2, smid=smid):
                    s.wait_ge(s_gc.h, g2)
                    s.activation(sg[0:np_, :], t_f[0:np_, :], AF.Sign).then_inc(s_s.h, 1)
                    s.wait_ge(s_s.h, smid)
                    s.activation(t_f[0:np_, :], t_f[0:np_, :], AF.Abs).then_inc(s_s.h, 1)
                ssec(fsgg, 2)
                s1 = s_s.n
                a_f = t_f
                gcb, gtmp = cbch, tmp

                def fg3(g, s1=s1):
                    g.wait_ge(s_s.h, s1)
                    g.wait_ge(s_v.h, s_v.n)
                    g.wait_ge(s_gc.h, s_gc.n)
                    g.tensor_scalar(gcb[0:np_, :], a_f[0:np_, :], 2.5, None,
                                    op0=ALU.is_gt).then_inc(s_gc.h, 1); s_gc.n += 1
                    g.tensor_scalar(gtmp[0:np_, :], a_f[0:np_, :], 3.5, None,
                                    op0=ALU.is_gt).then_inc(s_gc.h, 1); s_gc.n += 1
                    g.wait_ge(s_gc.h, s_gc.n)
                    g.tensor_tensor(gcb[0:np_, :], gcb[0:np_, :], gtmp[0:np_, :],
                                    op=ALU.add).then_inc(s_gc.h, 1); s_gc.n += 1
                    g.wait_ge(s_gc.h, s_gc.n)
                    g.tensor_scalar(gtmp[0:np_, :], a_f[0:np_, :], 5.0, 2.0,
                                    op0=ALU.is_gt, op1=ALU.mult).then_inc(s_gc.h, 1); s_gc.n += 1
                    g.wait_ge(s_gc.h, s_gc.n)
                    g.tensor_tensor(gcb[0:np_, :], gcb[0:np_, :], gtmp[0:np_, :],
                                    op=ALU.add).then_inc(s_gc.h, 1); s_gc.n += 1
                sec("gpsimd", fg3)
                g3 = s_gc.n

                def fvA(v, s1=s1, g3=g3):
                    v.wait_ge(s_s.h, s1)
                    v.wait_ge(s_v.h, s_v.n)
                    v.tensor_scalar(ca[0:np_, :], a_f[0:np_, :], 0.25, None,
                                    op0=ALU.is_gt).then_inc(s_v.h, 1); s_v.n += 1
                    for th in (0.75, 1.25, 1.75):
                        v.wait_ge(s_v.h, s_v.n)
                        v.scalar_tensor_tensor(ca[0:np_, :], a_f[0:np_, :], th, ca[0:np_, :],
                                               op0=ALU.is_gt, op1=ALU.add).then_inc(s_v.h, 1); s_v.n += 1
                    v.wait_ge(s_gc.h, g3)
                    v.wait_ge(s_v.h, s_v.n)
                    v.scalar_tensor_tensor(ca[0:np_, :], ca[0:np_, :], 0.5, gcb[0:np_, :],
                                           op0=ALU.mult, op1=ALU.add).then_inc(s_v.h, 1); s_v.n += 1
                    sbb = bass.AP(s_b.tensor, s_b.offset,
                                  [[s_b.ap[0][0], np_], [1, blocks], [0, 16]])
                    v.wait_ge(s_v.h, s_v.n)
                    v.tensor_tensor(sg[0:np_, :], sg[0:np_, :], sbb,
                                    op=ALU.mult).then_inc(s_v.h, 1); s_v.n += 1
                    v.wait_ge(s_v.h, s_v.n)
                    v.tensor_tensor(dst_bf[0:np_, 0:F], ca[0:np_, :], sg[0:np_, :],
                                    op=ALU.mult).then_inc(s_v.h, 1); s_v.n += 1
                sec("vector", fvA)
                return

            def fv(v):
                v.wait_ge(s_v.h, s_v.n)
                v.wait_ge(s_s.h, s_s.n)
                if tguard:
                    v.wait_ge(s_t.h, tguard)
                src3 = bass.AP(src_f32.tensor, src_f32.offset,
                               [[src_f32.ap[0][0], np_], [16, blocks], [1, 16]])
                v.tensor_reduce(m_f[0:np_, :], src3, axis=AXX, op=ALU.max,
                                apply_absolute_value=True).then_inc(s_v.h, 1); s_v.n += 1
                v.wait_ge(s_v.h, s_v.n)
                v.tensor_scalar(sc_f[0:np_, :], m_f[0:np_, :], 1.0 / 6.0, 1e-12,
                                op0=ALU.mult, op1=ALU.max).then_inc(s_v.h, 1); s_v.n += 1
                v.wait_ge(s_v.h, s_v.n)
                v.reciprocal(r_f[0:np_, :], sc_f[0:np_, :]).then_inc(s_v.h, 1); s_v.n += 1
                v.tensor_copy(s_b[0:np_, :], sc_f[0:np_, :]).then_inc(s_v.h, 1); s_v.n += 1
                v.wait_ge(s_v.h, s_v.n)
                rb = bass.AP(r_f.tensor, r_f.offset,
                             [[r_f.ap[0][0], np_], [1, blocks], [0, 16]])
                v.tensor_tensor(t_f[0:np_, :], src_f32[0:np_, 0:F], rb,
                                op=ALU.mult).then_inc(s_v.h, 1); s_v.n += 1
            sec("vector", fv)
            v1 = s_v.n
            smid = s_s.n + 1

            def fsg(s, v1=v1, smid=smid):
                s.wait_ge(s_v.h, v1)
                s.activation(sg[0:np_, :], t_f[0:np_, :], AF.Sign).then_inc(s_s.h, 1)
                s.wait_ge(s_s.h, smid)
                s.activation(t_f[0:np_, :], t_f[0:np_, :], AF.Abs).then_inc(s_s.h, 1)
            ssec(fsg, 2)
            s1 = s_s.n

            if tmp is not None:
                def fv2(v):
                    v.wait_ge(s_s.h, s1)
                    a_f = t_f
                    v.wait_ge(s_v.h, s_v.n)
                    v.tensor_scalar(ca[0:np_, :], a_f[0:np_, :], 0.25, None,
                                    op0=ALU.is_gt).then_inc(s_v.h, 1); s_v.n += 1
                    v.wait_ge(s_v.h, s_v.n)
                    v.tensor_scalar(tmp[0:np_, :], a_f[0:np_, :], 0.75, None,
                                    op0=ALU.is_gt).then_inc(s_v.h, 1); s_v.n += 1
                    v.wait_ge(s_v.h, s_v.n)
                    v.tensor_tensor(ca[0:np_, :], ca[0:np_, :], tmp[0:np_, :],
                                    op=ALU.add).then_inc(s_v.h, 1); s_v.n += 1
                    v.wait_ge(s_v.h, s_v.n)
                    v.tensor_scalar(cbch[0:np_, :], a_f[0:np_, :], 1.25, None,
                                    op0=ALU.is_gt).then_inc(s_v.h, 1); s_v.n += 1
                    v.wait_ge(s_v.h, s_v.n)
                    v.tensor_scalar(tmp[0:np_, :], a_f[0:np_, :], 1.75, None,
                                    op0=ALU.is_gt).then_inc(s_v.h, 1); s_v.n += 1
                    v.wait_ge(s_v.h, s_v.n)
                    v.tensor_tensor(cbch[0:np_, :], cbch[0:np_, :], tmp[0:np_, :],
                                    op=ALU.add).then_inc(s_v.h, 1); s_v.n += 1
                    v.wait_ge(s_v.h, s_v.n)
                    v.tensor_tensor(ca[0:np_, :], ca[0:np_, :], cbch[0:np_, :],
                                    op=ALU.add).then_inc(s_v.h, 1); s_v.n += 1
                    v.wait_ge(s_v.h, s_v.n)
                    v.tensor_scalar(cbch[0:np_, :], a_f[0:np_, :], 2.5, None,
                                    op0=ALU.is_gt).then_inc(s_v.h, 1); s_v.n += 1
                    v.wait_ge(s_v.h, s_v.n)
                    v.tensor_scalar(tmp[0:np_, :], a_f[0:np_, :], 3.5, None,
                                    op0=ALU.is_gt).then_inc(s_v.h, 1); s_v.n += 1
                    v.wait_ge(s_v.h, s_v.n)
                    v.tensor_tensor(cbch[0:np_, :], cbch[0:np_, :], tmp[0:np_, :],
                                    op=ALU.add).then_inc(s_v.h, 1); s_v.n += 1
                    v.wait_ge(s_v.h, s_v.n)
                    v.tensor_scalar(tmp[0:np_, :], a_f[0:np_, :], 5.0, 2.0,
                                    op0=ALU.is_gt, op1=ALU.mult).then_inc(s_v.h, 1); s_v.n += 1
                    v.wait_ge(s_v.h, s_v.n)
                    v.tensor_tensor(cbch[0:np_, :], cbch[0:np_, :], tmp[0:np_, :],
                                    op=ALU.add).then_inc(s_v.h, 1); s_v.n += 1
                    v.wait_ge(s_v.h, s_v.n)
                    v.wait_ge(s_v.h, s_v.n)
                    v.scalar_tensor_tensor(ca[0:np_, :], ca[0:np_, :], 0.5, cbch[0:np_, :],
                                           op0=ALU.mult, op1=ALU.add).then_inc(s_v.h, 1); s_v.n += 1
                    sbb = bass.AP(s_b.tensor, s_b.offset,
                                  [[s_b.ap[0][0], np_], [1, blocks], [0, 16]])
                    v.wait_ge(s_v.h, s_v.n)
                    v.tensor_tensor(sg[0:np_, :], sg[0:np_, :], sbb, op=ALU.mult).then_inc(s_v.h, 1); s_v.n += 1
                    v.wait_ge(s_v.h, s_v.n)
                    v.tensor_tensor(dst_bf[0:np_, 0:F], ca[0:np_, :], sg[0:np_, :],
                                    op=ALU.mult).then_inc(s_v.h, 1); s_v.n += 1
                sec("vector", fv2)
            else:
                def fv2(v):
                    v.wait_ge(s_s.h, s1)
                    a_f = t_f
                    v.tensor_scalar(ca[0:np_, :], a_f[0:np_, :], FP4_TH_A[0], None,
                                    op0=ALU.is_gt).then_inc(s_v.h, 1); s_v.n += 1
                    for th in FP4_TH_A[1:]:
                        v.wait_ge(s_v.h, s_v.n)
                        v.scalar_tensor_tensor(ca[0:np_, :], a_f[0:np_, :], th, ca[0:np_, :],
                                               op0=ALU.is_gt, op1=ALU.add).then_inc(s_v.h, 1); s_v.n += 1
                    v.tensor_scalar(cbch[0:np_, :], a_f[0:np_, :], FP4_TH_B[0], None,
                                    op0=ALU.is_gt).then_inc(s_v.h, 1); s_v.n += 1
                    for th in FP4_TH_B[1:]:
                        v.wait_ge(s_v.h, s_v.n)
                        v.scalar_tensor_tensor(cbch[0:np_, :], a_f[0:np_, :], th, cbch[0:np_, :],
                                               op0=ALU.is_gt, op1=ALU.add).then_inc(s_v.h, 1); s_v.n += 1
                    v.wait_ge(s_v.h, s_v.n)
                    v.scalar_tensor_tensor(ca[0:np_, :], ca[0:np_, :], 0.5, cbch[0:np_, :],
                                           op0=ALU.mult, op1=ALU.add).then_inc(s_v.h, 1); s_v.n += 1
                    sbb = bass.AP(s_b.tensor, s_b.offset,
                                  [[s_b.ap[0][0], np_], [1, blocks], [0, 16]])
                    v.wait_ge(s_v.h, s_v.n)
                    v.tensor_tensor(sg[0:np_, :], sg[0:np_, :], sbb, op=ALU.mult).then_inc(s_v.h, 1); s_v.n += 1
                    v.wait_ge(s_v.h, s_v.n)
                    v.tensor_tensor(dst_bf[0:np_, 0:F], ca[0:np_, :], sg[0:np_, :],
                                    op=ALU.mult).then_inc(s_v.h, 1); s_v.n += 1
                sec("vector", fv2)

        def emit_elu1(bank, dst_bf, np_, n):
            t0 = s_t.n

            def fv(v):
                v.wait_ge(s_t.h, t0)
                v.wait_ge(s_s.h, s_s.n)
                v.wait_ge(s_v.h, s_v.n)
                v.tensor_scalar(elu1[0:np_, 0:n], PS[bank][0:np_, 0:n], 0.0, None,
                                op0=ALU.min).then_inc(s_v.h, 1); s_v.n += 1
            sec("vector", fv)
            v1 = s_v.n

            def fs(s, v1=v1):
                s.wait_ge(s_v.h, v1)
                s.activation(elu2[0:np_, 0:n], elu1[0:np_, 0:n], AF.Exp).then_inc(s_s.h, 1)
            ssec(fs, 1)
            s1 = s_s.n

            def fv2(v):
                v.wait_ge(s_s.h, s1)
                v.scalar_tensor_tensor(dst_bf[0:np_, 0:n], PS[bank][0:np_, 0:n], 0.0,
                                       elu2[0:np_, 0:n], op0=ALU.max,
                                       op1=ALU.add).then_inc(s_v.h, 1); s_v.n += 1
            sec("vector", fv2)
            return s_v.n

        XN_FM = O_R1

        emit_ln_stats()
        for i in range(17):
            emit_ln(i, save_stats=True)
            emit_tm_to_fm(xnbf, XN_FM, tile_colbase(i), tile_np(i), 6 + (i % 2))
        emit_wcoop()

        def xn_fm_view(k, a, b):
            return bfv(XN_FM + 2 * (XT * k + a), b - a)

        if STAGE < 2:
            early_finish()
            return nc

        sflush()
        Q_BF = O_R3
        bank_free = {}
        for gi, (a, b) in enumerate(TGROUPS):
            n = b - a
            for c in range(KC):
                bank = (gi * KC + c) % 4
                tw = tr_evac.get(6, 0)
                tw2 = tr_evac.get(7, 0)
                bf = bank_free.get(bank, 0)

                def fmm(t, c=c, a=a, b=b, n=n, bank=bank, tw=tw, tw2=tw2, bf=bf):
                    t.wait_ge(s_wg.h, wg_wq)
                    t.wait_ge(s_in.h, wcoop_done[0])
                    t.wait_ge(s_v.h, max(tw, tw2, bf))
                    for k in range(KC):
                        mi = t.matmul(PS[bank][:, 0:n],
                                 bfv(O_WQQ + 2048 * k + 256 * c, 128),
                                 xn_fm_view(k, a, b),
                                 start=(k == 0), stop=(k == KC - 1))
                    mi.then_inc(s_t.h, 1); s_t.n += 1
                sec("tensor", fmm)
                bank_free[bank] = emit_elu1(bank, bfv(Q_BF + 2 * (XT * c + a), n), 128, n)

        def q_bf_view(c, a, b):
            return bfv(Q_BF + 2 * (XT * c + a), b - a)

        if STAGE < 3:
            early_finish()
            return nc

        sflush()
        k_bf = bfv(O_KBF, D)
        v_ex = bfv(O_VEX, 1040)

        def fvo(v):
            v.wait_ge(s_v.h, s_v.n)
            v.memset(v_ex[:, :], 1.0).then_inc(s_v.h, 1); s_v.n += 1
        sec("vector", fvo)

        for i in range(NT):
            vfree = s_v.n

            def fkv(t, i=i, vfree=vfree):
                t.wait_ge(s_wg.h, wg_wkv)
                t.wait_ge(s_v.h, vfree)
                for half in range(2):
                    for k in range(KC):
                        mi = t.matmul(PS[half][:, :],
                                 xn_fm_view(k, 128 * i, 128 * (i + 1)),
                                 bfv(O_R2 + 4096 * k + 1024 * half, 512),
                                 start=(k == 0), stop=(k == KC - 1))
                for half in range(2):
                    for k in range(KC):
                        mi = t.matmul(PS[2 + half][:, :],
                                 xn_fm_view(k, 128 * i, 128 * (i + 1)),
                                 bfv(O_R2 + 4096 * k + 2048 + 1024 * half, 512),
                                 start=(k == 0), stop=(k == KC - 1))
                mi.then_inc(s_t.h, 1); s_t.n += 1
            sec("tensor", fkv)
            emit_elu1(0, bfv(O_KBF, 512), 128, 512)
            emit_elu1(1, bfv(O_KBF + 1024, 512), 128, 512)
            t_v = s_t.n

            def fvv(v, t_v=t_v):
                v.wait_ge(s_t.h, t_v)
                v.wait_ge(s_v.h, s_v.n)
                for half in range(2):
                    dst = bfap(O_VEX + 2 * 520 * half, [[65, 8], [1, 64]])
                    src = psap(2 + half, [[64, 8], [1, 64]])
                    v.tensor_copy(dst, src).then_inc(s_v.h, 1); s_v.n += 1
            sec("vector", fvv)
            v_kv = s_v.n

            te6 = tr_evac.get(6, 0)
            te7 = tr_evac.get(7, 0)

            def fkvmm(t, i=i, v_kv=v_kv, te6=te6, te7=te7):
                t.wait_ge(s_v.h, v_kv)
                if i == 0:
                    t.wait_ge(s_v.h, max(te6, te7))
                for p in range(8):
                    bank = 4 + p // 2
                    colb = 130 * (p % 2)
                    first = (i == 0 and p % 2 == 0)
                    last = (i == NT - 1 and p % 2 == 1)
                    mi = t.matmul(PS[bank][:, colb:colb + 130],
                                  k_bf[:, 128 * p: 128 * (p + 1)],
                                  v_ex[:, 130 * p: 130 * (p + 1)], start=first, stop=last)
                mi.then_inc(s_t.h, 1); s_t.n += 1
            sec("tensor", fkvmm)

        if STAGE < 4:
            early_finish()
            return nc

        t_kv = s_t.n

        def fkvout(v):
            v.wait_ge(s_t.h, t_kv)
            for h in range(H):
                p = h // 2
                bank = 4 + p // 2
                colb = 130 * (p % 2) + 65 * (h % 2)
                po = 64 * (h % 2)
                dst_c = 260 * (h // 8) + 65 * ((h // 2) % 4)
                v.tensor_copy(kvsb[po:po + 64, dst_c:dst_c + 65],
                              PS[bank][po:po + 64, colb:colb + 65]).then_inc(s_v.h, 1); s_v.n += 1
        sec("vector", fkvout)
        v_kvsb = s_v.n

        def fag(g):
            g.wait_ge(s_in.h, wcoop_done[0])
            g.collective_compute(
                "AllGather", ALU.bypass,
                replica_groups=[list(range(8))],
                ins=[wsh.ap().opt()],
                outs=[wsh_g.ap().opt()],
            ).then_inc(s_cc.h, 1); s_cc.n += 1
        sec("gpsimd", fag)
        cc_ag = s_cc.n

        def fcct(g):
            g.wait_ge(s_v.h, v_kvsb)
            g.dma_start(out=cc_in[:, :], in_=kvsb[:, :]).then_inc(s_g.h, 16); s_g.n += 16
            g.wait_ge(s_g.h, s_g.n)
            g.collective_compute(
                "AllReduce", ALU.add,
                replica_groups=[[0, 1, 2, 3], [4, 5, 6, 7]],
                ins=[cc_in.ap().opt()],
                outs=[cc_out.ap().opt()],
            ).then_inc(s_cc.h, 1); s_cc.n += 1
        sec("gpsimd", fcct)
        cc_kv = s_cc.n
        _gkv = [0]

        def emit_kv_readback():
            g0 = s_g.n

            def fccr(g, g0=g0):
                g.wait_ge(s_cc.h, cc_kv)
                g.wait_ge(s_g.h, g0)
                g.dma_start(out=kvsb[:, :], in_=cc_out[:, :]).then_inc(s_g.h, 16); s_g.n += 16
            sec("gpsimd", fccr)
            _gkv[0] = s_g.n
        g_kv = None

        wg0 = s_wg.n

        def fgwout(g):
            g.wait_ge(s_wg.h, wg0)
            g.wait_ge(s_t.h, t_kv)
            for c in range(KC):
                g.dma_start(out=bfv(O_R2 + 2048 * c, D),
                            in_=wout_d[128 * c: 128 * (c + 1), :]).then_inc(s_wg.h, 16); s_wg.n += 16
        sec("gpsimd", fgwout)
        wg_wout = s_wg.n



        sflush()
        XQ_FM = O_R1
        tr_evac.clear()
        def xq_fm_view(k, a, b):
            return bfv(XQ_FM + 2 * (XT * k + a), b - a)
        xsrc = f32v(O_QA, D)
        for i in range(17):
            np_ = tile_np(i)
            vguard = s_v.n

            in0 = s_in.n

            def fxd(sy, i=i, np_=np_, vguard=vguard, in0=in0):
                sy.wait_ge(s_in.h, in0)
                sy.wait_ge(s_v.h, vguard)
                src = x_d[128 * i: 128 * (i + 1), :] if i < NT else xg_d[:, :]
                sy.dma_start(out=f32v(O_QA, D, np_=np_), in_=src).then_inc(s_in.h, 16); s_in.n += 16
            sec("sync", fxd)
            d_x = s_in.n

            def fx(v, i=i, np_=np_, d_x=d_x):
                v.wait_ge(s_in.h, d_x)
                v.wait_ge(s_v.h, s_v.n)
                v.wait_ge(s_s.h, s_s.n)
                v.tensor_scalar(xsrc[0:np_, :], xsrc[0:np_, :],
                                stats[0:np_, 2 * i: 2 * i + 1],
                                stats[0:np_, 2 * i + 1: 2 * i + 2],
                                op0=ALU.subtract, op1=ALU.mult).then_inc(s_v.h, 1); s_v.n += 1
            sec("vector", fx)
            emit_quant(xsrc, xnbf, np_, D, 64, destructive=True, geng=True)
            emit_tm_to_fm(xnbf, XQ_FM, tile_colbase(i), np_, 6 + (i % 2))
            if i == 3:
                emit_kv_readback()


        def fbd(v):
            v.wait_ge(s_g.h, _gkv[0])
            v.wait_ge(s_v.h, v_consts)
            kv_all = f32_base[:, O_KV // 4: O_KV // 4 + 520]
            bd_all = bf_base[:, O_BD // 2: O_BD // 2 + KC * 128]
            zst_all = bf_base[:, O_ZST // 2: O_ZST // 2 + KC * 16]
            for c in range(KC):
                for hh, po in ((2 * c, 0), (2 * c + 1, 64)):
                    base = 260 * (hh // 8) + 65 * ((hh // 2) % 4)
                    sp = 64 * (hh % 2)
                    v.tensor_copy(bd_all[po:po + 64, 128 * c + po: 128 * c + po + 64],
                                  kv_all[sp:sp + 64, base: base + 64]).then_inc(s_v.h, 1); s_v.n += 1
                    v.tensor_copy(zst_all[po:po + 64, 16 * c + hh: 16 * c + hh + 1],
                                  kv_all[sp:sp + 64, base + 64: base + 65]).then_inc(s_v.h, 1); s_v.n += 1
        sec("vector", fbd)
        v_bd = s_v.n

        attn_b = bfv(O_ATT, KC * 512)
        denr = f32v(O_DEN, 512, np_=16)
        recb = f32v(O_RCB, 512)
        c_ev = [v_bd]

        for gi, (a, b) in enumerate(TGROUPS):
            n = b - a
            cv0 = c_ev[0]

            def fden(t, a=a, b=b, n=n, cv0=cv0):
                t.wait_ge(s_v.h, max(v_bd, cv0))
                for c in range(KC):
                    mi = t.matmul(PS[2][0:16, 0:n], zst[:, 16 * c: 16 * (c + 1)],
                             q_bf_view(c, a, b), start=(c == 0), stop=(c == KC - 1))
                mi.then_inc(s_t.h, 1); s_t.n += 1
            sec("tensor", fden)
            t_den = s_t.n

            def fdr(v, n=n, t_den=t_den):
                v.wait_ge(s_t.h, t_den)
                v.tensor_scalar(denr[0:16, 0:n], PS[2][0:16, 0:n], 1e-6, None,
                                op0=ALU.add).then_inc(s_v.h, 1); s_v.n += 1
                v.wait_ge(s_v.h, s_v.n)
                v.reciprocal(denr[0:16, 0:n], denr[0:16, 0:n]).then_inc(s_v.h, 1); s_v.n += 1
            sec("vector", fdr)
            v_den = s_v.n

            for c in range(KC):
                nb = c % 2
                cv = c_ev[0]

                def fnum(t, c=c, a=a, b=b, n=n, nb=nb, v_den=v_den, cv=cv):
                    t.wait_ge(s_v.h, max(v_den, cv))
                    mi = t.matmul(PS[nb][:, 0:n], bd[:, 128 * c: 128 * (c + 1)],
                             q_bf_view(c, a, b), start=True, stop=True)
                    mi = t.matmul(PS[3][:, 0:n], sel[0:16, 128 * c: 128 * (c + 1)],
                             denr[0:16, 0:n], start=True, stop=True)
                    mi.then_inc(s_t.h, 1); s_t.n += 1
                sec("tensor", fnum)
                t_num = s_t.n

                def fnev(v, c=c, n=n, nb=nb, t_num=t_num):
                    v.wait_ge(s_t.h, t_num)
                    v.tensor_copy(recb[:, 0:n], PS[3][:, 0:n]).then_inc(s_v.h, 1); s_v.n += 1
                    v.wait_ge(s_v.h, s_v.n)
                    v.tensor_tensor(attn_b[:, 512 * c: 512 * c + n], PS[nb][:, 0:n],
                                    recb[:, 0:n], op=ALU.mult).then_inc(s_v.h, 1); s_v.n += 1
                sec("vector", fnev)
                c_ev[0] = s_v.n
            v_attn = s_v.n

            for ti in range((n + 127) // 128):
                i = (a + 128 * ti) // 128 if a < TPC else NT
                np_ = tile_np(i)
                cv = c_ev[0]

                def fwo(t, ti=ti, np_=np_, v_attn=v_attn, cv=cv):
                    t.wait_ge(s_v.h, max(v_attn, cv, v_consts))
                    t.wait_ge(s_wg.h, wg_wout)
                    for half in range(2):
                        for c in range(KC):
                            mi = t.matmul(PS[4 + half][0:np_, :],
                                     bfv(O_ATT + 2 * (512 * c + 128 * ti), np_),
                                     bfv(O_R2 + 2048 * c + 1024 * half, 512),
                                     start=(c == 0), stop=False)
                        mi = t.matmul(PS[4 + half][0:np_, :], ones[0:1, 0:np_],
                                 boutb[0:1, 512 * half: 512 * (half + 1)],
                                 start=False, stop=True)
                    mi.then_inc(s_t.h, 1); s_t.n += 1
                sec("tensor", fwo)
                t_wo = s_t.n

                def fwe(v, i=i, np_=np_, t_wo=t_wo):
                    v.wait_ge(s_t.h, t_wo)
                    dst = rtile(i)
                    for half in range(2):
                        v.tensor_tensor(dst[0:np_, 512 * half: 512 * (half + 1)],
                                        dst[0:np_, 512 * half: 512 * (half + 1)],
                                        PS[4 + half][0:np_, :], op=ALU.add).then_inc(s_v.h, 1); s_v.n += 1
                sec("vector", fwe)
                c_ev[0] = s_v.n

        if STAGE < 5:
            early_finish()
            return nc

        sflush()


        hg_f = f32v(O_HG, 512)
        hf_f = f32v(O_HF, 512)
        hq_t = bfv(O_HQ, 512)
        hq_fm = bfv(O_HQF, 4 * 128)
        wT_ready = {}
        wT_free = {}
        b_free = [c_ev[0]]

        def wg_off(par): return O_R3 + par * 8192
        def wu_off(par): return O_R3 + 16384 + par * 8192
        def wd_off(par): return O_R2 + 16384 + par * 8192

        def emit_fld(si):
            par = si % 2
            n0, n1 = NSL[si]
            ns = n1 - n0
            in0 = s_in.n
            wfree = wT_free.get(par, s_t.n)

            def fld(sy, n0=n0, n1=n1, ns=ns, par=par, in0=in0, wfree=wfree):
                sy.wait_ge(s_in.h, in0)
                sy.wait_ge(s_cc.h, cc_ag)
                sy.wait_ge(s_t.h, wfree)
                for c in range(KC):
                    sy.dma_start(out=bfv(wg_off(par) + 2 * 512 * c, ns),
                                 in_=bass.AP(wsh_g.ap().tensor, c * SHN + SH_GATE + n0,
                                             [[IF, 128], [1, ns]])).then_inc(s_in.h, 16); s_in.n += 16
                for c in range(KC):
                    sy.dma_start(out=bfv(wu_off(par) + 2 * 512 * c, ns),
                                 in_=bass.AP(wsh_g.ap().tensor, c * SHN + SH_UP + n0,
                                             [[IF, 128], [1, ns]])).then_inc(s_in.h, 16); s_in.n += 16
                for c in range(ns // 128):
                    srcap = bass.AP(wsh_g.ap().tensor, SH_DOWN + (n0 + 128 * c) * 128,
                                    [[128, 128], [SHN, 8], [1, 128]])
                    dstap = bfap(wd_off(par) + 2 * D * c, [[128, 8], [1, 128]])
                    sy.dma_start(out=dstap, in_=srcap).then_inc(s_in.h, 16); s_in.n += 16
            sec("sync", fld)
            wT_ready[si] = s_in.n

        items = [(si, p) for si in range(len(NSL)) for p in range(9)]
        hq_evac_s = {}
        bguard = [0, 0]
        t_gu_of = {}
        bhf = f32v(O_BHF, 1024)
        bhg = f32v(O_BHG, 1024)

        def emit_fgu(si, p):
            n0, n1 = NSL[si]
            ns = n1 - n0
            par = si % 2
            wr = wT_ready[si]
            gs, gv = bguard
            tiles = [2 * p, 2 * p + 1] if p < 8 else [16]

            def fgu(t, si=si, p=p, ns=ns, par=par, wr=wr, gs=gs, gv=gv, tiles=tiles):
                t.wait_ge(s_in.h, wr)
                if gs:
                    t.wait_ge(s_s.h, gs)
                if gv:
                    t.wait_ge(s_v.h, gv)
                for ti, i in enumerate(tiles):
                    np_ = tile_np(i)
                    colbase = tile_colbase(i)
                    bp = 2 * ti
                    for c in range(KC):
                        mi = t.matmul(PS[bp][0:np_, 0:ns],
                                      xq_fm_view(c, colbase, colbase + np_),
                                      bfv(wg_off(par) + 2 * 512 * c, ns),
                                      start=(c == 0), stop=(c == KC - 1))
                    for c in range(KC):
                        mi = t.matmul(PS[bp + 1][0:np_, 0:ns],
                                      xq_fm_view(c, colbase, colbase + np_),
                                      bfv(wu_off(par) + 2 * 512 * c, ns),
                                      start=(c == 0), stop=(c == KC - 1))
                mi.then_inc(s_t.h, 1); s_t.n += 1
            sec("tensor", fgu)
            t_gu_of[(si, p)] = s_t.n

        emit_fld(0)
        emit_fld(1)
        emit_fgu(0, 0)
        t_th_prev = [0]
        pend_fra = []
        for j, (si, p) in enumerate(items):
            par = si % 2
            n0, n1 = NSL[si]
            ns = n1 - n0
            tiles = [2 * p, 2 * p + 1] if p < 8 else [16]
            t_gu = t_gu_of[(si, p)]

            for ti, i in enumerate(tiles):
                np_ = tile_np(i)
                bp = 2 * ti
                vhg = s_v.n

                def fsl(s, np_=np_, ns=ns, bp=bp, ti=ti, t_gu=t_gu, vhg=vhg):
                    s.wait_ge(s_t.h, t_gu)
                    s.wait_ge(s_v.h, vhg)
                    s.activation(bhg[0:np_, 512 * ti: 512 * ti + ns], PS[bp][0:np_, 0:ns],
                                 AF.Silu).then_inc(s_s.h, 1)
                ssec(fsl, 1)
                s_si = s_s.n

                def fmu(v, np_=np_, ns=ns, bp=bp, ti=ti, s_si=s_si):
                    v.wait_ge(s_s.h, s_si)
                    v.wait_ge(s_v.h, s_v.n)
                    v.tensor_tensor(bhf[0:np_, 512 * ti: 512 * ti + ns],
                                    bhg[0:np_, 512 * ti: 512 * ti + ns],
                                    PS[bp + 1][0:np_, 0:ns], op=ALU.mult).then_inc(s_v.h, 1); s_v.n += 1
                sec("vector", fmu)
            bguard[0] = s_s.n
            bguard[1] = s_v.n

            if j + 1 < len(items):
                emit_fgu(*items[j + 1])

            npq = tile_np(tiles[0])
            Fq = ns * len(tiles)
            if len(tiles) == 2 and ns == 512:
                emit_quant(bhf, bfv(O_BHQ, 1024), 128, 1024, 64,
                           tguard=t_th_prev[0], destructive=True, geng=True)
                hq_views = [bfv(O_BHQ + 2 * 512 * ti, 512) for ti in range(2)]
            else:
                hq_views = []
                for ti, i in enumerate(tiles):
                    emit_quant(f32v(O_BHF + 4 * 512 * ti, ns), bfv(O_BHQ + 2 * 512 * ti, ns),
                               tile_np(i), ns, ns // 16, tguard=t_th_prev[0], destructive=True,
                               geng=True)
                    hq_views.append(bfv(O_BHQ + 2 * 512 * ti, ns))
            v_hq = s_v.n

            for fn in pend_fra:
                sec("vector", fn); s_v.n += 2
            del pend_fra[:]
            b_free[0] = s_v.n

            for ti, i in enumerate(tiles):
                np_ = tile_np(i)
                pb = 6 + (ti % 2)
                te_s = hq_evac_s.get(pb, 0)

                def fth(t, np_=np_, ns=ns, pb=pb, ti=ti, v_hq=v_hq, te_s=te_s):
                    t.wait_ge(s_v.h, v_hq)
                    if te_s:
                        t.wait_ge(s_s.h, te_s)
                    for c in range(ns // 128):
                        t.transpose(bass.AP(PB[pb].tensor, PB[pb].offset + 128 * c,
                                            [[PB[pb].ap[0][0], 128], [1, np_]]),
                                    bfv(O_BHQ + 2 * (512 * ti + 128 * c), 128, np_=np_),
                                    ident[0:np_, 0:np_]).then_inc(s_t.h, 1); s_t.n += 1
                sec("tensor", fth)
                t_th = s_t.n
                t_th_prev[0] = t_th

                def fhe(s, np_=np_, ns=ns, pb=pb, ti=ti, t_th=t_th):
                    s.wait_ge(s_t.h, t_th)
                    srcap = pbap(pb, [[128, ns // 128], [1, np_]])
                    dstap = bfap(O_HQF2 + 1024 * ti, [[128, ns // 128], [1, np_]])
                    s.copy(dstap, srcap).then_inc(s_s.h, 1)
                ssec(fhe, 1)
                hq_evac_s[pb] = s_s.n
                v_hf = s_s.n
                bfree = b_free[0]

                def fdn(t, np_=np_, ns=ns, par=par, ti=ti, v_hf=v_hf, bfree=bfree):
                    t.wait_ge(s_s.h, v_hf)
                    t.wait_ge(s_v.h, bfree)
                    for half in range(2):
                        for c in range(ns // 128):
                            mi = t.matmul(PS[4 + half][0:np_, :],
                                          bfv(O_HQF2 + 1024 * ti + 256 * c, np_),
                                          bfv(wd_off(par) + 2 * (D * c + 512 * half), 512),
                                          start=(c == 0), stop=(c == ns // 128 - 1))
                    mi.then_inc(s_t.h, 1); s_t.n += 1
                sec("tensor", fdn)
                t_dn = s_t.n

                def fra(v, i=i, np_=np_, t_dn=t_dn):
                    v.wait_ge(s_t.h, t_dn)
                    dst = rtile(i)
                    for half in range(2):
                        v.tensor_tensor(dst[0:np_, 512 * half: 512 * (half + 1)],
                                        dst[0:np_, 512 * half: 512 * (half + 1)],
                                        PS[4 + half][0:np_, :], op=ALU.add).then_inc(s_v.h, 1)
                pend_fra.append(fra)
                if ti == 0 and len(tiles) == 2:
                    for fn in pend_fra:
                        sec("vector", fn); s_v.n += 2
                    del pend_fra[:]
                    b_free[0] = s_v.n
            if p == 8:
                wT_free[par] = s_t.n
                if si + 2 < len(NSL):
                    emit_fld(si + 2)
        for fn in pend_fra:
            sec("vector", fn); s_v.n += 2
        del pend_fra[:]
        b_free[0] = s_v.n

        sflush()
        XN1_FM = O_R3
        tr_evac.clear()

        def xn1_view(k, a, b):
            return bfv(XN1_FM + 2 * (XT * k + a), b - a)

        emit_ln_stats()
        for i in range(17):
            emit_ln(i, save_stats=False)
            emit_tm_to_fm(xnbf, XN1_FM, tile_colbase(i), tile_np(i), 6 + (i % 2))
        v_xn1 = s_v.n

        O_W1A = O_R2
        O_W1B = O_R2 + 16384

        W1OFF = {"wq": SH_WQ, "wk": SH_WK, "wv": SH_WV, "wo": SH_WO}

        def emit_w1T(dst_off, src_s, key, guard_t):
            in0 = s_in.n

            def ftr(sy):
                sy.wait_ge(s_in.h, in0)
                sy.wait_ge(s_cc.h, cc_ag)
                sy.wait_ge(s_t.h, guard_t)
                for c in range(KC):
                    sy.dma_start(out=bfv(dst_off + 2 * D * c, D),
                                 in_=bass.AP(wsh_g.ap().tensor, c * SHN + W1OFF[key],
                                             [[1024, 128], [1, D]])).then_inc(s_in.h, 16); s_in.n += 16
            sec("sync", ftr)
            return s_in.n

        O_QH = O_R1
        O_KH = O_R1 + 16384
        probs = bfv(O_PROBS, 192)
        pT = bfv(O_PT, 256)
        av_f = bfv(O_AVF, KC * 128)
        zbv = f32v(O_ZB, 1)
        zrv = f32v(O_ZR, 1)
        l1b = {}
        l1_ev = [b_free[0]]
        probs_free = [0]

        for half in range(2):
            hb = TPC // 2 * half
            HG2 = [(hb, hb + 512), (hb + 512, hb + 1024)]
            tr_wq = emit_w1T(O_W1A, None, "wq", s_t.n)

            for gi, (a, b) in enumerate(HG2):
                for c in range(KC):
                    bank = (gi * KC + c) % 4
                    bg = l1b.get(("b", bank), 0)
                    lev = l1_ev[0]

                    def fq(t, c=c, a=a, b=b, bank=bank, bg=bg, lev=lev, tr_wq=tr_wq):
                        t.wait_ge(s_in.h, tr_wq)
                        t.wait_ge(s_v.h, max(lev, v_xn1))
                        if bg:
                            t.wait_ge(s_s.h, bg)
                        for k in range(KC):
                            mi = t.matmul(PS[bank][:, :],
                                     bfv(O_W1A + 2 * (D * k + 128 * c), 128),
                                     xn1_view(k, a, b),
                                     start=(k == 0), stop=(k == KC - 1))
                        mi.then_inc(s_t.h, 1); s_t.n += 1
                    sec("tensor", fq)
                    t_q = s_t.n

                    def fqe(s, c=c, gi=gi, bank=bank, t_q=t_q):
                        s.wait_ge(s_t.h, t_q)
                        s.copy(bfv(O_QH + 2 * (D * c + 512 * gi), 512)[:, :],
                               PS[bank][:, :]).then_inc(s_s.h, 1)
                    ssec(fqe, 1)
                    l1b[("b", bank)] = s_s.n

            tr_wk = emit_w1T(O_W1B, None, "wk", s_t.n)
            kg2 = list(HG2) + ([(TPC, XT)] if half == 0 else [])
            for gi, (a, b) in enumerate(kg2):
                n = b - a
                for c in range(KC):
                    bank = (gi * KC + c) % 4
                    bg = l1b.get(("b", bank), 0)

                    lev = l1_ev[0]

                    def fk(t, c=c, a=a, b=b, n=n, bank=bank, bg=bg, tr_wk=tr_wk, lev=lev):
                        t.wait_ge(s_in.h, tr_wk)
                        if lev:
                            t.wait_ge(s_v.h, lev)
                        if bg:
                            t.wait_ge(s_s.h, bg)
                        for k in range(KC):
                            mi = t.matmul(PS[bank][:, 0:n],
                                     bfv(O_W1B + 2 * (D * k + 128 * c), 128),
                                     xn1_view(k, a, b),
                                     start=(k == 0), stop=(k == KC - 1))
                        mi.then_inc(s_t.h, 1); s_t.n += 1
                    sec("tensor", fk)
                    t_k = s_t.n

                    def fke(s, c=c, gi=gi, a=a, n=n, bank=bank, t_k=t_k):
                        s.wait_ge(s_t.h, t_k)
                        if a < TPC:
                            dst = bfv(O_KH + 2 * (D * c + 512 * gi), n)
                        else:
                            dst = bfv(O_KG + 2 * (G * c), n)
                        s.copy(dst[:, :], PS[bank][0:128, 0:n]).then_inc(s_s.h, 1)
                    ssec(fke, 1)
                    l1b[("b", bank)] = s_s.n

            tr_wv = emit_w1T(O_W1A, None, "wv", s_t.n)
            vtl = list(range(8)) + ([16] if half == 0 else [])
            for vi in vtl:
                np_ = 128 if vi < 16 else G
                colbase = hb + 128 * vi if vi < 16 else TPC
                bank2 = 2 * ((vi % 2))
                bg = max(l1b.get(("b", bank2), 0), l1b.get(("b", bank2 + 1), 0))

                lev = l1_ev[0]

                def fv3(t, vi=vi, np_=np_, colbase=colbase, bank2=bank2, bg=bg, tr_wv=tr_wv, lev=lev):
                    t.wait_ge(s_in.h, tr_wv)
                    if lev:
                        t.wait_ge(s_v.h, lev)
                    if bg:
                        t.wait_ge(s_s.h, bg)
                    for hf in range(2):
                        for k in range(KC):
                            mi = t.matmul(PS[bank2 + hf][0:np_, :],
                                     xn1_view(k, colbase, colbase + np_),
                                     bfv(O_W1A + 2 * (D * k + 512 * hf), 512),
                                     start=(k == 0), stop=(k == KC - 1))
                    mi.then_inc(s_t.h, 1); s_t.n += 1
                sec("tensor", fv3)
                t_v3 = s_t.n

                def fve(s, vi=vi, np_=np_, bank2=bank2, t_v3=t_v3):
                    s.wait_ge(s_t.h, t_v3)
                    for hf in range(2):
                        if vi < 16:
                            dst = bfv(O_VH + 2 * (D * vi + 512 * hf), 512)
                        else:
                            dst = bfv(O_VG + 2 * 512 * hf, 512, np_=G)
                        s.copy(dst[0:np_, :], PS[bank2 + hf][0:np_, :]).then_inc(s_s.h, 1)
                ssec(fve, 2)
                l1b[("b", bank2)] = l1b[("b", bank2 + 1)] = s_s.n
            s_qkv = s_s.n

            tr_wo = emit_w1T(O_W1B, None, "wo", s_t.n)

            wseq = [(w, h) for w in range(8) for h in range(H)]
            t_sc_of = {}
            probs_par_free = {}

            def emit_fsc(j):
                w, h = wseq[j]
                scb = j % 2
                bg = l1b.get(("sc", scb), 0)

                def fsc(t, w=w, h=h, scb=scb, bg=bg):
                    t.wait_ge(s_s.h, max(s_qkv, bg))
                    po = 64 * (h % 2)
                    qb_ = O_QH // 2 + D * (h // 2) + 128 * w
                    kb_ = O_KH // 2 + D * (h // 2) + 128 * w
                    gb_ = O_KG // 2 + G * (h // 2)
                    qv = bf_base[po:po + 64, qb_:qb_ + 128]
                    kvw = bf_base[po:po + 64, kb_:kb_ + 128]
                    kgv = bf_base[po:po + 64, gb_:gb_ + G]
                    mi = t.matmul(PS[scb][:, 0:G], qv, kgv, start=True, stop=True)
                    mi = t.matmul(PS[scb][:, G:G + 128], qv, kvw, start=True, stop=True)
                    mi.then_inc(s_t.h, 1); s_t.n += 1
                sec("tensor", fsc)
                t_sc_of[j] = s_t.n

            emit_fsc(0)
            for j, (w, h) in enumerate(wseq):
                par = j % 2
                scb = par
                probsv = bfv(O_PROBS + 384 * par, 192)
                pTv = bfv(O_PT + 512 * par, 256)
                zbvp = f32v(O_ZB + 4 * par, 1)
                zrvp = f32v(O_ZR + 4 * par, 1)
                t_sc = t_sc_of[j]
                pf = probs_par_free.get(par, 0)

                def fex(s, scb=scb, t_sc=t_sc, pf=pf, probsv=probsv, zbvp=zbvp):
                    s.wait_ge(s_t.h, t_sc)
                    if pf:
                        s.wait_ge(s_v.h, pf)
                    s.activation(probsv[:, :], PS[scb][:, 0:192], AF.Exp,
                                 bias=cbc[:, 0:1], scale=0.125,
                                 accum_out=zbvp[:, 0:1]).then_inc(s_s.h, 1)
                ssec(fex, 1)
                l1b[("sc", scb)] = s_s.n
                l1b[("b", scb)] = s_s.n
                s_ex = s_s.n

                def fzr(v, s_ex=s_ex, probsv=probsv, zbvp=zbvp, zrvp=zrvp):
                    v.wait_ge(s_s.h, s_ex)
                    v.reciprocal(zrvp[:, 0:1], zbvp[:, 0:1]).then_inc(s_v.h, 1); s_v.n += 1
                    v.wait_ge(s_v.h, s_v.n)
                    v.tensor_scalar(probsv[:, :], probsv[:, :], zrvp[:, 0:1], None,
                                    op0=ALU.mult).then_inc(s_v.h, 1); s_v.n += 1
                sec("vector", fzr)
                v_pn = s_v.n

                if j + 1 < len(wseq):
                    emit_fsc(j + 1)

                ptb = 6 + par
                bg2 = l1b.get(("pt", ptb), 0)

                def fpt(t, ptb=ptb, v_pn=v_pn, bg2=bg2, probsv=probsv):
                    t.wait_ge(s_v.h, max(v_pn, bg2))
                    t.transpose(bass.AP(PB[ptb].tensor, PB[ptb].offset,
                                        [[PB[ptb].ap[0][0], 128], [1, 128]]),
                                probsv[:, G:192], ident[:, :]).then_inc(s_t.h, 1); s_t.n += 1
                    t.transpose(bass.AP(PB[ptb].tensor, PB[ptb].offset + 128,
                                        [[PB[ptb].ap[0][0], G], [1, 128]]),
                                probsv[:, 0:G], ident[:, :]).then_inc(s_t.h, 1); s_t.n += 1
                sec("tensor", fpt)
                t_pt = s_t.n

                def fpe(v, ptb=ptb, t_pt=t_pt, pTv=pTv):
                    v.wait_ge(s_t.h, t_pt)
                    v.tensor_copy(pTv[:, :], PB[ptb][:, 0:256]).then_inc(s_v.h, 1); s_v.n += 1
                sec("vector", fpe)
                l1b[("pt", ptb)] = s_v.n
                v_pt = s_v.n
                probs_par_free[par] = v_pt

                avg = l1b.get(("av",), 0)

                def fav(t, w=w, h=h, v_pt=v_pt, avg=avg, pTv=pTv):
                    t.wait_ge(s_v.h, v_pt)
                    if avg and h % 2 == 0:
                        t.wait_ge(s_v.h, avg)
                    po = 64 * (h % 2)
                    vv = bfv(O_VH + 2 * (D * w + 64 * h), 64)
                    vgv = bfv(O_VG + 2 * 64 * h, 64, np_=G)
                    outap = PS[4][po:po + 64, 0:128]
                    mi = t.matmul(outap, vv, pTv[:, 0:128], start=True, stop=False)
                    mi = t.matmul(outap, vgv[0:G, :], pTv[0:G, 128:256], start=False, stop=True)
                    mi.then_inc(s_t.h, 1); s_t.n += 1
                sec("tensor", fav)
                t_av = s_t.n

                if h % 2 == 1:
                    def fae(v, h=h, t_av=t_av):
                        v.wait_ge(s_t.h, t_av)
                        v.tensor_copy(av_f[:, 128 * (h // 2): 128 * (h // 2) + 128],
                                      PS[4][:, 0:128]).then_inc(s_v.h, 1); s_v.n += 1
                    sec("vector", fae)
                    l1b[("av",)] = s_v.n

                if h == H - 1:
                    s_av = s_v.n
                    lev = l1_ev[0]

                    def fw1(t, w=w, hb=hb, s_av=s_av, lev=lev, tr_wo=tr_wo):
                        t.wait_ge(s_v.h, max(s_av, lev))
                        t.wait_ge(s_in.h, tr_wo)
                        for hf in range(2):
                            for c in range(KC):
                                mi = t.matmul(PS[2 + hf][:, :],
                                              av_f[:, 128 * c: 128 * (c + 1)],
                                              bfv(O_W1B + 2 * (D * c + 512 * hf), 512),
                                              start=(c == 0), stop=(c == KC - 1))
                        mi.then_inc(s_t.h, 1); s_t.n += 1
                    sec("tensor", fw1)
                    t_w1 = s_t.n

                    def fwe1(v, w=w, hb=hb, t_w1=t_w1):
                        v.wait_ge(s_t.h, t_w1)
                        dst = rtile((hb + 128 * w) // 128)
                        for hf in range(2):
                            v.tensor_tensor(dst[:, 512 * hf: 512 * (hf + 1)],
                                            dst[:, 512 * hf: 512 * (hf + 1)],
                                            PS[2 + hf][:, :], op=ALU.add).then_inc(s_v.h, 1); s_v.n += 1
                    sec("vector", fwe1)
                    l1_ev[0] = s_v.n

        if STAGE < 7:
            early_finish()
            return nc

        sflush()
        O_YA = O_R1
        O_FCT = O_R2
        O_PJT = O_R2 + 16384
        h2v = bfv(O_H2, 4 * D)
        spa = bfv(O_SPA, D)
        l1f = {}
        h2_free = [l1_ev[0]]
        ya_guard = [0]

        for half in range(2):
            hb = TPC // 2 * half
            for si, (f0, f1) in enumerate(FSL):
                par = si % 2
                O_FT = O_FCT + par * 8192
                O_PT2 = O_PJT + par * 8192
                guard_t = l1f.get(("trg", par), s_t.n)

                in0 = s_in.n

                def ffw(sy, f0=f0, f1=f1, O_FT=O_FT, O_PT2=O_PT2, guard_t=guard_t, in0=in0):
                    sy.wait_ge(s_in.h, in0)
                    sy.wait_ge(s_cc.h, cc_ag)
                    if guard_t:
                        sy.wait_ge(s_t.h, guard_t)
                    for c in range(KC):
                        sy.dma_start(out=bfv(O_FT + 2 * 512 * c, 512),
                                     in_=bass.AP(wsh_g.ap().tensor, c * SHN + SH_FCT + f0,
                                                 [[FF, 128], [1, 512]])).then_inc(s_in.h, 16); s_in.n += 16
                    for c in range(4):
                        rank = f0 // 512
                        row = (f0 % 512) + 128 * c
                        sy.dma_start(out=bfv(O_PT2 + 2 * D * c, D),
                                     in_=bass.AP(wsh_g.ap().tensor, rank * SHN + SH_PJT + row * 1024,
                                                 [[1024, 128], [1, D]])).then_inc(s_in.h, 16); s_in.n += 16
                sec("sync", ffw)
                tr_f = s_in.n

                for fc_ in range(4):
                    for gi in range(2):
                        t0 = hb + 512 * gi
                        bank = (fc_ * 2 + gi) % 4
                        bg = l1f.get(("fc", bank), 0)
                        h2f = h2_free[0]

                        def ffc(t, fc_=fc_, t0=t0, bank=bank, tr_f=tr_f, bg=bg, h2f=h2f, O_FT=O_FT):
                            t.wait_ge(s_in.h, tr_f)
                            t.wait_ge(s_v.h, max(bg, h2f))
                            for k in range(KC):
                                mi = t.matmul(PS[bank][:, :],
                                         bfv(O_FT + 2 * (512 * k + 128 * fc_), 128),
                                         xn1_view(k, t0, t0 + 512),
                                         start=(k == 0), stop=(k == KC - 1))
                            mi.then_inc(s_t.h, 1); s_t.n += 1
                        sec("tensor", ffc)
                        t_fc = s_t.n

                        vhg = s_v.n

                        def frl(s, bank=bank, t_fc=t_fc, vhg=vhg):
                            s.wait_ge(s_t.h, t_fc)
                            s.wait_ge(s_v.h, vhg)
                            s.activation(hg_f[:, :], PS[bank][:, :], AF.Relu).then_inc(s_s.h, 1)
                        ssec(frl, 1)
                        s_rl = s_s.n

                        def fsq(v, fc_=fc_, gi=gi, s_rl=s_rl):
                            v.wait_ge(s_s.h, s_rl)
                            dst = bfv(O_H2 + 2 * (D * fc_ + 512 * gi), 512)
                            v.tensor_tensor(dst[:, :], hg_f[:, :], hg_f[:, :],
                                            op=ALU.mult).then_inc(s_v.h, 1); s_v.n += 1
                        sec("vector", fsq)
                        l1f[("fc", bank)] = s_v.n
                v_h2 = s_v.n
                l1f[("trg", par)] = s_t.n

                for ti in range(8):
                    pb2 = 4 + 2 * (ti % 2)
                    bg = l1f.get(("pj", pb2), 0)

                    def fpj(t, ti=ti, pb2=pb2, v_h2=v_h2, bg=bg, O_PT2=O_PT2):
                        t.wait_ge(s_v.h, max(v_h2, bg))
                        for hf in range(2):
                            for c in range(4):
                                mi = t.matmul(PS[pb2 + hf][:, :],
                                         bfv(O_H2 + 2 * (D * c + 128 * ti), 128),
                                         bfv(O_PT2 + 2 * (D * c + 512 * hf), 512),
                                         start=(c == 0), stop=(c == 3))
                        mi.then_inc(s_t.h, 1); s_t.n += 1
                    sec("tensor", fpj)
                    t_pj = s_t.n
                    l1f[("trg", par)] = s_t.n

                    def fya(v, ti=ti, si=si, pb2=pb2, t_pj=t_pj):
                        v.wait_ge(s_t.h, t_pj)
                        for hf in range(2):
                            dst = f32v(O_YA + 4 * (D * ti + 512 * hf), 512)
                            if si == 0:
                                v.tensor_copy(dst[:, :], PS[pb2 + hf][:, :]).then_inc(s_v.h, 1); s_v.n += 1
                            else:
                                v.tensor_tensor(dst[:, :], dst[:, :], PS[pb2 + hf][:, :],
                                                op=ALU.add).then_inc(s_v.h, 1); s_v.n += 1
                    sec("vector", fya)
                    l1f[("pj", pb2)] = s_v.n
                h2_free[0] = s_v.n

            for ti in range(8):
                i = (hb + 128 * ti) // 128

                def fsp(v, ti=ti, i=i):
                    v.wait_ge(s_v.h, s_v.n)
                    y = f32v(O_YA + 4 * D * ti, D)
                    v.tensor_scalar(spa[:, :], y[:, :], SPIKE_B[0], None,
                                    op0=ALU.is_gt).then_inc(s_v.h, 1); s_v.n += 1
                    for bth in SPIKE_B[1:]:
                        v.wait_ge(s_v.h, s_v.n)
                        v.scalar_tensor_tensor(spa[:, :], y[:, :], bth, spa[:, :],
                                               op0=ALU.is_gt, op1=ALU.add).then_inc(s_v.h, 1); s_v.n += 1
                    v.wait_ge(s_v.h, s_v.n)
                    dst = rtile(i)
                    v.scalar_tensor_tensor(dst[:, :], spa[:, :], 0.2, dst[:, :],
                                           op0=ALU.mult, op1=ALU.add).then_inc(s_v.h, 1); s_v.n += 1
                sec("vector", fsp)
                v_done = s_v.n

                st0 = s_st.n

                def fst(sy, i=i, v_done=v_done, st0=st0):
                    sy.wait_ge(s_st.h, st0)
                    sy.wait_ge(s_v.h, v_done)
                    sy.dma_start(out=out_d[128 * i: 128 * (i + 1), :],
                                 in_=f32v(O_R + 4096 * i, D)).then_inc(s_st.h, 16); s_st.n += 16
                sec("sync", fst)

        sflush()

        def ffin(sy):
            sy.wait_ge(s_st.h, s_st.n)
        sec("sync", ffin)

    return nc


class Sem:
    def __init__(self, h):
        self.h = h
        self.n = 0


_NC_CACHE = {}


def kernel(L0_attn_x, L0_attn_w_qkv, L0_attn_w_out, L0_attn_b_out,
           L0_ffn_w_gate, L0_ffn_w_up, L0_ffn_w_down,
           L1_attn_Wq, L1_attn_Wk, L1_attn_Wv, L1_attn_Wo,
           L1_ffn_W_fc, L1_ffn_W_proj):
    x = np.ascontiguousarray(np.asarray(L0_attn_x, dtype=np.float32))
    B, S, _ = x.shape
    CPS = 4
    ident = np.eye(128, dtype=np.float32)
    sel = np.zeros((16, KC * 128), dtype=np.float32)
    for c in range(KC):
        sel[2 * c, 128 * c: 128 * c + 64] = 1.0
        sel[2 * c + 1, 128 * c + 64: 128 * c + 128] = 1.0
    w = {
        "wqkv": np.ascontiguousarray(L0_attn_w_qkv, dtype=np.float32),
        "wout": np.ascontiguousarray(L0_attn_w_out, dtype=np.float32),
        "bout": np.ascontiguousarray(np.asarray(L0_attn_b_out, dtype=np.float32).reshape(1, D)),
        "wgate": np.ascontiguousarray(L0_ffn_w_gate, dtype=np.float32),
        "wup": np.ascontiguousarray(L0_ffn_w_up, dtype=np.float32),
        "wdown": np.ascontiguousarray(L0_ffn_w_down, dtype=np.float32),
        "wq": np.ascontiguousarray(L1_attn_Wq, dtype=np.float32),
        "wk": np.ascontiguousarray(L1_attn_Wk, dtype=np.float32),
        "wv": np.ascontiguousarray(L1_attn_Wv, dtype=np.float32),
        "wo": np.ascontiguousarray(L1_attn_Wo, dtype=np.float32),
        "wfc": np.ascontiguousarray(L1_ffn_W_fc, dtype=np.float32),
        "wproj": np.ascontiguousarray(L1_ffn_W_proj, dtype=np.float32),
        "ident": ident, "sel": sel,
    }
    if "nc" not in _NC_CACHE:
        _NC_CACHE["nc"] = build_program()
    nc = _NC_CACHE["nc"]
    in_maps = []
    for core in range(8):
        b = core // CPS
        t0 = (core % CPS) * TPC
        m = dict(w)
        m["x"] = np.ascontiguousarray(x[b, t0:t0 + TPC, :])
        m["xg"] = np.ascontiguousarray(x[b, 0:G, :])
        m["wg_sh"] = np.ascontiguousarray(w["wgate"][:, 128 * core: 128 * (core + 1)])
        m["wu_sh"] = np.ascontiguousarray(w["wup"][:, 128 * core: 128 * (core + 1)])
        m["wd_sh"] = np.ascontiguousarray(w["wdown"][128 * core: 128 * (core + 1), :])
        m["wfc_sh"] = np.ascontiguousarray(w["wfc"][:, 128 * core: 128 * (core + 1)])
        m["wpj_sh"] = np.ascontiguousarray(w["wproj"][:, 512 * core: 512 * (core + 1)])
        for k in ("wq", "wk", "wv", "wo"):
            m[k + "_sh"] = np.ascontiguousarray(w[k][:, 128 * core: 128 * (core + 1)])
        in_maps.append(m)
    res = run_bass_kernel_spmd(nc, in_maps, core_ids=list(range(8)))
    out = np.empty((B, S, D), dtype=np.float32)
    for core in range(8):
        b = core // CPS
        t0 = (core % CPS) * TPC
        out[b, t0:t0 + TPC, :] = res.results[core]["out"]
    return out


# version 15
# speedup vs baseline: 1.5778x; 1.5778x over previous
import contextlib
import numpy as np
import concourse.bass as bass
import concourse.mybir as mybir
from concourse.bass_utils import run_bass_kernel_spmd

F32 = mybir.dt.float32
BF16 = mybir.dt.bfloat16
U8 = mybir.dt.uint8
ALU = mybir.AluOpType
AF = mybir.ActivationFunctionType
AXX = mybir.AxisListType.X

D = 1024
KC = 8
H = 16
TPC = 2048
NT = 16
G = 64
IF = 2816
FF = 4096
EPS = 1e-5
XT = TPC + G

FP4_TH_A = (0.25, 0.75, 1.25, 1.75)
FP4_TH_B = (2.5, 3.5, 5.0, 5.0)
SPIKE_B = (0.2441942809699397, 0.46396913384288535, 0.6861859295255305,
           0.9059607823984762, 1.083978413225562)
SOFTMAX_C = 24.0

NSL = [(0, 512), (512, 1024), (1024, 1536), (1536, 2048), (2048, 2560), (2560, 2816)]
FSL = [(s, s + 512) for s in range(0, FF, 512)]
TGROUPS = [(0, 512), (512, 1024), (1024, 1536), (1536, 2048), (2048, 2112)]

STAGE = 7
SH_GATE = 0
SH_UP = 360448
SH_DOWN = 720896
SH_FCT = 1081344
SH_PJT = 1605632
SH_WQ = 2129920
SH_WK = 2260992
SH_WV = 2392064
SH_WO = 2523136
SHN = 2654208

SBUF_CAP = 212800

O_R = 0
O_RG = 65536
O_IDENT = 69632
O_ONES = 69888
O_SEL = 70144
O_BOUT = 74240
O_KV = 76288
O_STATS = 78400
O_CB = 78592
O_BD = 78656
O_ZST = 80704
O_KG = 80960
O_VG = 81984
PERS_END = 84032
O_R1 = 84032
O_R2 = 117824
O_R3 = 150592
O_R5 = 184384
R5_END = O_R5 + 28416
assert R5_END <= SBUF_CAP, R5_END

O_E1 = O_R5
O_E2 = O_R5 + 2048
O_XNF = O_R5
O_BNST = O_R5 + 4096
O_MV = O_R5 + 4160
O_XNBF = O_R5 + 4224
O_QT = O_R5 + 6272
O_QA = O_R5 + 10368
O_QM = O_R5 + 14464
O_QS = O_R5 + 14720
O_QR = O_R5 + 14976
O_QSB = O_R5 + 15232
O_QCA = O_R5 + 15424
O_QCB = O_R5 + 17472
O_QSG = O_R5 + 19520
O_KBF = O_R5 + 6272
O_VEX = O_R5 + 8320
O_WQQ = O_R5 + 10432
O_ATT = O_R5 + 10432
O_DEN = O_R5
O_RCB = O_R5 + 2048
O_HG = O_R5 + 21568
O_HF = O_R5 + 23616
O_HQ = O_R5 + 25664
O_HQF = O_R5 + 26688
O_PROBS = O_R5 + 6272
O_PT = O_R5 + 7040
O_AVF = O_R5 + 8064
O_ZB = O_R5 + 10112
O_ZR = O_R5 + 10176
O_VH = O_R5 + 10432
O_H2 = O_R5 + 10432
O_SPA = O_R5 + 18688
O_BHF = O_R5 + 6272
O_BHG = O_R5 + 10368
O_QTMP = O_R5 + 21568
O_BHQ = O_R5 + 23616
O_HQF2 = O_R5 + 25664



def build_program():
    nc = bass.Bass("TRN2", target_bir_lowering=False)

    x_d = nc.dram_tensor("x", [TPC, D], F32, kind="ExternalInput")
    xg_d = nc.dram_tensor("xg", [G, D], F32, kind="ExternalInput")
    wqkv_d = nc.dram_tensor("wqkv", [D, 3 * D], F32, kind="ExternalInput")
    wout_d = nc.dram_tensor("wout", [D, D], F32, kind="ExternalInput")
    bout_d = nc.dram_tensor("bout", [1, D], F32, kind="ExternalInput")
    wgate_d = nc.dram_tensor("wgate", [IF, D], F32, kind="ExternalInput")
    wup_d = nc.dram_tensor("wup", [IF, D], F32, kind="ExternalInput")
    wdown_d = nc.dram_tensor("wdown", [D, IF], F32, kind="ExternalInput")
    wq_d = nc.dram_tensor("wq", [D, D], F32, kind="ExternalInput")
    wk_d = nc.dram_tensor("wk", [D, D], F32, kind="ExternalInput")
    wv_d = nc.dram_tensor("wv", [D, D], F32, kind="ExternalInput")
    wo_d = nc.dram_tensor("wo", [D, D], F32, kind="ExternalInput")
    wfc_d = nc.dram_tensor("wfc", [FF, D], F32, kind="ExternalInput")
    wproj_d = nc.dram_tensor("wproj", [D, FF], F32, kind="ExternalInput")
    ident_d = nc.dram_tensor("ident", [128, 128], F32, kind="ExternalInput")
    sel_d = nc.dram_tensor("sel", [16, KC * 128], F32, kind="ExternalInput")
    out_d = nc.dram_tensor("out", [TPC, D], F32, kind="ExternalOutput")

    wg_sh = nc.dram_tensor("wg_sh", [IF, 128], F32, kind="ExternalInput")
    wu_sh = nc.dram_tensor("wu_sh", [IF, 128], F32, kind="ExternalInput")
    wd_sh = nc.dram_tensor("wd_sh", [128, IF], F32, kind="ExternalInput")
    wfc_sh = nc.dram_tensor("wfc_sh", [FF, 128], F32, kind="ExternalInput")
    wpj_sh = nc.dram_tensor("wpj_sh", [D, 512], F32, kind="ExternalInput")
    w1_sh = {k: nc.dram_tensor(k + "_sh", [D, 128], F32, kind="ExternalInput")
             for k in ("wq", "wk", "wv", "wo")}

    cc_in = nc.dram_tensor("cc_in", [128, 520], F32)
    cc_out = nc.dram_tensor("cc_out", [128, 520], F32)
    wfc_shs = nc.dram_tensor("wfc_shs", [FF, 128], BF16)
    wpj_shs = nc.dram_tensor("wpj_shs", [D, 512], BF16)
    w1_shs = {k: nc.dram_tensor(k + "_shs", [D, 128], BF16) for k in ("wq", "wk", "wv", "wo")}
    wsh = nc.dram_tensor("wsh", [SHN], BF16)
    wsh_g = nc.dram_tensor("wsh_g", [8 * SHN], BF16)

    with contextlib.ExitStack() as es:
        E = es.enter_context
        sb = E(nc.sbuf_tensor("arena", [128, SBUF_CAP], U8))
        s_in = Sem(E(nc.semaphore("s_in")))
        s_tr = Sem(E(nc.semaphore("s_tr")))
        s_st = Sem(E(nc.semaphore("s_st")))
        s_wg = Sem(E(nc.semaphore("s_wg")))
        s_g = Sem(E(nc.semaphore("s_g")))
        s_cc = Sem(E(nc.semaphore("s_cc")))
        s_t = Sem(E(nc.semaphore("s_t")))
        s_v = Sem(E(nc.semaphore("s_v")))
        s_s = Sem(E(nc.semaphore("s_s")))
        s_gc = Sem(E(nc.semaphore("s_gc")))
        PS = [E(nc.psum_tensor(f"ps{_i}", [128, 512], F32)) for _i in range(8)]
        block = E(nc.Block())

        bf_base = sb.ap().bitcast(BF16)
        f32_base = sb.ap().bitcast(F32)
        PB = [p.ap().bitcast(BF16) for p in PS]

        def bfv(off, cols, np_=128):
            assert off % 2 == 0
            return bass.AP(bf_base.tensor, bf_base.offset + off // 2,
                           [[bf_base.ap[0][0], np_], [1, cols]])

        def f32v(off, cols, np_=128):
            assert off % 4 == 0
            return bass.AP(f32_base.tensor, f32_base.offset + off // 4,
                           [[f32_base.ap[0][0], np_], [1, cols]])

        def bfap(off, dims, np_=128):
            assert off % 2 == 0
            return bass.AP(bf_base.tensor, bf_base.offset + off // 2,
                           [[bf_base.ap[0][0], np_]] + [list(d) for d in dims])

        def f32ap(off, dims, np_=128):
            assert off % 4 == 0
            return bass.AP(f32_base.tensor, f32_base.offset + off // 4,
                           [[f32_base.ap[0][0], np_]] + [list(d) for d in dims])

        def psap(i, dims, np_=128):
            a = PS[i].ap()
            return bass.AP(a.tensor, a.offset,
                           [[a.ap[0][0], np_]] + [list(d) for d in dims])

        def pbap(i, dims, np_=128):
            a = PB[i]
            return bass.AP(a.tensor, a.offset, [[a.ap[0][0], np_]] + [list(d) for d in dims])

        def sec(name, fn):
            getattr(block, name)(fn)

        _sbuf = []

        def ssec(fn, incs):
            s_s.n += incs
            _sbuf.append(fn)

        def sflush():
            if not _sbuf:
                return
            fns = list(_sbuf)
            del _sbuf[:]

            def _run(s):
                for f in fns:
                    f(s)
            sec("scalar", _run)

        ident = bfv(O_IDENT, 128)
        ones = bfv(O_ONES, 128)
        sel = f32v(O_SEL, KC * 128)
        boutb = bfv(O_BOUT, D)
        kvsb = f32v(O_KV, 520)
        stats = f32v(O_STATS, 34)
        cbc = f32v(O_CB, 16)
        bd = bfv(O_BD, KC * 128)
        zst = bfv(O_ZST, KC * 16)

        def tile_np(i):
            return 128 if i < NT else G

        def rtile(i):
            if i < NT:
                return f32v(O_R + 4096 * i, D)
            return f32v(O_RG, D, np_=G)

        def tile_colbase(i):
            return 128 * i if i < NT else TPC

        def early_finish():
            sflush()
            vv = s_v.n

            def fstall(sy):
                sy.wait_ge(s_v.h, vv)
                sy.wait_ge(s_s.h, s_s.n)
                sy.wait_ge(s_t.h, s_t.n)
                if STAGE >= 4:
                    sy.wait_ge(s_g.h, s_g.n)
                for i in range(NT):
                    sy.dma_start(out=out_d[128 * i: 128 * (i + 1), :],
                                 in_=f32v(O_R + 4096 * i, D)).then_inc(s_st.h, 16); s_st.n += 16
                sy.wait_ge(s_st.h, s_st.n)
            sec("sync", fstall)

        def _loads(sy):
            for i in range(NT):
                sy.dma_start(out=f32v(O_R + 4096 * i, D), in_=x_d[128 * i: 128 * (i + 1), :])\
                  .then_inc(s_in.h, 16); s_in.n += 16
            sy.dma_start(out=f32v(O_RG, D, np_=G), in_=xg_d[:, :]).then_inc(s_in.h, 16); s_in.n += 16
            sy.dma_start(out=bass.AP(f32_base.tensor, f32_base.offset + O_SEL // 4,
                                     [[f32_base.ap[0][0], 16], [1, KC * 128]]),
                         in_=sel_d[:, :]).then_inc(s_in.h, 16); s_in.n += 16
        sec("sync", _loads)
        in_x = s_in.n

        wqkv_q = bfv(O_WQQ, KC * D)
        wqkv_kv = bfv(O_R2, KC * 2048)

        def _gload(g):
            g.dma_start(out=ident[:, :], in_=ident_d[:, :]).then_inc(s_wg.h, 16); s_wg.n += 16
            for c in range(KC):
                g.dma_start(out=bfv(O_WQQ + 2048 * c, D),
                            in_=wqkv_d[128 * c: 128 * (c + 1), 0:D]).then_inc(s_wg.h, 16); s_wg.n += 16
            for c in range(KC):
                g.dma_start(out=bfv(O_R2 + 4096 * c, 2048),
                            in_=wqkv_d[128 * c: 128 * (c + 1), D:3 * D]).then_inc(s_wg.h, 16); s_wg.n += 16
            g.dma_start(out=bass.AP(bf_base.tensor, bf_base.offset + O_BOUT // 2,
                                    [[bf_base.ap[0][0], 1], [1, D]]),
                        in_=bout_d[:, :]).then_inc(s_wg.h, 16); s_wg.n += 16
            g.dma_start(out=wfc_shs[:, :], in_=wfc_sh[:, :]).then_inc(s_wg.h, 16); s_wg.n += 16
            g.dma_start(out=wpj_shs[:, :], in_=wpj_sh[:, :]).then_inc(s_wg.h, 16); s_wg.n += 16
            for k in ("wq", "wk", "wv", "wo"):
                g.dma_start(out=w1_shs[k][:, :], in_=w1_sh[k][:, :]).then_inc(s_wg.h, 16); s_wg.n += 16
        sec("gpsimd", _gload)
        wg_ident = wg_wq = wg_wkv = wg_bout = s_wg.n

        def _consts(v):
            v.memset(ones[:, :], 1.0).then_inc(s_v.h, 1); s_v.n += 1
            v.memset(cbc[:, 0:1], -SOFTMAX_C).then_inc(s_v.h, 1); s_v.n += 1
            v.memset(bd[:, :], 0.0).then_inc(s_v.h, 1); s_v.n += 1
            v.memset(zst[:, :], 0.0).then_inc(s_v.h, 1); s_v.n += 1
        sec("vector", _consts)
        v_consts = s_v.n

        W_STG = O_R3
        W_QB = O_R3 + 11264
        wcoop_done = [0]

        def emit_wcoop():
            for wi, (shd, outd) in enumerate(((wg_sh, None), (wu_sh, None),
                                              (wd_sh, None))):
                in0 = s_in.n
                vg = s_v.n
                tg = s_t.n

                def fld(sy, wi=wi, shd=shd, in0=in0, vg=vg, tg=tg):
                    sy.wait_ge(s_in.h, in0)
                    if vg:
                        sy.wait_ge(s_v.h, vg)
                    if tg:
                        sy.wait_ge(s_t.h, tg)
                    if wi < 2:
                        for j in range(22):
                            sy.dma_start(out=f32v(W_STG + 4 * 128 * j, 128),
                                         in_=shd[128 * j: 128 * (j + 1), :]).then_inc(s_in.h, 16); s_in.n += 16
                    else:
                        sy.dma_start(out=f32v(W_STG, IF), in_=shd[:, :]).then_inc(s_in.h, 16); s_in.n += 16
                sec("sync", fld)
                dma_done = s_in.n

                def fwt(v, dma_done=dma_done):
                    v.wait_ge(s_in.h, dma_done)
                sec("vector", fwt)
                for (c0, c1) in ((0, 1024), (1024, 2048), (2048, 2816)):
                    emit_quant(f32v(W_STG + 4 * c0, c1 - c0),
                               bfv(W_QB + 2 * c0, c1 - c0), 128, c1 - c0, (c1 - c0) // 16,
                               base=O_R3 + 22528)
                v_q = s_v.n

                evacs = []
                for g3 in range(3):
                    jlist = range(8 * g3, min(22, 8 * g3 + 8))
                    pb = 6 + (g3 % 2)
                    te = tr_evac.get(pb, 0)

                    def ftw(t, jlist=jlist, pb=pb, v_q=v_q, te=te):
                        t.wait_ge(s_v.h, max(v_q, te))
                        t.wait_ge(s_wg.h, wg_ident)
                        for jj, j in enumerate(jlist):
                            t.transpose(bass.AP(PB[pb].tensor, PB[pb].offset + 128 * jj,
                                                [[PB[pb].ap[0][0], 128], [1, 128]]),
                                        bfv(W_QB + 2 * 128 * j, 128),
                                        ident[:, :]).then_inc(s_t.h, 1); s_t.n += 1
                    sec("tensor", ftw)
                    td = s_t.n

                    def fte(v, g3=g3, pb=pb, td=td, nj=len(list(jlist))):
                        v.wait_ge(s_t.h, td)
                        srcap = pbap(pb, [[128, nj], [1, 128]])
                        dstap = bfap(W_QB + 11264 * 0 + 2 * (IF + 128 * 8 * g3) if False else 0, [])
                        dstap = bfap(W_QB + 2 * IF + 2 * 128 * 8 * g3, [[128, nj], [1, 128]])
                        v.tensor_copy(dstap, srcap).then_inc(s_v.h, 1); s_v.n += 1
                    sec("vector", fte)
                    tr_evac[pb] = s_v.n
                v_T = s_v.n

                st0 = s_in.n

                def fst2(sy, wi=wi, outd=outd, v_T=v_T, st0=st0):
                    sy.wait_ge(s_in.h, st0)
                    sy.wait_ge(s_v.h, v_T)
                    if wi < 2:
                        dst = bass.AP(wsh.ap().tensor, SH_GATE if wi == 0 else SH_UP,
                                      [[IF, 128], [1, IF]])
                        sy.dma_start(out=dst, in_=bfv(W_QB + 2 * IF, IF)).then_inc(s_in.h, 16); s_in.n += 16
                    else:
                        srcap = bfap(W_QB + 2 * IF, [[128, 22], [1, 128]])
                        dstap = bass.AP(wsh.ap().tensor, SH_DOWN,
                                        [[128, 128], [128 * 128, 22], [1, 128]])
                        sy.dma_start(out=dstap, in_=srcap).then_inc(s_in.h, 16); s_in.n += 16
                sec("sync", fst2)

            tr0 = s_tr.n
            stg_t = bfv(W_STG, 4096)
            vg2 = s_v.n
            tg2 = s_t.n

            def ftr1(sy, tr0=tr0, vg2=vg2, tg2=tg2):
                sy.wait_ge(s_tr.h, tr0)
                sy.wait_ge(s_v.h, vg2)
                sy.wait_ge(s_t.h, tg2)
                sy.wait_ge(s_wg.h, wg_bout)
                sy.dma_start_transpose(stg_t[:, :], wfc_shs[0:FF, 0:128]).then_inc(s_tr.h, 16); s_tr.n += 16
            sec("sync", ftr1)
            tr1 = s_tr.n
            st1 = s_in.n

            def fsv1(sy, tr1=tr1, st1=st1):
                sy.wait_ge(s_in.h, st1)
                sy.wait_ge(s_tr.h, tr1)
                sy.dma_start(out=bass.AP(wsh.ap().tensor, SH_FCT, [[FF, 128], [1, FF]]),
                             in_=stg_t[:, :]).then_inc(s_in.h, 16); s_in.n += 16
            sec("sync", fsv1)

            tr0 = s_tr.n
            st1b = s_in.n

            def ftr2(sy, tr0=tr0, st1b=st1b):
                sy.wait_ge(s_tr.h, tr0)
                sy.wait_ge(s_in.h, st1b)
                for t4 in range(4):
                    sy.dma_start_transpose(bfv(W_STG + 2048 * t4, 1024),
                                           wpj_shs[0:D, 128 * t4: 128 * (t4 + 1)]).then_inc(s_tr.h, 16); s_tr.n += 16
            sec("sync", ftr2)
            tr2 = s_tr.n
            st2 = s_in.n

            def fsv2(sy, tr2=tr2, st2=st2):
                sy.wait_ge(s_in.h, st2)
                sy.wait_ge(s_tr.h, tr2)
                srcap = bfap(W_STG, [[1024, 4], [1, 1024]])
                dstap = bass.AP(wsh.ap().tensor, SH_PJT,
                                [[1024, 128], [128 * 1024, 4], [1, 1024]])
                sy.dma_start(out=dstap, in_=srcap).then_inc(s_in.h, 16); s_in.n += 16
            sec("sync", fsv2)

            tr0 = s_tr.n
            st3 = s_in.n

            def ftr3(sy, tr0=tr0, st3=st3):
                sy.wait_ge(s_tr.h, tr0)
                sy.wait_ge(s_in.h, st3)
                for ki, k in enumerate(("wq", "wk", "wv", "wo")):
                    sy.dma_start_transpose(bfv(W_STG + 2048 * ki, 1024),
                                           w1_shs[k][0:D, 0:128]).then_inc(s_tr.h, 16); s_tr.n += 16
            sec("sync", ftr3)
            tr3 = s_tr.n
            st4 = s_in.n

            def fsv3(sy, tr3=tr3, st4=st4):
                sy.wait_ge(s_in.h, st4)
                sy.wait_ge(s_tr.h, tr3)
                for ki in range(4):
                    sy.dma_start(out=bass.AP(wsh.ap().tensor, SH_WQ + 131072 * ki,
                                             [[1024, 128], [1, 1024]]),
                                 in_=bfv(W_STG + 2048 * ki, 1024)).then_inc(s_in.h, 16); s_in.n += 16
            sec("sync", fsv3)
            wcoop_done[0] = s_in.n

        xnf = f32v(O_XNF, D)
        bnst = f32v(O_BNST, 12)
        mvv = f32v(O_MV, 2)
        vep = f32v(O_MV + 8, 1)
        rstd = f32v(O_MV + 12, 1)
        xnbf = bfv(O_XNBF, D)
        elu1 = f32v(O_E1, 512)
        elu2 = f32v(O_E2, 512)

        vepal = f32v(O_E1, 17)

        def emit_ln_stats():
            def fv(v):
                v.wait_ge(s_in.h, in_x)
                v.wait_ge(s_v.h, s_v.n)
                v.wait_ge(s_s.h, s_s.n)
                for i in range(17):
                    np_ = tile_np(i)
                    srct = rtile(i)
                    v.bn_stats(bnst[0:np_, 0:6], srct[0:np_, 0:512]).then_inc(s_v.h, 1); s_v.n += 1
                    v.bn_stats(bnst[0:np_, 6:12], srct[0:np_, 512:1024]).then_inc(s_v.h, 1); s_v.n += 1
                    v.wait_ge(s_v.h, s_v.n)
                    v.bn_aggr(mvv[0:np_, :], bnst[0:np_, :]).then_inc(s_v.h, 1); s_v.n += 1
                    v.wait_ge(s_v.h, s_v.n)
                    v.tensor_copy(stats[0:np_, 2 * i: 2 * i + 1], mvv[0:np_, 0:1]).then_inc(s_v.h, 1); s_v.n += 1
                    v.tensor_scalar(vepal[0:np_, i:i + 1], mvv[0:np_, 1:2], EPS, None,
                                    op0=ALU.add).then_inc(s_v.h, 1); s_v.n += 1
            sec("vector", fv)
            v1 = s_v.n
            mid = s_s.n + 17

            def fs(s, v1=v1, mid=mid):
                s.wait_ge(s_v.h, v1)
                for i in range(17):
                    np_ = tile_np(i)
                    s.activation(stats[0:np_, 2 * i + 1: 2 * i + 2], vepal[0:np_, i:i + 1],
                                 AF.Ln).then_inc(s_s.h, 1)
                s.wait_ge(s_s.h, mid)
                for i in range(17):
                    np_ = tile_np(i)
                    s.activation(stats[0:np_, 2 * i + 1: 2 * i + 2],
                                 stats[0:np_, 2 * i + 1: 2 * i + 2], AF.Exp,
                                 scale=-0.5).then_inc(s_s.h, 1)
            ssec(fs, 34)

        def emit_ln(i, save_stats):
            np_ = tile_np(i)
            srct = rtile(i)
            s1 = s_s.n

            def fv2(v):
                v.wait_ge(s_s.h, s1)
                v.wait_ge(s_v.h, s_v.n)
                v.wait_ge(s_t.h, s_t.n)
                v.tensor_scalar(xnbf[0:np_, :], srct[0:np_, :], stats[0:np_, 2 * i: 2 * i + 1],
                                stats[0:np_, 2 * i + 1: 2 * i + 2], op0=ALU.subtract,
                                op1=ALU.mult).then_inc(s_v.h, 1); s_v.n += 1
            sec("vector", fv2)

        tr_evac = {}

        def emit_tm_to_fm(src_bf, dst_off, colbase, np_, pbank):
            v_src = s_v.n

            def ft(t):
                t.wait_ge(s_v.h, v_src)
                t.wait_ge(s_wg.h, wg_ident)
                if pbank in tr_evac:
                    t.wait_ge(s_v.h, tr_evac[pbank])
                for c in range(KC):
                    t.transpose(bass.AP(PB[pbank].tensor, PB[pbank].offset + 128 * c,
                                        [[PB[pbank].ap[0][0], 128], [1, np_]]),
                                src_bf[0:np_, 128 * c: 128 * (c + 1)],
                                ident[0:np_, 0:np_]).then_inc(s_t.h, 1)
                    s_t.n += 1
            sec("tensor", ft)
            t1 = s_t.n

            def fe(v):
                v.wait_ge(s_t.h, t1)
                srcap = pbap(pbank, [[128, KC], [1, np_]])
                dstap = bfap(dst_off + 2 * colbase, [[XT, KC], [1, np_]])
                v.tensor_copy(dstap, srcap).then_inc(s_v.h, 1); s_v.n += 1
            sec("vector", fe)
            tr_evac[pbank] = s_v.n

        def emit_quant(src_f32, dst_bf, np_, F, blocks, base=None, tguard=0, destructive=False,
                       geng=False):
            if base is None:
                offs = (O_QT, O_QM, O_QS, O_QR, O_QSB, O_QCA, O_QCB, O_QSG, O_QTMP)
            else:
                assert not geng
                offs = (base, base + 4096, base + 4352, base + 4608, base + 4864,
                        base + 4992, base + 7040, base + 9088, None)
            t_f = src_f32 if destructive else f32v(offs[0], F)
            m_f = f32v(offs[1], blocks)
            sc_f = f32v(offs[2], blocks)
            r_f = f32v(offs[3], blocks)
            s_b = bfv(offs[4], blocks)
            ca = bfv(offs[5], F)
            cbch = bfv(offs[6], F)
            sg = bfv(offs[7], F)
            tmp = bfv(offs[8], F) if offs[8] is not None else None

            if geng:
                def fvr(v, tguard=tguard):
                    v.wait_ge(s_v.h, s_v.n)
                    v.wait_ge(s_s.h, s_s.n)
                    if tguard:
                        v.wait_ge(s_t.h, tguard)
                    src3 = bass.AP(src_f32.tensor, src_f32.offset,
                                   [[src_f32.ap[0][0], np_], [16, blocks], [1, 16]])
                    v.tensor_reduce(m_f[0:np_, :], src3, axis=AXX, op=ALU.max,
                                    apply_absolute_value=True).then_inc(s_v.h, 1); s_v.n += 1
                    v.wait_ge(s_v.h, s_v.n)
                    v.tensor_scalar(sc_f[0:np_, :], m_f[0:np_, :], 1.0 / 6.0, 1e-12,
                                    op0=ALU.mult, op1=ALU.max).then_inc(s_v.h, 1); s_v.n += 1
                    v.wait_ge(s_v.h, s_v.n)
                    v.reciprocal(r_f[0:np_, :], sc_f[0:np_, :]).then_inc(s_v.h, 1); s_v.n += 1
                    v.tensor_copy(s_b[0:np_, :], sc_f[0:np_, :]).then_inc(s_v.h, 1); s_v.n += 1
                sec("vector", fvr)
                vr = s_v.n

                def fg2(g, vr=vr):
                    g.wait_ge(s_v.h, vr)
                    g.wait_ge(s_gc.h, s_gc.n)
                    rb = bass.AP(r_f.tensor, r_f.offset,
                                 [[r_f.ap[0][0], np_], [1, blocks], [0, 16]])
                    g.tensor_tensor(t_f[0:np_, :], src_f32[0:np_, 0:F], rb,
                                    op=ALU.mult).then_inc(s_gc.h, 1); s_gc.n += 1
                sec("gpsimd", fg2)
                g2 = s_gc.n
                smid = s_s.n + 1

                def fsgg(s, g2=g2, smid=smid):
                    s.wait_ge(s_gc.h, g2)
                    s.activation(sg[0:np_, :], t_f[0:np_, :], AF.Sign).then_inc(s_s.h, 1)
                    s.wait_ge(s_s.h, smid)
                    s.activation(t_f[0:np_, :], t_f[0:np_, :], AF.Abs).then_inc(s_s.h, 1)
                ssec(fsgg, 2)
                s1 = s_s.n
                a_f = t_f

                def fg3(g, s1=s1):
                    g.wait_ge(s_s.h, s1)
                    g.wait_ge(s_v.h, s_v.n)
                    g.wait_ge(s_gc.h, s_gc.n)
                    sbb = bass.AP(s_b.tensor, s_b.offset,
                                  [[s_b.ap[0][0], np_], [1, blocks], [0, 16]])
                    g.tensor_tensor(sg[0:np_, :], sg[0:np_, :], sbb,
                                    op=ALU.mult).then_inc(s_gc.h, 1); s_gc.n += 1
                sec("gpsimd", fg3)
                g3 = s_gc.n

                def fvA(v, s1=s1, g3=g3):
                    v.wait_ge(s_s.h, s1)
                    v.wait_ge(s_v.h, s_v.n)
                    v.tensor_scalar(ca[0:np_, :], a_f[0:np_, :], 0.25, None,
                                    op0=ALU.is_gt).then_inc(s_v.h, 1); s_v.n += 1
                    for th in (0.75, 1.25, 1.75):
                        v.wait_ge(s_v.h, s_v.n)
                        v.scalar_tensor_tensor(ca[0:np_, :], a_f[0:np_, :], th, ca[0:np_, :],
                                               op0=ALU.is_gt, op1=ALU.add).then_inc(s_v.h, 1); s_v.n += 1
                    v.tensor_scalar(cbch[0:np_, :], a_f[0:np_, :], 2.5, None,
                                    op0=ALU.is_gt).then_inc(s_v.h, 1); s_v.n += 1
                    v.wait_ge(s_v.h, s_v.n)
                    v.scalar_tensor_tensor(cbch[0:np_, :], a_f[0:np_, :], 3.5, cbch[0:np_, :],
                                           op0=ALU.is_gt, op1=ALU.add).then_inc(s_v.h, 1); s_v.n += 1
                    v.tensor_scalar(tmp[0:np_, :], a_f[0:np_, :], 5.0, 2.0,
                                    op0=ALU.is_gt, op1=ALU.mult).then_inc(s_v.h, 1); s_v.n += 1
                    v.wait_ge(s_v.h, s_v.n)
                    v.tensor_tensor(cbch[0:np_, :], cbch[0:np_, :], tmp[0:np_, :],
                                    op=ALU.add).then_inc(s_v.h, 1); s_v.n += 1
                    v.wait_ge(s_v.h, s_v.n)
                    v.scalar_tensor_tensor(ca[0:np_, :], ca[0:np_, :], 0.5, cbch[0:np_, :],
                                           op0=ALU.mult, op1=ALU.add).then_inc(s_v.h, 1); s_v.n += 1
                    v.wait_ge(s_gc.h, g3)
                    v.wait_ge(s_v.h, s_v.n)
                    v.tensor_tensor(dst_bf[0:np_, 0:F], ca[0:np_, :], sg[0:np_, :],
                                    op=ALU.mult).then_inc(s_v.h, 1); s_v.n += 1
                sec("vector", fvA)
                return

            def fv(v):
                v.wait_ge(s_v.h, s_v.n)
                v.wait_ge(s_s.h, s_s.n)
                if tguard:
                    v.wait_ge(s_t.h, tguard)
                src3 = bass.AP(src_f32.tensor, src_f32.offset,
                               [[src_f32.ap[0][0], np_], [16, blocks], [1, 16]])
                v.tensor_reduce(m_f[0:np_, :], src3, axis=AXX, op=ALU.max,
                                apply_absolute_value=True).then_inc(s_v.h, 1); s_v.n += 1
                v.wait_ge(s_v.h, s_v.n)
                v.tensor_scalar(sc_f[0:np_, :], m_f[0:np_, :], 1.0 / 6.0, 1e-12,
                                op0=ALU.mult, op1=ALU.max).then_inc(s_v.h, 1); s_v.n += 1
                v.wait_ge(s_v.h, s_v.n)
                v.reciprocal(r_f[0:np_, :], sc_f[0:np_, :]).then_inc(s_v.h, 1); s_v.n += 1
                v.tensor_copy(s_b[0:np_, :], sc_f[0:np_, :]).then_inc(s_v.h, 1); s_v.n += 1
                v.wait_ge(s_v.h, s_v.n)
                rb = bass.AP(r_f.tensor, r_f.offset,
                             [[r_f.ap[0][0], np_], [1, blocks], [0, 16]])
                v.tensor_tensor(t_f[0:np_, :], src_f32[0:np_, 0:F], rb,
                                op=ALU.mult).then_inc(s_v.h, 1); s_v.n += 1
            sec("vector", fv)
            v1 = s_v.n
            smid = s_s.n + 1

            def fsg(s, v1=v1, smid=smid):
                s.wait_ge(s_v.h, v1)
                s.activation(sg[0:np_, :], t_f[0:np_, :], AF.Sign).then_inc(s_s.h, 1)
                s.wait_ge(s_s.h, smid)
                s.activation(t_f[0:np_, :], t_f[0:np_, :], AF.Abs).then_inc(s_s.h, 1)
            ssec(fsg, 2)
            s1 = s_s.n

            if tmp is not None:
                def fv2(v):
                    v.wait_ge(s_s.h, s1)
                    a_f = t_f
                    v.wait_ge(s_v.h, s_v.n)
                    v.tensor_scalar(ca[0:np_, :], a_f[0:np_, :], 0.25, None,
                                    op0=ALU.is_gt).then_inc(s_v.h, 1); s_v.n += 1
                    v.wait_ge(s_v.h, s_v.n)
                    v.tensor_scalar(tmp[0:np_, :], a_f[0:np_, :], 0.75, None,
                                    op0=ALU.is_gt).then_inc(s_v.h, 1); s_v.n += 1
                    v.wait_ge(s_v.h, s_v.n)
                    v.tensor_tensor(ca[0:np_, :], ca[0:np_, :], tmp[0:np_, :],
                                    op=ALU.add).then_inc(s_v.h, 1); s_v.n += 1
                    v.wait_ge(s_v.h, s_v.n)
                    v.tensor_scalar(cbch[0:np_, :], a_f[0:np_, :], 1.25, None,
                                    op0=ALU.is_gt).then_inc(s_v.h, 1); s_v.n += 1
                    v.wait_ge(s_v.h, s_v.n)
                    v.tensor_scalar(tmp[0:np_, :], a_f[0:np_, :], 1.75, None,
                                    op0=ALU.is_gt).then_inc(s_v.h, 1); s_v.n += 1
                    v.wait_ge(s_v.h, s_v.n)
                    v.tensor_tensor(cbch[0:np_, :], cbch[0:np_, :], tmp[0:np_, :],
                                    op=ALU.add).then_inc(s_v.h, 1); s_v.n += 1
                    v.wait_ge(s_v.h, s_v.n)
                    v.tensor_tensor(ca[0:np_, :], ca[0:np_, :], cbch[0:np_, :],
                                    op=ALU.add).then_inc(s_v.h, 1); s_v.n += 1
                    v.wait_ge(s_v.h, s_v.n)
                    v.tensor_scalar(cbch[0:np_, :], a_f[0:np_, :], 2.5, None,
                                    op0=ALU.is_gt).then_inc(s_v.h, 1); s_v.n += 1
                    v.wait_ge(s_v.h, s_v.n)
                    v.tensor_scalar(tmp[0:np_, :], a_f[0:np_, :], 3.5, None,
                                    op0=ALU.is_gt).then_inc(s_v.h, 1); s_v.n += 1
                    v.wait_ge(s_v.h, s_v.n)
                    v.tensor_tensor(cbch[0:np_, :], cbch[0:np_, :], tmp[0:np_, :],
                                    op=ALU.add).then_inc(s_v.h, 1); s_v.n += 1
                    v.wait_ge(s_v.h, s_v.n)
                    v.tensor_scalar(tmp[0:np_, :], a_f[0:np_, :], 5.0, 2.0,
                                    op0=ALU.is_gt, op1=ALU.mult).then_inc(s_v.h, 1); s_v.n += 1
                    v.wait_ge(s_v.h, s_v.n)
                    v.tensor_tensor(cbch[0:np_, :], cbch[0:np_, :], tmp[0:np_, :],
                                    op=ALU.add).then_inc(s_v.h, 1); s_v.n += 1
                    v.wait_ge(s_v.h, s_v.n)
                    v.wait_ge(s_v.h, s_v.n)
                    v.scalar_tensor_tensor(ca[0:np_, :], ca[0:np_, :], 0.5, cbch[0:np_, :],
                                           op0=ALU.mult, op1=ALU.add).then_inc(s_v.h, 1); s_v.n += 1
                    sbb = bass.AP(s_b.tensor, s_b.offset,
                                  [[s_b.ap[0][0], np_], [1, blocks], [0, 16]])
                    v.wait_ge(s_v.h, s_v.n)
                    v.tensor_tensor(sg[0:np_, :], sg[0:np_, :], sbb, op=ALU.mult).then_inc(s_v.h, 1); s_v.n += 1
                    v.wait_ge(s_v.h, s_v.n)
                    v.tensor_tensor(dst_bf[0:np_, 0:F], ca[0:np_, :], sg[0:np_, :],
                                    op=ALU.mult).then_inc(s_v.h, 1); s_v.n += 1
                sec("vector", fv2)
            else:
                def fv2(v):
                    v.wait_ge(s_s.h, s1)
                    a_f = t_f
                    v.tensor_scalar(ca[0:np_, :], a_f[0:np_, :], FP4_TH_A[0], None,
                                    op0=ALU.is_gt).then_inc(s_v.h, 1); s_v.n += 1
                    for th in FP4_TH_A[1:]:
                        v.wait_ge(s_v.h, s_v.n)
                        v.scalar_tensor_tensor(ca[0:np_, :], a_f[0:np_, :], th, ca[0:np_, :],
                                               op0=ALU.is_gt, op1=ALU.add).then_inc(s_v.h, 1); s_v.n += 1
                    v.tensor_scalar(cbch[0:np_, :], a_f[0:np_, :], FP4_TH_B[0], None,
                                    op0=ALU.is_gt).then_inc(s_v.h, 1); s_v.n += 1
                    for th in FP4_TH_B[1:]:
                        v.wait_ge(s_v.h, s_v.n)
                        v.scalar_tensor_tensor(cbch[0:np_, :], a_f[0:np_, :], th, cbch[0:np_, :],
                                               op0=ALU.is_gt, op1=ALU.add).then_inc(s_v.h, 1); s_v.n += 1
                    v.wait_ge(s_v.h, s_v.n)
                    v.scalar_tensor_tensor(ca[0:np_, :], ca[0:np_, :], 0.5, cbch[0:np_, :],
                                           op0=ALU.mult, op1=ALU.add).then_inc(s_v.h, 1); s_v.n += 1
                    sbb = bass.AP(s_b.tensor, s_b.offset,
                                  [[s_b.ap[0][0], np_], [1, blocks], [0, 16]])
                    v.wait_ge(s_v.h, s_v.n)
                    v.tensor_tensor(sg[0:np_, :], sg[0:np_, :], sbb, op=ALU.mult).then_inc(s_v.h, 1); s_v.n += 1
                    v.wait_ge(s_v.h, s_v.n)
                    v.tensor_tensor(dst_bf[0:np_, 0:F], ca[0:np_, :], sg[0:np_, :],
                                    op=ALU.mult).then_inc(s_v.h, 1); s_v.n += 1
                sec("vector", fv2)

        def emit_elu1(bank, dst_bf, np_, n):
            t0 = s_t.n

            def fv(v):
                v.wait_ge(s_t.h, t0)
                v.wait_ge(s_s.h, s_s.n)
                v.wait_ge(s_v.h, s_v.n)
                v.tensor_scalar(elu1[0:np_, 0:n], PS[bank][0:np_, 0:n], 0.0, None,
                                op0=ALU.min).then_inc(s_v.h, 1); s_v.n += 1
            sec("vector", fv)
            v1 = s_v.n

            def fs(s, v1=v1):
                s.wait_ge(s_v.h, v1)
                s.activation(elu2[0:np_, 0:n], elu1[0:np_, 0:n], AF.Exp).then_inc(s_s.h, 1)
            ssec(fs, 1)
            s1 = s_s.n

            def fv2(v):
                v.wait_ge(s_s.h, s1)
                v.scalar_tensor_tensor(dst_bf[0:np_, 0:n], PS[bank][0:np_, 0:n], 0.0,
                                       elu2[0:np_, 0:n], op0=ALU.max,
                                       op1=ALU.add).then_inc(s_v.h, 1); s_v.n += 1
            sec("vector", fv2)
            return s_v.n

        XN_FM = O_R1

        emit_ln_stats()
        for i in range(17):
            emit_ln(i, save_stats=True)
            emit_tm_to_fm(xnbf, XN_FM, tile_colbase(i), tile_np(i), 6 + (i % 2))
        emit_wcoop()

        def xn_fm_view(k, a, b):
            return bfv(XN_FM + 2 * (XT * k + a), b - a)

        if STAGE < 2:
            early_finish()
            return nc

        sflush()
        Q_BF = O_R3
        bank_free = {}
        for gi, (a, b) in enumerate(TGROUPS):
            n = b - a
            for c in range(KC):
                bank = (gi * KC + c) % 4
                tw = tr_evac.get(6, 0)
                tw2 = tr_evac.get(7, 0)
                bf = bank_free.get(bank, 0)

                def fmm(t, c=c, a=a, b=b, n=n, bank=bank, tw=tw, tw2=tw2, bf=bf):
                    t.wait_ge(s_wg.h, wg_wq)
                    t.wait_ge(s_in.h, wcoop_done[0])
                    t.wait_ge(s_v.h, max(tw, tw2, bf))
                    for k in range(KC):
                        mi = t.matmul(PS[bank][:, 0:n],
                                 bfv(O_WQQ + 2048 * k + 256 * c, 128),
                                 xn_fm_view(k, a, b),
                                 start=(k == 0), stop=(k == KC - 1))
                    mi.then_inc(s_t.h, 1); s_t.n += 1
                sec("tensor", fmm)
                bank_free[bank] = emit_elu1(bank, bfv(Q_BF + 2 * (XT * c + a), n), 128, n)

        def q_bf_view(c, a, b):
            return bfv(Q_BF + 2 * (XT * c + a), b - a)

        if STAGE < 3:
            early_finish()
            return nc

        sflush()
        k_bf = bfv(O_KBF, D)
        v_ex = bfv(O_VEX, 1040)

        def fvo(v):
            v.wait_ge(s_v.h, s_v.n)
            v.memset(v_ex[:, :], 1.0).then_inc(s_v.h, 1); s_v.n += 1
        sec("vector", fvo)

        for i in range(NT):
            vfree = s_v.n

            def fkv(t, i=i, vfree=vfree):
                t.wait_ge(s_wg.h, wg_wkv)
                t.wait_ge(s_v.h, vfree)
                for half in range(2):
                    for k in range(KC):
                        mi = t.matmul(PS[half][:, :],
                                 xn_fm_view(k, 128 * i, 128 * (i + 1)),
                                 bfv(O_R2 + 4096 * k + 1024 * half, 512),
                                 start=(k == 0), stop=(k == KC - 1))
                for half in range(2):
                    for k in range(KC):
                        mi = t.matmul(PS[2 + half][:, :],
                                 xn_fm_view(k, 128 * i, 128 * (i + 1)),
                                 bfv(O_R2 + 4096 * k + 2048 + 1024 * half, 512),
                                 start=(k == 0), stop=(k == KC - 1))
                mi.then_inc(s_t.h, 1); s_t.n += 1
            sec("tensor", fkv)
            emit_elu1(0, bfv(O_KBF, 512), 128, 512)
            emit_elu1(1, bfv(O_KBF + 1024, 512), 128, 512)
            t_v = s_t.n

            def fvv(v, t_v=t_v):
                v.wait_ge(s_t.h, t_v)
                v.wait_ge(s_v.h, s_v.n)
                for half in range(2):
                    dst = bfap(O_VEX + 2 * 520 * half, [[65, 8], [1, 64]])
                    src = psap(2 + half, [[64, 8], [1, 64]])
                    v.tensor_copy(dst, src).then_inc(s_v.h, 1); s_v.n += 1
            sec("vector", fvv)
            v_kv = s_v.n

            te6 = tr_evac.get(6, 0)
            te7 = tr_evac.get(7, 0)

            def fkvmm(t, i=i, v_kv=v_kv, te6=te6, te7=te7):
                t.wait_ge(s_v.h, v_kv)
                if i == 0:
                    t.wait_ge(s_v.h, max(te6, te7))
                for p in range(8):
                    bank = 4 + p // 2
                    colb = 130 * (p % 2)
                    first = (i == 0 and p % 2 == 0)
                    last = (i == NT - 1 and p % 2 == 1)
                    mi = t.matmul(PS[bank][:, colb:colb + 130],
                                  k_bf[:, 128 * p: 128 * (p + 1)],
                                  v_ex[:, 130 * p: 130 * (p + 1)], start=first, stop=last)
                mi.then_inc(s_t.h, 1); s_t.n += 1
            sec("tensor", fkvmm)

        if STAGE < 4:
            early_finish()
            return nc

        t_kv = s_t.n

        def fkvout(v):
            v.wait_ge(s_t.h, t_kv)
            for h in range(H):
                p = h // 2
                bank = 4 + p // 2
                colb = 130 * (p % 2) + 65 * (h % 2)
                po = 64 * (h % 2)
                dst_c = 260 * (h // 8) + 65 * ((h // 2) % 4)
                v.tensor_copy(kvsb[po:po + 64, dst_c:dst_c + 65],
                              PS[bank][po:po + 64, colb:colb + 65]).then_inc(s_v.h, 1); s_v.n += 1
        sec("vector", fkvout)
        v_kvsb = s_v.n

        def fag(g):
            g.wait_ge(s_in.h, wcoop_done[0])
            g.collective_compute(
                "AllGather", ALU.bypass,
                replica_groups=[list(range(8))],
                ins=[wsh.ap().opt()],
                outs=[wsh_g.ap().opt()],
            ).then_inc(s_cc.h, 1); s_cc.n += 1
        sec("gpsimd", fag)
        cc_ag = s_cc.n

        def fcct(g):
            g.wait_ge(s_v.h, v_kvsb)
            g.dma_start(out=cc_in[:, :], in_=kvsb[:, :]).then_inc(s_g.h, 16); s_g.n += 16
            g.wait_ge(s_g.h, s_g.n)
            g.collective_compute(
                "AllReduce", ALU.add,
                replica_groups=[[0, 1, 2, 3], [4, 5, 6, 7]],
                ins=[cc_in.ap().opt()],
                outs=[cc_out.ap().opt()],
            ).then_inc(s_cc.h, 1); s_cc.n += 1
        sec("gpsimd", fcct)
        cc_kv = s_cc.n
        _gkv = [0]

        def emit_kv_readback():
            g0 = s_g.n

            def fccr(g, g0=g0):
                g.wait_ge(s_cc.h, cc_kv)
                g.wait_ge(s_g.h, g0)
                g.dma_start(out=kvsb[:, :], in_=cc_out[:, :]).then_inc(s_g.h, 16); s_g.n += 16
            sec("gpsimd", fccr)
            _gkv[0] = s_g.n
        g_kv = None

        wg0 = s_wg.n

        def fgwout(g):
            g.wait_ge(s_wg.h, wg0)
            g.wait_ge(s_t.h, t_kv)
            for c in range(KC):
                g.dma_start(out=bfv(O_R2 + 2048 * c, D),
                            in_=wout_d[128 * c: 128 * (c + 1), :]).then_inc(s_wg.h, 16); s_wg.n += 16
        sec("gpsimd", fgwout)
        wg_wout = s_wg.n



        sflush()
        XQ_FM = O_R1
        tr_evac.clear()
        def xq_fm_view(k, a, b):
            return bfv(XQ_FM + 2 * (XT * k + a), b - a)
        xsrc = f32v(O_QA, D)
        for i in range(17):
            np_ = tile_np(i)
            vguard = s_v.n

            in0 = s_in.n

            def fxd(sy, i=i, np_=np_, vguard=vguard, in0=in0):
                sy.wait_ge(s_in.h, in0)
                sy.wait_ge(s_v.h, vguard)
                src = x_d[128 * i: 128 * (i + 1), :] if i < NT else xg_d[:, :]
                sy.dma_start(out=f32v(O_QA, D, np_=np_), in_=src).then_inc(s_in.h, 16); s_in.n += 16
            sec("sync", fxd)
            d_x = s_in.n

            def fx(v, i=i, np_=np_, d_x=d_x):
                v.wait_ge(s_in.h, d_x)
                v.wait_ge(s_v.h, s_v.n)
                v.wait_ge(s_s.h, s_s.n)
                v.tensor_scalar(xsrc[0:np_, :], xsrc[0:np_, :],
                                stats[0:np_, 2 * i: 2 * i + 1],
                                stats[0:np_, 2 * i + 1: 2 * i + 2],
                                op0=ALU.subtract, op1=ALU.mult).then_inc(s_v.h, 1); s_v.n += 1
            sec("vector", fx)
            emit_quant(xsrc, xnbf, np_, D, 64, destructive=True, geng=True)
            emit_tm_to_fm(xnbf, XQ_FM, tile_colbase(i), np_, 6 + (i % 2))
            if i == 3:
                emit_kv_readback()


        def fbd(v):
            v.wait_ge(s_g.h, _gkv[0])
            v.wait_ge(s_v.h, v_consts)
            kv_all = f32_base[:, O_KV // 4: O_KV // 4 + 520]
            bd_all = bf_base[:, O_BD // 2: O_BD // 2 + KC * 128]
            zst_all = bf_base[:, O_ZST // 2: O_ZST // 2 + KC * 16]
            for c in range(KC):
                for hh, po in ((2 * c, 0), (2 * c + 1, 64)):
                    base = 260 * (hh // 8) + 65 * ((hh // 2) % 4)
                    sp = 64 * (hh % 2)
                    v.tensor_copy(bd_all[po:po + 64, 128 * c + po: 128 * c + po + 64],
                                  kv_all[sp:sp + 64, base: base + 64]).then_inc(s_v.h, 1); s_v.n += 1
                    v.tensor_copy(zst_all[po:po + 64, 16 * c + hh: 16 * c + hh + 1],
                                  kv_all[sp:sp + 64, base + 64: base + 65]).then_inc(s_v.h, 1); s_v.n += 1
        sec("vector", fbd)
        v_bd = s_v.n

        attn_b = bfv(O_ATT, KC * 512)
        denr = f32v(O_DEN, 512, np_=16)
        recb = f32v(O_RCB, 512)
        c_ev = [v_bd]

        for gi, (a, b) in enumerate(TGROUPS):
            n = b - a
            cv0 = c_ev[0]

            def fden(t, a=a, b=b, n=n, cv0=cv0):
                t.wait_ge(s_v.h, max(v_bd, cv0))
                for c in range(KC):
                    mi = t.matmul(PS[2][0:16, 0:n], zst[:, 16 * c: 16 * (c + 1)],
                             q_bf_view(c, a, b), start=(c == 0), stop=(c == KC - 1))
                mi.then_inc(s_t.h, 1); s_t.n += 1
            sec("tensor", fden)
            t_den = s_t.n

            def fdr(v, n=n, t_den=t_den):
                v.wait_ge(s_t.h, t_den)
                v.tensor_scalar(denr[0:16, 0:n], PS[2][0:16, 0:n], 1e-6, None,
                                op0=ALU.add).then_inc(s_v.h, 1); s_v.n += 1
                v.wait_ge(s_v.h, s_v.n)
                v.reciprocal(denr[0:16, 0:n], denr[0:16, 0:n]).then_inc(s_v.h, 1); s_v.n += 1
            sec("vector", fdr)
            v_den = s_v.n

            for c in range(KC):
                nb = c % 2
                cv = c_ev[0]

                def fnum(t, c=c, a=a, b=b, n=n, nb=nb, v_den=v_den, cv=cv):
                    t.wait_ge(s_v.h, max(v_den, cv))
                    mi = t.matmul(PS[nb][:, 0:n], bd[:, 128 * c: 128 * (c + 1)],
                             q_bf_view(c, a, b), start=True, stop=True)
                    mi = t.matmul(PS[3][:, 0:n], sel[0:16, 128 * c: 128 * (c + 1)],
                             denr[0:16, 0:n], start=True, stop=True)
                    mi.then_inc(s_t.h, 1); s_t.n += 1
                sec("tensor", fnum)
                t_num = s_t.n

                def fnev(v, c=c, n=n, nb=nb, t_num=t_num):
                    v.wait_ge(s_t.h, t_num)
                    v.tensor_copy(recb[:, 0:n], PS[3][:, 0:n]).then_inc(s_v.h, 1); s_v.n += 1
                    v.wait_ge(s_v.h, s_v.n)
                    v.tensor_tensor(attn_b[:, 512 * c: 512 * c + n], PS[nb][:, 0:n],
                                    recb[:, 0:n], op=ALU.mult).then_inc(s_v.h, 1); s_v.n += 1
                sec("vector", fnev)
                c_ev[0] = s_v.n
            v_attn = s_v.n

            for ti in range((n + 127) // 128):
                i = (a + 128 * ti) // 128 if a < TPC else NT
                np_ = tile_np(i)
                cv = c_ev[0]

                def fwo(t, ti=ti, np_=np_, v_attn=v_attn, cv=cv):
                    t.wait_ge(s_v.h, max(v_attn, cv, v_consts))
                    t.wait_ge(s_wg.h, wg_wout)
                    for half in range(2):
                        for c in range(KC):
                            mi = t.matmul(PS[4 + half][0:np_, :],
                                     bfv(O_ATT + 2 * (512 * c + 128 * ti), np_),
                                     bfv(O_R2 + 2048 * c + 1024 * half, 512),
                                     start=(c == 0), stop=False)
                        mi = t.matmul(PS[4 + half][0:np_, :], ones[0:1, 0:np_],
                                 boutb[0:1, 512 * half: 512 * (half + 1)],
                                 start=False, stop=True)
                    mi.then_inc(s_t.h, 1); s_t.n += 1
                sec("tensor", fwo)
                t_wo = s_t.n

                def fwe(v, i=i, np_=np_, t_wo=t_wo):
                    v.wait_ge(s_t.h, t_wo)
                    dst = rtile(i)
                    for half in range(2):
                        v.tensor_tensor(dst[0:np_, 512 * half: 512 * (half + 1)],
                                        dst[0:np_, 512 * half: 512 * (half + 1)],
                                        PS[4 + half][0:np_, :], op=ALU.add).then_inc(s_v.h, 1); s_v.n += 1
                sec("vector", fwe)
                c_ev[0] = s_v.n

        if STAGE < 5:
            early_finish()
            return nc

        sflush()


        hg_f = f32v(O_HG, 512)
        hf_f = f32v(O_HF, 512)
        hq_t = bfv(O_HQ, 512)
        hq_fm = bfv(O_HQF, 4 * 128)
        wT_ready = {}
        wT_free = {}
        b_free = [c_ev[0]]

        def wg_off(par): return O_R3 + par * 8192
        def wu_off(par): return O_R3 + 16384 + par * 8192
        def wd_off(par): return O_R2 + 16384 + par * 8192

        def emit_fld(si):
            par = si % 2
            n0, n1 = NSL[si]
            ns = n1 - n0
            in0 = s_in.n
            wfree = wT_free.get(par, s_t.n)

            def fld(sy, n0=n0, n1=n1, ns=ns, par=par, in0=in0, wfree=wfree):
                sy.wait_ge(s_in.h, in0)
                sy.wait_ge(s_cc.h, cc_ag)
                sy.wait_ge(s_t.h, wfree)
                for c in range(KC):
                    sy.dma_start(out=bfv(wg_off(par) + 2 * 512 * c, ns),
                                 in_=bass.AP(wsh_g.ap().tensor, c * SHN + SH_GATE + n0,
                                             [[IF, 128], [1, ns]])).then_inc(s_in.h, 16); s_in.n += 16
                for c in range(KC):
                    sy.dma_start(out=bfv(wu_off(par) + 2 * 512 * c, ns),
                                 in_=bass.AP(wsh_g.ap().tensor, c * SHN + SH_UP + n0,
                                             [[IF, 128], [1, ns]])).then_inc(s_in.h, 16); s_in.n += 16
                for c in range(ns // 128):
                    srcap = bass.AP(wsh_g.ap().tensor, SH_DOWN + (n0 + 128 * c) * 128,
                                    [[128, 128], [SHN, 8], [1, 128]])
                    dstap = bfap(wd_off(par) + 2 * D * c, [[128, 8], [1, 128]])
                    sy.dma_start(out=dstap, in_=srcap).then_inc(s_in.h, 16); s_in.n += 16
            sec("sync", fld)
            wT_ready[si] = s_in.n

        items = [(si, p) for si in range(len(NSL)) for p in range(9)]
        hq_evac_s = {}
        bguard = [0, 0]
        t_gu_of = {}
        bhf = f32v(O_BHF, 1024)
        bhg = f32v(O_BHG, 1024)

        def emit_fgu(si, p):
            n0, n1 = NSL[si]
            ns = n1 - n0
            par = si % 2
            wr = wT_ready[si]
            gs, gv = bguard
            tiles = [2 * p, 2 * p + 1] if p < 8 else [16]

            def fgu(t, si=si, p=p, ns=ns, par=par, wr=wr, gs=gs, gv=gv, tiles=tiles):
                t.wait_ge(s_in.h, wr)
                if gs:
                    t.wait_ge(s_s.h, gs)
                if gv:
                    t.wait_ge(s_v.h, gv)
                for ti, i in enumerate(tiles):
                    np_ = tile_np(i)
                    colbase = tile_colbase(i)
                    bp = 2 * ti
                    for c in range(KC):
                        mi = t.matmul(PS[bp][0:np_, 0:ns],
                                      xq_fm_view(c, colbase, colbase + np_),
                                      bfv(wg_off(par) + 2 * 512 * c, ns),
                                      start=(c == 0), stop=(c == KC - 1))
                    for c in range(KC):
                        mi = t.matmul(PS[bp + 1][0:np_, 0:ns],
                                      xq_fm_view(c, colbase, colbase + np_),
                                      bfv(wu_off(par) + 2 * 512 * c, ns),
                                      start=(c == 0), stop=(c == KC - 1))
                mi.then_inc(s_t.h, 1); s_t.n += 1
            sec("tensor", fgu)
            t_gu_of[(si, p)] = s_t.n

        emit_fld(0)
        emit_fld(1)
        emit_fgu(0, 0)
        t_th_prev = [0]
        pend_fra = []
        for j, (si, p) in enumerate(items):
            par = si % 2
            n0, n1 = NSL[si]
            ns = n1 - n0
            tiles = [2 * p, 2 * p + 1] if p < 8 else [16]
            t_gu = t_gu_of[(si, p)]

            for ti, i in enumerate(tiles):
                np_ = tile_np(i)
                bp = 2 * ti
                vhg = s_v.n

                def fsl(s, np_=np_, ns=ns, bp=bp, ti=ti, t_gu=t_gu, vhg=vhg):
                    s.wait_ge(s_t.h, t_gu)
                    s.wait_ge(s_v.h, vhg)
                    s.activation(bhg[0:np_, 512 * ti: 512 * ti + ns], PS[bp][0:np_, 0:ns],
                                 AF.Silu).then_inc(s_s.h, 1)
                ssec(fsl, 1)
                s_si = s_s.n

                def fmu(v, np_=np_, ns=ns, bp=bp, ti=ti, s_si=s_si):
                    v.wait_ge(s_s.h, s_si)
                    v.wait_ge(s_v.h, s_v.n)
                    v.tensor_tensor(bhf[0:np_, 512 * ti: 512 * ti + ns],
                                    bhg[0:np_, 512 * ti: 512 * ti + ns],
                                    PS[bp + 1][0:np_, 0:ns], op=ALU.mult).then_inc(s_v.h, 1); s_v.n += 1
                sec("vector", fmu)
            bguard[0] = s_s.n
            bguard[1] = s_v.n

            if j + 1 < len(items):
                emit_fgu(*items[j + 1])

            npq = tile_np(tiles[0])
            Fq = ns * len(tiles)
            if len(tiles) == 2 and ns == 512:
                emit_quant(bhf, bfv(O_BHQ, 1024), 128, 1024, 64,
                           tguard=t_th_prev[0], destructive=True, geng=True)
                hq_views = [bfv(O_BHQ + 2 * 512 * ti, 512) for ti in range(2)]
            else:
                hq_views = []
                for ti, i in enumerate(tiles):
                    emit_quant(f32v(O_BHF + 4 * 512 * ti, ns), bfv(O_BHQ + 2 * 512 * ti, ns),
                               tile_np(i), ns, ns // 16, tguard=t_th_prev[0], destructive=True,
                               geng=True)
                    hq_views.append(bfv(O_BHQ + 2 * 512 * ti, ns))
            v_hq = s_v.n

            for fn in pend_fra:
                sec("vector", fn); s_v.n += 2
            del pend_fra[:]
            b_free[0] = s_v.n

            for ti, i in enumerate(tiles):
                np_ = tile_np(i)
                pb = 6 + (ti % 2)
                te_s = hq_evac_s.get(pb, 0)

                def fth(t, np_=np_, ns=ns, pb=pb, ti=ti, v_hq=v_hq, te_s=te_s):
                    t.wait_ge(s_v.h, v_hq)
                    if te_s:
                        t.wait_ge(s_s.h, te_s)
                    for c in range(ns // 128):
                        t.transpose(bass.AP(PB[pb].tensor, PB[pb].offset + 128 * c,
                                            [[PB[pb].ap[0][0], 128], [1, np_]]),
                                    bfv(O_BHQ + 2 * (512 * ti + 128 * c), 128, np_=np_),
                                    ident[0:np_, 0:np_]).then_inc(s_t.h, 1); s_t.n += 1
                sec("tensor", fth)
                t_th = s_t.n
                t_th_prev[0] = t_th

                def fhe(s, np_=np_, ns=ns, pb=pb, ti=ti, t_th=t_th):
                    s.wait_ge(s_t.h, t_th)
                    srcap = pbap(pb, [[128, ns // 128], [1, np_]])
                    dstap = bfap(O_HQF2 + 1024 * ti, [[128, ns // 128], [1, np_]])
                    s.copy(dstap, srcap).then_inc(s_s.h, 1)
                ssec(fhe, 1)
                hq_evac_s[pb] = s_s.n
                v_hf = s_s.n
                bfree = b_free[0]

                def fdn(t, np_=np_, ns=ns, par=par, ti=ti, v_hf=v_hf, bfree=bfree):
                    t.wait_ge(s_s.h, v_hf)
                    t.wait_ge(s_v.h, bfree)
                    for half in range(2):
                        for c in range(ns // 128):
                            mi = t.matmul(PS[4 + half][0:np_, :],
                                          bfv(O_HQF2 + 1024 * ti + 256 * c, np_),
                                          bfv(wd_off(par) + 2 * (D * c + 512 * half), 512),
                                          start=(c == 0), stop=(c == ns // 128 - 1))
                    mi.then_inc(s_t.h, 1); s_t.n += 1
                sec("tensor", fdn)
                t_dn = s_t.n

                def fra(v, i=i, np_=np_, t_dn=t_dn):
                    v.wait_ge(s_t.h, t_dn)
                    dst = rtile(i)
                    for half in range(2):
                        v.tensor_tensor(dst[0:np_, 512 * half: 512 * (half + 1)],
                                        dst[0:np_, 512 * half: 512 * (half + 1)],
                                        PS[4 + half][0:np_, :], op=ALU.add).then_inc(s_v.h, 1)
                pend_fra.append(fra)
                if ti == 0 and len(tiles) == 2:
                    for fn in pend_fra:
                        sec("vector", fn); s_v.n += 2
                    del pend_fra[:]
                    b_free[0] = s_v.n
            if p == 8:
                wT_free[par] = s_t.n
                if si + 2 < len(NSL):
                    emit_fld(si + 2)
        for fn in pend_fra:
            sec("vector", fn); s_v.n += 2
        del pend_fra[:]
        b_free[0] = s_v.n

        sflush()
        XN1_FM = O_R3
        tr_evac.clear()

        def xn1_view(k, a, b):
            return bfv(XN1_FM + 2 * (XT * k + a), b - a)

        emit_ln_stats()
        for i in range(17):
            emit_ln(i, save_stats=False)
            emit_tm_to_fm(xnbf, XN1_FM, tile_colbase(i), tile_np(i), 6 + (i % 2))
        v_xn1 = s_v.n

        O_W1A = O_R2
        O_W1B = O_R2 + 16384

        W1OFF = {"wq": SH_WQ, "wk": SH_WK, "wv": SH_WV, "wo": SH_WO}

        def emit_w1T(dst_off, src_s, key, guard_t):
            in0 = s_in.n

            def ftr(sy):
                sy.wait_ge(s_in.h, in0)
                sy.wait_ge(s_cc.h, cc_ag)
                sy.wait_ge(s_t.h, guard_t)
                for c in range(KC):
                    sy.dma_start(out=bfv(dst_off + 2 * D * c, D),
                                 in_=bass.AP(wsh_g.ap().tensor, c * SHN + W1OFF[key],
                                             [[1024, 128], [1, D]])).then_inc(s_in.h, 16); s_in.n += 16
            sec("sync", ftr)
            return s_in.n

        O_QH = O_R1
        O_KH = O_R1 + 16384
        probs = bfv(O_PROBS, 192)
        pT = bfv(O_PT, 256)
        av_f = bfv(O_AVF, KC * 128)
        zbv = f32v(O_ZB, 1)
        zrv = f32v(O_ZR, 1)
        l1b = {}
        l1_ev = [b_free[0]]
        probs_free = [0]

        for half in range(2):
            hb = TPC // 2 * half
            HG2 = [(hb, hb + 512), (hb + 512, hb + 1024)]
            tr_wq = emit_w1T(O_W1A, None, "wq", s_t.n)

            for gi, (a, b) in enumerate(HG2):
                for c in range(KC):
                    bank = (gi * KC + c) % 4
                    bg = l1b.get(("b", bank), 0)
                    lev = l1_ev[0]

                    def fq(t, c=c, a=a, b=b, bank=bank, bg=bg, lev=lev, tr_wq=tr_wq):
                        t.wait_ge(s_in.h, tr_wq)
                        t.wait_ge(s_v.h, max(lev, v_xn1))
                        if bg:
                            t.wait_ge(s_s.h, bg)
                        for k in range(KC):
                            mi = t.matmul(PS[bank][:, :],
                                     bfv(O_W1A + 2 * (D * k + 128 * c), 128),
                                     xn1_view(k, a, b),
                                     start=(k == 0), stop=(k == KC - 1))
                        mi.then_inc(s_t.h, 1); s_t.n += 1
                    sec("tensor", fq)
                    t_q = s_t.n

                    def fqe(s, c=c, gi=gi, bank=bank, t_q=t_q):
                        s.wait_ge(s_t.h, t_q)
                        s.copy(bfv(O_QH + 2 * (D * c + 512 * gi), 512)[:, :],
                               PS[bank][:, :]).then_inc(s_s.h, 1)
                    ssec(fqe, 1)
                    l1b[("b", bank)] = s_s.n

            tr_wk = emit_w1T(O_W1B, None, "wk", s_t.n)
            kg2 = list(HG2) + ([(TPC, XT)] if half == 0 else [])
            for gi, (a, b) in enumerate(kg2):
                n = b - a
                for c in range(KC):
                    bank = (gi * KC + c) % 4
                    bg = l1b.get(("b", bank), 0)

                    lev = l1_ev[0]

                    def fk(t, c=c, a=a, b=b, n=n, bank=bank, bg=bg, tr_wk=tr_wk, lev=lev):
                        t.wait_ge(s_in.h, tr_wk)
                        if lev:
                            t.wait_ge(s_v.h, lev)
                        if bg:
                            t.wait_ge(s_s.h, bg)
                        for k in range(KC):
                            mi = t.matmul(PS[bank][:, 0:n],
                                     bfv(O_W1B + 2 * (D * k + 128 * c), 128),
                                     xn1_view(k, a, b),
                                     start=(k == 0), stop=(k == KC - 1))
                        mi.then_inc(s_t.h, 1); s_t.n += 1
                    sec("tensor", fk)
                    t_k = s_t.n

                    def fke(s, c=c, gi=gi, a=a, n=n, bank=bank, t_k=t_k):
                        s.wait_ge(s_t.h, t_k)
                        if a < TPC:
                            dst = bfv(O_KH + 2 * (D * c + 512 * gi), n)
                        else:
                            dst = bfv(O_KG + 2 * (G * c), n)
                        s.copy(dst[:, :], PS[bank][0:128, 0:n]).then_inc(s_s.h, 1)
                    ssec(fke, 1)
                    l1b[("b", bank)] = s_s.n

            tr_wv = emit_w1T(O_W1A, None, "wv", s_t.n)
            vtl = list(range(8)) + ([16] if half == 0 else [])
            for vi in vtl:
                np_ = 128 if vi < 16 else G
                colbase = hb + 128 * vi if vi < 16 else TPC
                bank2 = 2 * ((vi % 2))
                bg = max(l1b.get(("b", bank2), 0), l1b.get(("b", bank2 + 1), 0))

                lev = l1_ev[0]

                def fv3(t, vi=vi, np_=np_, colbase=colbase, bank2=bank2, bg=bg, tr_wv=tr_wv, lev=lev):
                    t.wait_ge(s_in.h, tr_wv)
                    if lev:
                        t.wait_ge(s_v.h, lev)
                    if bg:
                        t.wait_ge(s_s.h, bg)
                    for hf in range(2):
                        for k in range(KC):
                            mi = t.matmul(PS[bank2 + hf][0:np_, :],
                                     xn1_view(k, colbase, colbase + np_),
                                     bfv(O_W1A + 2 * (D * k + 512 * hf), 512),
                                     start=(k == 0), stop=(k == KC - 1))
                    mi.then_inc(s_t.h, 1); s_t.n += 1
                sec("tensor", fv3)
                t_v3 = s_t.n

                def fve(s, vi=vi, np_=np_, bank2=bank2, t_v3=t_v3):
                    s.wait_ge(s_t.h, t_v3)
                    for hf in range(2):
                        if vi < 16:
                            dst = bfv(O_VH + 2 * (D * vi + 512 * hf), 512)
                        else:
                            dst = bfv(O_VG + 2 * 512 * hf, 512, np_=G)
                        s.copy(dst[0:np_, :], PS[bank2 + hf][0:np_, :]).then_inc(s_s.h, 1)
                ssec(fve, 2)
                l1b[("b", bank2)] = l1b[("b", bank2 + 1)] = s_s.n
            s_qkv = s_s.n

            tr_wo = emit_w1T(O_W1B, None, "wo", s_t.n)

            wseq = [(w, h) for w in range(8) for h in range(H)]
            t_sc_of = {}
            probs_par_free = {}

            def emit_fsc(j):
                w, h = wseq[j]
                scb = j % 2
                bg = l1b.get(("sc", scb), 0)

                def fsc(t, w=w, h=h, scb=scb, bg=bg):
                    t.wait_ge(s_s.h, max(s_qkv, bg))
                    po = 64 * (h % 2)
                    qb_ = O_QH // 2 + D * (h // 2) + 128 * w
                    kb_ = O_KH // 2 + D * (h // 2) + 128 * w
                    gb_ = O_KG // 2 + G * (h // 2)
                    qv = bf_base[po:po + 64, qb_:qb_ + 128]
                    kvw = bf_base[po:po + 64, kb_:kb_ + 128]
                    kgv = bf_base[po:po + 64, gb_:gb_ + G]
                    mi = t.matmul(PS[scb][:, 0:G], qv, kgv, start=True, stop=True)
                    mi = t.matmul(PS[scb][:, G:G + 128], qv, kvw, start=True, stop=True)
                    mi.then_inc(s_t.h, 1); s_t.n += 1
                sec("tensor", fsc)
                t_sc_of[j] = s_t.n

            emit_fsc(0)
            for j, (w, h) in enumerate(wseq):
                par = j % 2
                scb = par
                probsv = bfv(O_PROBS + 384 * par, 192)
                pTv = bfv(O_PT + 512 * par, 256)
                zbvp = f32v(O_ZB + 4 * par, 1)
                zrvp = f32v(O_ZR + 4 * par, 1)
                t_sc = t_sc_of[j]
                pf = probs_par_free.get(par, 0)

                def fex(s, scb=scb, t_sc=t_sc, pf=pf, probsv=probsv, zbvp=zbvp):
                    s.wait_ge(s_t.h, t_sc)
                    if pf:
                        s.wait_ge(s_v.h, pf)
                    s.activation(probsv[:, :], PS[scb][:, 0:192], AF.Exp,
                                 bias=cbc[:, 0:1], scale=0.125,
                                 accum_out=zbvp[:, 0:1]).then_inc(s_s.h, 1)
                ssec(fex, 1)
                l1b[("sc", scb)] = s_s.n
                l1b[("b", scb)] = s_s.n
                s_ex = s_s.n

                def fzr(v, s_ex=s_ex, probsv=probsv, zbvp=zbvp, zrvp=zrvp):
                    v.wait_ge(s_s.h, s_ex)
                    v.reciprocal(zrvp[:, 0:1], zbvp[:, 0:1]).then_inc(s_v.h, 1); s_v.n += 1
                    v.wait_ge(s_v.h, s_v.n)
                    v.tensor_scalar(probsv[:, :], probsv[:, :], zrvp[:, 0:1], None,
                                    op0=ALU.mult).then_inc(s_v.h, 1); s_v.n += 1
                sec("vector", fzr)
                v_pn = s_v.n

                if j + 1 < len(wseq):
                    emit_fsc(j + 1)

                ptb = 6 + par
                bg2 = l1b.get(("pt", ptb), 0)

                def fpt(t, ptb=ptb, v_pn=v_pn, bg2=bg2, probsv=probsv):
                    t.wait_ge(s_v.h, max(v_pn, bg2))
                    t.transpose(bass.AP(PB[ptb].tensor, PB[ptb].offset,
                                        [[PB[ptb].ap[0][0], 128], [1, 128]]),
                                probsv[:, G:192], ident[:, :]).then_inc(s_t.h, 1); s_t.n += 1
                    t.transpose(bass.AP(PB[ptb].tensor, PB[ptb].offset + 128,
                                        [[PB[ptb].ap[0][0], G], [1, 128]]),
                                probsv[:, 0:G], ident[:, :]).then_inc(s_t.h, 1); s_t.n += 1
                sec("tensor", fpt)
                t_pt = s_t.n

                def fpe(v, ptb=ptb, t_pt=t_pt, pTv=pTv):
                    v.wait_ge(s_t.h, t_pt)
                    v.tensor_copy(pTv[:, :], PB[ptb][:, 0:256]).then_inc(s_v.h, 1); s_v.n += 1
                sec("vector", fpe)
                l1b[("pt", ptb)] = s_v.n
                v_pt = s_v.n
                probs_par_free[par] = v_pt

                avg = l1b.get(("av",), 0)

                def fav(t, w=w, h=h, v_pt=v_pt, avg=avg, pTv=pTv):
                    t.wait_ge(s_v.h, v_pt)
                    if avg and h % 2 == 0:
                        t.wait_ge(s_v.h, avg)
                    po = 64 * (h % 2)
                    vv = bfv(O_VH + 2 * (D * w + 64 * h), 64)
                    vgv = bfv(O_VG + 2 * 64 * h, 64, np_=G)
                    outap = PS[4][po:po + 64, 0:128]
                    mi = t.matmul(outap, vv, pTv[:, 0:128], start=True, stop=False)
                    mi = t.matmul(outap, vgv[0:G, :], pTv[0:G, 128:256], start=False, stop=True)
                    mi.then_inc(s_t.h, 1); s_t.n += 1
                sec("tensor", fav)
                t_av = s_t.n

                if h % 2 == 1:
                    def fae(v, h=h, t_av=t_av):
                        v.wait_ge(s_t.h, t_av)
                        v.tensor_copy(av_f[:, 128 * (h // 2): 128 * (h // 2) + 128],
                                      PS[4][:, 0:128]).then_inc(s_v.h, 1); s_v.n += 1
                    sec("vector", fae)
                    l1b[("av",)] = s_v.n

                if h == H - 1:
                    s_av = s_v.n
                    lev = l1_ev[0]

                    def fw1(t, w=w, hb=hb, s_av=s_av, lev=lev, tr_wo=tr_wo):
                        t.wait_ge(s_v.h, max(s_av, lev))
                        t.wait_ge(s_in.h, tr_wo)
                        for hf in range(2):
                            for c in range(KC):
                                mi = t.matmul(PS[2 + hf][:, :],
                                              av_f[:, 128 * c: 128 * (c + 1)],
                                              bfv(O_W1B + 2 * (D * c + 512 * hf), 512),
                                              start=(c == 0), stop=(c == KC - 1))
                        mi.then_inc(s_t.h, 1); s_t.n += 1
                    sec("tensor", fw1)
                    t_w1 = s_t.n

                    def fwe1(v, w=w, hb=hb, t_w1=t_w1):
                        v.wait_ge(s_t.h, t_w1)
                        dst = rtile((hb + 128 * w) // 128)
                        for hf in range(2):
                            v.tensor_tensor(dst[:, 512 * hf: 512 * (hf + 1)],
                                            dst[:, 512 * hf: 512 * (hf + 1)],
                                            PS[2 + hf][:, :], op=ALU.add).then_inc(s_v.h, 1); s_v.n += 1
                    sec("vector", fwe1)
                    l1_ev[0] = s_v.n

        if STAGE < 7:
            early_finish()
            return nc

        sflush()
        O_YA = O_R1
        O_FCT = O_R2
        O_PJT = O_R2 + 16384
        h2v = bfv(O_H2, 4 * D)
        spa = bfv(O_SPA, D)
        l1f = {}
        h2_free = [l1_ev[0]]
        ya_guard = [0]

        for half in range(2):
            hb = TPC // 2 * half
            for si, (f0, f1) in enumerate(FSL):
                par = si % 2
                O_FT = O_FCT + par * 8192
                O_PT2 = O_PJT + par * 8192
                guard_t = l1f.get(("trg", par), s_t.n)

                in0 = s_in.n

                def ffw(sy, f0=f0, f1=f1, O_FT=O_FT, O_PT2=O_PT2, guard_t=guard_t, in0=in0):
                    sy.wait_ge(s_in.h, in0)
                    sy.wait_ge(s_cc.h, cc_ag)
                    if guard_t:
                        sy.wait_ge(s_t.h, guard_t)
                    for c in range(KC):
                        sy.dma_start(out=bfv(O_FT + 2 * 512 * c, 512),
                                     in_=bass.AP(wsh_g.ap().tensor, c * SHN + SH_FCT + f0,
                                                 [[FF, 128], [1, 512]])).then_inc(s_in.h, 16); s_in.n += 16
                    for c in range(4):
                        rank = f0 // 512
                        row = (f0 % 512) + 128 * c
                        sy.dma_start(out=bfv(O_PT2 + 2 * D * c, D),
                                     in_=bass.AP(wsh_g.ap().tensor, rank * SHN + SH_PJT + row * 1024,
                                                 [[1024, 128], [1, D]])).then_inc(s_in.h, 16); s_in.n += 16
                sec("sync", ffw)
                tr_f = s_in.n

                for fc_ in range(4):
                    for gi in range(2):
                        t0 = hb + 512 * gi
                        bank = (fc_ * 2 + gi) % 4
                        bg = l1f.get(("fc", bank), 0)
                        h2f = h2_free[0]

                        def ffc(t, fc_=fc_, t0=t0, bank=bank, tr_f=tr_f, bg=bg, h2f=h2f, O_FT=O_FT):
                            t.wait_ge(s_in.h, tr_f)
                            t.wait_ge(s_v.h, max(bg, h2f))
                            for k in range(KC):
                                mi = t.matmul(PS[bank][:, :],
                                         bfv(O_FT + 2 * (512 * k + 128 * fc_), 128),
                                         xn1_view(k, t0, t0 + 512),
                                         start=(k == 0), stop=(k == KC - 1))
                            mi.then_inc(s_t.h, 1); s_t.n += 1
                        sec("tensor", ffc)
                        t_fc = s_t.n

                        vhg = s_v.n

                        def frl(s, bank=bank, t_fc=t_fc, vhg=vhg):
                            s.wait_ge(s_t.h, t_fc)
                            s.wait_ge(s_v.h, vhg)
                            s.activation(hg_f[:, :], PS[bank][:, :], AF.Relu).then_inc(s_s.h, 1)
                        ssec(frl, 1)
                        s_rl = s_s.n

                        def fsq(v, fc_=fc_, gi=gi, s_rl=s_rl):
                            v.wait_ge(s_s.h, s_rl)
                            dst = bfv(O_H2 + 2 * (D * fc_ + 512 * gi), 512)
                            v.tensor_tensor(dst[:, :], hg_f[:, :], hg_f[:, :],
                                            op=ALU.mult).then_inc(s_v.h, 1); s_v.n += 1
                        sec("vector", fsq)
                        l1f[("fc", bank)] = s_v.n
                v_h2 = s_v.n
                l1f[("trg", par)] = s_t.n

                for ti in range(8):
                    pb2 = 4 + 2 * (ti % 2)
                    bg = l1f.get(("pj", pb2), 0)

                    def fpj(t, ti=ti, pb2=pb2, v_h2=v_h2, bg=bg, O_PT2=O_PT2):
                        t.wait_ge(s_v.h, max(v_h2, bg))
                        for hf in range(2):
                            for c in range(4):
                                mi = t.matmul(PS[pb2 + hf][:, :],
                                         bfv(O_H2 + 2 * (D * c + 128 * ti), 128),
                                         bfv(O_PT2 + 2 * (D * c + 512 * hf), 512),
                                         start=(c == 0), stop=(c == 3))
                        mi.then_inc(s_t.h, 1); s_t.n += 1
                    sec("tensor", fpj)
                    t_pj = s_t.n
                    l1f[("trg", par)] = s_t.n

                    def fya(v, ti=ti, si=si, pb2=pb2, t_pj=t_pj):
                        v.wait_ge(s_t.h, t_pj)
                        for hf in range(2):
                            dst = f32v(O_YA + 4 * (D * ti + 512 * hf), 512)
                            if si == 0:
                                v.tensor_copy(dst[:, :], PS[pb2 + hf][:, :]).then_inc(s_v.h, 1); s_v.n += 1
                            else:
                                v.tensor_tensor(dst[:, :], dst[:, :], PS[pb2 + hf][:, :],
                                                op=ALU.add).then_inc(s_v.h, 1); s_v.n += 1
                    sec("vector", fya)
                    l1f[("pj", pb2)] = s_v.n
                h2_free[0] = s_v.n

            for ti in range(8):
                i = (hb + 128 * ti) // 128

                def fsp(v, ti=ti, i=i):
                    v.wait_ge(s_v.h, s_v.n)
                    y = f32v(O_YA + 4 * D * ti, D)
                    v.tensor_scalar(spa[:, :], y[:, :], SPIKE_B[0], None,
                                    op0=ALU.is_gt).then_inc(s_v.h, 1); s_v.n += 1
                    for bth in SPIKE_B[1:]:
                        v.wait_ge(s_v.h, s_v.n)
                        v.scalar_tensor_tensor(spa[:, :], y[:, :], bth, spa[:, :],
                                               op0=ALU.is_gt, op1=ALU.add).then_inc(s_v.h, 1); s_v.n += 1
                    v.wait_ge(s_v.h, s_v.n)
                    dst = rtile(i)
                    v.scalar_tensor_tensor(dst[:, :], spa[:, :], 0.2, dst[:, :],
                                           op0=ALU.mult, op1=ALU.add).then_inc(s_v.h, 1); s_v.n += 1
                sec("vector", fsp)
                v_done = s_v.n

                st0 = s_st.n

                def fst(sy, i=i, v_done=v_done, st0=st0):
                    sy.wait_ge(s_st.h, st0)
                    sy.wait_ge(s_v.h, v_done)
                    sy.dma_start(out=out_d[128 * i: 128 * (i + 1), :],
                                 in_=f32v(O_R + 4096 * i, D)).then_inc(s_st.h, 16); s_st.n += 16
                sec("sync", fst)

        sflush()

        def ffin(sy):
            sy.wait_ge(s_st.h, s_st.n)
        sec("sync", ffin)

    return nc


class Sem:
    def __init__(self, h):
        self.h = h
        self.n = 0


_NC_CACHE = {}


def kernel(L0_attn_x, L0_attn_w_qkv, L0_attn_w_out, L0_attn_b_out,
           L0_ffn_w_gate, L0_ffn_w_up, L0_ffn_w_down,
           L1_attn_Wq, L1_attn_Wk, L1_attn_Wv, L1_attn_Wo,
           L1_ffn_W_fc, L1_ffn_W_proj):
    x = np.ascontiguousarray(np.asarray(L0_attn_x, dtype=np.float32))
    B, S, _ = x.shape
    CPS = 4
    ident = np.eye(128, dtype=np.float32)
    sel = np.zeros((16, KC * 128), dtype=np.float32)
    for c in range(KC):
        sel[2 * c, 128 * c: 128 * c + 64] = 1.0
        sel[2 * c + 1, 128 * c + 64: 128 * c + 128] = 1.0
    w = {
        "wqkv": np.ascontiguousarray(L0_attn_w_qkv, dtype=np.float32),
        "wout": np.ascontiguousarray(L0_attn_w_out, dtype=np.float32),
        "bout": np.ascontiguousarray(np.asarray(L0_attn_b_out, dtype=np.float32).reshape(1, D)),
        "wgate": np.ascontiguousarray(L0_ffn_w_gate, dtype=np.float32),
        "wup": np.ascontiguousarray(L0_ffn_w_up, dtype=np.float32),
        "wdown": np.ascontiguousarray(L0_ffn_w_down, dtype=np.float32),
        "wq": np.ascontiguousarray(L1_attn_Wq, dtype=np.float32),
        "wk": np.ascontiguousarray(L1_attn_Wk, dtype=np.float32),
        "wv": np.ascontiguousarray(L1_attn_Wv, dtype=np.float32),
        "wo": np.ascontiguousarray(L1_attn_Wo, dtype=np.float32),
        "wfc": np.ascontiguousarray(L1_ffn_W_fc, dtype=np.float32),
        "wproj": np.ascontiguousarray(L1_ffn_W_proj, dtype=np.float32),
        "ident": ident, "sel": sel,
    }
    if "nc" not in _NC_CACHE:
        _NC_CACHE["nc"] = build_program()
    nc = _NC_CACHE["nc"]
    in_maps = []
    for core in range(8):
        b = core // CPS
        t0 = (core % CPS) * TPC
        m = dict(w)
        m["x"] = np.ascontiguousarray(x[b, t0:t0 + TPC, :])
        m["xg"] = np.ascontiguousarray(x[b, 0:G, :])
        m["wg_sh"] = np.ascontiguousarray(w["wgate"][:, 128 * core: 128 * (core + 1)])
        m["wu_sh"] = np.ascontiguousarray(w["wup"][:, 128 * core: 128 * (core + 1)])
        m["wd_sh"] = np.ascontiguousarray(w["wdown"][128 * core: 128 * (core + 1), :])
        m["wfc_sh"] = np.ascontiguousarray(w["wfc"][:, 128 * core: 128 * (core + 1)])
        m["wpj_sh"] = np.ascontiguousarray(w["wproj"][:, 512 * core: 512 * (core + 1)])
        for k in ("wq", "wk", "wv", "wo"):
            m[k + "_sh"] = np.ascontiguousarray(w[k][:, 128 * core: 128 * (core + 1)])
        in_maps.append(m)
    res = run_bass_kernel_spmd(nc, in_maps, core_ids=list(range(8)))
    out = np.empty((B, S, D), dtype=np.float32)
    for core in range(8):
        b = core // CPS
        t0 = (core % CPS) * TPC
        out[b, t0:t0 + TPC, :] = res.results[core]["out"]
    return out


# version 16
# speedup vs baseline: 1.5852x; 1.0047x over previous
import contextlib
import numpy as np
import concourse.bass as bass
import concourse.mybir as mybir
from concourse.bass_utils import run_bass_kernel_spmd

F32 = mybir.dt.float32
BF16 = mybir.dt.bfloat16
U8 = mybir.dt.uint8
ALU = mybir.AluOpType
AF = mybir.ActivationFunctionType
AXX = mybir.AxisListType.X

D = 1024
KC = 8
H = 16
TPC = 2048
NT = 16
G = 64
IF = 2816
FF = 4096
EPS = 1e-5
XT = TPC + G

FP4_TH_A = (0.25, 0.75, 1.25, 1.75)
FP4_TH_B = (2.5, 3.5, 5.0, 5.0)
SPIKE_B = (0.2441942809699397, 0.46396913384288535, 0.6861859295255305,
           0.9059607823984762, 1.083978413225562)
SOFTMAX_C = 24.0

NSL = [(0, 512), (512, 1024), (1024, 1536), (1536, 2048), (2048, 2560), (2560, 2816)]
FSL = [(s, s + 512) for s in range(0, FF, 512)]
TGROUPS = [(0, 512), (512, 1024), (1024, 1536), (1536, 2048), (2048, 2112)]

STAGE = 7
SH_GATE = 0
SH_UP = 360448
SH_DOWN = 720896
SH_FCT = 1081344
SH_PJT = 1605632
SH_WQ = 2129920
SH_WK = 2260992
SH_WV = 2392064
SH_WO = 2523136
SHN = 2654208

SBUF_CAP = 212800

O_R = 0
O_RG = 65536
O_IDENT = 69632
O_ONES = 69888
O_SEL = 70144
O_BOUT = 74240
O_KV = 76288
O_STATS = 78400
O_CB = 78592
O_BD = 78656
O_ZST = 80704
O_KG = 80960
O_VG = 81984
PERS_END = 84032
O_R1 = 84032
O_R2 = 117824
O_R3 = 150592
O_R5 = 184384
R5_END = O_R5 + 28416
assert R5_END <= SBUF_CAP, R5_END

O_E1 = O_R5
O_E2 = O_R5 + 2048
O_XNF = O_R5
O_BNST = O_R5 + 4096
O_MV = O_R5 + 4160
O_XNBF = O_R5 + 4224
O_QT = O_R5 + 6272
O_QA = O_R5 + 10368
O_QM = O_R5 + 14464
O_QS = O_R5 + 14720
O_QR = O_R5 + 14976
O_QSB = O_R5 + 15232
O_QCA = O_R5 + 15424
O_QCB = O_R5 + 17472
O_QSG = O_R5 + 19520
O_KBF = O_R5 + 6272
O_VEX = O_R5 + 8320
O_WQQ = O_R5 + 10432
O_ATT = O_R5 + 10432
O_DEN = O_R5
O_RCB = O_R5 + 2048
O_HG = O_R5 + 21568
O_HF = O_R5 + 23616
O_HQ = O_R5 + 25664
O_HQF = O_R5 + 26688
O_PROBS = O_R5 + 6272
O_PT = O_R5 + 7040
O_AVF = O_R5 + 8064
O_ZB = O_R5 + 10112
O_ZR = O_R5 + 10176
O_VH = O_R5 + 10432
O_H2 = O_R5 + 10432
O_SPA = O_R5 + 18688
O_BHF = O_R5 + 6272
O_BHG = O_R5 + 10368
O_QTMP = O_R5 + 21568
O_BHQ = O_R5 + 23616
O_HQF2 = O_R5 + 25664



def build_program():
    nc = bass.Bass("TRN2", target_bir_lowering=False)

    x_d = nc.dram_tensor("x", [TPC, D], F32, kind="ExternalInput")
    xg_d = nc.dram_tensor("xg", [G, D], F32, kind="ExternalInput")
    wqkv_d = nc.dram_tensor("wqkv", [D, 3 * D], F32, kind="ExternalInput")
    wout_d = nc.dram_tensor("wout", [D, D], F32, kind="ExternalInput")
    bout_d = nc.dram_tensor("bout", [1, D], F32, kind="ExternalInput")
    wgate_d = nc.dram_tensor("wgate", [IF, D], F32, kind="ExternalInput")
    wup_d = nc.dram_tensor("wup", [IF, D], F32, kind="ExternalInput")
    wdown_d = nc.dram_tensor("wdown", [D, IF], F32, kind="ExternalInput")
    wq_d = nc.dram_tensor("wq", [D, D], F32, kind="ExternalInput")
    wk_d = nc.dram_tensor("wk", [D, D], F32, kind="ExternalInput")
    wv_d = nc.dram_tensor("wv", [D, D], F32, kind="ExternalInput")
    wo_d = nc.dram_tensor("wo", [D, D], F32, kind="ExternalInput")
    wfc_d = nc.dram_tensor("wfc", [FF, D], F32, kind="ExternalInput")
    wproj_d = nc.dram_tensor("wproj", [D, FF], F32, kind="ExternalInput")
    ident_d = nc.dram_tensor("ident", [128, 128], F32, kind="ExternalInput")
    sel_d = nc.dram_tensor("sel", [16, KC * 128], F32, kind="ExternalInput")
    out_d = nc.dram_tensor("out", [TPC, D], F32, kind="ExternalOutput")

    wg_sh = nc.dram_tensor("wg_sh", [IF, 128], F32, kind="ExternalInput")
    wu_sh = nc.dram_tensor("wu_sh", [IF, 128], F32, kind="ExternalInput")
    wd_sh = nc.dram_tensor("wd_sh", [128, IF], F32, kind="ExternalInput")
    wfc_sh = nc.dram_tensor("wfc_sh", [FF, 128], F32, kind="ExternalInput")
    wpj_sh = nc.dram_tensor("wpj_sh", [D, 512], F32, kind="ExternalInput")
    w1_sh = {k: nc.dram_tensor(k + "_sh", [D, 128], F32, kind="ExternalInput")
             for k in ("wq", "wk", "wv", "wo")}

    cc_in = nc.dram_tensor("cc_in", [128, 520], F32)
    cc_out = nc.dram_tensor("cc_out", [128, 520], F32)
    wfc_shs = nc.dram_tensor("wfc_shs", [FF, 128], BF16)
    wpj_shs = nc.dram_tensor("wpj_shs", [D, 512], BF16)
    w1_shs = {k: nc.dram_tensor(k + "_shs", [D, 128], BF16) for k in ("wq", "wk", "wv", "wo")}
    wsh = nc.dram_tensor("wsh", [SHN], BF16)
    wsh_g = nc.dram_tensor("wsh_g", [8 * SHN], BF16)

    with contextlib.ExitStack() as es:
        E = es.enter_context
        sb = E(nc.sbuf_tensor("arena", [128, SBUF_CAP], U8))
        s_in = Sem(E(nc.semaphore("s_in")))
        s_tr = Sem(E(nc.semaphore("s_tr")))
        s_st = Sem(E(nc.semaphore("s_st")))
        s_wg = Sem(E(nc.semaphore("s_wg")))
        s_g = Sem(E(nc.semaphore("s_g")))
        s_cc = Sem(E(nc.semaphore("s_cc")))
        s_t = Sem(E(nc.semaphore("s_t")))
        s_v = Sem(E(nc.semaphore("s_v")))
        s_s = Sem(E(nc.semaphore("s_s")))
        s_gc = Sem(E(nc.semaphore("s_gc")))
        PS = [E(nc.psum_tensor(f"ps{_i}", [128, 512], F32)) for _i in range(8)]
        block = E(nc.Block())

        bf_base = sb.ap().bitcast(BF16)
        f32_base = sb.ap().bitcast(F32)
        PB = [p.ap().bitcast(BF16) for p in PS]

        def bfv(off, cols, np_=128):
            assert off % 2 == 0
            return bass.AP(bf_base.tensor, bf_base.offset + off // 2,
                           [[bf_base.ap[0][0], np_], [1, cols]])

        def f32v(off, cols, np_=128):
            assert off % 4 == 0
            return bass.AP(f32_base.tensor, f32_base.offset + off // 4,
                           [[f32_base.ap[0][0], np_], [1, cols]])

        def bfap(off, dims, np_=128):
            assert off % 2 == 0
            return bass.AP(bf_base.tensor, bf_base.offset + off // 2,
                           [[bf_base.ap[0][0], np_]] + [list(d) for d in dims])

        def f32ap(off, dims, np_=128):
            assert off % 4 == 0
            return bass.AP(f32_base.tensor, f32_base.offset + off // 4,
                           [[f32_base.ap[0][0], np_]] + [list(d) for d in dims])

        def psap(i, dims, np_=128):
            a = PS[i].ap()
            return bass.AP(a.tensor, a.offset,
                           [[a.ap[0][0], np_]] + [list(d) for d in dims])

        def pbap(i, dims, np_=128):
            a = PB[i]
            return bass.AP(a.tensor, a.offset, [[a.ap[0][0], np_]] + [list(d) for d in dims])

        def sec(name, fn):
            getattr(block, name)(fn)

        _sbuf = []

        def ssec(fn, incs):
            s_s.n += incs
            _sbuf.append(fn)

        def sflush():
            if not _sbuf:
                return
            fns = list(_sbuf)
            del _sbuf[:]

            def _run(s):
                for f in fns:
                    f(s)
            sec("scalar", _run)

        ident = bfv(O_IDENT, 128)
        ones = bfv(O_ONES, 128)
        sel = f32v(O_SEL, KC * 128)
        boutb = bfv(O_BOUT, D)
        kvsb = f32v(O_KV, 520)
        stats = f32v(O_STATS, 34)
        cbc = f32v(O_CB, 16)
        bd = bfv(O_BD, KC * 128)
        zst = bfv(O_ZST, KC * 16)

        def tile_np(i):
            return 128 if i < NT else G

        def rtile(i):
            if i < NT:
                return f32v(O_R + 4096 * i, D)
            return f32v(O_RG, D, np_=G)

        def tile_colbase(i):
            return 128 * i if i < NT else TPC

        def early_finish():
            sflush()
            vv = s_v.n

            def fstall(sy):
                sy.wait_ge(s_v.h, vv)
                sy.wait_ge(s_s.h, s_s.n)
                sy.wait_ge(s_t.h, s_t.n)
                if STAGE >= 4:
                    sy.wait_ge(s_g.h, s_g.n)
                for i in range(NT):
                    sy.dma_start(out=out_d[128 * i: 128 * (i + 1), :],
                                 in_=f32v(O_R + 4096 * i, D)).then_inc(s_st.h, 16); s_st.n += 16
                sy.wait_ge(s_st.h, s_st.n)
            sec("sync", fstall)

        def _loads(sy):
            for i in range(NT):
                sy.dma_start(out=f32v(O_R + 4096 * i, D), in_=x_d[128 * i: 128 * (i + 1), :])\
                  .then_inc(s_in.h, 16); s_in.n += 16
            sy.dma_start(out=f32v(O_RG, D, np_=G), in_=xg_d[:, :]).then_inc(s_in.h, 16); s_in.n += 16
            sy.dma_start(out=bass.AP(f32_base.tensor, f32_base.offset + O_SEL // 4,
                                     [[f32_base.ap[0][0], 16], [1, KC * 128]]),
                         in_=sel_d[:, :]).then_inc(s_in.h, 16); s_in.n += 16
        sec("sync", _loads)
        in_x = s_in.n

        wqkv_q = bfv(O_WQQ, KC * D)
        wqkv_kv = bfv(O_R2, KC * 2048)

        def _gload(g):
            g.dma_start(out=ident[:, :], in_=ident_d[:, :]).then_inc(s_wg.h, 16); s_wg.n += 16
            for c in range(KC):
                g.dma_start(out=bfv(O_WQQ + 2048 * c, D),
                            in_=wqkv_d[128 * c: 128 * (c + 1), 0:D]).then_inc(s_wg.h, 16); s_wg.n += 16
            for c in range(KC):
                g.dma_start(out=bfv(O_R2 + 4096 * c, 2048),
                            in_=wqkv_d[128 * c: 128 * (c + 1), D:3 * D]).then_inc(s_wg.h, 16); s_wg.n += 16
            g.dma_start(out=bass.AP(bf_base.tensor, bf_base.offset + O_BOUT // 2,
                                    [[bf_base.ap[0][0], 1], [1, D]]),
                        in_=bout_d[:, :]).then_inc(s_wg.h, 16); s_wg.n += 16
            g.dma_start(out=wfc_shs[:, :], in_=wfc_sh[:, :]).then_inc(s_wg.h, 16); s_wg.n += 16
            g.dma_start(out=wpj_shs[:, :], in_=wpj_sh[:, :]).then_inc(s_wg.h, 16); s_wg.n += 16
            for k in ("wq", "wk", "wv", "wo"):
                g.dma_start(out=w1_shs[k][:, :], in_=w1_sh[k][:, :]).then_inc(s_wg.h, 16); s_wg.n += 16
        sec("gpsimd", _gload)
        wg_ident = wg_wq = wg_wkv = wg_bout = s_wg.n

        def _consts(v):
            v.memset(ones[:, :], 1.0).then_inc(s_v.h, 1); s_v.n += 1
            v.memset(cbc[:, 0:1], -SOFTMAX_C).then_inc(s_v.h, 1); s_v.n += 1
            v.memset(bd[:, :], 0.0).then_inc(s_v.h, 1); s_v.n += 1
            v.memset(zst[:, :], 0.0).then_inc(s_v.h, 1); s_v.n += 1
        sec("vector", _consts)
        v_consts = s_v.n

        W_STG = O_R3
        W_QB = O_R3 + 11264
        wcoop_done = [0]

        def emit_wcoop():
            for wi, (shd, outd) in enumerate(((wg_sh, None), (wu_sh, None),
                                              (wd_sh, None))):
                in0 = s_in.n
                vg = s_v.n
                tg = s_t.n

                def fld(sy, wi=wi, shd=shd, in0=in0, vg=vg, tg=tg):
                    sy.wait_ge(s_in.h, in0)
                    if vg:
                        sy.wait_ge(s_v.h, vg)
                    if tg:
                        sy.wait_ge(s_t.h, tg)
                    if wi < 2:
                        for j in range(22):
                            sy.dma_start(out=f32v(W_STG + 4 * 128 * j, 128),
                                         in_=shd[128 * j: 128 * (j + 1), :]).then_inc(s_in.h, 16); s_in.n += 16
                    else:
                        sy.dma_start(out=f32v(W_STG, IF), in_=shd[:, :]).then_inc(s_in.h, 16); s_in.n += 16
                sec("sync", fld)
                dma_done = s_in.n

                def fwt(v, dma_done=dma_done):
                    v.wait_ge(s_in.h, dma_done)
                sec("vector", fwt)
                for (c0, c1) in ((0, 1024), (1024, 2048), (2048, 2816)):
                    emit_quant(f32v(W_STG + 4 * c0, c1 - c0),
                               bfv(W_QB + 2 * c0, c1 - c0), 128, c1 - c0, (c1 - c0) // 16,
                               base=O_R3 + 22528)
                v_q = s_v.n

                evacs = []
                for g3 in range(3):
                    jlist = range(8 * g3, min(22, 8 * g3 + 8))
                    pb = 6 + (g3 % 2)
                    te = tr_evac.get(pb, 0)

                    def ftw(t, jlist=jlist, pb=pb, v_q=v_q, te=te):
                        t.wait_ge(s_v.h, max(v_q, te))
                        t.wait_ge(s_wg.h, wg_ident)
                        for jj, j in enumerate(jlist):
                            t.transpose(bass.AP(PB[pb].tensor, PB[pb].offset + 128 * jj,
                                                [[PB[pb].ap[0][0], 128], [1, 128]]),
                                        bfv(W_QB + 2 * 128 * j, 128),
                                        ident[:, :]).then_inc(s_t.h, 1); s_t.n += 1
                    sec("tensor", ftw)
                    td = s_t.n

                    def fte(v, g3=g3, pb=pb, td=td, nj=len(list(jlist))):
                        v.wait_ge(s_t.h, td)
                        srcap = pbap(pb, [[128, nj], [1, 128]])
                        dstap = bfap(W_QB + 11264 * 0 + 2 * (IF + 128 * 8 * g3) if False else 0, [])
                        dstap = bfap(W_QB + 2 * IF + 2 * 128 * 8 * g3, [[128, nj], [1, 128]])
                        v.tensor_copy(dstap, srcap).then_inc(s_v.h, 1); s_v.n += 1
                    sec("vector", fte)
                    tr_evac[pb] = s_v.n
                v_T = s_v.n

                st0 = s_in.n

                def fst2(sy, wi=wi, outd=outd, v_T=v_T, st0=st0):
                    sy.wait_ge(s_in.h, st0)
                    sy.wait_ge(s_v.h, v_T)
                    if wi < 2:
                        dst = bass.AP(wsh.ap().tensor, SH_GATE if wi == 0 else SH_UP,
                                      [[IF, 128], [1, IF]])
                        sy.dma_start(out=dst, in_=bfv(W_QB + 2 * IF, IF)).then_inc(s_in.h, 16); s_in.n += 16
                    else:
                        srcap = bfap(W_QB + 2 * IF, [[128, 22], [1, 128]])
                        dstap = bass.AP(wsh.ap().tensor, SH_DOWN,
                                        [[128, 128], [128 * 128, 22], [1, 128]])
                        sy.dma_start(out=dstap, in_=srcap).then_inc(s_in.h, 16); s_in.n += 16
                sec("sync", fst2)

            tr0 = s_tr.n
            stg_t = bfv(W_STG, 4096)
            vg2 = s_v.n
            tg2 = s_t.n

            def ftr1(sy, tr0=tr0, vg2=vg2, tg2=tg2):
                sy.wait_ge(s_tr.h, tr0)
                sy.wait_ge(s_v.h, vg2)
                sy.wait_ge(s_t.h, tg2)
                sy.wait_ge(s_wg.h, wg_bout)
                sy.dma_start_transpose(stg_t[:, :], wfc_shs[0:FF, 0:128]).then_inc(s_tr.h, 16); s_tr.n += 16
            sec("sync", ftr1)
            tr1 = s_tr.n
            st1 = s_in.n

            def fsv1(sy, tr1=tr1, st1=st1):
                sy.wait_ge(s_in.h, st1)
                sy.wait_ge(s_tr.h, tr1)
                sy.dma_start(out=bass.AP(wsh.ap().tensor, SH_FCT, [[FF, 128], [1, FF]]),
                             in_=stg_t[:, :]).then_inc(s_in.h, 16); s_in.n += 16
            sec("sync", fsv1)

            tr0 = s_tr.n
            st1b = s_in.n

            def ftr2(sy, tr0=tr0, st1b=st1b):
                sy.wait_ge(s_tr.h, tr0)
                sy.wait_ge(s_in.h, st1b)
                for t4 in range(4):
                    sy.dma_start_transpose(bfv(W_STG + 2048 * t4, 1024),
                                           wpj_shs[0:D, 128 * t4: 128 * (t4 + 1)]).then_inc(s_tr.h, 16); s_tr.n += 16
            sec("sync", ftr2)
            tr2 = s_tr.n
            st2 = s_in.n

            def fsv2(sy, tr2=tr2, st2=st2):
                sy.wait_ge(s_in.h, st2)
                sy.wait_ge(s_tr.h, tr2)
                srcap = bfap(W_STG, [[1024, 4], [1, 1024]])
                dstap = bass.AP(wsh.ap().tensor, SH_PJT,
                                [[1024, 128], [128 * 1024, 4], [1, 1024]])
                sy.dma_start(out=dstap, in_=srcap).then_inc(s_in.h, 16); s_in.n += 16
            sec("sync", fsv2)

            tr0 = s_tr.n
            st3 = s_in.n

            def ftr3(sy, tr0=tr0, st3=st3):
                sy.wait_ge(s_tr.h, tr0)
                sy.wait_ge(s_in.h, st3)
                for ki, k in enumerate(("wq", "wk", "wv", "wo")):
                    sy.dma_start_transpose(bfv(W_STG + 2048 * ki, 1024),
                                           w1_shs[k][0:D, 0:128]).then_inc(s_tr.h, 16); s_tr.n += 16
            sec("sync", ftr3)
            tr3 = s_tr.n
            st4 = s_in.n

            def fsv3(sy, tr3=tr3, st4=st4):
                sy.wait_ge(s_in.h, st4)
                sy.wait_ge(s_tr.h, tr3)
                for ki in range(4):
                    sy.dma_start(out=bass.AP(wsh.ap().tensor, SH_WQ + 131072 * ki,
                                             [[1024, 128], [1, 1024]]),
                                 in_=bfv(W_STG + 2048 * ki, 1024)).then_inc(s_in.h, 16); s_in.n += 16
            sec("sync", fsv3)
            wcoop_done[0] = s_in.n

        xnf = f32v(O_XNF, D)
        bnst = f32v(O_BNST, 12)
        mvv = f32v(O_MV, 2)
        vep = f32v(O_MV + 8, 1)
        rstd = f32v(O_MV + 12, 1)
        xnbf = bfv(O_XNBF, D)
        elu1 = f32v(O_E1, 512)
        elu2 = f32v(O_E2, 512)

        vepal = f32v(O_E1, 17)

        def emit_ln_stats():
            def fv(v):
                v.wait_ge(s_in.h, in_x)
                v.wait_ge(s_v.h, s_v.n)
                v.wait_ge(s_s.h, s_s.n)
                for i in range(17):
                    np_ = tile_np(i)
                    srct = rtile(i)
                    v.bn_stats(bnst[0:np_, 0:6], srct[0:np_, 0:512]).then_inc(s_v.h, 1); s_v.n += 1
                    v.bn_stats(bnst[0:np_, 6:12], srct[0:np_, 512:1024]).then_inc(s_v.h, 1); s_v.n += 1
                    v.wait_ge(s_v.h, s_v.n)
                    v.bn_aggr(mvv[0:np_, :], bnst[0:np_, :]).then_inc(s_v.h, 1); s_v.n += 1
                    v.wait_ge(s_v.h, s_v.n)
                    v.tensor_copy(stats[0:np_, 2 * i: 2 * i + 1], mvv[0:np_, 0:1]).then_inc(s_v.h, 1); s_v.n += 1
                    v.tensor_scalar(vepal[0:np_, i:i + 1], mvv[0:np_, 1:2], EPS, None,
                                    op0=ALU.add).then_inc(s_v.h, 1); s_v.n += 1
            sec("vector", fv)
            v1 = s_v.n
            mid = s_s.n + 17

            def fs(s, v1=v1, mid=mid):
                s.wait_ge(s_v.h, v1)
                for i in range(17):
                    np_ = tile_np(i)
                    s.activation(stats[0:np_, 2 * i + 1: 2 * i + 2], vepal[0:np_, i:i + 1],
                                 AF.Ln).then_inc(s_s.h, 1)
                s.wait_ge(s_s.h, mid)
                for i in range(17):
                    np_ = tile_np(i)
                    s.activation(stats[0:np_, 2 * i + 1: 2 * i + 2],
                                 stats[0:np_, 2 * i + 1: 2 * i + 2], AF.Exp,
                                 scale=-0.5).then_inc(s_s.h, 1)
            ssec(fs, 34)

        def emit_ln(i, save_stats):
            np_ = tile_np(i)
            srct = rtile(i)
            s1 = s_s.n

            def fv2(v):
                v.wait_ge(s_s.h, s1)
                v.wait_ge(s_v.h, s_v.n)
                v.wait_ge(s_t.h, s_t.n)
                v.tensor_scalar(xnbf[0:np_, :], srct[0:np_, :], stats[0:np_, 2 * i: 2 * i + 1],
                                stats[0:np_, 2 * i + 1: 2 * i + 2], op0=ALU.subtract,
                                op1=ALU.mult).then_inc(s_v.h, 1); s_v.n += 1
            sec("vector", fv2)

        tr_evac = {}

        def emit_tm_to_fm(src_bf, dst_off, colbase, np_, pbank):
            v_src = s_v.n

            def ft(t):
                t.wait_ge(s_v.h, v_src)
                t.wait_ge(s_wg.h, wg_ident)
                if pbank in tr_evac:
                    t.wait_ge(s_v.h, tr_evac[pbank])
                for c in range(KC):
                    t.transpose(bass.AP(PB[pbank].tensor, PB[pbank].offset + 128 * c,
                                        [[PB[pbank].ap[0][0], 128], [1, np_]]),
                                src_bf[0:np_, 128 * c: 128 * (c + 1)],
                                ident[0:np_, 0:np_]).then_inc(s_t.h, 1)
                    s_t.n += 1
            sec("tensor", ft)
            t1 = s_t.n

            def fe(v):
                v.wait_ge(s_t.h, t1)
                srcap = pbap(pbank, [[128, KC], [1, np_]])
                dstap = bfap(dst_off + 2 * colbase, [[XT, KC], [1, np_]])
                v.tensor_copy(dstap, srcap).then_inc(s_v.h, 1); s_v.n += 1
            sec("vector", fe)
            tr_evac[pbank] = s_v.n

        def emit_quant(src_f32, dst_bf, np_, F, blocks, base=None, tguard=0, destructive=False,
                       geng=False, tmp_off=None, sguard=None, split=None):
            if base is None:
                offs = (O_QT, O_QM, O_QS, O_QR, O_QSB, O_QCA, O_QCB, O_QSG, O_QTMP)
            else:
                assert not geng or tmp_off is not None
                offs = (base, base + 4096, base + 4352, base + 4608, base + 4864,
                        base + 4992, base + 7040, base + 9088, tmp_off)
            t_f = src_f32 if destructive else f32v(offs[0], F)
            m_f = f32v(offs[1], blocks)
            sc_f = f32v(offs[2], blocks)
            r_f = f32v(offs[3], blocks)
            s_b = bfv(offs[4], blocks)
            ca = bfv(offs[5], F)
            cbch = bfv(offs[6], F)
            sg = bfv(offs[7], F)
            tmp = bfv(offs[8], F) if offs[8] is not None else None

            if geng:
                def fvr(v, tguard=tguard, sguard=(s_s.n if sguard is None else sguard)):
                    v.wait_ge(s_v.h, s_v.n)
                    v.wait_ge(s_s.h, sguard)
                    if tguard:
                        v.wait_ge(s_t.h, tguard)
                    src3 = bass.AP(src_f32.tensor, src_f32.offset,
                                   [[src_f32.ap[0][0], np_], [16, blocks], [1, 16]])
                    v.tensor_reduce(m_f[0:np_, :], src3, axis=AXX, op=ALU.max,
                                    apply_absolute_value=True).then_inc(s_v.h, 1); s_v.n += 1
                    v.wait_ge(s_v.h, s_v.n)
                    v.tensor_scalar(sc_f[0:np_, :], m_f[0:np_, :], 1.0 / 6.0, 1e-12,
                                    op0=ALU.mult, op1=ALU.max).then_inc(s_v.h, 1); s_v.n += 1
                    v.wait_ge(s_v.h, s_v.n)
                    v.reciprocal(r_f[0:np_, :], sc_f[0:np_, :]).then_inc(s_v.h, 1); s_v.n += 1
                    v.tensor_copy(s_b[0:np_, :], sc_f[0:np_, :]).then_inc(s_v.h, 1); s_v.n += 1
                sec("vector", fvr)
                vr = s_v.n

                def fg2(g, vr=vr):
                    g.wait_ge(s_v.h, vr)
                    g.wait_ge(s_gc.h, s_gc.n)
                    rb = bass.AP(r_f.tensor, r_f.offset,
                                 [[r_f.ap[0][0], np_], [1, blocks], [0, 16]])
                    g.tensor_tensor(t_f[0:np_, :], src_f32[0:np_, 0:F], rb,
                                    op=ALU.mult).then_inc(s_gc.h, 1); s_gc.n += 1
                sec("gpsimd", fg2)
                g2 = s_gc.n
                smid = s_s.n + 1

                def fsgg(s, g2=g2, smid=smid):
                    s.wait_ge(s_gc.h, g2)
                    s.activation(sg[0:np_, :], t_f[0:np_, :], AF.Sign).then_inc(s_s.h, 1)
                    s.wait_ge(s_s.h, smid)
                    s.activation(t_f[0:np_, :], t_f[0:np_, :], AF.Abs).then_inc(s_s.h, 1)
                ssec(fsgg, 2)
                s1 = s_s.n
                a_f = t_f

                def _tail(s1=s1):
                    _emit_tail(s1)

                def _emit_tail(s1):
                    def fg3(g, s1=s1):
                        g.wait_ge(s_s.h, s1)
                        g.wait_ge(s_v.h, s_v.n)
                        g.wait_ge(s_gc.h, s_gc.n)
                        sbb = bass.AP(s_b.tensor, s_b.offset,
                                      [[s_b.ap[0][0], np_], [1, blocks], [0, 16]])
                        g.tensor_tensor(sg[0:np_, :], sg[0:np_, :], sbb,
                                        op=ALU.mult).then_inc(s_gc.h, 1); s_gc.n += 1
                    sec("gpsimd", fg3)
                    g3 = s_gc.n

                    def fvA(v, s1=s1, g3=g3):
                        v.wait_ge(s_s.h, s1)
                        v.wait_ge(s_v.h, s_v.n)
                        v.tensor_scalar(ca[0:np_, :], a_f[0:np_, :], 0.25, None,
                                        op0=ALU.is_gt).then_inc(s_v.h, 1); s_v.n += 1
                        for th in (0.75, 1.25, 1.75):
                            v.wait_ge(s_v.h, s_v.n)
                            v.scalar_tensor_tensor(ca[0:np_, :], a_f[0:np_, :], th, ca[0:np_, :],
                                                   op0=ALU.is_gt, op1=ALU.add).then_inc(s_v.h, 1); s_v.n += 1
                        v.tensor_scalar(cbch[0:np_, :], a_f[0:np_, :], 2.5, None,
                                        op0=ALU.is_gt).then_inc(s_v.h, 1); s_v.n += 1
                        v.wait_ge(s_v.h, s_v.n)
                        v.scalar_tensor_tensor(cbch[0:np_, :], a_f[0:np_, :], 3.5, cbch[0:np_, :],
                                               op0=ALU.is_gt, op1=ALU.add).then_inc(s_v.h, 1); s_v.n += 1
                        v.tensor_scalar(tmp[0:np_, :], a_f[0:np_, :], 5.0, 2.0,
                                        op0=ALU.is_gt, op1=ALU.mult).then_inc(s_v.h, 1); s_v.n += 1
                        v.wait_ge(s_v.h, s_v.n)
                        v.tensor_tensor(cbch[0:np_, :], cbch[0:np_, :], tmp[0:np_, :],
                                        op=ALU.add).then_inc(s_v.h, 1); s_v.n += 1
                        v.wait_ge(s_v.h, s_v.n)
                        v.scalar_tensor_tensor(ca[0:np_, :], ca[0:np_, :], 0.5, cbch[0:np_, :],
                                               op0=ALU.mult, op1=ALU.add).then_inc(s_v.h, 1); s_v.n += 1
                        v.wait_ge(s_gc.h, g3)
                        v.wait_ge(s_v.h, s_v.n)
                        v.tensor_tensor(dst_bf[0:np_, 0:F], ca[0:np_, :], sg[0:np_, :],
                                        op=ALU.mult).then_inc(s_v.h, 1); s_v.n += 1
                    sec("vector", fvA)

                if split is None:
                    _tail()
                else:
                    split.append(_tail)
                return s1

            def fv(v):
                v.wait_ge(s_v.h, s_v.n)
                v.wait_ge(s_s.h, s_s.n)
                if tguard:
                    v.wait_ge(s_t.h, tguard)
                src3 = bass.AP(src_f32.tensor, src_f32.offset,
                               [[src_f32.ap[0][0], np_], [16, blocks], [1, 16]])
                v.tensor_reduce(m_f[0:np_, :], src3, axis=AXX, op=ALU.max,
                                apply_absolute_value=True).then_inc(s_v.h, 1); s_v.n += 1
                v.wait_ge(s_v.h, s_v.n)
                v.tensor_scalar(sc_f[0:np_, :], m_f[0:np_, :], 1.0 / 6.0, 1e-12,
                                op0=ALU.mult, op1=ALU.max).then_inc(s_v.h, 1); s_v.n += 1
                v.wait_ge(s_v.h, s_v.n)
                v.reciprocal(r_f[0:np_, :], sc_f[0:np_, :]).then_inc(s_v.h, 1); s_v.n += 1
                v.tensor_copy(s_b[0:np_, :], sc_f[0:np_, :]).then_inc(s_v.h, 1); s_v.n += 1
                v.wait_ge(s_v.h, s_v.n)
                rb = bass.AP(r_f.tensor, r_f.offset,
                             [[r_f.ap[0][0], np_], [1, blocks], [0, 16]])
                v.tensor_tensor(t_f[0:np_, :], src_f32[0:np_, 0:F], rb,
                                op=ALU.mult).then_inc(s_v.h, 1); s_v.n += 1
            sec("vector", fv)
            v1 = s_v.n
            smid = s_s.n + 1

            def fsg(s, v1=v1, smid=smid):
                s.wait_ge(s_v.h, v1)
                s.activation(sg[0:np_, :], t_f[0:np_, :], AF.Sign).then_inc(s_s.h, 1)
                s.wait_ge(s_s.h, smid)
                s.activation(t_f[0:np_, :], t_f[0:np_, :], AF.Abs).then_inc(s_s.h, 1)
            ssec(fsg, 2)
            s1 = s_s.n

            if tmp is not None:
                def fv2(v):
                    v.wait_ge(s_s.h, s1)
                    a_f = t_f
                    v.wait_ge(s_v.h, s_v.n)
                    v.tensor_scalar(ca[0:np_, :], a_f[0:np_, :], 0.25, None,
                                    op0=ALU.is_gt).then_inc(s_v.h, 1); s_v.n += 1
                    v.wait_ge(s_v.h, s_v.n)
                    v.tensor_scalar(tmp[0:np_, :], a_f[0:np_, :], 0.75, None,
                                    op0=ALU.is_gt).then_inc(s_v.h, 1); s_v.n += 1
                    v.wait_ge(s_v.h, s_v.n)
                    v.tensor_tensor(ca[0:np_, :], ca[0:np_, :], tmp[0:np_, :],
                                    op=ALU.add).then_inc(s_v.h, 1); s_v.n += 1
                    v.wait_ge(s_v.h, s_v.n)
                    v.tensor_scalar(cbch[0:np_, :], a_f[0:np_, :], 1.25, None,
                                    op0=ALU.is_gt).then_inc(s_v.h, 1); s_v.n += 1
                    v.wait_ge(s_v.h, s_v.n)
                    v.tensor_scalar(tmp[0:np_, :], a_f[0:np_, :], 1.75, None,
                                    op0=ALU.is_gt).then_inc(s_v.h, 1); s_v.n += 1
                    v.wait_ge(s_v.h, s_v.n)
                    v.tensor_tensor(cbch[0:np_, :], cbch[0:np_, :], tmp[0:np_, :],
                                    op=ALU.add).then_inc(s_v.h, 1); s_v.n += 1
                    v.wait_ge(s_v.h, s_v.n)
                    v.tensor_tensor(ca[0:np_, :], ca[0:np_, :], cbch[0:np_, :],
                                    op=ALU.add).then_inc(s_v.h, 1); s_v.n += 1
                    v.wait_ge(s_v.h, s_v.n)
                    v.tensor_scalar(cbch[0:np_, :], a_f[0:np_, :], 2.5, None,
                                    op0=ALU.is_gt).then_inc(s_v.h, 1); s_v.n += 1
                    v.wait_ge(s_v.h, s_v.n)
                    v.tensor_scalar(tmp[0:np_, :], a_f[0:np_, :], 3.5, None,
                                    op0=ALU.is_gt).then_inc(s_v.h, 1); s_v.n += 1
                    v.wait_ge(s_v.h, s_v.n)
                    v.tensor_tensor(cbch[0:np_, :], cbch[0:np_, :], tmp[0:np_, :],
                                    op=ALU.add).then_inc(s_v.h, 1); s_v.n += 1
                    v.wait_ge(s_v.h, s_v.n)
                    v.tensor_scalar(tmp[0:np_, :], a_f[0:np_, :], 5.0, 2.0,
                                    op0=ALU.is_gt, op1=ALU.mult).then_inc(s_v.h, 1); s_v.n += 1
                    v.wait_ge(s_v.h, s_v.n)
                    v.tensor_tensor(cbch[0:np_, :], cbch[0:np_, :], tmp[0:np_, :],
                                    op=ALU.add).then_inc(s_v.h, 1); s_v.n += 1
                    v.wait_ge(s_v.h, s_v.n)
                    v.wait_ge(s_v.h, s_v.n)
                    v.scalar_tensor_tensor(ca[0:np_, :], ca[0:np_, :], 0.5, cbch[0:np_, :],
                                           op0=ALU.mult, op1=ALU.add).then_inc(s_v.h, 1); s_v.n += 1
                    sbb = bass.AP(s_b.tensor, s_b.offset,
                                  [[s_b.ap[0][0], np_], [1, blocks], [0, 16]])
                    v.wait_ge(s_v.h, s_v.n)
                    v.tensor_tensor(sg[0:np_, :], sg[0:np_, :], sbb, op=ALU.mult).then_inc(s_v.h, 1); s_v.n += 1
                    v.wait_ge(s_v.h, s_v.n)
                    v.tensor_tensor(dst_bf[0:np_, 0:F], ca[0:np_, :], sg[0:np_, :],
                                    op=ALU.mult).then_inc(s_v.h, 1); s_v.n += 1
                sec("vector", fv2)
            else:
                def fv2(v):
                    v.wait_ge(s_s.h, s1)
                    a_f = t_f
                    v.tensor_scalar(ca[0:np_, :], a_f[0:np_, :], FP4_TH_A[0], None,
                                    op0=ALU.is_gt).then_inc(s_v.h, 1); s_v.n += 1
                    for th in FP4_TH_A[1:]:
                        v.wait_ge(s_v.h, s_v.n)
                        v.scalar_tensor_tensor(ca[0:np_, :], a_f[0:np_, :], th, ca[0:np_, :],
                                               op0=ALU.is_gt, op1=ALU.add).then_inc(s_v.h, 1); s_v.n += 1
                    v.tensor_scalar(cbch[0:np_, :], a_f[0:np_, :], FP4_TH_B[0], None,
                                    op0=ALU.is_gt).then_inc(s_v.h, 1); s_v.n += 1
                    for th in FP4_TH_B[1:]:
                        v.wait_ge(s_v.h, s_v.n)
                        v.scalar_tensor_tensor(cbch[0:np_, :], a_f[0:np_, :], th, cbch[0:np_, :],
                                               op0=ALU.is_gt, op1=ALU.add).then_inc(s_v.h, 1); s_v.n += 1
                    v.wait_ge(s_v.h, s_v.n)
                    v.scalar_tensor_tensor(ca[0:np_, :], ca[0:np_, :], 0.5, cbch[0:np_, :],
                                           op0=ALU.mult, op1=ALU.add).then_inc(s_v.h, 1); s_v.n += 1
                    sbb = bass.AP(s_b.tensor, s_b.offset,
                                  [[s_b.ap[0][0], np_], [1, blocks], [0, 16]])
                    v.wait_ge(s_v.h, s_v.n)
                    v.tensor_tensor(sg[0:np_, :], sg[0:np_, :], sbb, op=ALU.mult).then_inc(s_v.h, 1); s_v.n += 1
                    v.wait_ge(s_v.h, s_v.n)
                    v.tensor_tensor(dst_bf[0:np_, 0:F], ca[0:np_, :], sg[0:np_, :],
                                    op=ALU.mult).then_inc(s_v.h, 1); s_v.n += 1
                sec("vector", fv2)

        def emit_elu1(bank, dst_bf, np_, n):
            t0 = s_t.n

            def fv(v):
                v.wait_ge(s_t.h, t0)
                v.wait_ge(s_s.h, s_s.n)
                v.wait_ge(s_v.h, s_v.n)
                v.tensor_scalar(elu1[0:np_, 0:n], PS[bank][0:np_, 0:n], 0.0, None,
                                op0=ALU.min).then_inc(s_v.h, 1); s_v.n += 1
            sec("vector", fv)
            v1 = s_v.n

            def fs(s, v1=v1):
                s.wait_ge(s_v.h, v1)
                s.activation(elu2[0:np_, 0:n], elu1[0:np_, 0:n], AF.Exp).then_inc(s_s.h, 1)
            ssec(fs, 1)
            s1 = s_s.n

            def fv2(v):
                v.wait_ge(s_s.h, s1)
                v.scalar_tensor_tensor(dst_bf[0:np_, 0:n], PS[bank][0:np_, 0:n], 0.0,
                                       elu2[0:np_, 0:n], op0=ALU.max,
                                       op1=ALU.add).then_inc(s_v.h, 1); s_v.n += 1
            sec("vector", fv2)
            return s_v.n

        XN_FM = O_R1

        emit_ln_stats()
        for i in range(17):
            emit_ln(i, save_stats=True)
            emit_tm_to_fm(xnbf, XN_FM, tile_colbase(i), tile_np(i), 6 + (i % 2))
        emit_wcoop()

        def xn_fm_view(k, a, b):
            return bfv(XN_FM + 2 * (XT * k + a), b - a)

        if STAGE < 2:
            early_finish()
            return nc

        sflush()
        Q_BF = O_R3
        bank_free = {}
        for gi, (a, b) in enumerate(TGROUPS):
            n = b - a
            for c in range(KC):
                bank = (gi * KC + c) % 4
                tw = tr_evac.get(6, 0)
                tw2 = tr_evac.get(7, 0)
                bf = bank_free.get(bank, 0)

                def fmm(t, c=c, a=a, b=b, n=n, bank=bank, tw=tw, tw2=tw2, bf=bf):
                    t.wait_ge(s_wg.h, wg_wq)
                    t.wait_ge(s_in.h, wcoop_done[0])
                    t.wait_ge(s_v.h, max(tw, tw2, bf))
                    for k in range(KC):
                        mi = t.matmul(PS[bank][:, 0:n],
                                 bfv(O_WQQ + 2048 * k + 256 * c, 128),
                                 xn_fm_view(k, a, b),
                                 start=(k == 0), stop=(k == KC - 1))
                    mi.then_inc(s_t.h, 1); s_t.n += 1
                sec("tensor", fmm)
                bank_free[bank] = emit_elu1(bank, bfv(Q_BF + 2 * (XT * c + a), n), 128, n)

        def q_bf_view(c, a, b):
            return bfv(Q_BF + 2 * (XT * c + a), b - a)

        if STAGE < 3:
            early_finish()
            return nc

        sflush()
        k_bf = bfv(O_KBF, D)
        v_ex = bfv(O_VEX, 1040)

        def fvo(v):
            v.wait_ge(s_v.h, s_v.n)
            v.memset(v_ex[:, :], 1.0).then_inc(s_v.h, 1); s_v.n += 1
        sec("vector", fvo)

        for i in range(NT):
            vfree = s_v.n

            def fkv(t, i=i, vfree=vfree):
                t.wait_ge(s_wg.h, wg_wkv)
                t.wait_ge(s_v.h, vfree)
                for half in range(2):
                    for k in range(KC):
                        mi = t.matmul(PS[half][:, :],
                                 xn_fm_view(k, 128 * i, 128 * (i + 1)),
                                 bfv(O_R2 + 4096 * k + 1024 * half, 512),
                                 start=(k == 0), stop=(k == KC - 1))
                for half in range(2):
                    for k in range(KC):
                        mi = t.matmul(PS[2 + half][:, :],
                                 xn_fm_view(k, 128 * i, 128 * (i + 1)),
                                 bfv(O_R2 + 4096 * k + 2048 + 1024 * half, 512),
                                 start=(k == 0), stop=(k == KC - 1))
                mi.then_inc(s_t.h, 1); s_t.n += 1
            sec("tensor", fkv)
            emit_elu1(0, bfv(O_KBF, 512), 128, 512)
            emit_elu1(1, bfv(O_KBF + 1024, 512), 128, 512)
            t_v = s_t.n

            def fvv(v, t_v=t_v):
                v.wait_ge(s_t.h, t_v)
                v.wait_ge(s_v.h, s_v.n)
                for half in range(2):
                    dst = bfap(O_VEX + 2 * 520 * half, [[65, 8], [1, 64]])
                    src = psap(2 + half, [[64, 8], [1, 64]])
                    v.tensor_copy(dst, src).then_inc(s_v.h, 1); s_v.n += 1
            sec("vector", fvv)
            v_kv = s_v.n

            te6 = tr_evac.get(6, 0)
            te7 = tr_evac.get(7, 0)

            def fkvmm(t, i=i, v_kv=v_kv, te6=te6, te7=te7):
                t.wait_ge(s_v.h, v_kv)
                if i == 0:
                    t.wait_ge(s_v.h, max(te6, te7))
                for p in range(8):
                    bank = 4 + p // 2
                    colb = 130 * (p % 2)
                    first = (i == 0 and p % 2 == 0)
                    last = (i == NT - 1 and p % 2 == 1)
                    mi = t.matmul(PS[bank][:, colb:colb + 130],
                                  k_bf[:, 128 * p: 128 * (p + 1)],
                                  v_ex[:, 130 * p: 130 * (p + 1)], start=first, stop=last)
                mi.then_inc(s_t.h, 1); s_t.n += 1
            sec("tensor", fkvmm)

        if STAGE < 4:
            early_finish()
            return nc

        t_kv = s_t.n

        def fkvout(v):
            v.wait_ge(s_t.h, t_kv)
            for h in range(H):
                p = h // 2
                bank = 4 + p // 2
                colb = 130 * (p % 2) + 65 * (h % 2)
                po = 64 * (h % 2)
                dst_c = 260 * (h // 8) + 65 * ((h // 2) % 4)
                v.tensor_copy(kvsb[po:po + 64, dst_c:dst_c + 65],
                              PS[bank][po:po + 64, colb:colb + 65]).then_inc(s_v.h, 1); s_v.n += 1
        sec("vector", fkvout)
        v_kvsb = s_v.n

        def fag(g):
            g.wait_ge(s_in.h, wcoop_done[0])
            g.collective_compute(
                "AllGather", ALU.bypass,
                replica_groups=[list(range(8))],
                ins=[wsh.ap().opt()],
                outs=[wsh_g.ap().opt()],
            ).then_inc(s_cc.h, 1); s_cc.n += 1
        sec("gpsimd", fag)
        cc_ag = s_cc.n

        def fcct(g):
            g.wait_ge(s_v.h, v_kvsb)
            g.dma_start(out=cc_in[:, :], in_=kvsb[:, :]).then_inc(s_g.h, 16); s_g.n += 16
            g.wait_ge(s_g.h, s_g.n)
            g.collective_compute(
                "AllReduce", ALU.add,
                replica_groups=[[0, 1, 2, 3], [4, 5, 6, 7]],
                ins=[cc_in.ap().opt()],
                outs=[cc_out.ap().opt()],
            ).then_inc(s_cc.h, 1); s_cc.n += 1
        sec("gpsimd", fcct)
        cc_kv = s_cc.n
        _gkv = [0]

        def emit_kv_readback():
            g0 = s_g.n

            def fccr(g, g0=g0):
                g.wait_ge(s_cc.h, cc_kv)
                g.wait_ge(s_g.h, g0)
                g.dma_start(out=kvsb[:, :], in_=cc_out[:, :]).then_inc(s_g.h, 16); s_g.n += 16
            sec("gpsimd", fccr)
            _gkv[0] = s_g.n
        g_kv = None

        wg0 = s_wg.n

        def fgwout(g):
            g.wait_ge(s_wg.h, wg0)
            g.wait_ge(s_t.h, t_kv)
            for c in range(KC):
                g.dma_start(out=bfv(O_R2 + 2048 * c, D),
                            in_=wout_d[128 * c: 128 * (c + 1), :]).then_inc(s_wg.h, 16); s_wg.n += 16
        sec("gpsimd", fgwout)
        wg_wout = s_wg.n



        sflush()
        XQ_FM = O_R1
        tr_evac.clear()
        def xq_fm_view(k, a, b):
            return bfv(XQ_FM + 2 * (XT * k + a), b - a)
        XQ2 = O_R2 + 16384
        xsrc_of = {0: f32v(O_QA, D), 1: f32v(XQ2, D)}
        dst_of = {0: xnbf, 1: bfv(XQ2 + 13184, D)}
        vq = {0: s_v.n, 1: s_v.n}
        sq = {0: s_s.n, 1: s_s.n}
        tq = {0: 0, 1: 0}
        pend_tail = []
        for i in range(17):
            par = i % 2
            np_ = tile_np(i)
            in0 = s_in.n
            vguard = vq[par]

            def fxd(sy, i=i, np_=np_, vguard=vguard, in0=in0, par=par):
                sy.wait_ge(s_in.h, in0)
                sy.wait_ge(s_v.h, vguard)
                src = x_d[128 * i: 128 * (i + 1), :] if i < NT else xg_d[:, :]
                sy.dma_start(out=xsrc_of[par][0:np_, :],
                             in_=src).then_inc(s_in.h, 16); s_in.n += 16
            sec("sync", fxd)
            d_x = s_in.n

            def fx(v, i=i, np_=np_, d_x=d_x, par=par, vguard=vguard, sg_=sq[par]):
                v.wait_ge(s_in.h, d_x)
                v.wait_ge(s_v.h, vguard)
                v.wait_ge(s_s.h, sg_)
                xs = xsrc_of[par]
                v.tensor_scalar(xs[0:np_, :], xs[0:np_, :],
                                stats[0:np_, 2 * i: 2 * i + 1],
                                stats[0:np_, 2 * i + 1: 2 * i + 2],
                                op0=ALU.subtract, op1=ALU.mult).then_inc(s_v.h, 1); s_v.n += 1
            sec("vector", fx)
            tail = []
            s1_i = emit_quant(xsrc_of[par], dst_of[par], np_, D, 64, destructive=True,
                              geng=True, base=(XQ2 if par else None),
                              tmp_off=(XQ2 + 11136 if par else None),
                              tguard=tq[par], sguard=sq[par], split=tail)
            if i == 3:
                emit_kv_readback()
            if pend_tail:
                pend_tail.pop(0)()

            def mk_tail(tail=tail, i=i, par=par, np_=np_, s1_i=s1_i):
                def run():
                    tail[0]()
                    emit_tm_to_fm(dst_of[par], XQ_FM, tile_colbase(i), np_, 6 + (i % 2))
                    vq[par] = s_v.n
                    sq[par] = s1_i
                    tq[par] = s_t.n
                return run
            pend_tail.append(mk_tail())
        pend_tail.pop(0)()
        vxq = s_v.n


        def fbd(v):
            v.wait_ge(s_g.h, _gkv[0])
            v.wait_ge(s_v.h, v_consts)
            kv_all = f32_base[:, O_KV // 4: O_KV // 4 + 520]
            bd_all = bf_base[:, O_BD // 2: O_BD // 2 + KC * 128]
            zst_all = bf_base[:, O_ZST // 2: O_ZST // 2 + KC * 16]
            for c in range(KC):
                for hh, po in ((2 * c, 0), (2 * c + 1, 64)):
                    base = 260 * (hh // 8) + 65 * ((hh // 2) % 4)
                    sp = 64 * (hh % 2)
                    v.tensor_copy(bd_all[po:po + 64, 128 * c + po: 128 * c + po + 64],
                                  kv_all[sp:sp + 64, base: base + 64]).then_inc(s_v.h, 1); s_v.n += 1
                    v.tensor_copy(zst_all[po:po + 64, 16 * c + hh: 16 * c + hh + 1],
                                  kv_all[sp:sp + 64, base + 64: base + 65]).then_inc(s_v.h, 1); s_v.n += 1
        sec("vector", fbd)
        v_bd = s_v.n

        attn_b = bfv(O_ATT, KC * 512)
        denr = f32v(O_DEN, 512, np_=16)
        recb = f32v(O_RCB, 512)
        c_ev = [v_bd]

        for gi, (a, b) in enumerate(TGROUPS):
            n = b - a
            cv0 = c_ev[0]

            def fden(t, a=a, b=b, n=n, cv0=cv0):
                t.wait_ge(s_v.h, max(v_bd, cv0))
                for c in range(KC):
                    mi = t.matmul(PS[2][0:16, 0:n], zst[:, 16 * c: 16 * (c + 1)],
                             q_bf_view(c, a, b), start=(c == 0), stop=(c == KC - 1))
                mi.then_inc(s_t.h, 1); s_t.n += 1
            sec("tensor", fden)
            t_den = s_t.n

            def fdr(v, n=n, t_den=t_den):
                v.wait_ge(s_t.h, t_den)
                v.tensor_scalar(denr[0:16, 0:n], PS[2][0:16, 0:n], 1e-6, None,
                                op0=ALU.add).then_inc(s_v.h, 1); s_v.n += 1
                v.wait_ge(s_v.h, s_v.n)
                v.reciprocal(denr[0:16, 0:n], denr[0:16, 0:n]).then_inc(s_v.h, 1); s_v.n += 1
            sec("vector", fdr)
            v_den = s_v.n

            for c in range(KC):
                nb = c % 2
                cv = c_ev[0]

                def fnum(t, c=c, a=a, b=b, n=n, nb=nb, v_den=v_den, cv=cv):
                    t.wait_ge(s_v.h, max(v_den, cv))
                    mi = t.matmul(PS[nb][:, 0:n], bd[:, 128 * c: 128 * (c + 1)],
                             q_bf_view(c, a, b), start=True, stop=True)
                    mi = t.matmul(PS[3][:, 0:n], sel[0:16, 128 * c: 128 * (c + 1)],
                             denr[0:16, 0:n], start=True, stop=True)
                    mi.then_inc(s_t.h, 1); s_t.n += 1
                sec("tensor", fnum)
                t_num = s_t.n

                def fnev(v, c=c, n=n, nb=nb, t_num=t_num):
                    v.wait_ge(s_t.h, t_num)
                    v.tensor_copy(recb[:, 0:n], PS[3][:, 0:n]).then_inc(s_v.h, 1); s_v.n += 1
                    v.wait_ge(s_v.h, s_v.n)
                    v.tensor_tensor(attn_b[:, 512 * c: 512 * c + n], PS[nb][:, 0:n],
                                    recb[:, 0:n], op=ALU.mult).then_inc(s_v.h, 1); s_v.n += 1
                sec("vector", fnev)
                c_ev[0] = s_v.n
            v_attn = s_v.n

            for ti in range((n + 127) // 128):
                i = (a + 128 * ti) // 128 if a < TPC else NT
                np_ = tile_np(i)
                cv = c_ev[0]

                def fwo(t, ti=ti, np_=np_, v_attn=v_attn, cv=cv):
                    t.wait_ge(s_v.h, max(v_attn, cv, v_consts))
                    t.wait_ge(s_wg.h, wg_wout)
                    for half in range(2):
                        for c in range(KC):
                            mi = t.matmul(PS[4 + half][0:np_, :],
                                     bfv(O_ATT + 2 * (512 * c + 128 * ti), np_),
                                     bfv(O_R2 + 2048 * c + 1024 * half, 512),
                                     start=(c == 0), stop=False)
                        mi = t.matmul(PS[4 + half][0:np_, :], ones[0:1, 0:np_],
                                 boutb[0:1, 512 * half: 512 * (half + 1)],
                                 start=False, stop=True)
                    mi.then_inc(s_t.h, 1); s_t.n += 1
                sec("tensor", fwo)
                t_wo = s_t.n

                def fwe(v, i=i, np_=np_, t_wo=t_wo):
                    v.wait_ge(s_t.h, t_wo)
                    dst = rtile(i)
                    for half in range(2):
                        v.tensor_tensor(dst[0:np_, 512 * half: 512 * (half + 1)],
                                        dst[0:np_, 512 * half: 512 * (half + 1)],
                                        PS[4 + half][0:np_, :], op=ALU.add).then_inc(s_v.h, 1); s_v.n += 1
                sec("vector", fwe)
                c_ev[0] = s_v.n

        if STAGE < 5:
            early_finish()
            return nc

        sflush()


        hg_f = f32v(O_HG, 512)
        hf_f = f32v(O_HF, 512)
        hq_t = bfv(O_HQ, 512)
        hq_fm = bfv(O_HQF, 4 * 128)
        wT_ready = {}
        wT_free = {}
        b_free = [c_ev[0]]

        def wg_off(par): return O_R3 + par * 8192
        def wu_off(par): return O_R3 + 16384 + par * 8192
        def wd_off(par): return O_R2 + 16384 + par * 8192

        def emit_fld(si):
            par = si % 2
            n0, n1 = NSL[si]
            ns = n1 - n0
            in0 = s_in.n
            wfree = wT_free.get(par, s_t.n)

            def fld(sy, n0=n0, n1=n1, ns=ns, par=par, in0=in0, wfree=wfree):
                sy.wait_ge(s_in.h, in0)
                sy.wait_ge(s_cc.h, cc_ag)
                sy.wait_ge(s_t.h, wfree)
                sy.wait_ge(s_v.h, vxq)
                for c in range(KC):
                    sy.dma_start(out=bfv(wg_off(par) + 2 * 512 * c, ns),
                                 in_=bass.AP(wsh_g.ap().tensor, c * SHN + SH_GATE + n0,
                                             [[IF, 128], [1, ns]])).then_inc(s_in.h, 16); s_in.n += 16
                for c in range(KC):
                    sy.dma_start(out=bfv(wu_off(par) + 2 * 512 * c, ns),
                                 in_=bass.AP(wsh_g.ap().tensor, c * SHN + SH_UP + n0,
                                             [[IF, 128], [1, ns]])).then_inc(s_in.h, 16); s_in.n += 16
                for c in range(ns // 128):
                    srcap = bass.AP(wsh_g.ap().tensor, SH_DOWN + (n0 + 128 * c) * 128,
                                    [[128, 128], [SHN, 8], [1, 128]])
                    dstap = bfap(wd_off(par) + 2 * D * c, [[128, 8], [1, 128]])
                    sy.dma_start(out=dstap, in_=srcap).then_inc(s_in.h, 16); s_in.n += 16
            sec("sync", fld)
            wT_ready[si] = s_in.n

        items = [(si, p) for si in range(len(NSL)) for p in range(9)]
        hq_evac_s = {}
        bguard = [0, 0]
        t_gu_of = {}
        bhf = f32v(O_BHF, 1024)
        bhg = f32v(O_BHG, 1024)

        def emit_fgu(si, p):
            n0, n1 = NSL[si]
            ns = n1 - n0
            par = si % 2
            wr = wT_ready[si]
            gs, gv = bguard
            tiles = [2 * p, 2 * p + 1] if p < 8 else [16]

            def fgu(t, si=si, p=p, ns=ns, par=par, wr=wr, gs=gs, gv=gv, tiles=tiles):
                t.wait_ge(s_in.h, wr)
                if gs:
                    t.wait_ge(s_s.h, gs)
                if gv:
                    t.wait_ge(s_v.h, gv)
                for ti, i in enumerate(tiles):
                    np_ = tile_np(i)
                    colbase = tile_colbase(i)
                    bp = 2 * ti
                    for c in range(KC):
                        mi = t.matmul(PS[bp][0:np_, 0:ns],
                                      xq_fm_view(c, colbase, colbase + np_),
                                      bfv(wg_off(par) + 2 * 512 * c, ns),
                                      start=(c == 0), stop=(c == KC - 1))
                    for c in range(KC):
                        mi = t.matmul(PS[bp + 1][0:np_, 0:ns],
                                      xq_fm_view(c, colbase, colbase + np_),
                                      bfv(wu_off(par) + 2 * 512 * c, ns),
                                      start=(c == 0), stop=(c == KC - 1))
                mi.then_inc(s_t.h, 1); s_t.n += 1
            sec("tensor", fgu)
            t_gu_of[(si, p)] = s_t.n

        emit_fld(0)
        emit_fld(1)
        emit_fgu(0, 0)
        t_th_prev = [0]
        pend_fra = []
        for j, (si, p) in enumerate(items):
            par = si % 2
            n0, n1 = NSL[si]
            ns = n1 - n0
            tiles = [2 * p, 2 * p + 1] if p < 8 else [16]
            t_gu = t_gu_of[(si, p)]

            for ti, i in enumerate(tiles):
                np_ = tile_np(i)
                bp = 2 * ti
                vhg = s_v.n

                def fsl(s, np_=np_, ns=ns, bp=bp, ti=ti, t_gu=t_gu, vhg=vhg):
                    s.wait_ge(s_t.h, t_gu)
                    s.wait_ge(s_v.h, vhg)
                    s.activation(bhg[0:np_, 512 * ti: 512 * ti + ns], PS[bp][0:np_, 0:ns],
                                 AF.Silu).then_inc(s_s.h, 1)
                ssec(fsl, 1)
                s_si = s_s.n

                def fmu(v, np_=np_, ns=ns, bp=bp, ti=ti, s_si=s_si):
                    v.wait_ge(s_s.h, s_si)
                    v.wait_ge(s_v.h, s_v.n)
                    v.tensor_tensor(bhf[0:np_, 512 * ti: 512 * ti + ns],
                                    bhg[0:np_, 512 * ti: 512 * ti + ns],
                                    PS[bp + 1][0:np_, 0:ns], op=ALU.mult).then_inc(s_v.h, 1); s_v.n += 1
                sec("vector", fmu)
            bguard[0] = s_s.n
            bguard[1] = s_v.n

            if j + 1 < len(items):
                emit_fgu(*items[j + 1])

            npq = tile_np(tiles[0])
            Fq = ns * len(tiles)
            if len(tiles) == 2 and ns == 512:
                emit_quant(bhf, bfv(O_BHQ, 1024), 128, 1024, 64,
                           tguard=t_th_prev[0], destructive=True, geng=True)
                hq_views = [bfv(O_BHQ + 2 * 512 * ti, 512) for ti in range(2)]
            else:
                hq_views = []
                for ti, i in enumerate(tiles):
                    emit_quant(f32v(O_BHF + 4 * 512 * ti, ns), bfv(O_BHQ + 2 * 512 * ti, ns),
                               tile_np(i), ns, ns // 16, tguard=t_th_prev[0], destructive=True,
                               geng=True)
                    hq_views.append(bfv(O_BHQ + 2 * 512 * ti, ns))
            v_hq = s_v.n

            for fn in pend_fra:
                sec("vector", fn); s_v.n += 2
            del pend_fra[:]
            b_free[0] = s_v.n

            for ti, i in enumerate(tiles):
                np_ = tile_np(i)
                pb = 6 + (ti % 2)
                te_s = hq_evac_s.get(pb, 0)

                def fth(t, np_=np_, ns=ns, pb=pb, ti=ti, v_hq=v_hq, te_s=te_s):
                    t.wait_ge(s_v.h, v_hq)
                    if te_s:
                        t.wait_ge(s_s.h, te_s)
                    for c in range(ns // 128):
                        t.transpose(bass.AP(PB[pb].tensor, PB[pb].offset + 128 * c,
                                            [[PB[pb].ap[0][0], 128], [1, np_]]),
                                    bfv(O_BHQ + 2 * (512 * ti + 128 * c), 128, np_=np_),
                                    ident[0:np_, 0:np_]).then_inc(s_t.h, 1); s_t.n += 1
                sec("tensor", fth)
                t_th = s_t.n
                t_th_prev[0] = t_th

                def fhe(s, np_=np_, ns=ns, pb=pb, ti=ti, t_th=t_th):
                    s.wait_ge(s_t.h, t_th)
                    srcap = pbap(pb, [[128, ns // 128], [1, np_]])
                    dstap = bfap(O_HQF2 + 1024 * ti, [[128, ns // 128], [1, np_]])
                    s.copy(dstap, srcap).then_inc(s_s.h, 1)
                ssec(fhe, 1)
                hq_evac_s[pb] = s_s.n
                v_hf = s_s.n
                bfree = b_free[0]

                def fdn(t, np_=np_, ns=ns, par=par, ti=ti, v_hf=v_hf, bfree=bfree):
                    t.wait_ge(s_s.h, v_hf)
                    t.wait_ge(s_v.h, bfree)
                    for half in range(2):
                        for c in range(ns // 128):
                            mi = t.matmul(PS[4 + half][0:np_, :],
                                          bfv(O_HQF2 + 1024 * ti + 256 * c, np_),
                                          bfv(wd_off(par) + 2 * (D * c + 512 * half), 512),
                                          start=(c == 0), stop=(c == ns // 128 - 1))
                    mi.then_inc(s_t.h, 1); s_t.n += 1
                sec("tensor", fdn)
                t_dn = s_t.n

                def fra(v, i=i, np_=np_, t_dn=t_dn):
                    v.wait_ge(s_t.h, t_dn)
                    dst = rtile(i)
                    for half in range(2):
                        v.tensor_tensor(dst[0:np_, 512 * half: 512 * (half + 1)],
                                        dst[0:np_, 512 * half: 512 * (half + 1)],
                                        PS[4 + half][0:np_, :], op=ALU.add).then_inc(s_v.h, 1)
                pend_fra.append(fra)
                if ti == 0 and len(tiles) == 2:
                    for fn in pend_fra:
                        sec("vector", fn); s_v.n += 2
                    del pend_fra[:]
                    b_free[0] = s_v.n
            if p == 8:
                wT_free[par] = s_t.n
                if si + 2 < len(NSL):
                    emit_fld(si + 2)
        for fn in pend_fra:
            sec("vector", fn); s_v.n += 2
        del pend_fra[:]
        b_free[0] = s_v.n

        sflush()
        XN1_FM = O_R3
        tr_evac.clear()

        def xn1_view(k, a, b):
            return bfv(XN1_FM + 2 * (XT * k + a), b - a)

        emit_ln_stats()
        for i in range(17):
            emit_ln(i, save_stats=False)
            emit_tm_to_fm(xnbf, XN1_FM, tile_colbase(i), tile_np(i), 6 + (i % 2))
        v_xn1 = s_v.n

        O_W1A = O_R2
        O_W1B = O_R2 + 16384

        W1OFF = {"wq": SH_WQ, "wk": SH_WK, "wv": SH_WV, "wo": SH_WO}

        def emit_w1T(dst_off, src_s, key, guard_t):
            in0 = s_in.n

            def ftr(sy):
                sy.wait_ge(s_in.h, in0)
                sy.wait_ge(s_cc.h, cc_ag)
                sy.wait_ge(s_t.h, guard_t)
                for c in range(KC):
                    sy.dma_start(out=bfv(dst_off + 2 * D * c, D),
                                 in_=bass.AP(wsh_g.ap().tensor, c * SHN + W1OFF[key],
                                             [[1024, 128], [1, D]])).then_inc(s_in.h, 16); s_in.n += 16
            sec("sync", ftr)
            return s_in.n

        O_QH = O_R1
        O_KH = O_R1 + 16384
        probs = bfv(O_PROBS, 192)
        pT = bfv(O_PT, 256)
        av_f = bfv(O_AVF, KC * 128)
        zbv = f32v(O_ZB, 1)
        zrv = f32v(O_ZR, 1)
        l1b = {}
        l1_ev = [b_free[0]]
        probs_free = [0]

        for half in range(2):
            hb = TPC // 2 * half
            HG2 = [(hb, hb + 512), (hb + 512, hb + 1024)]
            tr_wq = emit_w1T(O_W1A, None, "wq", s_t.n)

            for gi, (a, b) in enumerate(HG2):
                for c in range(KC):
                    bank = (gi * KC + c) % 4
                    bg = l1b.get(("b", bank), 0)
                    lev = l1_ev[0]

                    def fq(t, c=c, a=a, b=b, bank=bank, bg=bg, lev=lev, tr_wq=tr_wq):
                        t.wait_ge(s_in.h, tr_wq)
                        t.wait_ge(s_v.h, max(lev, v_xn1))
                        if bg:
                            t.wait_ge(s_s.h, bg)
                        for k in range(KC):
                            mi = t.matmul(PS[bank][:, :],
                                     bfv(O_W1A + 2 * (D * k + 128 * c), 128),
                                     xn1_view(k, a, b),
                                     start=(k == 0), stop=(k == KC - 1))
                        mi.then_inc(s_t.h, 1); s_t.n += 1
                    sec("tensor", fq)
                    t_q = s_t.n

                    def fqe(s, c=c, gi=gi, bank=bank, t_q=t_q):
                        s.wait_ge(s_t.h, t_q)
                        s.copy(bfv(O_QH + 2 * (D * c + 512 * gi), 512)[:, :],
                               PS[bank][:, :]).then_inc(s_s.h, 1)
                    ssec(fqe, 1)
                    l1b[("b", bank)] = s_s.n

            tr_wk = emit_w1T(O_W1B, None, "wk", s_t.n)
            kg2 = list(HG2) + ([(TPC, XT)] if half == 0 else [])
            for gi, (a, b) in enumerate(kg2):
                n = b - a
                for c in range(KC):
                    bank = (gi * KC + c) % 4
                    bg = l1b.get(("b", bank), 0)

                    lev = l1_ev[0]

                    def fk(t, c=c, a=a, b=b, n=n, bank=bank, bg=bg, tr_wk=tr_wk, lev=lev):
                        t.wait_ge(s_in.h, tr_wk)
                        if lev:
                            t.wait_ge(s_v.h, lev)
                        if bg:
                            t.wait_ge(s_s.h, bg)
                        for k in range(KC):
                            mi = t.matmul(PS[bank][:, 0:n],
                                     bfv(O_W1B + 2 * (D * k + 128 * c), 128),
                                     xn1_view(k, a, b),
                                     start=(k == 0), stop=(k == KC - 1))
                        mi.then_inc(s_t.h, 1); s_t.n += 1
                    sec("tensor", fk)
                    t_k = s_t.n

                    def fke(s, c=c, gi=gi, a=a, n=n, bank=bank, t_k=t_k):
                        s.wait_ge(s_t.h, t_k)
                        if a < TPC:
                            dst = bfv(O_KH + 2 * (D * c + 512 * gi), n)
                        else:
                            dst = bfv(O_KG + 2 * (G * c), n)
                        s.copy(dst[:, :], PS[bank][0:128, 0:n]).then_inc(s_s.h, 1)
                    ssec(fke, 1)
                    l1b[("b", bank)] = s_s.n

            tr_wv = emit_w1T(O_W1A, None, "wv", s_t.n)
            vtl = list(range(8)) + ([16] if half == 0 else [])
            for vi in vtl:
                np_ = 128 if vi < 16 else G
                colbase = hb + 128 * vi if vi < 16 else TPC
                bank2 = 2 * ((vi % 2))
                bg = max(l1b.get(("b", bank2), 0), l1b.get(("b", bank2 + 1), 0))

                lev = l1_ev[0]

                def fv3(t, vi=vi, np_=np_, colbase=colbase, bank2=bank2, bg=bg, tr_wv=tr_wv, lev=lev):
                    t.wait_ge(s_in.h, tr_wv)
                    if lev:
                        t.wait_ge(s_v.h, lev)
                    if bg:
                        t.wait_ge(s_s.h, bg)
                    for hf in range(2):
                        for k in range(KC):
                            mi = t.matmul(PS[bank2 + hf][0:np_, :],
                                     xn1_view(k, colbase, colbase + np_),
                                     bfv(O_W1A + 2 * (D * k + 512 * hf), 512),
                                     start=(k == 0), stop=(k == KC - 1))
                    mi.then_inc(s_t.h, 1); s_t.n += 1
                sec("tensor", fv3)
                t_v3 = s_t.n

                def fve(s, vi=vi, np_=np_, bank2=bank2, t_v3=t_v3):
                    s.wait_ge(s_t.h, t_v3)
                    for hf in range(2):
                        if vi < 16:
                            dst = bfv(O_VH + 2 * (D * vi + 512 * hf), 512)
                        else:
                            dst = bfv(O_VG + 2 * 512 * hf, 512, np_=G)
                        s.copy(dst[0:np_, :], PS[bank2 + hf][0:np_, :]).then_inc(s_s.h, 1)
                ssec(fve, 2)
                l1b[("b", bank2)] = l1b[("b", bank2 + 1)] = s_s.n
            s_qkv = s_s.n

            tr_wo = emit_w1T(O_W1B, None, "wo", s_t.n)

            wseq = [(w, h) for w in range(8) for h in range(H)]
            t_sc_of = {}
            probs_par_free = {}

            def emit_fsc(j):
                w, h = wseq[j]
                scb = j % 2
                bg = l1b.get(("sc", scb), 0)

                def fsc(t, w=w, h=h, scb=scb, bg=bg):
                    t.wait_ge(s_s.h, max(s_qkv, bg))
                    po = 64 * (h % 2)
                    qb_ = O_QH // 2 + D * (h // 2) + 128 * w
                    kb_ = O_KH // 2 + D * (h // 2) + 128 * w
                    gb_ = O_KG // 2 + G * (h // 2)
                    qv = bf_base[po:po + 64, qb_:qb_ + 128]
                    kvw = bf_base[po:po + 64, kb_:kb_ + 128]
                    kgv = bf_base[po:po + 64, gb_:gb_ + G]
                    mi = t.matmul(PS[scb][:, 0:G], qv, kgv, start=True, stop=True)
                    mi = t.matmul(PS[scb][:, G:G + 128], qv, kvw, start=True, stop=True)
                    mi.then_inc(s_t.h, 1); s_t.n += 1
                sec("tensor", fsc)
                t_sc_of[j] = s_t.n

            emit_fsc(0)
            for j, (w, h) in enumerate(wseq):
                par = j % 2
                scb = par
                probsv = bfv(O_PROBS + 384 * par, 192)
                pTv = bfv(O_PT + 512 * par, 256)
                zbvp = f32v(O_ZB + 4 * par, 1)
                zrvp = f32v(O_ZR + 4 * par, 1)
                t_sc = t_sc_of[j]
                pf = probs_par_free.get(par, 0)

                def fex(s, scb=scb, t_sc=t_sc, pf=pf, probsv=probsv, zbvp=zbvp):
                    s.wait_ge(s_t.h, t_sc)
                    if pf:
                        s.wait_ge(s_v.h, pf)
                    s.activation(probsv[:, :], PS[scb][:, 0:192], AF.Exp,
                                 bias=cbc[:, 0:1], scale=0.125,
                                 accum_out=zbvp[:, 0:1]).then_inc(s_s.h, 1)
                ssec(fex, 1)
                l1b[("sc", scb)] = s_s.n
                l1b[("b", scb)] = s_s.n
                s_ex = s_s.n

                def fzr(v, s_ex=s_ex, probsv=probsv, zbvp=zbvp, zrvp=zrvp):
                    v.wait_ge(s_s.h, s_ex)
                    v.reciprocal(zrvp[:, 0:1], zbvp[:, 0:1]).then_inc(s_v.h, 1); s_v.n += 1
                    v.wait_ge(s_v.h, s_v.n)
                    v.tensor_scalar(probsv[:, :], probsv[:, :], zrvp[:, 0:1], None,
                                    op0=ALU.mult).then_inc(s_v.h, 1); s_v.n += 1
                sec("vector", fzr)
                v_pn = s_v.n

                if j + 1 < len(wseq):
                    emit_fsc(j + 1)

                ptb = 6 + par
                bg2 = l1b.get(("pt", ptb), 0)

                def fpt(t, ptb=ptb, v_pn=v_pn, bg2=bg2, probsv=probsv):
                    t.wait_ge(s_v.h, max(v_pn, bg2))
                    t.transpose(bass.AP(PB[ptb].tensor, PB[ptb].offset,
                                        [[PB[ptb].ap[0][0], 128], [1, 128]]),
                                probsv[:, G:192], ident[:, :]).then_inc(s_t.h, 1); s_t.n += 1
                    t.transpose(bass.AP(PB[ptb].tensor, PB[ptb].offset + 128,
                                        [[PB[ptb].ap[0][0], G], [1, 128]]),
                                probsv[:, 0:G], ident[:, :]).then_inc(s_t.h, 1); s_t.n += 1
                sec("tensor", fpt)
                t_pt = s_t.n

                def fpe(v, ptb=ptb, t_pt=t_pt, pTv=pTv):
                    v.wait_ge(s_t.h, t_pt)
                    v.tensor_copy(pTv[:, :], PB[ptb][:, 0:256]).then_inc(s_v.h, 1); s_v.n += 1
                sec("vector", fpe)
                l1b[("pt", ptb)] = s_v.n
                v_pt = s_v.n
                probs_par_free[par] = v_pt

                avg = l1b.get(("av",), 0)

                def fav(t, w=w, h=h, v_pt=v_pt, avg=avg, pTv=pTv):
                    t.wait_ge(s_v.h, v_pt)
                    if avg and h % 2 == 0:
                        t.wait_ge(s_v.h, avg)
                    po = 64 * (h % 2)
                    vv = bfv(O_VH + 2 * (D * w + 64 * h), 64)
                    vgv = bfv(O_VG + 2 * 64 * h, 64, np_=G)
                    outap = PS[4][po:po + 64, 0:128]
                    mi = t.matmul(outap, vv, pTv[:, 0:128], start=True, stop=False)
                    mi = t.matmul(outap, vgv[0:G, :], pTv[0:G, 128:256], start=False, stop=True)
                    mi.then_inc(s_t.h, 1); s_t.n += 1
                sec("tensor", fav)
                t_av = s_t.n

                if h % 2 == 1:
                    def fae(v, h=h, t_av=t_av):
                        v.wait_ge(s_t.h, t_av)
                        v.tensor_copy(av_f[:, 128 * (h // 2): 128 * (h // 2) + 128],
                                      PS[4][:, 0:128]).then_inc(s_v.h, 1); s_v.n += 1
                    sec("vector", fae)
                    l1b[("av",)] = s_v.n

                if h == H - 1:
                    s_av = s_v.n
                    lev = l1_ev[0]

                    def fw1(t, w=w, hb=hb, s_av=s_av, lev=lev, tr_wo=tr_wo):
                        t.wait_ge(s_v.h, max(s_av, lev))
                        t.wait_ge(s_in.h, tr_wo)
                        for hf in range(2):
                            for c in range(KC):
                                mi = t.matmul(PS[2 + hf][:, :],
                                              av_f[:, 128 * c: 128 * (c + 1)],
                                              bfv(O_W1B + 2 * (D * c + 512 * hf), 512),
                                              start=(c == 0), stop=(c == KC - 1))
                        mi.then_inc(s_t.h, 1); s_t.n += 1
                    sec("tensor", fw1)
                    t_w1 = s_t.n

                    def fwe1(v, w=w, hb=hb, t_w1=t_w1):
                        v.wait_ge(s_t.h, t_w1)
                        dst = rtile((hb + 128 * w) // 128)
                        for hf in range(2):
                            v.tensor_tensor(dst[:, 512 * hf: 512 * (hf + 1)],
                                            dst[:, 512 * hf: 512 * (hf + 1)],
                                            PS[2 + hf][:, :], op=ALU.add).then_inc(s_v.h, 1); s_v.n += 1
                    sec("vector", fwe1)
                    l1_ev[0] = s_v.n

        if STAGE < 7:
            early_finish()
            return nc

        sflush()
        O_YA = O_R1
        O_FCT = O_R2
        O_PJT = O_R2 + 16384
        h2v = bfv(O_H2, 4 * D)
        spa = bfv(O_SPA, D)
        l1f = {}
        h2_free = [l1_ev[0]]
        ya_guard = [0]

        for half in range(2):
            hb = TPC // 2 * half
            for si, (f0, f1) in enumerate(FSL):
                par = si % 2
                O_FT = O_FCT + par * 8192
                O_PT2 = O_PJT + par * 8192
                guard_t = l1f.get(("trg", par), s_t.n)

                in0 = s_in.n

                def ffw(sy, f0=f0, f1=f1, O_FT=O_FT, O_PT2=O_PT2, guard_t=guard_t, in0=in0):
                    sy.wait_ge(s_in.h, in0)
                    sy.wait_ge(s_cc.h, cc_ag)
                    if guard_t:
                        sy.wait_ge(s_t.h, guard_t)
                    for c in range(KC):
                        sy.dma_start(out=bfv(O_FT + 2 * 512 * c, 512),
                                     in_=bass.AP(wsh_g.ap().tensor, c * SHN + SH_FCT + f0,
                                                 [[FF, 128], [1, 512]])).then_inc(s_in.h, 16); s_in.n += 16
                    for c in range(4):
                        rank = f0 // 512
                        row = (f0 % 512) + 128 * c
                        sy.dma_start(out=bfv(O_PT2 + 2 * D * c, D),
                                     in_=bass.AP(wsh_g.ap().tensor, rank * SHN + SH_PJT + row * 1024,
                                                 [[1024, 128], [1, D]])).then_inc(s_in.h, 16); s_in.n += 16
                sec("sync", ffw)
                tr_f = s_in.n

                for fc_ in range(4):
                    for gi in range(2):
                        t0 = hb + 512 * gi
                        bank = (fc_ * 2 + gi) % 4
                        bg = l1f.get(("fc", bank), 0)
                        h2f = h2_free[0]

                        def ffc(t, fc_=fc_, t0=t0, bank=bank, tr_f=tr_f, bg=bg, h2f=h2f, O_FT=O_FT):
                            t.wait_ge(s_in.h, tr_f)
                            t.wait_ge(s_v.h, max(bg, h2f))
                            for k in range(KC):
                                mi = t.matmul(PS[bank][:, :],
                                         bfv(O_FT + 2 * (512 * k + 128 * fc_), 128),
                                         xn1_view(k, t0, t0 + 512),
                                         start=(k == 0), stop=(k == KC - 1))
                            mi.then_inc(s_t.h, 1); s_t.n += 1
                        sec("tensor", ffc)
                        t_fc = s_t.n

                        vhg = s_v.n

                        def frl(s, bank=bank, t_fc=t_fc, vhg=vhg):
                            s.wait_ge(s_t.h, t_fc)
                            s.wait_ge(s_v.h, vhg)
                            s.activation(hg_f[:, :], PS[bank][:, :], AF.Relu).then_inc(s_s.h, 1)
                        ssec(frl, 1)
                        s_rl = s_s.n

                        def fsq(v, fc_=fc_, gi=gi, s_rl=s_rl):
                            v.wait_ge(s_s.h, s_rl)
                            dst = bfv(O_H2 + 2 * (D * fc_ + 512 * gi), 512)
                            v.tensor_tensor(dst[:, :], hg_f[:, :], hg_f[:, :],
                                            op=ALU.mult).then_inc(s_v.h, 1); s_v.n += 1
                        sec("vector", fsq)
                        l1f[("fc", bank)] = s_v.n
                v_h2 = s_v.n
                l1f[("trg", par)] = s_t.n

                for ti in range(8):
                    pb2 = 4 + 2 * (ti % 2)
                    bg = l1f.get(("pj", pb2), 0)

                    def fpj(t, ti=ti, pb2=pb2, v_h2=v_h2, bg=bg, O_PT2=O_PT2):
                        t.wait_ge(s_v.h, max(v_h2, bg))
                        for hf in range(2):
                            for c in range(4):
                                mi = t.matmul(PS[pb2 + hf][:, :],
                                         bfv(O_H2 + 2 * (D * c + 128 * ti), 128),
                                         bfv(O_PT2 + 2 * (D * c + 512 * hf), 512),
                                         start=(c == 0), stop=(c == 3))
                        mi.then_inc(s_t.h, 1); s_t.n += 1
                    sec("tensor", fpj)
                    t_pj = s_t.n
                    l1f[("trg", par)] = s_t.n

                    def fya(v, ti=ti, si=si, pb2=pb2, t_pj=t_pj):
                        v.wait_ge(s_t.h, t_pj)
                        for hf in range(2):
                            dst = f32v(O_YA + 4 * (D * ti + 512 * hf), 512)
                            if si == 0:
                                v.tensor_copy(dst[:, :], PS[pb2 + hf][:, :]).then_inc(s_v.h, 1); s_v.n += 1
                            else:
                                v.tensor_tensor(dst[:, :], dst[:, :], PS[pb2 + hf][:, :],
                                                op=ALU.add).then_inc(s_v.h, 1); s_v.n += 1
                    sec("vector", fya)
                    l1f[("pj", pb2)] = s_v.n
                h2_free[0] = s_v.n

            for ti in range(8):
                i = (hb + 128 * ti) // 128

                def fsp(v, ti=ti, i=i):
                    v.wait_ge(s_v.h, s_v.n)
                    y = f32v(O_YA + 4 * D * ti, D)
                    v.tensor_scalar(spa[:, :], y[:, :], SPIKE_B[0], None,
                                    op0=ALU.is_gt).then_inc(s_v.h, 1); s_v.n += 1
                    for bth in SPIKE_B[1:]:
                        v.wait_ge(s_v.h, s_v.n)
                        v.scalar_tensor_tensor(spa[:, :], y[:, :], bth, spa[:, :],
                                               op0=ALU.is_gt, op1=ALU.add).then_inc(s_v.h, 1); s_v.n += 1
                    v.wait_ge(s_v.h, s_v.n)
                    dst = rtile(i)
                    v.scalar_tensor_tensor(dst[:, :], spa[:, :], 0.2, dst[:, :],
                                           op0=ALU.mult, op1=ALU.add).then_inc(s_v.h, 1); s_v.n += 1
                sec("vector", fsp)
                v_done = s_v.n

                st0 = s_st.n

                def fst(sy, i=i, v_done=v_done, st0=st0):
                    sy.wait_ge(s_st.h, st0)
                    sy.wait_ge(s_v.h, v_done)
                    sy.dma_start(out=out_d[128 * i: 128 * (i + 1), :],
                                 in_=f32v(O_R + 4096 * i, D)).then_inc(s_st.h, 16); s_st.n += 16
                sec("sync", fst)

        sflush()

        def ffin(sy):
            sy.wait_ge(s_st.h, s_st.n)
        sec("sync", ffin)

    return nc


class Sem:
    def __init__(self, h):
        self.h = h
        self.n = 0


_NC_CACHE = {}


def kernel(L0_attn_x, L0_attn_w_qkv, L0_attn_w_out, L0_attn_b_out,
           L0_ffn_w_gate, L0_ffn_w_up, L0_ffn_w_down,
           L1_attn_Wq, L1_attn_Wk, L1_attn_Wv, L1_attn_Wo,
           L1_ffn_W_fc, L1_ffn_W_proj):
    x = np.ascontiguousarray(np.asarray(L0_attn_x, dtype=np.float32))
    B, S, _ = x.shape
    CPS = 4
    ident = np.eye(128, dtype=np.float32)
    sel = np.zeros((16, KC * 128), dtype=np.float32)
    for c in range(KC):
        sel[2 * c, 128 * c: 128 * c + 64] = 1.0
        sel[2 * c + 1, 128 * c + 64: 128 * c + 128] = 1.0
    w = {
        "wqkv": np.ascontiguousarray(L0_attn_w_qkv, dtype=np.float32),
        "wout": np.ascontiguousarray(L0_attn_w_out, dtype=np.float32),
        "bout": np.ascontiguousarray(np.asarray(L0_attn_b_out, dtype=np.float32).reshape(1, D)),
        "wgate": np.ascontiguousarray(L0_ffn_w_gate, dtype=np.float32),
        "wup": np.ascontiguousarray(L0_ffn_w_up, dtype=np.float32),
        "wdown": np.ascontiguousarray(L0_ffn_w_down, dtype=np.float32),
        "wq": np.ascontiguousarray(L1_attn_Wq, dtype=np.float32),
        "wk": np.ascontiguousarray(L1_attn_Wk, dtype=np.float32),
        "wv": np.ascontiguousarray(L1_attn_Wv, dtype=np.float32),
        "wo": np.ascontiguousarray(L1_attn_Wo, dtype=np.float32),
        "wfc": np.ascontiguousarray(L1_ffn_W_fc, dtype=np.float32),
        "wproj": np.ascontiguousarray(L1_ffn_W_proj, dtype=np.float32),
        "ident": ident, "sel": sel,
    }
    if "nc" not in _NC_CACHE:
        _NC_CACHE["nc"] = build_program()
    nc = _NC_CACHE["nc"]
    in_maps = []
    for core in range(8):
        b = core // CPS
        t0 = (core % CPS) * TPC
        m = dict(w)
        m["x"] = np.ascontiguousarray(x[b, t0:t0 + TPC, :])
        m["xg"] = np.ascontiguousarray(x[b, 0:G, :])
        m["wg_sh"] = np.ascontiguousarray(w["wgate"][:, 128 * core: 128 * (core + 1)])
        m["wu_sh"] = np.ascontiguousarray(w["wup"][:, 128 * core: 128 * (core + 1)])
        m["wd_sh"] = np.ascontiguousarray(w["wdown"][128 * core: 128 * (core + 1), :])
        m["wfc_sh"] = np.ascontiguousarray(w["wfc"][:, 128 * core: 128 * (core + 1)])
        m["wpj_sh"] = np.ascontiguousarray(w["wproj"][:, 512 * core: 512 * (core + 1)])
        for k in ("wq", "wk", "wv", "wo"):
            m[k + "_sh"] = np.ascontiguousarray(w[k][:, 128 * core: 128 * (core + 1)])
        in_maps.append(m)
    res = run_bass_kernel_spmd(nc, in_maps, core_ids=list(range(8)))
    out = np.empty((B, S, D), dtype=np.float32)
    for core in range(8):
        b = core // CPS
        t0 = (core % CPS) * TPC
        out[b, t0:t0 + TPC, :] = res.results[core]["out"]
    return out


# version 17
# speedup vs baseline: 1.5891x; 1.0024x over previous
import contextlib
import numpy as np
import concourse.bass as bass
import concourse.mybir as mybir
from concourse.bass_utils import run_bass_kernel_spmd

F32 = mybir.dt.float32
BF16 = mybir.dt.bfloat16
U8 = mybir.dt.uint8
ALU = mybir.AluOpType
AF = mybir.ActivationFunctionType
AXX = mybir.AxisListType.X

D = 1024
KC = 8
H = 16
TPC = 2048
NT = 16
G = 64
IF = 2816
FF = 4096
EPS = 1e-5
XT = TPC + G

FP4_TH_A = (0.25, 0.75, 1.25, 1.75)
FP4_TH_B = (2.5, 3.5, 5.0, 5.0)
SPIKE_B = (0.2441942809699397, 0.46396913384288535, 0.6861859295255305,
           0.9059607823984762, 1.083978413225562)
SOFTMAX_C = 24.0

NSL = [(0, 512), (512, 1024), (1024, 1536), (1536, 2048), (2048, 2560), (2560, 2816)]
FSL = [(s, s + 512) for s in range(0, FF, 512)]
TGROUPS = [(0, 512), (512, 1024), (1024, 1536), (1536, 2048), (2048, 2112)]

STAGE = 7
SH_GATE = 0
SH_UP = 360448
SH_DOWN = 720896
SH_FCT = 1081344
SH_PJT = 1605632
SH_WQ = 2129920
SH_WK = 2260992
SH_WV = 2392064
SH_WO = 2523136
SHN = 2654208

SBUF_CAP = 212800

O_R = 0
O_RG = 65536
O_IDENT = 69632
O_ONES = 69888
O_SEL = 70144
O_BOUT = 74240
O_KV = 76288
O_STATS = 78400
O_CB = 78592
O_BD = 78656
O_ZST = 80704
O_KG = 80960
O_VG = 81984
PERS_END = 84032
O_R1 = 84032
O_R2 = 117824
O_R3 = 150592
O_R5 = 184384
R5_END = O_R5 + 28416
assert R5_END <= SBUF_CAP, R5_END

O_E1 = O_R5
O_E2 = O_R5 + 2048
O_XNF = O_R5
O_BNST = O_R5 + 4096
O_MV = O_R5 + 4160
O_XNBF = O_R5 + 4224
O_QT = O_R5 + 6272
O_QA = O_R5 + 10368
O_QM = O_R5 + 14464
O_QS = O_R5 + 14720
O_QR = O_R5 + 14976
O_QSB = O_R5 + 15232
O_QCA = O_R5 + 15424
O_QCB = O_R5 + 17472
O_QSG = O_R5 + 19520
O_KBF = O_R5 + 6272
O_VEX = O_R5 + 8320
O_WQQ = O_R5 + 10432
O_ATT = O_R5 + 10432
O_DEN = O_R5
O_RCB = O_R5 + 2048
O_HG = O_R5 + 21568
O_HF = O_R5 + 23616
O_HQ = O_R5 + 25664
O_HQF = O_R5 + 26688
O_PROBS = O_R5 + 6272
O_PT = O_R5 + 7040
O_AVF = O_R5 + 8064
O_ZB = O_R5 + 10112
O_ZR = O_R5 + 10176
O_VH = O_R5 + 10432
O_H2 = O_R5 + 10432
O_SPA = O_R5 + 18688
O_BHF = O_R5 + 6272
O_BHG = O_R5 + 10368
O_QTMP = O_R5 + 21568
O_BHQ = O_R5 + 23616
O_HQF2 = O_R5 + 25664



def build_program():
    nc = bass.Bass("TRN2", target_bir_lowering=False)

    x_d = nc.dram_tensor("x", [TPC, D], F32, kind="ExternalInput")
    xg_d = nc.dram_tensor("xg", [G, D], F32, kind="ExternalInput")
    wqkv_d = nc.dram_tensor("wqkv", [D, 3 * D], F32, kind="ExternalInput")
    wout_d = nc.dram_tensor("wout", [D, D], F32, kind="ExternalInput")
    bout_d = nc.dram_tensor("bout", [1, D], F32, kind="ExternalInput")
    wgate_d = nc.dram_tensor("wgate", [IF, D], F32, kind="ExternalInput")
    wup_d = nc.dram_tensor("wup", [IF, D], F32, kind="ExternalInput")
    wdown_d = nc.dram_tensor("wdown", [D, IF], F32, kind="ExternalInput")
    wq_d = nc.dram_tensor("wq", [D, D], F32, kind="ExternalInput")
    wk_d = nc.dram_tensor("wk", [D, D], F32, kind="ExternalInput")
    wv_d = nc.dram_tensor("wv", [D, D], F32, kind="ExternalInput")
    wo_d = nc.dram_tensor("wo", [D, D], F32, kind="ExternalInput")
    wfc_d = nc.dram_tensor("wfc", [FF, D], F32, kind="ExternalInput")
    wproj_d = nc.dram_tensor("wproj", [D, FF], F32, kind="ExternalInput")
    ident_d = nc.dram_tensor("ident", [128, 128], F32, kind="ExternalInput")
    sel_d = nc.dram_tensor("sel", [16, KC * 128], F32, kind="ExternalInput")
    out_d = nc.dram_tensor("out", [TPC, D], F32, kind="ExternalOutput")

    wg_sh = nc.dram_tensor("wg_sh", [IF, 128], F32, kind="ExternalInput")
    wu_sh = nc.dram_tensor("wu_sh", [IF, 128], F32, kind="ExternalInput")
    wd_sh = nc.dram_tensor("wd_sh", [128, IF], F32, kind="ExternalInput")
    wfc_sh = nc.dram_tensor("wfc_sh", [FF, 128], F32, kind="ExternalInput")
    wpj_sh = nc.dram_tensor("wpj_sh", [D, 512], F32, kind="ExternalInput")
    w1_sh = {k: nc.dram_tensor(k + "_sh", [D, 128], F32, kind="ExternalInput")
             for k in ("wq", "wk", "wv", "wo")}

    cc_in = nc.dram_tensor("cc_in", [128, 520], F32)
    cc_out = nc.dram_tensor("cc_out", [128, 520], F32)
    wfc_shs = nc.dram_tensor("wfc_shs", [FF, 128], BF16)
    wpj_shs = nc.dram_tensor("wpj_shs", [D, 512], BF16)
    w1_shs = {k: nc.dram_tensor(k + "_shs", [D, 128], BF16) for k in ("wq", "wk", "wv", "wo")}
    wsh = nc.dram_tensor("wsh", [SHN], BF16)
    wsh_g = nc.dram_tensor("wsh_g", [8 * SHN], BF16)

    with contextlib.ExitStack() as es:
        E = es.enter_context
        sb = E(nc.sbuf_tensor("arena", [128, SBUF_CAP], U8))
        s_in = Sem(E(nc.semaphore("s_in")))
        s_tr = Sem(E(nc.semaphore("s_tr")))
        s_st = Sem(E(nc.semaphore("s_st")))
        s_wg = Sem(E(nc.semaphore("s_wg")))
        s_g = Sem(E(nc.semaphore("s_g")))
        s_cc = Sem(E(nc.semaphore("s_cc")))
        s_t = Sem(E(nc.semaphore("s_t")))
        s_v = Sem(E(nc.semaphore("s_v")))
        s_s = Sem(E(nc.semaphore("s_s")))
        s_gc = Sem(E(nc.semaphore("s_gc")))
        PS = [E(nc.psum_tensor(f"ps{_i}", [128, 512], F32)) for _i in range(8)]
        block = E(nc.Block())

        bf_base = sb.ap().bitcast(BF16)
        f32_base = sb.ap().bitcast(F32)
        PB = [p.ap().bitcast(BF16) for p in PS]

        def bfv(off, cols, np_=128):
            assert off % 2 == 0
            return bass.AP(bf_base.tensor, bf_base.offset + off // 2,
                           [[bf_base.ap[0][0], np_], [1, cols]])

        def f32v(off, cols, np_=128):
            assert off % 4 == 0
            return bass.AP(f32_base.tensor, f32_base.offset + off // 4,
                           [[f32_base.ap[0][0], np_], [1, cols]])

        def bfap(off, dims, np_=128):
            assert off % 2 == 0
            return bass.AP(bf_base.tensor, bf_base.offset + off // 2,
                           [[bf_base.ap[0][0], np_]] + [list(d) for d in dims])

        def f32ap(off, dims, np_=128):
            assert off % 4 == 0
            return bass.AP(f32_base.tensor, f32_base.offset + off // 4,
                           [[f32_base.ap[0][0], np_]] + [list(d) for d in dims])

        def psap(i, dims, np_=128):
            a = PS[i].ap()
            return bass.AP(a.tensor, a.offset,
                           [[a.ap[0][0], np_]] + [list(d) for d in dims])

        def pbap(i, dims, np_=128):
            a = PB[i]
            return bass.AP(a.tensor, a.offset, [[a.ap[0][0], np_]] + [list(d) for d in dims])

        def sec(name, fn):
            getattr(block, name)(fn)

        _sbuf = []

        def ssec(fn, incs):
            s_s.n += incs
            _sbuf.append(fn)

        def sflush():
            if not _sbuf:
                return
            fns = list(_sbuf)
            del _sbuf[:]

            def _run(s):
                for f in fns:
                    f(s)
            sec("scalar", _run)

        ident = bfv(O_IDENT, 128)
        ones = bfv(O_ONES, 128)
        sel = f32v(O_SEL, KC * 128)
        boutb = bfv(O_BOUT, D)
        kvsb = f32v(O_KV, 520)
        stats = f32v(O_STATS, 34)
        cbc = f32v(O_CB, 16)
        bd = bfv(O_BD, KC * 128)
        zst = bfv(O_ZST, KC * 16)

        def tile_np(i):
            return 128 if i < NT else G

        def rtile(i):
            if i < NT:
                return f32v(O_R + 4096 * i, D)
            return f32v(O_RG, D, np_=G)

        def tile_colbase(i):
            return 128 * i if i < NT else TPC

        def early_finish():
            sflush()
            vv = s_v.n

            def fstall(sy):
                sy.wait_ge(s_v.h, vv)
                sy.wait_ge(s_s.h, s_s.n)
                sy.wait_ge(s_t.h, s_t.n)
                if STAGE >= 4:
                    sy.wait_ge(s_g.h, s_g.n)
                for i in range(NT):
                    sy.dma_start(out=out_d[128 * i: 128 * (i + 1), :],
                                 in_=f32v(O_R + 4096 * i, D)).then_inc(s_st.h, 16); s_st.n += 16
                sy.wait_ge(s_st.h, s_st.n)
            sec("sync", fstall)

        def _loads(sy):
            for i in range(NT):
                sy.dma_start(out=f32v(O_R + 4096 * i, D), in_=x_d[128 * i: 128 * (i + 1), :])\
                  .then_inc(s_in.h, 16); s_in.n += 16
            sy.dma_start(out=f32v(O_RG, D, np_=G), in_=xg_d[:, :]).then_inc(s_in.h, 16); s_in.n += 16
            sy.dma_start(out=bass.AP(f32_base.tensor, f32_base.offset + O_SEL // 4,
                                     [[f32_base.ap[0][0], 16], [1, KC * 128]]),
                         in_=sel_d[:, :]).then_inc(s_in.h, 16); s_in.n += 16
        sec("sync", _loads)
        in_x = s_in.n

        wqkv_q = bfv(O_WQQ, KC * D)
        wqkv_kv = bfv(O_R2, KC * 2048)

        def _gload(g):
            g.dma_start(out=ident[:, :], in_=ident_d[:, :]).then_inc(s_wg.h, 16); s_wg.n += 16
            for c in range(KC):
                g.dma_start(out=bfv(O_WQQ + 2048 * c, D),
                            in_=wqkv_d[128 * c: 128 * (c + 1), 0:D]).then_inc(s_wg.h, 16); s_wg.n += 16
            for c in range(KC):
                g.dma_start(out=bfv(O_R2 + 4096 * c, 2048),
                            in_=wqkv_d[128 * c: 128 * (c + 1), D:3 * D]).then_inc(s_wg.h, 16); s_wg.n += 16
            g.dma_start(out=bass.AP(bf_base.tensor, bf_base.offset + O_BOUT // 2,
                                    [[bf_base.ap[0][0], 1], [1, D]]),
                        in_=bout_d[:, :]).then_inc(s_wg.h, 16); s_wg.n += 16
            g.dma_start(out=wfc_shs[:, :], in_=wfc_sh[:, :]).then_inc(s_wg.h, 16); s_wg.n += 16
            g.dma_start(out=wpj_shs[:, :], in_=wpj_sh[:, :]).then_inc(s_wg.h, 16); s_wg.n += 16
            for k in ("wq", "wk", "wv", "wo"):
                g.dma_start(out=w1_shs[k][:, :], in_=w1_sh[k][:, :]).then_inc(s_wg.h, 16); s_wg.n += 16
        sec("gpsimd", _gload)
        wg_ident = wg_wq = wg_wkv = wg_bout = s_wg.n

        def _consts(v):
            v.memset(ones[:, :], 1.0).then_inc(s_v.h, 1); s_v.n += 1
            v.memset(cbc[:, 0:1], -SOFTMAX_C).then_inc(s_v.h, 1); s_v.n += 1
            v.memset(bd[:, :], 0.0).then_inc(s_v.h, 1); s_v.n += 1
            v.memset(zst[:, :], 0.0).then_inc(s_v.h, 1); s_v.n += 1
        sec("vector", _consts)
        v_consts = s_v.n

        W_STG = O_R3
        W_QB = O_R3 + 11264
        wcoop_done = [0]

        def emit_wcoop():
            for wi, (shd, outd) in enumerate(((wg_sh, None), (wu_sh, None),
                                              (wd_sh, None))):
                in0 = s_in.n
                vg = s_v.n
                tg = s_t.n

                def fld(sy, wi=wi, shd=shd, in0=in0, vg=vg, tg=tg):
                    sy.wait_ge(s_in.h, in0)
                    if vg:
                        sy.wait_ge(s_v.h, vg)
                    if tg:
                        sy.wait_ge(s_t.h, tg)
                    if wi < 2:
                        for j in range(22):
                            sy.dma_start(out=f32v(W_STG + 4 * 128 * j, 128),
                                         in_=shd[128 * j: 128 * (j + 1), :]).then_inc(s_in.h, 16); s_in.n += 16
                    else:
                        sy.dma_start(out=f32v(W_STG, IF), in_=shd[:, :]).then_inc(s_in.h, 16); s_in.n += 16
                sec("sync", fld)
                dma_done = s_in.n

                def fwt(v, dma_done=dma_done):
                    v.wait_ge(s_in.h, dma_done)
                sec("vector", fwt)
                for (c0, c1) in ((0, 1024), (1024, 2048), (2048, 2816)):
                    emit_quant(f32v(W_STG + 4 * c0, c1 - c0),
                               bfv(W_QB + 2 * c0, c1 - c0), 128, c1 - c0, (c1 - c0) // 16,
                               base=O_R3 + 22528)
                v_q = s_v.n

                evacs = []
                for g3 in range(3):
                    jlist = range(8 * g3, min(22, 8 * g3 + 8))
                    pb = 6 + (g3 % 2)
                    te = tr_evac.get(pb, 0)

                    def ftw(t, jlist=jlist, pb=pb, v_q=v_q, te=te):
                        t.wait_ge(s_v.h, max(v_q, te))
                        t.wait_ge(s_wg.h, wg_ident)
                        for jj, j in enumerate(jlist):
                            t.transpose(bass.AP(PB[pb].tensor, PB[pb].offset + 128 * jj,
                                                [[PB[pb].ap[0][0], 128], [1, 128]]),
                                        bfv(W_QB + 2 * 128 * j, 128),
                                        ident[:, :]).then_inc(s_t.h, 1); s_t.n += 1
                    sec("tensor", ftw)
                    td = s_t.n

                    def fte(v, g3=g3, pb=pb, td=td, nj=len(list(jlist))):
                        v.wait_ge(s_t.h, td)
                        srcap = pbap(pb, [[128, nj], [1, 128]])
                        dstap = bfap(W_QB + 11264 * 0 + 2 * (IF + 128 * 8 * g3) if False else 0, [])
                        dstap = bfap(W_QB + 2 * IF + 2 * 128 * 8 * g3, [[128, nj], [1, 128]])
                        v.tensor_copy(dstap, srcap).then_inc(s_v.h, 1); s_v.n += 1
                    sec("vector", fte)
                    tr_evac[pb] = s_v.n
                v_T = s_v.n

                st0 = s_in.n

                def fst2(sy, wi=wi, outd=outd, v_T=v_T, st0=st0):
                    sy.wait_ge(s_in.h, st0)
                    sy.wait_ge(s_v.h, v_T)
                    if wi < 2:
                        dst = bass.AP(wsh.ap().tensor, SH_GATE if wi == 0 else SH_UP,
                                      [[IF, 128], [1, IF]])
                        sy.dma_start(out=dst, in_=bfv(W_QB + 2 * IF, IF)).then_inc(s_in.h, 16); s_in.n += 16
                    else:
                        srcap = bfap(W_QB + 2 * IF, [[128, 22], [1, 128]])
                        dstap = bass.AP(wsh.ap().tensor, SH_DOWN,
                                        [[128, 128], [128 * 128, 22], [1, 128]])
                        sy.dma_start(out=dstap, in_=srcap).then_inc(s_in.h, 16); s_in.n += 16
                sec("sync", fst2)

            tr0 = s_tr.n
            stg_t = bfv(W_STG, 4096)
            vg2 = s_v.n
            tg2 = s_t.n

            def ftr1(sy, tr0=tr0, vg2=vg2, tg2=tg2):
                sy.wait_ge(s_tr.h, tr0)
                sy.wait_ge(s_v.h, vg2)
                sy.wait_ge(s_t.h, tg2)
                sy.wait_ge(s_wg.h, wg_bout)
                sy.dma_start_transpose(stg_t[:, :], wfc_shs[0:FF, 0:128]).then_inc(s_tr.h, 16); s_tr.n += 16
            sec("sync", ftr1)
            tr1 = s_tr.n
            st1 = s_in.n

            def fsv1(sy, tr1=tr1, st1=st1):
                sy.wait_ge(s_in.h, st1)
                sy.wait_ge(s_tr.h, tr1)
                sy.dma_start(out=bass.AP(wsh.ap().tensor, SH_FCT, [[FF, 128], [1, FF]]),
                             in_=stg_t[:, :]).then_inc(s_in.h, 16); s_in.n += 16
            sec("sync", fsv1)

            tr0 = s_tr.n
            st1b = s_in.n

            def ftr2(sy, tr0=tr0, st1b=st1b):
                sy.wait_ge(s_tr.h, tr0)
                sy.wait_ge(s_in.h, st1b)
                for t4 in range(4):
                    sy.dma_start_transpose(bfv(W_STG + 2048 * t4, 1024),
                                           wpj_shs[0:D, 128 * t4: 128 * (t4 + 1)]).then_inc(s_tr.h, 16); s_tr.n += 16
            sec("sync", ftr2)
            tr2 = s_tr.n
            st2 = s_in.n

            def fsv2(sy, tr2=tr2, st2=st2):
                sy.wait_ge(s_in.h, st2)
                sy.wait_ge(s_tr.h, tr2)
                srcap = bfap(W_STG, [[1024, 4], [1, 1024]])
                dstap = bass.AP(wsh.ap().tensor, SH_PJT,
                                [[1024, 128], [128 * 1024, 4], [1, 1024]])
                sy.dma_start(out=dstap, in_=srcap).then_inc(s_in.h, 16); s_in.n += 16
            sec("sync", fsv2)

            tr0 = s_tr.n
            st3 = s_in.n

            def ftr3(sy, tr0=tr0, st3=st3):
                sy.wait_ge(s_tr.h, tr0)
                sy.wait_ge(s_in.h, st3)
                for ki, k in enumerate(("wq", "wk", "wv", "wo")):
                    sy.dma_start_transpose(bfv(W_STG + 2048 * ki, 1024),
                                           w1_shs[k][0:D, 0:128]).then_inc(s_tr.h, 16); s_tr.n += 16
            sec("sync", ftr3)
            tr3 = s_tr.n
            st4 = s_in.n

            def fsv3(sy, tr3=tr3, st4=st4):
                sy.wait_ge(s_in.h, st4)
                sy.wait_ge(s_tr.h, tr3)
                for ki in range(4):
                    sy.dma_start(out=bass.AP(wsh.ap().tensor, SH_WQ + 131072 * ki,
                                             [[1024, 128], [1, 1024]]),
                                 in_=bfv(W_STG + 2048 * ki, 1024)).then_inc(s_in.h, 16); s_in.n += 16
            sec("sync", fsv3)
            wcoop_done[0] = s_in.n

        xnf = f32v(O_XNF, D)
        bnst = f32v(O_BNST, 12)
        mvv = f32v(O_MV, 2)
        vep = f32v(O_MV + 8, 1)
        rstd = f32v(O_MV + 12, 1)
        xnbf = bfv(O_XNBF, D)
        elu1 = f32v(O_E1, 512)
        elu2 = f32v(O_E2, 512)

        vepal = f32v(O_E1, 17)

        def emit_ln_stats():
            def fv(v):
                v.wait_ge(s_in.h, in_x)
                v.wait_ge(s_v.h, s_v.n)
                v.wait_ge(s_s.h, s_s.n)
                for i in range(17):
                    np_ = tile_np(i)
                    srct = rtile(i)
                    v.bn_stats(bnst[0:np_, 0:6], srct[0:np_, 0:512]).then_inc(s_v.h, 1); s_v.n += 1
                    v.bn_stats(bnst[0:np_, 6:12], srct[0:np_, 512:1024]).then_inc(s_v.h, 1); s_v.n += 1
                    v.wait_ge(s_v.h, s_v.n)
                    v.bn_aggr(mvv[0:np_, :], bnst[0:np_, :]).then_inc(s_v.h, 1); s_v.n += 1
                    v.wait_ge(s_v.h, s_v.n)
                    v.tensor_copy(stats[0:np_, 2 * i: 2 * i + 1], mvv[0:np_, 0:1]).then_inc(s_v.h, 1); s_v.n += 1
                    v.tensor_scalar(vepal[0:np_, i:i + 1], mvv[0:np_, 1:2], EPS, None,
                                    op0=ALU.add).then_inc(s_v.h, 1); s_v.n += 1
            sec("vector", fv)
            v1 = s_v.n
            mid = s_s.n + 17

            def fs(s, v1=v1, mid=mid):
                s.wait_ge(s_v.h, v1)
                for i in range(17):
                    np_ = tile_np(i)
                    s.activation(stats[0:np_, 2 * i + 1: 2 * i + 2], vepal[0:np_, i:i + 1],
                                 AF.Ln).then_inc(s_s.h, 1)
                s.wait_ge(s_s.h, mid)
                for i in range(17):
                    np_ = tile_np(i)
                    s.activation(stats[0:np_, 2 * i + 1: 2 * i + 2],
                                 stats[0:np_, 2 * i + 1: 2 * i + 2], AF.Exp,
                                 scale=-0.5).then_inc(s_s.h, 1)
            ssec(fs, 34)

        def emit_ln(i, save_stats):
            np_ = tile_np(i)
            srct = rtile(i)
            s1 = s_s.n

            def fv2(v):
                v.wait_ge(s_s.h, s1)
                v.wait_ge(s_v.h, s_v.n)
                v.wait_ge(s_t.h, s_t.n)
                v.tensor_scalar(xnbf[0:np_, :], srct[0:np_, :], stats[0:np_, 2 * i: 2 * i + 1],
                                stats[0:np_, 2 * i + 1: 2 * i + 2], op0=ALU.subtract,
                                op1=ALU.mult).then_inc(s_v.h, 1); s_v.n += 1
            sec("vector", fv2)

        tr_evac = {}

        def emit_tm_to_fm(src_bf, dst_off, colbase, np_, pbank):
            v_src = s_v.n

            def ft(t):
                t.wait_ge(s_v.h, v_src)
                t.wait_ge(s_wg.h, wg_ident)
                if pbank in tr_evac:
                    t.wait_ge(s_v.h, tr_evac[pbank])
                for c in range(KC):
                    t.transpose(bass.AP(PB[pbank].tensor, PB[pbank].offset + 128 * c,
                                        [[PB[pbank].ap[0][0], 128], [1, np_]]),
                                src_bf[0:np_, 128 * c: 128 * (c + 1)],
                                ident[0:np_, 0:np_]).then_inc(s_t.h, 1)
                    s_t.n += 1
            sec("tensor", ft)
            t1 = s_t.n

            def fe(v):
                v.wait_ge(s_t.h, t1)
                srcap = pbap(pbank, [[128, KC], [1, np_]])
                dstap = bfap(dst_off + 2 * colbase, [[XT, KC], [1, np_]])
                v.tensor_copy(dstap, srcap).then_inc(s_v.h, 1); s_v.n += 1
            sec("vector", fe)
            tr_evac[pbank] = s_v.n

        def emit_quant(src_f32, dst_bf, np_, F, blocks, base=None, tguard=0, destructive=False,
                       geng=False, tmp_off=None, sguard=None, split=None):
            if base is None:
                offs = (O_QT, O_QM, O_QS, O_QR, O_QSB, O_QCA, O_QCB, O_QSG, O_QTMP)
            else:
                assert not geng or tmp_off is not None
                offs = (base, base + 4096, base + 4352, base + 4608, base + 4864,
                        base + 4992, base + 7040, base + 9088, tmp_off)
            t_f = src_f32 if destructive else f32v(offs[0], F)
            m_f = f32v(offs[1], blocks)
            sc_f = f32v(offs[2], blocks)
            r_f = f32v(offs[3], blocks)
            s_b = bfv(offs[4], blocks)
            ca = bfv(offs[5], F)
            cbch = bfv(offs[6], F)
            sg = bfv(offs[7], F)
            tmp = bfv(offs[8], F) if offs[8] is not None else None

            if geng:
                def fvr(v, tguard=tguard, sguard=(s_s.n if sguard is None else sguard)):
                    v.wait_ge(s_v.h, s_v.n)
                    v.wait_ge(s_s.h, sguard)
                    if tguard:
                        v.wait_ge(s_t.h, tguard)
                    src3 = bass.AP(src_f32.tensor, src_f32.offset,
                                   [[src_f32.ap[0][0], np_], [16, blocks], [1, 16]])
                    v.tensor_reduce(m_f[0:np_, :], src3, axis=AXX, op=ALU.max,
                                    apply_absolute_value=True).then_inc(s_v.h, 1); s_v.n += 1
                    v.wait_ge(s_v.h, s_v.n)
                    v.tensor_scalar(sc_f[0:np_, :], m_f[0:np_, :], 1.0 / 6.0, 1e-12,
                                    op0=ALU.mult, op1=ALU.max).then_inc(s_v.h, 1); s_v.n += 1
                    v.wait_ge(s_v.h, s_v.n)
                    v.reciprocal(r_f[0:np_, :], sc_f[0:np_, :]).then_inc(s_v.h, 1); s_v.n += 1
                    v.tensor_copy(s_b[0:np_, :], sc_f[0:np_, :]).then_inc(s_v.h, 1); s_v.n += 1
                sec("vector", fvr)
                vr = s_v.n

                def fg2(g, vr=vr):
                    g.wait_ge(s_v.h, vr)
                    g.wait_ge(s_gc.h, s_gc.n)
                    rb = bass.AP(r_f.tensor, r_f.offset,
                                 [[r_f.ap[0][0], np_], [1, blocks], [0, 16]])
                    g.tensor_tensor(t_f[0:np_, :], src_f32[0:np_, 0:F], rb,
                                    op=ALU.mult).then_inc(s_gc.h, 1); s_gc.n += 1
                sec("gpsimd", fg2)
                g2 = s_gc.n
                smid = s_s.n + 1

                def fsgg(s, g2=g2, smid=smid):
                    s.wait_ge(s_gc.h, g2)
                    s.activation(sg[0:np_, :], t_f[0:np_, :], AF.Sign).then_inc(s_s.h, 1)
                    s.wait_ge(s_s.h, smid)
                    s.activation(t_f[0:np_, :], t_f[0:np_, :], AF.Abs).then_inc(s_s.h, 1)
                ssec(fsgg, 2)
                s1 = s_s.n
                a_f = t_f

                def _tail(s1=s1):
                    _emit_tail(s1)

                def _emit_tail(s1):
                    def fg3(g, s1=s1):
                        g.wait_ge(s_s.h, s1)
                        g.wait_ge(s_v.h, s_v.n)
                        g.wait_ge(s_gc.h, s_gc.n)
                        sbb = bass.AP(s_b.tensor, s_b.offset,
                                      [[s_b.ap[0][0], np_], [1, blocks], [0, 16]])
                        g.tensor_tensor(sg[0:np_, :], sg[0:np_, :], sbb,
                                        op=ALU.mult).then_inc(s_gc.h, 1); s_gc.n += 1
                    sec("gpsimd", fg3)
                    g3 = s_gc.n

                    def fvA(v, s1=s1, g3=g3):
                        v.wait_ge(s_s.h, s1)
                        v.wait_ge(s_v.h, s_v.n)
                        v.tensor_scalar(ca[0:np_, :], a_f[0:np_, :], 0.25, None,
                                        op0=ALU.is_gt).then_inc(s_v.h, 1); s_v.n += 1
                        for th in (0.75, 1.25, 1.75):
                            v.wait_ge(s_v.h, s_v.n)
                            v.scalar_tensor_tensor(ca[0:np_, :], a_f[0:np_, :], th, ca[0:np_, :],
                                                   op0=ALU.is_gt, op1=ALU.add).then_inc(s_v.h, 1); s_v.n += 1
                        v.tensor_scalar(cbch[0:np_, :], a_f[0:np_, :], 2.5, None,
                                        op0=ALU.is_gt).then_inc(s_v.h, 1); s_v.n += 1
                        v.wait_ge(s_v.h, s_v.n)
                        v.scalar_tensor_tensor(cbch[0:np_, :], a_f[0:np_, :], 3.5, cbch[0:np_, :],
                                               op0=ALU.is_gt, op1=ALU.add).then_inc(s_v.h, 1); s_v.n += 1
                        v.tensor_scalar(tmp[0:np_, :], a_f[0:np_, :], 5.0, 2.0,
                                        op0=ALU.is_gt, op1=ALU.mult).then_inc(s_v.h, 1); s_v.n += 1
                        v.wait_ge(s_v.h, s_v.n)
                        v.tensor_tensor(cbch[0:np_, :], cbch[0:np_, :], tmp[0:np_, :],
                                        op=ALU.add).then_inc(s_v.h, 1); s_v.n += 1
                        v.wait_ge(s_v.h, s_v.n)
                        v.scalar_tensor_tensor(ca[0:np_, :], ca[0:np_, :], 0.5, cbch[0:np_, :],
                                               op0=ALU.mult, op1=ALU.add).then_inc(s_v.h, 1); s_v.n += 1
                        v.wait_ge(s_gc.h, g3)
                        v.wait_ge(s_v.h, s_v.n)
                        v.tensor_tensor(dst_bf[0:np_, 0:F], ca[0:np_, :], sg[0:np_, :],
                                        op=ALU.mult).then_inc(s_v.h, 1); s_v.n += 1
                    sec("vector", fvA)

                if split is None:
                    _tail()
                else:
                    split.append(_tail)
                return s1

            def fv(v):
                v.wait_ge(s_v.h, s_v.n)
                v.wait_ge(s_s.h, s_s.n)
                if tguard:
                    v.wait_ge(s_t.h, tguard)
                src3 = bass.AP(src_f32.tensor, src_f32.offset,
                               [[src_f32.ap[0][0], np_], [16, blocks], [1, 16]])
                v.tensor_reduce(m_f[0:np_, :], src3, axis=AXX, op=ALU.max,
                                apply_absolute_value=True).then_inc(s_v.h, 1); s_v.n += 1
                v.wait_ge(s_v.h, s_v.n)
                v.tensor_scalar(sc_f[0:np_, :], m_f[0:np_, :], 1.0 / 6.0, 1e-12,
                                op0=ALU.mult, op1=ALU.max).then_inc(s_v.h, 1); s_v.n += 1
                v.wait_ge(s_v.h, s_v.n)
                v.reciprocal(r_f[0:np_, :], sc_f[0:np_, :]).then_inc(s_v.h, 1); s_v.n += 1
                v.tensor_copy(s_b[0:np_, :], sc_f[0:np_, :]).then_inc(s_v.h, 1); s_v.n += 1
                v.wait_ge(s_v.h, s_v.n)
                rb = bass.AP(r_f.tensor, r_f.offset,
                             [[r_f.ap[0][0], np_], [1, blocks], [0, 16]])
                v.tensor_tensor(t_f[0:np_, :], src_f32[0:np_, 0:F], rb,
                                op=ALU.mult).then_inc(s_v.h, 1); s_v.n += 1
            sec("vector", fv)
            v1 = s_v.n
            smid = s_s.n + 1

            def fsg(s, v1=v1, smid=smid):
                s.wait_ge(s_v.h, v1)
                s.activation(sg[0:np_, :], t_f[0:np_, :], AF.Sign).then_inc(s_s.h, 1)
                s.wait_ge(s_s.h, smid)
                s.activation(t_f[0:np_, :], t_f[0:np_, :], AF.Abs).then_inc(s_s.h, 1)
            ssec(fsg, 2)
            s1 = s_s.n

            if tmp is not None:
                def fv2(v):
                    v.wait_ge(s_s.h, s1)
                    a_f = t_f
                    v.wait_ge(s_v.h, s_v.n)
                    v.tensor_scalar(ca[0:np_, :], a_f[0:np_, :], 0.25, None,
                                    op0=ALU.is_gt).then_inc(s_v.h, 1); s_v.n += 1
                    v.wait_ge(s_v.h, s_v.n)
                    v.tensor_scalar(tmp[0:np_, :], a_f[0:np_, :], 0.75, None,
                                    op0=ALU.is_gt).then_inc(s_v.h, 1); s_v.n += 1
                    v.wait_ge(s_v.h, s_v.n)
                    v.tensor_tensor(ca[0:np_, :], ca[0:np_, :], tmp[0:np_, :],
                                    op=ALU.add).then_inc(s_v.h, 1); s_v.n += 1
                    v.wait_ge(s_v.h, s_v.n)
                    v.tensor_scalar(cbch[0:np_, :], a_f[0:np_, :], 1.25, None,
                                    op0=ALU.is_gt).then_inc(s_v.h, 1); s_v.n += 1
                    v.wait_ge(s_v.h, s_v.n)
                    v.tensor_scalar(tmp[0:np_, :], a_f[0:np_, :], 1.75, None,
                                    op0=ALU.is_gt).then_inc(s_v.h, 1); s_v.n += 1
                    v.wait_ge(s_v.h, s_v.n)
                    v.tensor_tensor(cbch[0:np_, :], cbch[0:np_, :], tmp[0:np_, :],
                                    op=ALU.add).then_inc(s_v.h, 1); s_v.n += 1
                    v.wait_ge(s_v.h, s_v.n)
                    v.tensor_tensor(ca[0:np_, :], ca[0:np_, :], cbch[0:np_, :],
                                    op=ALU.add).then_inc(s_v.h, 1); s_v.n += 1
                    v.wait_ge(s_v.h, s_v.n)
                    v.tensor_scalar(cbch[0:np_, :], a_f[0:np_, :], 2.5, None,
                                    op0=ALU.is_gt).then_inc(s_v.h, 1); s_v.n += 1
                    v.wait_ge(s_v.h, s_v.n)
                    v.tensor_scalar(tmp[0:np_, :], a_f[0:np_, :], 3.5, None,
                                    op0=ALU.is_gt).then_inc(s_v.h, 1); s_v.n += 1
                    v.wait_ge(s_v.h, s_v.n)
                    v.tensor_tensor(cbch[0:np_, :], cbch[0:np_, :], tmp[0:np_, :],
                                    op=ALU.add).then_inc(s_v.h, 1); s_v.n += 1
                    v.wait_ge(s_v.h, s_v.n)
                    v.tensor_scalar(tmp[0:np_, :], a_f[0:np_, :], 5.0, 2.0,
                                    op0=ALU.is_gt, op1=ALU.mult).then_inc(s_v.h, 1); s_v.n += 1
                    v.wait_ge(s_v.h, s_v.n)
                    v.tensor_tensor(cbch[0:np_, :], cbch[0:np_, :], tmp[0:np_, :],
                                    op=ALU.add).then_inc(s_v.h, 1); s_v.n += 1
                    v.wait_ge(s_v.h, s_v.n)
                    v.wait_ge(s_v.h, s_v.n)
                    v.scalar_tensor_tensor(ca[0:np_, :], ca[0:np_, :], 0.5, cbch[0:np_, :],
                                           op0=ALU.mult, op1=ALU.add).then_inc(s_v.h, 1); s_v.n += 1
                    sbb = bass.AP(s_b.tensor, s_b.offset,
                                  [[s_b.ap[0][0], np_], [1, blocks], [0, 16]])
                    v.wait_ge(s_v.h, s_v.n)
                    v.tensor_tensor(sg[0:np_, :], sg[0:np_, :], sbb, op=ALU.mult).then_inc(s_v.h, 1); s_v.n += 1
                    v.wait_ge(s_v.h, s_v.n)
                    v.tensor_tensor(dst_bf[0:np_, 0:F], ca[0:np_, :], sg[0:np_, :],
                                    op=ALU.mult).then_inc(s_v.h, 1); s_v.n += 1
                sec("vector", fv2)
            else:
                def fv2(v):
                    v.wait_ge(s_s.h, s1)
                    a_f = t_f
                    v.tensor_scalar(ca[0:np_, :], a_f[0:np_, :], FP4_TH_A[0], None,
                                    op0=ALU.is_gt).then_inc(s_v.h, 1); s_v.n += 1
                    for th in FP4_TH_A[1:]:
                        v.wait_ge(s_v.h, s_v.n)
                        v.scalar_tensor_tensor(ca[0:np_, :], a_f[0:np_, :], th, ca[0:np_, :],
                                               op0=ALU.is_gt, op1=ALU.add).then_inc(s_v.h, 1); s_v.n += 1
                    v.tensor_scalar(cbch[0:np_, :], a_f[0:np_, :], FP4_TH_B[0], None,
                                    op0=ALU.is_gt).then_inc(s_v.h, 1); s_v.n += 1
                    for th in FP4_TH_B[1:]:
                        v.wait_ge(s_v.h, s_v.n)
                        v.scalar_tensor_tensor(cbch[0:np_, :], a_f[0:np_, :], th, cbch[0:np_, :],
                                               op0=ALU.is_gt, op1=ALU.add).then_inc(s_v.h, 1); s_v.n += 1
                    v.wait_ge(s_v.h, s_v.n)
                    v.scalar_tensor_tensor(ca[0:np_, :], ca[0:np_, :], 0.5, cbch[0:np_, :],
                                           op0=ALU.mult, op1=ALU.add).then_inc(s_v.h, 1); s_v.n += 1
                    sbb = bass.AP(s_b.tensor, s_b.offset,
                                  [[s_b.ap[0][0], np_], [1, blocks], [0, 16]])
                    v.wait_ge(s_v.h, s_v.n)
                    v.tensor_tensor(sg[0:np_, :], sg[0:np_, :], sbb, op=ALU.mult).then_inc(s_v.h, 1); s_v.n += 1
                    v.wait_ge(s_v.h, s_v.n)
                    v.tensor_tensor(dst_bf[0:np_, 0:F], ca[0:np_, :], sg[0:np_, :],
                                    op=ALU.mult).then_inc(s_v.h, 1); s_v.n += 1
                sec("vector", fv2)

        def emit_elu1(bank, dst_bf, np_, n):
            t0 = s_t.n

            def fv(v):
                v.wait_ge(s_t.h, t0)
                v.wait_ge(s_s.h, s_s.n)
                v.wait_ge(s_v.h, s_v.n)
                v.tensor_scalar(elu1[0:np_, 0:n], PS[bank][0:np_, 0:n], 0.0, None,
                                op0=ALU.min).then_inc(s_v.h, 1); s_v.n += 1
            sec("vector", fv)
            v1 = s_v.n

            def fs(s, v1=v1):
                s.wait_ge(s_v.h, v1)
                s.activation(elu2[0:np_, 0:n], elu1[0:np_, 0:n], AF.Exp).then_inc(s_s.h, 1)
            ssec(fs, 1)
            s1 = s_s.n

            def fv2(v):
                v.wait_ge(s_s.h, s1)
                v.scalar_tensor_tensor(dst_bf[0:np_, 0:n], PS[bank][0:np_, 0:n], 0.0,
                                       elu2[0:np_, 0:n], op0=ALU.max,
                                       op1=ALU.add).then_inc(s_v.h, 1); s_v.n += 1
            sec("vector", fv2)
            return s_v.n

        XN_FM = O_R1

        emit_ln_stats()
        for i in range(17):
            emit_ln(i, save_stats=True)
            emit_tm_to_fm(xnbf, XN_FM, tile_colbase(i), tile_np(i), 6 + (i % 2))
        emit_wcoop()

        def xn_fm_view(k, a, b):
            return bfv(XN_FM + 2 * (XT * k + a), b - a)

        if STAGE < 2:
            early_finish()
            return nc

        sflush()
        Q_BF = O_R3
        bank_free = {}
        for gi, (a, b) in enumerate(TGROUPS):
            n = b - a
            for c in range(KC):
                bank = (gi * KC + c) % 4
                tw = tr_evac.get(6, 0)
                tw2 = tr_evac.get(7, 0)
                bf = bank_free.get(bank, 0)

                def fmm(t, c=c, a=a, b=b, n=n, bank=bank, tw=tw, tw2=tw2, bf=bf):
                    t.wait_ge(s_wg.h, wg_wq)
                    t.wait_ge(s_in.h, wcoop_done[0])
                    t.wait_ge(s_v.h, max(tw, tw2, bf))
                    for k in range(KC):
                        mi = t.matmul(PS[bank][:, 0:n],
                                 bfv(O_WQQ + 2048 * k + 256 * c, 128),
                                 xn_fm_view(k, a, b),
                                 start=(k == 0), stop=(k == KC - 1))
                    mi.then_inc(s_t.h, 1); s_t.n += 1
                sec("tensor", fmm)
                bank_free[bank] = emit_elu1(bank, bfv(Q_BF + 2 * (XT * c + a), n), 128, n)

        def q_bf_view(c, a, b):
            return bfv(Q_BF + 2 * (XT * c + a), b - a)

        if STAGE < 3:
            early_finish()
            return nc

        sflush()
        k_bf = bfv(O_KBF, D)
        v_ex = bfv(O_VEX, 1040)

        def fvo(v):
            v.wait_ge(s_v.h, s_v.n)
            v.memset(v_ex[:, :], 1.0).then_inc(s_v.h, 1); s_v.n += 1
        sec("vector", fvo)

        for i in range(NT):
            vfree = s_v.n

            def fkv(t, i=i, vfree=vfree):
                t.wait_ge(s_wg.h, wg_wkv)
                t.wait_ge(s_v.h, vfree)
                for half in range(2):
                    for k in range(KC):
                        mi = t.matmul(PS[half][:, :],
                                 xn_fm_view(k, 128 * i, 128 * (i + 1)),
                                 bfv(O_R2 + 4096 * k + 1024 * half, 512),
                                 start=(k == 0), stop=(k == KC - 1))
                for half in range(2):
                    for k in range(KC):
                        mi = t.matmul(PS[2 + half][:, :],
                                 xn_fm_view(k, 128 * i, 128 * (i + 1)),
                                 bfv(O_R2 + 4096 * k + 2048 + 1024 * half, 512),
                                 start=(k == 0), stop=(k == KC - 1))
                mi.then_inc(s_t.h, 1); s_t.n += 1
            sec("tensor", fkv)
            emit_elu1(0, bfv(O_KBF, 512), 128, 512)
            emit_elu1(1, bfv(O_KBF + 1024, 512), 128, 512)
            t_v = s_t.n

            def fvv(v, t_v=t_v):
                v.wait_ge(s_t.h, t_v)
                v.wait_ge(s_v.h, s_v.n)
                for half in range(2):
                    dst = bfap(O_VEX + 2 * 520 * half, [[65, 8], [1, 64]])
                    src = psap(2 + half, [[64, 8], [1, 64]])
                    v.tensor_copy(dst, src).then_inc(s_v.h, 1); s_v.n += 1
            sec("vector", fvv)
            v_kv = s_v.n

            te6 = tr_evac.get(6, 0)
            te7 = tr_evac.get(7, 0)

            def fkvmm(t, i=i, v_kv=v_kv, te6=te6, te7=te7):
                t.wait_ge(s_v.h, v_kv)
                if i == 0:
                    t.wait_ge(s_v.h, max(te6, te7))
                for p in range(8):
                    bank = 4 + p // 2
                    colb = 130 * (p % 2)
                    first = (i == 0 and p % 2 == 0)
                    last = (i == NT - 1 and p % 2 == 1)
                    mi = t.matmul(PS[bank][:, colb:colb + 130],
                                  k_bf[:, 128 * p: 128 * (p + 1)],
                                  v_ex[:, 130 * p: 130 * (p + 1)], start=first, stop=last)
                mi.then_inc(s_t.h, 1); s_t.n += 1
            sec("tensor", fkvmm)

        if STAGE < 4:
            early_finish()
            return nc

        t_kv = s_t.n

        def fkvout(v):
            v.wait_ge(s_t.h, t_kv)
            for h in range(H):
                p = h // 2
                bank = 4 + p // 2
                colb = 130 * (p % 2) + 65 * (h % 2)
                po = 64 * (h % 2)
                dst_c = 260 * (h // 8) + 65 * ((h // 2) % 4)
                v.tensor_copy(kvsb[po:po + 64, dst_c:dst_c + 65],
                              PS[bank][po:po + 64, colb:colb + 65]).then_inc(s_v.h, 1); s_v.n += 1
        sec("vector", fkvout)
        v_kvsb = s_v.n

        def fag(g):
            g.wait_ge(s_in.h, wcoop_done[0])
            g.collective_compute(
                "AllGather", ALU.bypass,
                replica_groups=[list(range(8))],
                ins=[wsh.ap().opt()],
                outs=[wsh_g.ap().opt()],
            ).then_inc(s_cc.h, 1); s_cc.n += 1
        sec("gpsimd", fag)
        cc_ag = s_cc.n

        def fcct(g):
            g.wait_ge(s_v.h, v_kvsb)
            g.dma_start(out=cc_in[:, :], in_=kvsb[:, :]).then_inc(s_g.h, 16); s_g.n += 16
            g.wait_ge(s_g.h, s_g.n)
            g.collective_compute(
                "AllReduce", ALU.add,
                replica_groups=[[0, 1, 2, 3], [4, 5, 6, 7]],
                ins=[cc_in.ap().opt()],
                outs=[cc_out.ap().opt()],
            ).then_inc(s_cc.h, 1); s_cc.n += 1
        sec("gpsimd", fcct)
        cc_kv = s_cc.n
        _gkv = [0]

        def emit_kv_readback():
            g0 = s_g.n

            def fccr(g, g0=g0):
                g.wait_ge(s_cc.h, cc_kv)
                g.wait_ge(s_g.h, g0)
                g.dma_start(out=kvsb[:, :], in_=cc_out[:, :]).then_inc(s_g.h, 16); s_g.n += 16
            sec("gpsimd", fccr)
            _gkv[0] = s_g.n
        g_kv = None

        wg0 = s_wg.n

        def fgwout(g):
            g.wait_ge(s_wg.h, wg0)
            g.wait_ge(s_t.h, t_kv)
            for c in range(KC):
                g.dma_start(out=bfv(O_R2 + 2048 * c, D),
                            in_=wout_d[128 * c: 128 * (c + 1), :]).then_inc(s_wg.h, 16); s_wg.n += 16
        sec("gpsimd", fgwout)
        wg_wout = s_wg.n



        sflush()
        XQ_FM = O_R1
        tr_evac.clear()
        def xq_fm_view(k, a, b):
            return bfv(XQ_FM + 2 * (XT * k + a), b - a)
        A_LO = O_R5 + 6272
        A_HI = O_R5 + 23616
        denr = f32v(O_DEN, 512, np_=16)
        recb = f32v(O_RCB, 512)
        v_bd_box = [0]
        c_ev = [0]

        def emit_c_prelude():
            def fbd(v):
                v.wait_ge(s_g.h, _gkv[0])
                v.wait_ge(s_v.h, v_consts)
                v.wait_ge(s_v.h, s_v.n)
                kv_all = f32_base[:, O_KV // 4: O_KV // 4 + 520]
                bd_all = bf_base[:, O_BD // 2: O_BD // 2 + KC * 128]
                zst_all = bf_base[:, O_ZST // 2: O_ZST // 2 + KC * 16]
                for c in range(KC):
                    for hh, po in ((2 * c, 0), (2 * c + 1, 64)):
                        base = 260 * (hh // 8) + 65 * ((hh // 2) % 4)
                        sp = 64 * (hh % 2)
                        v.tensor_copy(bd_all[po:po + 64, 128 * c + po: 128 * c + po + 64],
                                      kv_all[sp:sp + 64, base: base + 64]).then_inc(s_v.h, 1); s_v.n += 1
                        v.tensor_copy(zst_all[po:po + 64, 16 * c + hh: 16 * c + hh + 1],
                                      kv_all[sp:sp + 64, base + 64: base + 65]).then_inc(s_v.h, 1); s_v.n += 1
            sec("vector", fbd)
            v_bd_box[0] = s_v.n
            c_ev[0] = s_v.n

        def emit_c_group(gi, a, b):
            n = b - a
            v_bd = v_bd_box[0]
            cv0 = c_ev[0]

            def fden(t, a=a, b=b, n=n, cv0=cv0, v_bd=v_bd):
                t.wait_ge(s_v.h, max(v_bd, cv0))
                for c in range(KC):
                    mi = t.matmul(PS[2][0:16, 0:n], zst[:, 16 * c: 16 * (c + 1)],
                             q_bf_view(c, a, b), start=(c == 0), stop=(c == KC - 1))
                mi.then_inc(s_t.h, 1); s_t.n += 1
            sec("tensor", fden)
            t_den = s_t.n

            def fdr(v, n=n, t_den=t_den):
                v.wait_ge(s_t.h, t_den)
                v.wait_ge(s_v.h, s_v.n)
                v.tensor_scalar(denr[0:16, 0:n], PS[2][0:16, 0:n], 1e-6, None,
                                op0=ALU.add).then_inc(s_v.h, 1); s_v.n += 1
                v.wait_ge(s_v.h, s_v.n)
                v.reciprocal(denr[0:16, 0:n], denr[0:16, 0:n]).then_inc(s_v.h, 1); s_v.n += 1
            sec("vector", fdr)
            v_den = s_v.n

            for c in range(KC):
                nb = c % 2
                cv = c_ev[0]

                def fnum(t, c=c, a=a, b=b, n=n, nb=nb, v_den=v_den, cv=cv):
                    t.wait_ge(s_v.h, max(v_den, cv))
                    mi = t.matmul(PS[nb][:, 0:n], bd[:, 128 * c: 128 * (c + 1)],
                             q_bf_view(c, a, b), start=True, stop=True)
                    mi = t.matmul(PS[3][:, 0:n], sel[0:16, 128 * c: 128 * (c + 1)],
                             denr[0:16, 0:n], start=True, stop=True)
                    mi.then_inc(s_t.h, 1); s_t.n += 1
                sec("tensor", fnum)
                t_num = s_t.n

                def fnev(v, c=c, n=n, nb=nb, t_num=t_num):
                    v.wait_ge(s_t.h, t_num)
                    v.wait_ge(s_v.h, s_v.n)
                    v.tensor_copy(recb[:, 0:n], PS[3][:, 0:n]).then_inc(s_v.h, 1); s_v.n += 1
                    v.wait_ge(s_v.h, s_v.n)
                    ab = bfv(A_LO if c < 4 else A_HI, 4 * 512)
                    v.tensor_tensor(ab[:, 512 * (c % 4): 512 * (c % 4) + n], PS[nb][:, 0:n],
                                    recb[:, 0:n], op=ALU.mult).then_inc(s_v.h, 1); s_v.n += 1
                sec("vector", fnev)
                c_ev[0] = s_v.n
            v_attn = s_v.n

            for ti in range((n + 127) // 128):
                i = (a + 128 * ti) // 128 if a < TPC else NT
                np_ = tile_np(i)
                cv = c_ev[0]

                def fwo(t, ti=ti, np_=np_, v_attn=v_attn, cv=cv):
                    t.wait_ge(s_v.h, max(v_attn, cv, v_consts))
                    t.wait_ge(s_wg.h, wg_wout)
                    for half in range(2):
                        for c in range(KC):
                            off = (A_LO if c < 4 else A_HI) + 2 * (512 * (c % 4) + 128 * ti)
                            mi = t.matmul(PS[4 + half][0:np_, :],
                                     bfv(off, np_),
                                     bfv(O_R2 + 2048 * c + 1024 * half, 512),
                                     start=(c == 0), stop=False)
                        mi = t.matmul(PS[4 + half][0:np_, :], ones[0:1, 0:np_],
                                 boutb[0:1, 512 * half: 512 * (half + 1)],
                                 start=False, stop=True)
                    mi.then_inc(s_t.h, 1); s_t.n += 1
                sec("tensor", fwo)
                t_wo = s_t.n

                def fwe(v, i=i, np_=np_, t_wo=t_wo):
                    v.wait_ge(s_t.h, t_wo)
                    v.wait_ge(s_v.h, s_v.n)
                    dst = rtile(i)
                    for half in range(2):
                        v.tensor_tensor(dst[0:np_, 512 * half: 512 * (half + 1)],
                                        dst[0:np_, 512 * half: 512 * (half + 1)],
                                        PS[4 + half][0:np_, :], op=ALU.add).then_inc(s_v.h, 1); s_v.n += 1
                sec("vector", fwe)
                c_ev[0] = s_v.n

        XQ2 = O_R2 + 16384
        xsrc_of = {0: f32v(O_QA, D), 1: f32v(XQ2, D)}
        dst_of = {0: xnbf, 1: bfv(XQ2 + 13184, D)}
        vq = {0: s_v.n, 1: s_v.n}
        sq = {0: s_s.n, 1: s_s.n}
        tq = {0: 0, 1: 0}
        pend_tail = []
        for i in range(17):
            par = i % 2
            np_ = tile_np(i)
            in0 = s_in.n
            vguard = vq[par]

            def fxd(sy, i=i, np_=np_, vguard=vguard, in0=in0, par=par):
                sy.wait_ge(s_in.h, in0)
                sy.wait_ge(s_v.h, vguard)
                src = x_d[128 * i: 128 * (i + 1), :] if i < NT else xg_d[:, :]
                sy.dma_start(out=xsrc_of[par][0:np_, :],
                             in_=src).then_inc(s_in.h, 16); s_in.n += 16
            sec("sync", fxd)
            d_x = s_in.n

            def fx(v, i=i, np_=np_, d_x=d_x, par=par, vguard=vguard, sg_=sq[par]):
                v.wait_ge(s_in.h, d_x)
                v.wait_ge(s_v.h, vguard)
                v.wait_ge(s_s.h, sg_)
                xs = xsrc_of[par]
                v.tensor_scalar(xs[0:np_, :], xs[0:np_, :],
                                stats[0:np_, 2 * i: 2 * i + 1],
                                stats[0:np_, 2 * i + 1: 2 * i + 2],
                                op0=ALU.subtract, op1=ALU.mult).then_inc(s_v.h, 1); s_v.n += 1
            sec("vector", fx)
            tail = []
            s1_i = emit_quant(xsrc_of[par], dst_of[par], np_, D, 64, destructive=True,
                              geng=True, base=(XQ2 if par else None),
                              tmp_off=(XQ2 + 11136 if par else None),
                              tguard=tq[par], sguard=sq[par], split=tail)
            if i == 3:
                emit_kv_readback()
            if pend_tail:
                pend_tail.pop(0)()
            if i == 4:
                emit_c_prelude()
            elif i in (5, 7, 9, 11, 13):
                gi = (i - 5) // 2
                emit_c_group(gi, *TGROUPS[gi])

            def mk_tail(tail=tail, i=i, par=par, np_=np_, s1_i=s1_i):
                def run():
                    tail[0]()
                    emit_tm_to_fm(dst_of[par], XQ_FM, tile_colbase(i), np_, 6 + (i % 2))
                    vq[par] = s_v.n
                    sq[par] = s1_i
                    tq[par] = s_t.n
                return run
            pend_tail.append(mk_tail())
        pend_tail.pop(0)()
        vxq = s_v.n


        if STAGE < 5:
            early_finish()
            return nc

        sflush()


        hg_f = f32v(O_HG, 512)
        hf_f = f32v(O_HF, 512)
        hq_t = bfv(O_HQ, 512)
        hq_fm = bfv(O_HQF, 4 * 128)
        wT_ready = {}
        wT_free = {}
        b_free = [c_ev[0]]

        def wg_off(par): return O_R3 + par * 8192
        def wu_off(par): return O_R3 + 16384 + par * 8192
        def wd_off(par): return O_R2 + 16384 + par * 8192

        def emit_fld(si):
            par = si % 2
            n0, n1 = NSL[si]
            ns = n1 - n0
            in0 = s_in.n
            wfree = wT_free.get(par, s_t.n)

            def fld(sy, n0=n0, n1=n1, ns=ns, par=par, in0=in0, wfree=wfree):
                sy.wait_ge(s_in.h, in0)
                sy.wait_ge(s_cc.h, cc_ag)
                sy.wait_ge(s_t.h, wfree)
                sy.wait_ge(s_v.h, vxq)
                for c in range(KC):
                    sy.dma_start(out=bfv(wg_off(par) + 2 * 512 * c, ns),
                                 in_=bass.AP(wsh_g.ap().tensor, c * SHN + SH_GATE + n0,
                                             [[IF, 128], [1, ns]])).then_inc(s_in.h, 16); s_in.n += 16
                for c in range(KC):
                    sy.dma_start(out=bfv(wu_off(par) + 2 * 512 * c, ns),
                                 in_=bass.AP(wsh_g.ap().tensor, c * SHN + SH_UP + n0,
                                             [[IF, 128], [1, ns]])).then_inc(s_in.h, 16); s_in.n += 16
                for c in range(ns // 128):
                    srcap = bass.AP(wsh_g.ap().tensor, SH_DOWN + (n0 + 128 * c) * 128,
                                    [[128, 128], [SHN, 8], [1, 128]])
                    dstap = bfap(wd_off(par) + 2 * D * c, [[128, 8], [1, 128]])
                    sy.dma_start(out=dstap, in_=srcap).then_inc(s_in.h, 16); s_in.n += 16
            sec("sync", fld)
            wT_ready[si] = s_in.n

        items = [(si, p) for si in range(len(NSL)) for p in range(9)]
        hq_evac_s = {}
        bguard = [0, 0]
        t_gu_of = {}
        bhf = f32v(O_BHF, 1024)
        bhg = f32v(O_BHG, 1024)

        def emit_fgu(si, p):
            n0, n1 = NSL[si]
            ns = n1 - n0
            par = si % 2
            wr = wT_ready[si]
            gs, gv = bguard
            tiles = [2 * p, 2 * p + 1] if p < 8 else [16]

            def fgu(t, si=si, p=p, ns=ns, par=par, wr=wr, gs=gs, gv=gv, tiles=tiles):
                t.wait_ge(s_in.h, wr)
                if gs:
                    t.wait_ge(s_s.h, gs)
                if gv:
                    t.wait_ge(s_v.h, gv)
                for ti, i in enumerate(tiles):
                    np_ = tile_np(i)
                    colbase = tile_colbase(i)
                    bp = 2 * ti
                    for c in range(KC):
                        mi = t.matmul(PS[bp][0:np_, 0:ns],
                                      xq_fm_view(c, colbase, colbase + np_),
                                      bfv(wg_off(par) + 2 * 512 * c, ns),
                                      start=(c == 0), stop=(c == KC - 1))
                    for c in range(KC):
                        mi = t.matmul(PS[bp + 1][0:np_, 0:ns],
                                      xq_fm_view(c, colbase, colbase + np_),
                                      bfv(wu_off(par) + 2 * 512 * c, ns),
                                      start=(c == 0), stop=(c == KC - 1))
                mi.then_inc(s_t.h, 1); s_t.n += 1
            sec("tensor", fgu)
            t_gu_of[(si, p)] = s_t.n

        emit_fld(0)
        emit_fld(1)
        emit_fgu(0, 0)
        t_th_prev = [0]
        pend_fra = []
        for j, (si, p) in enumerate(items):
            par = si % 2
            n0, n1 = NSL[si]
            ns = n1 - n0
            tiles = [2 * p, 2 * p + 1] if p < 8 else [16]
            t_gu = t_gu_of[(si, p)]

            for ti, i in enumerate(tiles):
                np_ = tile_np(i)
                bp = 2 * ti
                vhg = s_v.n

                def fsl(s, np_=np_, ns=ns, bp=bp, ti=ti, t_gu=t_gu, vhg=vhg):
                    s.wait_ge(s_t.h, t_gu)
                    s.wait_ge(s_v.h, vhg)
                    s.activation(bhg[0:np_, 512 * ti: 512 * ti + ns], PS[bp][0:np_, 0:ns],
                                 AF.Silu).then_inc(s_s.h, 1)
                ssec(fsl, 1)
                s_si = s_s.n

                def fmu(v, np_=np_, ns=ns, bp=bp, ti=ti, s_si=s_si):
                    v.wait_ge(s_s.h, s_si)
                    v.wait_ge(s_v.h, s_v.n)
                    v.tensor_tensor(bhf[0:np_, 512 * ti: 512 * ti + ns],
                                    bhg[0:np_, 512 * ti: 512 * ti + ns],
                                    PS[bp + 1][0:np_, 0:ns], op=ALU.mult).then_inc(s_v.h, 1); s_v.n += 1
                sec("vector", fmu)
            bguard[0] = s_s.n
            bguard[1] = s_v.n

            if j + 1 < len(items):
                emit_fgu(*items[j + 1])

            npq = tile_np(tiles[0])
            Fq = ns * len(tiles)
            if len(tiles) == 2 and ns == 512:
                emit_quant(bhf, bfv(O_BHQ, 1024), 128, 1024, 64,
                           tguard=t_th_prev[0], destructive=True, geng=True)
                hq_views = [bfv(O_BHQ + 2 * 512 * ti, 512) for ti in range(2)]
            else:
                hq_views = []
                for ti, i in enumerate(tiles):
                    emit_quant(f32v(O_BHF + 4 * 512 * ti, ns), bfv(O_BHQ + 2 * 512 * ti, ns),
                               tile_np(i), ns, ns // 16, tguard=t_th_prev[0], destructive=True,
                               geng=True)
                    hq_views.append(bfv(O_BHQ + 2 * 512 * ti, ns))
            v_hq = s_v.n

            for fn in pend_fra:
                sec("vector", fn); s_v.n += 2
            del pend_fra[:]
            b_free[0] = s_v.n

            for ti, i in enumerate(tiles):
                np_ = tile_np(i)
                pb = 6 + (ti % 2)
                te_s = hq_evac_s.get(pb, 0)

                def fth(t, np_=np_, ns=ns, pb=pb, ti=ti, v_hq=v_hq, te_s=te_s):
                    t.wait_ge(s_v.h, v_hq)
                    if te_s:
                        t.wait_ge(s_s.h, te_s)
                    for c in range(ns // 128):
                        t.transpose(bass.AP(PB[pb].tensor, PB[pb].offset + 128 * c,
                                            [[PB[pb].ap[0][0], 128], [1, np_]]),
                                    bfv(O_BHQ + 2 * (512 * ti + 128 * c), 128, np_=np_),
                                    ident[0:np_, 0:np_]).then_inc(s_t.h, 1); s_t.n += 1
                sec("tensor", fth)
                t_th = s_t.n
                t_th_prev[0] = t_th

                def fhe(s, np_=np_, ns=ns, pb=pb, ti=ti, t_th=t_th):
                    s.wait_ge(s_t.h, t_th)
                    srcap = pbap(pb, [[128, ns // 128], [1, np_]])
                    dstap = bfap(O_HQF2 + 1024 * ti, [[128, ns // 128], [1, np_]])
                    s.copy(dstap, srcap).then_inc(s_s.h, 1)
                ssec(fhe, 1)
                hq_evac_s[pb] = s_s.n
                v_hf = s_s.n
                bfree = b_free[0]

                def fdn(t, np_=np_, ns=ns, par=par, ti=ti, v_hf=v_hf, bfree=bfree):
                    t.wait_ge(s_s.h, v_hf)
                    t.wait_ge(s_v.h, bfree)
                    for half in range(2):
                        for c in range(ns // 128):
                            mi = t.matmul(PS[4 + half][0:np_, :],
                                          bfv(O_HQF2 + 1024 * ti + 256 * c, np_),
                                          bfv(wd_off(par) + 2 * (D * c + 512 * half), 512),
                                          start=(c == 0), stop=(c == ns // 128 - 1))
                    mi.then_inc(s_t.h, 1); s_t.n += 1
                sec("tensor", fdn)
                t_dn = s_t.n

                def fra(v, i=i, np_=np_, t_dn=t_dn):
                    v.wait_ge(s_t.h, t_dn)
                    dst = rtile(i)
                    for half in range(2):
                        v.tensor_tensor(dst[0:np_, 512 * half: 512 * (half + 1)],
                                        dst[0:np_, 512 * half: 512 * (half + 1)],
                                        PS[4 + half][0:np_, :], op=ALU.add).then_inc(s_v.h, 1)
                pend_fra.append(fra)
                if ti == 0 and len(tiles) == 2:
                    for fn in pend_fra:
                        sec("vector", fn); s_v.n += 2
                    del pend_fra[:]
                    b_free[0] = s_v.n
            if p == 8:
                wT_free[par] = s_t.n
                if si + 2 < len(NSL):
                    emit_fld(si + 2)
        for fn in pend_fra:
            sec("vector", fn); s_v.n += 2
        del pend_fra[:]
        b_free[0] = s_v.n

        sflush()
        XN1_FM = O_R3
        tr_evac.clear()

        def xn1_view(k, a, b):
            return bfv(XN1_FM + 2 * (XT * k + a), b - a)

        emit_ln_stats()
        for i in range(17):
            emit_ln(i, save_stats=False)
            emit_tm_to_fm(xnbf, XN1_FM, tile_colbase(i), tile_np(i), 6 + (i % 2))
        v_xn1 = s_v.n

        O_W1A = O_R2
        O_W1B = O_R2 + 16384

        W1OFF = {"wq": SH_WQ, "wk": SH_WK, "wv": SH_WV, "wo": SH_WO}

        def emit_w1T(dst_off, src_s, key, guard_t):
            in0 = s_in.n

            def ftr(sy):
                sy.wait_ge(s_in.h, in0)
                sy.wait_ge(s_cc.h, cc_ag)
                sy.wait_ge(s_t.h, guard_t)
                for c in range(KC):
                    sy.dma_start(out=bfv(dst_off + 2 * D * c, D),
                                 in_=bass.AP(wsh_g.ap().tensor, c * SHN + W1OFF[key],
                                             [[1024, 128], [1, D]])).then_inc(s_in.h, 16); s_in.n += 16
            sec("sync", ftr)
            return s_in.n

        O_QH = O_R1
        O_KH = O_R1 + 16384
        probs = bfv(O_PROBS, 192)
        pT = bfv(O_PT, 256)
        av_f = bfv(O_AVF, KC * 128)
        zbv = f32v(O_ZB, 1)
        zrv = f32v(O_ZR, 1)
        l1b = {}
        l1_ev = [b_free[0]]
        probs_free = [0]

        for half in range(2):
            hb = TPC // 2 * half
            HG2 = [(hb, hb + 512), (hb + 512, hb + 1024)]
            tr_wq = emit_w1T(O_W1A, None, "wq", s_t.n)

            for gi, (a, b) in enumerate(HG2):
                for c in range(KC):
                    bank = (gi * KC + c) % 4
                    bg = l1b.get(("b", bank), 0)
                    lev = l1_ev[0]

                    def fq(t, c=c, a=a, b=b, bank=bank, bg=bg, lev=lev, tr_wq=tr_wq):
                        t.wait_ge(s_in.h, tr_wq)
                        t.wait_ge(s_v.h, max(lev, v_xn1))
                        if bg:
                            t.wait_ge(s_s.h, bg)
                        for k in range(KC):
                            mi = t.matmul(PS[bank][:, :],
                                     bfv(O_W1A + 2 * (D * k + 128 * c), 128),
                                     xn1_view(k, a, b),
                                     start=(k == 0), stop=(k == KC - 1))
                        mi.then_inc(s_t.h, 1); s_t.n += 1
                    sec("tensor", fq)
                    t_q = s_t.n

                    def fqe(s, c=c, gi=gi, bank=bank, t_q=t_q):
                        s.wait_ge(s_t.h, t_q)
                        s.copy(bfv(O_QH + 2 * (D * c + 512 * gi), 512)[:, :],
                               PS[bank][:, :]).then_inc(s_s.h, 1)
                    ssec(fqe, 1)
                    l1b[("b", bank)] = s_s.n

            tr_wk = emit_w1T(O_W1B, None, "wk", s_t.n)
            kg2 = list(HG2) + ([(TPC, XT)] if half == 0 else [])
            for gi, (a, b) in enumerate(kg2):
                n = b - a
                for c in range(KC):
                    bank = (gi * KC + c) % 4
                    bg = l1b.get(("b", bank), 0)

                    lev = l1_ev[0]

                    def fk(t, c=c, a=a, b=b, n=n, bank=bank, bg=bg, tr_wk=tr_wk, lev=lev):
                        t.wait_ge(s_in.h, tr_wk)
                        if lev:
                            t.wait_ge(s_v.h, lev)
                        if bg:
                            t.wait_ge(s_s.h, bg)
                        for k in range(KC):
                            mi = t.matmul(PS[bank][:, 0:n],
                                     bfv(O_W1B + 2 * (D * k + 128 * c), 128),
                                     xn1_view(k, a, b),
                                     start=(k == 0), stop=(k == KC - 1))
                        mi.then_inc(s_t.h, 1); s_t.n += 1
                    sec("tensor", fk)
                    t_k = s_t.n

                    def fke(s, c=c, gi=gi, a=a, n=n, bank=bank, t_k=t_k):
                        s.wait_ge(s_t.h, t_k)
                        if a < TPC:
                            dst = bfv(O_KH + 2 * (D * c + 512 * gi), n)
                        else:
                            dst = bfv(O_KG + 2 * (G * c), n)
                        s.copy(dst[:, :], PS[bank][0:128, 0:n]).then_inc(s_s.h, 1)
                    ssec(fke, 1)
                    l1b[("b", bank)] = s_s.n

            tr_wv = emit_w1T(O_W1A, None, "wv", s_t.n)
            vtl = list(range(8)) + ([16] if half == 0 else [])
            for vi in vtl:
                np_ = 128 if vi < 16 else G
                colbase = hb + 128 * vi if vi < 16 else TPC
                bank2 = 2 * ((vi % 2))
                bg = max(l1b.get(("b", bank2), 0), l1b.get(("b", bank2 + 1), 0))

                lev = l1_ev[0]

                def fv3(t, vi=vi, np_=np_, colbase=colbase, bank2=bank2, bg=bg, tr_wv=tr_wv, lev=lev):
                    t.wait_ge(s_in.h, tr_wv)
                    if lev:
                        t.wait_ge(s_v.h, lev)
                    if bg:
                        t.wait_ge(s_s.h, bg)
                    for hf in range(2):
                        for k in range(KC):
                            mi = t.matmul(PS[bank2 + hf][0:np_, :],
                                     xn1_view(k, colbase, colbase + np_),
                                     bfv(O_W1A + 2 * (D * k + 512 * hf), 512),
                                     start=(k == 0), stop=(k == KC - 1))
                    mi.then_inc(s_t.h, 1); s_t.n += 1
                sec("tensor", fv3)
                t_v3 = s_t.n

                def fve(s, vi=vi, np_=np_, bank2=bank2, t_v3=t_v3):
                    s.wait_ge(s_t.h, t_v3)
                    for hf in range(2):
                        if vi < 16:
                            dst = bfv(O_VH + 2 * (D * vi + 512 * hf), 512)
                        else:
                            dst = bfv(O_VG + 2 * 512 * hf, 512, np_=G)
                        s.copy(dst[0:np_, :], PS[bank2 + hf][0:np_, :]).then_inc(s_s.h, 1)
                ssec(fve, 2)
                l1b[("b", bank2)] = l1b[("b", bank2 + 1)] = s_s.n
            s_qkv = s_s.n

            tr_wo = emit_w1T(O_W1B, None, "wo", s_t.n)

            wseq = [(w, h) for w in range(8) for h in range(H)]
            t_sc_of = {}
            probs_par_free = {}

            def emit_fsc(j):
                w, h = wseq[j]
                scb = j % 2
                bg = l1b.get(("sc", scb), 0)

                def fsc(t, w=w, h=h, scb=scb, bg=bg):
                    t.wait_ge(s_s.h, max(s_qkv, bg))
                    po = 64 * (h % 2)
                    qb_ = O_QH // 2 + D * (h // 2) + 128 * w
                    kb_ = O_KH // 2 + D * (h // 2) + 128 * w
                    gb_ = O_KG // 2 + G * (h // 2)
                    qv = bf_base[po:po + 64, qb_:qb_ + 128]
                    kvw = bf_base[po:po + 64, kb_:kb_ + 128]
                    kgv = bf_base[po:po + 64, gb_:gb_ + G]
                    mi = t.matmul(PS[scb][:, 0:G], qv, kgv, start=True, stop=True)
                    mi = t.matmul(PS[scb][:, G:G + 128], qv, kvw, start=True, stop=True)
                    mi.then_inc(s_t.h, 1); s_t.n += 1
                sec("tensor", fsc)
                t_sc_of[j] = s_t.n

            emit_fsc(0)
            for j, (w, h) in enumerate(wseq):
                par = j % 2
                scb = par
                probsv = bfv(O_PROBS + 384 * par, 192)
                pTv = bfv(O_PT + 512 * par, 256)
                zbvp = f32v(O_ZB + 4 * par, 1)
                zrvp = f32v(O_ZR + 4 * par, 1)
                t_sc = t_sc_of[j]
                pf = probs_par_free.get(par, 0)

                def fex(s, scb=scb, t_sc=t_sc, pf=pf, probsv=probsv, zbvp=zbvp):
                    s.wait_ge(s_t.h, t_sc)
                    if pf:
                        s.wait_ge(s_v.h, pf)
                    s.activation(probsv[:, :], PS[scb][:, 0:192], AF.Exp,
                                 bias=cbc[:, 0:1], scale=0.125,
                                 accum_out=zbvp[:, 0:1]).then_inc(s_s.h, 1)
                ssec(fex, 1)
                l1b[("sc", scb)] = s_s.n
                l1b[("b", scb)] = s_s.n
                s_ex = s_s.n

                def fzr(v, s_ex=s_ex, probsv=probsv, zbvp=zbvp, zrvp=zrvp):
                    v.wait_ge(s_s.h, s_ex)
                    v.reciprocal(zrvp[:, 0:1], zbvp[:, 0:1]).then_inc(s_v.h, 1); s_v.n += 1
                    v.wait_ge(s_v.h, s_v.n)
                    v.tensor_scalar(probsv[:, :], probsv[:, :], zrvp[:, 0:1], None,
                                    op0=ALU.mult).then_inc(s_v.h, 1); s_v.n += 1
                sec("vector", fzr)
                v_pn = s_v.n

                if j + 1 < len(wseq):
                    emit_fsc(j + 1)

                ptb = 6 + par
                bg2 = l1b.get(("pt", ptb), 0)

                def fpt(t, ptb=ptb, v_pn=v_pn, bg2=bg2, probsv=probsv):
                    t.wait_ge(s_v.h, max(v_pn, bg2))
                    t.transpose(bass.AP(PB[ptb].tensor, PB[ptb].offset,
                                        [[PB[ptb].ap[0][0], 128], [1, 128]]),
                                probsv[:, G:192], ident[:, :]).then_inc(s_t.h, 1); s_t.n += 1
                    t.transpose(bass.AP(PB[ptb].tensor, PB[ptb].offset + 128,
                                        [[PB[ptb].ap[0][0], G], [1, 128]]),
                                probsv[:, 0:G], ident[:, :]).then_inc(s_t.h, 1); s_t.n += 1
                sec("tensor", fpt)
                t_pt = s_t.n

                def fpe(v, ptb=ptb, t_pt=t_pt, pTv=pTv):
                    v.wait_ge(s_t.h, t_pt)
                    v.tensor_copy(pTv[:, :], PB[ptb][:, 0:256]).then_inc(s_v.h, 1); s_v.n += 1
                sec("vector", fpe)
                l1b[("pt", ptb)] = s_v.n
                v_pt = s_v.n
                probs_par_free[par] = v_pt

                avg = l1b.get(("av",), 0)

                def fav(t, w=w, h=h, v_pt=v_pt, avg=avg, pTv=pTv):
                    t.wait_ge(s_v.h, v_pt)
                    if avg and h % 2 == 0:
                        t.wait_ge(s_v.h, avg)
                    po = 64 * (h % 2)
                    vv = bfv(O_VH + 2 * (D * w + 64 * h), 64)
                    vgv = bfv(O_VG + 2 * 64 * h, 64, np_=G)
                    outap = PS[4][po:po + 64, 0:128]
                    mi = t.matmul(outap, vv, pTv[:, 0:128], start=True, stop=False)
                    mi = t.matmul(outap, vgv[0:G, :], pTv[0:G, 128:256], start=False, stop=True)
                    mi.then_inc(s_t.h, 1); s_t.n += 1
                sec("tensor", fav)
                t_av = s_t.n

                if h % 2 == 1:
                    def fae(v, h=h, t_av=t_av):
                        v.wait_ge(s_t.h, t_av)
                        v.tensor_copy(av_f[:, 128 * (h // 2): 128 * (h // 2) + 128],
                                      PS[4][:, 0:128]).then_inc(s_v.h, 1); s_v.n += 1
                    sec("vector", fae)
                    l1b[("av",)] = s_v.n

                if h == H - 1:
                    s_av = s_v.n
                    lev = l1_ev[0]

                    def fw1(t, w=w, hb=hb, s_av=s_av, lev=lev, tr_wo=tr_wo):
                        t.wait_ge(s_v.h, max(s_av, lev))
                        t.wait_ge(s_in.h, tr_wo)
                        for hf in range(2):
                            for c in range(KC):
                                mi = t.matmul(PS[2 + hf][:, :],
                                              av_f[:, 128 * c: 128 * (c + 1)],
                                              bfv(O_W1B + 2 * (D * c + 512 * hf), 512),
                                              start=(c == 0), stop=(c == KC - 1))
                        mi.then_inc(s_t.h, 1); s_t.n += 1
                    sec("tensor", fw1)
                    t_w1 = s_t.n

                    def fwe1(v, w=w, hb=hb, t_w1=t_w1):
                        v.wait_ge(s_t.h, t_w1)
                        dst = rtile((hb + 128 * w) // 128)
                        for hf in range(2):
                            v.tensor_tensor(dst[:, 512 * hf: 512 * (hf + 1)],
                                            dst[:, 512 * hf: 512 * (hf + 1)],
                                            PS[2 + hf][:, :], op=ALU.add).then_inc(s_v.h, 1); s_v.n += 1
                    sec("vector", fwe1)
                    l1_ev[0] = s_v.n

        if STAGE < 7:
            early_finish()
            return nc

        sflush()
        O_YA = O_R1
        O_FCT = O_R2
        O_PJT = O_R2 + 16384
        h2v = bfv(O_H2, 4 * D)
        spa = bfv(O_SPA, D)
        l1f = {}
        h2_free = [l1_ev[0]]
        ya_guard = [0]

        for half in range(2):
            hb = TPC // 2 * half
            for si, (f0, f1) in enumerate(FSL):
                par = si % 2
                O_FT = O_FCT + par * 8192
                O_PT2 = O_PJT + par * 8192
                guard_t = l1f.get(("trg", par), s_t.n)

                in0 = s_in.n

                def ffw(sy, f0=f0, f1=f1, O_FT=O_FT, O_PT2=O_PT2, guard_t=guard_t, in0=in0):
                    sy.wait_ge(s_in.h, in0)
                    sy.wait_ge(s_cc.h, cc_ag)
                    if guard_t:
                        sy.wait_ge(s_t.h, guard_t)
                    for c in range(KC):
                        sy.dma_start(out=bfv(O_FT + 2 * 512 * c, 512),
                                     in_=bass.AP(wsh_g.ap().tensor, c * SHN + SH_FCT + f0,
                                                 [[FF, 128], [1, 512]])).then_inc(s_in.h, 16); s_in.n += 16
                    for c in range(4):
                        rank = f0 // 512
                        row = (f0 % 512) + 128 * c
                        sy.dma_start(out=bfv(O_PT2 + 2 * D * c, D),
                                     in_=bass.AP(wsh_g.ap().tensor, rank * SHN + SH_PJT + row * 1024,
                                                 [[1024, 128], [1, D]])).then_inc(s_in.h, 16); s_in.n += 16
                sec("sync", ffw)
                tr_f = s_in.n

                for fc_ in range(4):
                    for gi in range(2):
                        t0 = hb + 512 * gi
                        bank = (fc_ * 2 + gi) % 4
                        bg = l1f.get(("fc", bank), 0)
                        h2f = h2_free[0]

                        def ffc(t, fc_=fc_, t0=t0, bank=bank, tr_f=tr_f, bg=bg, h2f=h2f, O_FT=O_FT):
                            t.wait_ge(s_in.h, tr_f)
                            t.wait_ge(s_v.h, max(bg, h2f))
                            for k in range(KC):
                                mi = t.matmul(PS[bank][:, :],
                                         bfv(O_FT + 2 * (512 * k + 128 * fc_), 128),
                                         xn1_view(k, t0, t0 + 512),
                                         start=(k == 0), stop=(k == KC - 1))
                            mi.then_inc(s_t.h, 1); s_t.n += 1
                        sec("tensor", ffc)
                        t_fc = s_t.n

                        vhg = s_v.n

                        def frl(s, bank=bank, t_fc=t_fc, vhg=vhg):
                            s.wait_ge(s_t.h, t_fc)
                            s.wait_ge(s_v.h, vhg)
                            s.activation(hg_f[:, :], PS[bank][:, :], AF.Relu).then_inc(s_s.h, 1)
                        ssec(frl, 1)
                        s_rl = s_s.n

                        def fsq(v, fc_=fc_, gi=gi, s_rl=s_rl):
                            v.wait_ge(s_s.h, s_rl)
                            dst = bfv(O_H2 + 2 * (D * fc_ + 512 * gi), 512)
                            v.tensor_tensor(dst[:, :], hg_f[:, :], hg_f[:, :],
                                            op=ALU.mult).then_inc(s_v.h, 1); s_v.n += 1
                        sec("vector", fsq)
                        l1f[("fc", bank)] = s_v.n
                v_h2 = s_v.n
                l1f[("trg", par)] = s_t.n

                for ti in range(8):
                    pb2 = 4 + 2 * (ti % 2)
                    bg = l1f.get(("pj", pb2), 0)

                    def fpj(t, ti=ti, pb2=pb2, v_h2=v_h2, bg=bg, O_PT2=O_PT2):
                        t.wait_ge(s_v.h, max(v_h2, bg))
                        for hf in range(2):
                            for c in range(4):
                                mi = t.matmul(PS[pb2 + hf][:, :],
                                         bfv(O_H2 + 2 * (D * c + 128 * ti), 128),
                                         bfv(O_PT2 + 2 * (D * c + 512 * hf), 512),
                                         start=(c == 0), stop=(c == 3))
                        mi.then_inc(s_t.h, 1); s_t.n += 1
                    sec("tensor", fpj)
                    t_pj = s_t.n
                    l1f[("trg", par)] = s_t.n

                    def fya(v, ti=ti, si=si, pb2=pb2, t_pj=t_pj):
                        v.wait_ge(s_t.h, t_pj)
                        for hf in range(2):
                            dst = f32v(O_YA + 4 * (D * ti + 512 * hf), 512)
                            if si == 0:
                                v.tensor_copy(dst[:, :], PS[pb2 + hf][:, :]).then_inc(s_v.h, 1); s_v.n += 1
                            else:
                                v.tensor_tensor(dst[:, :], dst[:, :], PS[pb2 + hf][:, :],
                                                op=ALU.add).then_inc(s_v.h, 1); s_v.n += 1
                    sec("vector", fya)
                    l1f[("pj", pb2)] = s_v.n
                h2_free[0] = s_v.n

            for ti in range(8):
                i = (hb + 128 * ti) // 128

                def fsp(v, ti=ti, i=i):
                    v.wait_ge(s_v.h, s_v.n)
                    y = f32v(O_YA + 4 * D * ti, D)
                    v.tensor_scalar(spa[:, :], y[:, :], SPIKE_B[0], None,
                                    op0=ALU.is_gt).then_inc(s_v.h, 1); s_v.n += 1
                    for bth in SPIKE_B[1:]:
                        v.wait_ge(s_v.h, s_v.n)
                        v.scalar_tensor_tensor(spa[:, :], y[:, :], bth, spa[:, :],
                                               op0=ALU.is_gt, op1=ALU.add).then_inc(s_v.h, 1); s_v.n += 1
                    v.wait_ge(s_v.h, s_v.n)
                    dst = rtile(i)
                    v.scalar_tensor_tensor(dst[:, :], spa[:, :], 0.2, dst[:, :],
                                           op0=ALU.mult, op1=ALU.add).then_inc(s_v.h, 1); s_v.n += 1
                sec("vector", fsp)
                v_done = s_v.n

                st0 = s_st.n

                def fst(sy, i=i, v_done=v_done, st0=st0):
                    sy.wait_ge(s_st.h, st0)
                    sy.wait_ge(s_v.h, v_done)
                    sy.dma_start(out=out_d[128 * i: 128 * (i + 1), :],
                                 in_=f32v(O_R + 4096 * i, D)).then_inc(s_st.h, 16); s_st.n += 16
                sec("sync", fst)

        sflush()

        def ffin(sy):
            sy.wait_ge(s_st.h, s_st.n)
        sec("sync", ffin)

    return nc


class Sem:
    def __init__(self, h):
        self.h = h
        self.n = 0


_NC_CACHE = {}


def kernel(L0_attn_x, L0_attn_w_qkv, L0_attn_w_out, L0_attn_b_out,
           L0_ffn_w_gate, L0_ffn_w_up, L0_ffn_w_down,
           L1_attn_Wq, L1_attn_Wk, L1_attn_Wv, L1_attn_Wo,
           L1_ffn_W_fc, L1_ffn_W_proj):
    x = np.ascontiguousarray(np.asarray(L0_attn_x, dtype=np.float32))
    B, S, _ = x.shape
    CPS = 4
    ident = np.eye(128, dtype=np.float32)
    sel = np.zeros((16, KC * 128), dtype=np.float32)
    for c in range(KC):
        sel[2 * c, 128 * c: 128 * c + 64] = 1.0
        sel[2 * c + 1, 128 * c + 64: 128 * c + 128] = 1.0
    w = {
        "wqkv": np.ascontiguousarray(L0_attn_w_qkv, dtype=np.float32),
        "wout": np.ascontiguousarray(L0_attn_w_out, dtype=np.float32),
        "bout": np.ascontiguousarray(np.asarray(L0_attn_b_out, dtype=np.float32).reshape(1, D)),
        "wgate": np.ascontiguousarray(L0_ffn_w_gate, dtype=np.float32),
        "wup": np.ascontiguousarray(L0_ffn_w_up, dtype=np.float32),
        "wdown": np.ascontiguousarray(L0_ffn_w_down, dtype=np.float32),
        "wq": np.ascontiguousarray(L1_attn_Wq, dtype=np.float32),
        "wk": np.ascontiguousarray(L1_attn_Wk, dtype=np.float32),
        "wv": np.ascontiguousarray(L1_attn_Wv, dtype=np.float32),
        "wo": np.ascontiguousarray(L1_attn_Wo, dtype=np.float32),
        "wfc": np.ascontiguousarray(L1_ffn_W_fc, dtype=np.float32),
        "wproj": np.ascontiguousarray(L1_ffn_W_proj, dtype=np.float32),
        "ident": ident, "sel": sel,
    }
    if "nc" not in _NC_CACHE:
        _NC_CACHE["nc"] = build_program()
    nc = _NC_CACHE["nc"]
    in_maps = []
    for core in range(8):
        b = core // CPS
        t0 = (core % CPS) * TPC
        m = dict(w)
        m["x"] = np.ascontiguousarray(x[b, t0:t0 + TPC, :])
        m["xg"] = np.ascontiguousarray(x[b, 0:G, :])
        m["wg_sh"] = np.ascontiguousarray(w["wgate"][:, 128 * core: 128 * (core + 1)])
        m["wu_sh"] = np.ascontiguousarray(w["wup"][:, 128 * core: 128 * (core + 1)])
        m["wd_sh"] = np.ascontiguousarray(w["wdown"][128 * core: 128 * (core + 1), :])
        m["wfc_sh"] = np.ascontiguousarray(w["wfc"][:, 128 * core: 128 * (core + 1)])
        m["wpj_sh"] = np.ascontiguousarray(w["wproj"][:, 512 * core: 512 * (core + 1)])
        for k in ("wq", "wk", "wv", "wo"):
            m[k + "_sh"] = np.ascontiguousarray(w[k][:, 128 * core: 128 * (core + 1)])
        in_maps.append(m)
    res = run_bass_kernel_spmd(nc, in_maps, core_ids=list(range(8)))
    out = np.empty((B, S, D), dtype=np.float32)
    for core in range(8):
        b = core // CPS
        t0 = (core % CPS) * TPC
        out[b, t0:t0 + TPC, :] = res.results[core]["out"]
    return out


# version 18
# speedup vs baseline: 1.6036x; 1.0092x over previous
import contextlib
import numpy as np
import concourse.bass as bass
import concourse.mybir as mybir
from concourse.bass_utils import run_bass_kernel_spmd

F32 = mybir.dt.float32
BF16 = mybir.dt.bfloat16
U8 = mybir.dt.uint8
ALU = mybir.AluOpType
AF = mybir.ActivationFunctionType
AXX = mybir.AxisListType.X

D = 1024
KC = 8
H = 16
TPC = 2048
NT = 16
G = 64
IF = 2816
FF = 4096
EPS = 1e-5
XT = TPC + G

FP4_TH_A = (0.25, 0.75, 1.25, 1.75)
FP4_TH_B = (2.5, 3.5, 5.0, 5.0)
SPIKE_B = (0.2441942809699397, 0.46396913384288535, 0.6861859295255305,
           0.9059607823984762, 1.083978413225562)
SOFTMAX_C = 24.0

NSL = [(0, 512), (512, 1024), (1024, 1536), (1536, 2048), (2048, 2560), (2560, 2816)]
FSL = [(s, s + 512) for s in range(0, FF, 512)]
TGROUPS = [(0, 512), (512, 1024), (1024, 1536), (1536, 2048), (2048, 2112)]

STAGE = 7
SH_GATE = 0
SH_UP = 360448
SH_DOWN = 720896
SH_FCT = 1081344
SH_PJT = 1605632
SH_WQ = 2129920
SH_WK = 2260992
SH_WV = 2392064
SH_WO = 2523136
SHN = 2654208

SBUF_CAP = 212800

O_R = 0
O_RG = 65536
O_IDENT = 69632
O_ONES = 69888
O_SEL = 70144
O_BOUT = 74240
O_KV = 76288
O_STATS = 78400
O_CB = 78592
O_BD = 78656
O_ZST = 80704
O_KG = 80960
O_VG = 81984
PERS_END = 84032
O_R1 = 84032
O_R2 = 117824
O_R3 = 150592
O_R5 = 184384
R5_END = O_R5 + 28416
assert R5_END <= SBUF_CAP, R5_END

O_E1 = O_R5
O_E2 = O_R5 + 2048
O_XNF = O_R5
O_BNST = O_R5 + 4096
O_MV = O_R5 + 4160
O_XNBF = O_R5 + 4224
O_QT = O_R5 + 6272
O_QA = O_R5 + 10368
O_QM = O_R5 + 14464
O_QS = O_R5 + 14720
O_QR = O_R5 + 14976
O_QSB = O_R5 + 15232
O_QCA = O_R5 + 15424
O_QCB = O_R5 + 17472
O_QSG = O_R5 + 19520
O_KBF = O_R5 + 6272
O_VEX = O_R5 + 8320
O_WQQ = O_R5 + 10432
O_ATT = O_R5 + 10432
O_DEN = O_R5
O_RCB = O_R5 + 2048
O_HG = O_R5 + 21568
O_HF = O_R5 + 23616
O_HQ = O_R5 + 25664
O_HQF = O_R5 + 26688
O_PROBS = O_R5 + 6272
O_PT = O_R5 + 7040
O_AVF = O_R5 + 8064
O_ZB = O_R5 + 10112
O_ZR = O_R5 + 10176
O_VH = O_R5 + 10432
O_H2 = O_R5 + 10432
O_SPA = O_R5 + 18688
O_BHF = O_R5 + 6272
O_BHG = O_R5 + 10368
O_QTMP = O_R5 + 21568
O_BHQ = O_R5 + 23616
O_HQF2 = O_R5 + 25664



def build_program():
    nc = bass.Bass("TRN2", target_bir_lowering=False)

    x_d = nc.dram_tensor("x", [TPC, D], F32, kind="ExternalInput")
    xg_d = nc.dram_tensor("xg", [G, D], F32, kind="ExternalInput")
    wqkv_d = nc.dram_tensor("wqkv", [D, 3 * D], F32, kind="ExternalInput")
    wout_d = nc.dram_tensor("wout", [D, D], F32, kind="ExternalInput")
    bout_d = nc.dram_tensor("bout", [1, D], F32, kind="ExternalInput")
    wgate_d = nc.dram_tensor("wgate", [IF, D], F32, kind="ExternalInput")
    wup_d = nc.dram_tensor("wup", [IF, D], F32, kind="ExternalInput")
    wdown_d = nc.dram_tensor("wdown", [D, IF], F32, kind="ExternalInput")
    wq_d = nc.dram_tensor("wq", [D, D], F32, kind="ExternalInput")
    wk_d = nc.dram_tensor("wk", [D, D], F32, kind="ExternalInput")
    wv_d = nc.dram_tensor("wv", [D, D], F32, kind="ExternalInput")
    wo_d = nc.dram_tensor("wo", [D, D], F32, kind="ExternalInput")
    wfc_d = nc.dram_tensor("wfc", [FF, D], F32, kind="ExternalInput")
    wproj_d = nc.dram_tensor("wproj", [D, FF], F32, kind="ExternalInput")
    ident_d = nc.dram_tensor("ident", [128, 128], F32, kind="ExternalInput")
    sel_d = nc.dram_tensor("sel", [16, KC * 128], F32, kind="ExternalInput")
    out_d = nc.dram_tensor("out", [TPC, D], F32, kind="ExternalOutput")

    wg_sh = nc.dram_tensor("wg_sh", [IF, 128], F32, kind="ExternalInput")
    wu_sh = nc.dram_tensor("wu_sh", [IF, 128], F32, kind="ExternalInput")
    wd_sh = nc.dram_tensor("wd_sh", [128, IF], F32, kind="ExternalInput")
    wfc_sh = nc.dram_tensor("wfc_sh", [FF, 128], F32, kind="ExternalInput")
    wpj_sh = nc.dram_tensor("wpj_sh", [D, 512], F32, kind="ExternalInput")
    w1_sh = {k: nc.dram_tensor(k + "_sh", [D, 128], F32, kind="ExternalInput")
             for k in ("wq", "wk", "wv", "wo")}

    cc_in = nc.dram_tensor("cc_in", [128, 520], F32)
    cc_out = nc.dram_tensor("cc_out", [128, 520], F32)
    wfc_shs = nc.dram_tensor("wfc_shs", [FF, 128], BF16)
    wpj_shs = nc.dram_tensor("wpj_shs", [D, 512], BF16)
    w1_shs = {k: nc.dram_tensor(k + "_shs", [D, 128], BF16) for k in ("wq", "wk", "wv", "wo")}
    wsh = nc.dram_tensor("wsh", [SHN], BF16)
    wsh_g = nc.dram_tensor("wsh_g", [8 * SHN], BF16)

    with contextlib.ExitStack() as es:
        E = es.enter_context
        sb = E(nc.sbuf_tensor("arena", [128, SBUF_CAP], U8))
        s_in = Sem(E(nc.semaphore("s_in")))
        s_tr = Sem(E(nc.semaphore("s_tr")))
        s_st = Sem(E(nc.semaphore("s_st")))
        s_wg = Sem(E(nc.semaphore("s_wg")))
        s_g = Sem(E(nc.semaphore("s_g")))
        s_cc = Sem(E(nc.semaphore("s_cc")))
        s_t = Sem(E(nc.semaphore("s_t")))
        s_v = Sem(E(nc.semaphore("s_v")))
        s_s = Sem(E(nc.semaphore("s_s")))
        s_gc = Sem(E(nc.semaphore("s_gc")))
        PS = [E(nc.psum_tensor(f"ps{_i}", [128, 512], F32)) for _i in range(8)]
        block = E(nc.Block())

        bf_base = sb.ap().bitcast(BF16)
        f32_base = sb.ap().bitcast(F32)
        PB = [p.ap().bitcast(BF16) for p in PS]

        def bfv(off, cols, np_=128):
            assert off % 2 == 0
            return bass.AP(bf_base.tensor, bf_base.offset + off // 2,
                           [[bf_base.ap[0][0], np_], [1, cols]])

        def f32v(off, cols, np_=128):
            assert off % 4 == 0
            return bass.AP(f32_base.tensor, f32_base.offset + off // 4,
                           [[f32_base.ap[0][0], np_], [1, cols]])

        def bfap(off, dims, np_=128):
            assert off % 2 == 0
            return bass.AP(bf_base.tensor, bf_base.offset + off // 2,
                           [[bf_base.ap[0][0], np_]] + [list(d) for d in dims])

        def f32ap(off, dims, np_=128):
            assert off % 4 == 0
            return bass.AP(f32_base.tensor, f32_base.offset + off // 4,
                           [[f32_base.ap[0][0], np_]] + [list(d) for d in dims])

        def psap(i, dims, np_=128):
            a = PS[i].ap()
            return bass.AP(a.tensor, a.offset,
                           [[a.ap[0][0], np_]] + [list(d) for d in dims])

        def pbap(i, dims, np_=128):
            a = PB[i]
            return bass.AP(a.tensor, a.offset, [[a.ap[0][0], np_]] + [list(d) for d in dims])

        def sec(name, fn):
            getattr(block, name)(fn)

        _sbuf = []

        def ssec(fn, incs):
            s_s.n += incs
            _sbuf.append(fn)

        def sflush():
            if not _sbuf:
                return
            fns = list(_sbuf)
            del _sbuf[:]

            def _run(s):
                for f in fns:
                    f(s)
            sec("scalar", _run)

        ident = bfv(O_IDENT, 128)
        ones = bfv(O_ONES, 128)
        sel = f32v(O_SEL, KC * 128)
        boutb = bfv(O_BOUT, D)
        kvsb = f32v(O_KV, 520)
        stats = f32v(O_STATS, 34)
        cbc = f32v(O_CB, 16)
        bd = bfv(O_BD, KC * 128)
        zst = bfv(O_ZST, KC * 16)

        def tile_np(i):
            return 128 if i < NT else G

        def rtile(i):
            if i < NT:
                return f32v(O_R + 4096 * i, D)
            return f32v(O_RG, D, np_=G)

        def tile_colbase(i):
            return 128 * i if i < NT else TPC

        def early_finish():
            sflush()
            vv = s_v.n

            def fstall(sy):
                sy.wait_ge(s_v.h, vv)
                sy.wait_ge(s_s.h, s_s.n)
                sy.wait_ge(s_t.h, s_t.n)
                if STAGE >= 4:
                    sy.wait_ge(s_g.h, s_g.n)
                for i in range(NT):
                    sy.dma_start(out=out_d[128 * i: 128 * (i + 1), :],
                                 in_=f32v(O_R + 4096 * i, D)).then_inc(s_st.h, 16); s_st.n += 16
                sy.wait_ge(s_st.h, s_st.n)
            sec("sync", fstall)

        def _loads(sy):
            for i in range(NT):
                sy.dma_start(out=f32v(O_R + 4096 * i, D), in_=x_d[128 * i: 128 * (i + 1), :])\
                  .then_inc(s_in.h, 16); s_in.n += 16
            sy.dma_start(out=f32v(O_RG, D, np_=G), in_=xg_d[:, :]).then_inc(s_in.h, 16); s_in.n += 16
            sy.dma_start(out=bass.AP(f32_base.tensor, f32_base.offset + O_SEL // 4,
                                     [[f32_base.ap[0][0], 16], [1, KC * 128]]),
                         in_=sel_d[:, :]).then_inc(s_in.h, 16); s_in.n += 16
        sec("sync", _loads)
        in_x = s_in.n

        wqkv_q = bfv(O_WQQ, KC * D)
        wqkv_kv = bfv(O_R2, KC * 2048)

        def _gload(g):
            g.dma_start(out=ident[:, :], in_=ident_d[:, :]).then_inc(s_wg.h, 16); s_wg.n += 16
            for c in range(KC):
                g.dma_start(out=bfv(O_WQQ + 2048 * c, D),
                            in_=wqkv_d[128 * c: 128 * (c + 1), 0:D]).then_inc(s_wg.h, 16); s_wg.n += 16
            for c in range(KC):
                g.dma_start(out=bfv(O_R2 + 4096 * c, 2048),
                            in_=wqkv_d[128 * c: 128 * (c + 1), D:3 * D]).then_inc(s_wg.h, 16); s_wg.n += 16
            g.dma_start(out=bass.AP(bf_base.tensor, bf_base.offset + O_BOUT // 2,
                                    [[bf_base.ap[0][0], 1], [1, D]]),
                        in_=bout_d[:, :]).then_inc(s_wg.h, 16); s_wg.n += 16
            g.dma_start(out=wfc_shs[:, :], in_=wfc_sh[:, :]).then_inc(s_wg.h, 16); s_wg.n += 16
            g.dma_start(out=wpj_shs[:, :], in_=wpj_sh[:, :]).then_inc(s_wg.h, 16); s_wg.n += 16
            for k in ("wq", "wk", "wv", "wo"):
                g.dma_start(out=w1_shs[k][:, :], in_=w1_sh[k][:, :]).then_inc(s_wg.h, 16); s_wg.n += 16
        sec("gpsimd", _gload)
        wg_ident = wg_wq = wg_wkv = wg_bout = s_wg.n

        def _consts(v):
            v.memset(ones[:, :], 1.0).then_inc(s_v.h, 1); s_v.n += 1
            v.memset(cbc[:, 0:1], -SOFTMAX_C).then_inc(s_v.h, 1); s_v.n += 1
            v.memset(bd[:, :], 0.0).then_inc(s_v.h, 1); s_v.n += 1
            v.memset(zst[:, :], 0.0).then_inc(s_v.h, 1); s_v.n += 1
        sec("vector", _consts)
        v_consts = s_v.n

        W_STG = O_R3
        W_QB = O_R3 + 11264
        wcoop_done = [0]

        def emit_wcoop():
            for wi, (shd, outd) in enumerate(((wg_sh, None), (wu_sh, None),
                                              (wd_sh, None))):
                in0 = s_in.n
                vg = s_v.n
                tg = s_t.n

                def fld(sy, wi=wi, shd=shd, in0=in0, vg=vg, tg=tg):
                    sy.wait_ge(s_in.h, in0)
                    if vg:
                        sy.wait_ge(s_v.h, vg)
                    if tg:
                        sy.wait_ge(s_t.h, tg)
                    if wi < 2:
                        for j in range(22):
                            sy.dma_start(out=f32v(W_STG + 4 * 128 * j, 128),
                                         in_=shd[128 * j: 128 * (j + 1), :]).then_inc(s_in.h, 16); s_in.n += 16
                    else:
                        sy.dma_start(out=f32v(W_STG, IF), in_=shd[:, :]).then_inc(s_in.h, 16); s_in.n += 16
                sec("sync", fld)
                dma_done = s_in.n

                def fwt(v, dma_done=dma_done):
                    v.wait_ge(s_in.h, dma_done)
                sec("vector", fwt)
                for (c0, c1) in ((0, 1024), (1024, 2048), (2048, 2816)):
                    emit_quant(f32v(W_STG + 4 * c0, c1 - c0),
                               bfv(W_QB + 2 * c0, c1 - c0), 128, c1 - c0, (c1 - c0) // 16,
                               base=O_R3 + 22528, geng=True, tmp_off=O_QT)
                v_q = s_v.n

                evacs = []
                for g3 in range(3):
                    jlist = range(8 * g3, min(22, 8 * g3 + 8))
                    pb = 6 + (g3 % 2)
                    te = tr_evac.get(pb, 0)

                    def ftw(t, jlist=jlist, pb=pb, v_q=v_q, te=te):
                        t.wait_ge(s_v.h, max(v_q, te))
                        t.wait_ge(s_wg.h, wg_ident)
                        for jj, j in enumerate(jlist):
                            t.transpose(bass.AP(PB[pb].tensor, PB[pb].offset + 128 * jj,
                                                [[PB[pb].ap[0][0], 128], [1, 128]]),
                                        bfv(W_QB + 2 * 128 * j, 128),
                                        ident[:, :]).then_inc(s_t.h, 1); s_t.n += 1
                    sec("tensor", ftw)
                    td = s_t.n

                    def fte(v, g3=g3, pb=pb, td=td, nj=len(list(jlist))):
                        v.wait_ge(s_t.h, td)
                        srcap = pbap(pb, [[128, nj], [1, 128]])
                        dstap = bfap(W_QB + 11264 * 0 + 2 * (IF + 128 * 8 * g3) if False else 0, [])
                        dstap = bfap(W_QB + 2 * IF + 2 * 128 * 8 * g3, [[128, nj], [1, 128]])
                        v.tensor_copy(dstap, srcap).then_inc(s_v.h, 1); s_v.n += 1
                    sec("vector", fte)
                    tr_evac[pb] = s_v.n
                v_T = s_v.n

                st0 = s_in.n

                def fst2(sy, wi=wi, outd=outd, v_T=v_T, st0=st0):
                    sy.wait_ge(s_in.h, st0)
                    sy.wait_ge(s_v.h, v_T)
                    if wi < 2:
                        dst = bass.AP(wsh.ap().tensor, SH_GATE if wi == 0 else SH_UP,
                                      [[IF, 128], [1, IF]])
                        sy.dma_start(out=dst, in_=bfv(W_QB + 2 * IF, IF)).then_inc(s_in.h, 16); s_in.n += 16
                    else:
                        srcap = bfap(W_QB + 2 * IF, [[128, 22], [1, 128]])
                        dstap = bass.AP(wsh.ap().tensor, SH_DOWN,
                                        [[128, 128], [128 * 128, 22], [1, 128]])
                        sy.dma_start(out=dstap, in_=srcap).then_inc(s_in.h, 16); s_in.n += 16
                sec("sync", fst2)

            tr0 = s_tr.n
            stg_t = bfv(W_STG, 4096)
            vg2 = s_v.n
            tg2 = s_t.n

            def ftr1(sy, tr0=tr0, vg2=vg2, tg2=tg2):
                sy.wait_ge(s_tr.h, tr0)
                sy.wait_ge(s_v.h, vg2)
                sy.wait_ge(s_t.h, tg2)
                sy.wait_ge(s_wg.h, wg_bout)
                sy.dma_start_transpose(stg_t[:, :], wfc_shs[0:FF, 0:128]).then_inc(s_tr.h, 16); s_tr.n += 16
            sec("sync", ftr1)
            tr1 = s_tr.n
            st1 = s_in.n

            def fsv1(sy, tr1=tr1, st1=st1):
                sy.wait_ge(s_in.h, st1)
                sy.wait_ge(s_tr.h, tr1)
                sy.dma_start(out=bass.AP(wsh.ap().tensor, SH_FCT, [[FF, 128], [1, FF]]),
                             in_=stg_t[:, :]).then_inc(s_in.h, 16); s_in.n += 16
            sec("sync", fsv1)

            tr0 = s_tr.n
            st1b = s_in.n

            def ftr2(sy, tr0=tr0, st1b=st1b):
                sy.wait_ge(s_tr.h, tr0)
                sy.wait_ge(s_in.h, st1b)
                for t4 in range(4):
                    sy.dma_start_transpose(bfv(W_STG + 2048 * t4, 1024),
                                           wpj_shs[0:D, 128 * t4: 128 * (t4 + 1)]).then_inc(s_tr.h, 16); s_tr.n += 16
            sec("sync", ftr2)
            tr2 = s_tr.n
            st2 = s_in.n

            def fsv2(sy, tr2=tr2, st2=st2):
                sy.wait_ge(s_in.h, st2)
                sy.wait_ge(s_tr.h, tr2)
                srcap = bfap(W_STG, [[1024, 4], [1, 1024]])
                dstap = bass.AP(wsh.ap().tensor, SH_PJT,
                                [[1024, 128], [128 * 1024, 4], [1, 1024]])
                sy.dma_start(out=dstap, in_=srcap).then_inc(s_in.h, 16); s_in.n += 16
            sec("sync", fsv2)

            tr0 = s_tr.n
            st3 = s_in.n

            def ftr3(sy, tr0=tr0, st3=st3):
                sy.wait_ge(s_tr.h, tr0)
                sy.wait_ge(s_in.h, st3)
                for ki, k in enumerate(("wq", "wk", "wv", "wo")):
                    sy.dma_start_transpose(bfv(W_STG + 2048 * ki, 1024),
                                           w1_shs[k][0:D, 0:128]).then_inc(s_tr.h, 16); s_tr.n += 16
            sec("sync", ftr3)
            tr3 = s_tr.n
            st4 = s_in.n

            def fsv3(sy, tr3=tr3, st4=st4):
                sy.wait_ge(s_in.h, st4)
                sy.wait_ge(s_tr.h, tr3)
                for ki in range(4):
                    sy.dma_start(out=bass.AP(wsh.ap().tensor, SH_WQ + 131072 * ki,
                                             [[1024, 128], [1, 1024]]),
                                 in_=bfv(W_STG + 2048 * ki, 1024)).then_inc(s_in.h, 16); s_in.n += 16
            sec("sync", fsv3)
            wcoop_done[0] = s_in.n

        xnf = f32v(O_XNF, D)
        bnst = f32v(O_BNST, 12)
        mvv = f32v(O_MV, 2)
        vep = f32v(O_MV + 8, 1)
        rstd = f32v(O_MV + 12, 1)
        xnbf = bfv(O_XNBF, D)
        elu1 = f32v(O_E1, 512)
        elu2 = f32v(O_E2, 512)

        vepal = f32v(O_E1, 17)

        def emit_ln_stats():
            def fv(v):
                v.wait_ge(s_in.h, in_x)
                v.wait_ge(s_v.h, s_v.n)
                v.wait_ge(s_s.h, s_s.n)
                for i in range(17):
                    np_ = tile_np(i)
                    srct = rtile(i)
                    v.bn_stats(bnst[0:np_, 0:6], srct[0:np_, 0:512]).then_inc(s_v.h, 1); s_v.n += 1
                    v.bn_stats(bnst[0:np_, 6:12], srct[0:np_, 512:1024]).then_inc(s_v.h, 1); s_v.n += 1
                    v.wait_ge(s_v.h, s_v.n)
                    v.bn_aggr(mvv[0:np_, :], bnst[0:np_, :]).then_inc(s_v.h, 1); s_v.n += 1
                    v.wait_ge(s_v.h, s_v.n)
                    v.tensor_copy(stats[0:np_, 2 * i: 2 * i + 1], mvv[0:np_, 0:1]).then_inc(s_v.h, 1); s_v.n += 1
                    v.tensor_scalar(vepal[0:np_, i:i + 1], mvv[0:np_, 1:2], EPS, None,
                                    op0=ALU.add).then_inc(s_v.h, 1); s_v.n += 1
            sec("vector", fv)
            v1 = s_v.n
            mid = s_s.n + 17

            def fs(s, v1=v1, mid=mid):
                s.wait_ge(s_v.h, v1)
                for i in range(17):
                    np_ = tile_np(i)
                    s.activation(stats[0:np_, 2 * i + 1: 2 * i + 2], vepal[0:np_, i:i + 1],
                                 AF.Ln).then_inc(s_s.h, 1)
                s.wait_ge(s_s.h, mid)
                for i in range(17):
                    np_ = tile_np(i)
                    s.activation(stats[0:np_, 2 * i + 1: 2 * i + 2],
                                 stats[0:np_, 2 * i + 1: 2 * i + 2], AF.Exp,
                                 scale=-0.5).then_inc(s_s.h, 1)
            ssec(fs, 34)

        def emit_ln(i, save_stats):
            np_ = tile_np(i)
            srct = rtile(i)
            s1 = s_s.n

            def fv2(v):
                v.wait_ge(s_s.h, s1)
                v.wait_ge(s_v.h, s_v.n)
                v.wait_ge(s_t.h, s_t.n)
                v.tensor_scalar(xnbf[0:np_, :], srct[0:np_, :], stats[0:np_, 2 * i: 2 * i + 1],
                                stats[0:np_, 2 * i + 1: 2 * i + 2], op0=ALU.subtract,
                                op1=ALU.mult).then_inc(s_v.h, 1); s_v.n += 1
            sec("vector", fv2)

        tr_evac = {}

        def emit_tm_to_fm(src_bf, dst_off, colbase, np_, pbank):
            v_src = s_v.n

            def ft(t):
                t.wait_ge(s_v.h, v_src)
                t.wait_ge(s_wg.h, wg_ident)
                if pbank in tr_evac:
                    t.wait_ge(s_v.h, tr_evac[pbank])
                for c in range(KC):
                    t.transpose(bass.AP(PB[pbank].tensor, PB[pbank].offset + 128 * c,
                                        [[PB[pbank].ap[0][0], 128], [1, np_]]),
                                src_bf[0:np_, 128 * c: 128 * (c + 1)],
                                ident[0:np_, 0:np_]).then_inc(s_t.h, 1)
                    s_t.n += 1
            sec("tensor", ft)
            t1 = s_t.n

            def fe(v):
                v.wait_ge(s_t.h, t1)
                srcap = pbap(pbank, [[128, KC], [1, np_]])
                dstap = bfap(dst_off + 2 * colbase, [[XT, KC], [1, np_]])
                v.tensor_copy(dstap, srcap).then_inc(s_v.h, 1); s_v.n += 1
            sec("vector", fe)
            tr_evac[pbank] = s_v.n

        def emit_quant(src_f32, dst_bf, np_, F, blocks, base=None, tguard=0, destructive=False,
                       geng=False, tmp_off=None, sguard=None, split=None):
            if base is None:
                offs = (O_QT, O_QM, O_QS, O_QR, O_QSB, O_QCA, O_QCB, O_QSG, O_QTMP)
            else:
                assert not geng or tmp_off is not None
                offs = (base, base + 4096, base + 4352, base + 4608, base + 4864,
                        base + 4992, base + 7040, base + 9088, tmp_off)
            t_f = src_f32 if destructive else f32v(offs[0], F)
            m_f = f32v(offs[1], blocks)
            sc_f = f32v(offs[2], blocks)
            r_f = f32v(offs[3], blocks)
            s_b = bfv(offs[4], blocks)
            ca = bfv(offs[5], F)
            cbch = bfv(offs[6], F)
            sg = bfv(offs[7], F)
            tmp = bfv(offs[8], F) if offs[8] is not None else None

            if geng:
                def fvr(v, tguard=tguard, sguard=(s_s.n if sguard is None else sguard)):
                    v.wait_ge(s_v.h, s_v.n)
                    v.wait_ge(s_s.h, sguard)
                    if tguard:
                        v.wait_ge(s_t.h, tguard)
                    src3 = bass.AP(src_f32.tensor, src_f32.offset,
                                   [[src_f32.ap[0][0], np_], [16, blocks], [1, 16]])
                    v.tensor_reduce(m_f[0:np_, :], src3, axis=AXX, op=ALU.max,
                                    apply_absolute_value=True).then_inc(s_v.h, 1); s_v.n += 1
                    v.wait_ge(s_v.h, s_v.n)
                    v.tensor_scalar(sc_f[0:np_, :], m_f[0:np_, :], 1.0 / 6.0, 1e-12,
                                    op0=ALU.mult, op1=ALU.max).then_inc(s_v.h, 1); s_v.n += 1
                    v.wait_ge(s_v.h, s_v.n)
                    v.reciprocal(r_f[0:np_, :], sc_f[0:np_, :]).then_inc(s_v.h, 1); s_v.n += 1
                    v.tensor_copy(s_b[0:np_, :], sc_f[0:np_, :]).then_inc(s_v.h, 1); s_v.n += 1
                sec("vector", fvr)
                vr = s_v.n

                def fg2(g, vr=vr):
                    g.wait_ge(s_v.h, vr)
                    g.wait_ge(s_gc.h, s_gc.n)
                    rb = bass.AP(r_f.tensor, r_f.offset,
                                 [[r_f.ap[0][0], np_], [1, blocks], [0, 16]])
                    g.tensor_tensor(t_f[0:np_, :], src_f32[0:np_, 0:F], rb,
                                    op=ALU.mult).then_inc(s_gc.h, 1); s_gc.n += 1
                sec("gpsimd", fg2)
                g2 = s_gc.n
                smid = s_s.n + 1

                def fsgg(s, g2=g2, smid=smid):
                    s.wait_ge(s_gc.h, g2)
                    s.activation(sg[0:np_, :], t_f[0:np_, :], AF.Sign).then_inc(s_s.h, 1)
                    s.wait_ge(s_s.h, smid)
                    s.activation(t_f[0:np_, :], t_f[0:np_, :], AF.Abs).then_inc(s_s.h, 1)
                ssec(fsgg, 2)
                s1 = s_s.n
                a_f = t_f

                def _tail(s1=s1):
                    _emit_tail(s1)

                def _emit_tail(s1):
                    def fg3(g, s1=s1):
                        g.wait_ge(s_s.h, s1)
                        g.wait_ge(s_v.h, s_v.n)
                        g.wait_ge(s_gc.h, s_gc.n)
                        sbb = bass.AP(s_b.tensor, s_b.offset,
                                      [[s_b.ap[0][0], np_], [1, blocks], [0, 16]])
                        g.tensor_tensor(sg[0:np_, :], sg[0:np_, :], sbb,
                                        op=ALU.mult).then_inc(s_gc.h, 1); s_gc.n += 1
                    sec("gpsimd", fg3)
                    g3 = s_gc.n

                    def fvA(v, s1=s1, g3=g3):
                        v.wait_ge(s_s.h, s1)
                        v.wait_ge(s_v.h, s_v.n)
                        v.tensor_scalar(ca[0:np_, :], a_f[0:np_, :], 0.25, None,
                                        op0=ALU.is_gt).then_inc(s_v.h, 1); s_v.n += 1
                        for th in (0.75, 1.25, 1.75):
                            v.wait_ge(s_v.h, s_v.n)
                            v.scalar_tensor_tensor(ca[0:np_, :], a_f[0:np_, :], th, ca[0:np_, :],
                                                   op0=ALU.is_gt, op1=ALU.add).then_inc(s_v.h, 1); s_v.n += 1
                        v.tensor_scalar(cbch[0:np_, :], a_f[0:np_, :], 2.5, None,
                                        op0=ALU.is_gt).then_inc(s_v.h, 1); s_v.n += 1
                        v.wait_ge(s_v.h, s_v.n)
                        v.scalar_tensor_tensor(cbch[0:np_, :], a_f[0:np_, :], 3.5, cbch[0:np_, :],
                                               op0=ALU.is_gt, op1=ALU.add).then_inc(s_v.h, 1); s_v.n += 1
                        v.tensor_scalar(tmp[0:np_, :], a_f[0:np_, :], 5.0, 2.0,
                                        op0=ALU.is_gt, op1=ALU.mult).then_inc(s_v.h, 1); s_v.n += 1
                        v.wait_ge(s_v.h, s_v.n)
                        v.tensor_tensor(cbch[0:np_, :], cbch[0:np_, :], tmp[0:np_, :],
                                        op=ALU.add).then_inc(s_v.h, 1); s_v.n += 1
                        v.wait_ge(s_v.h, s_v.n)
                        v.scalar_tensor_tensor(ca[0:np_, :], ca[0:np_, :], 0.5, cbch[0:np_, :],
                                               op0=ALU.mult, op1=ALU.add).then_inc(s_v.h, 1); s_v.n += 1
                        v.wait_ge(s_gc.h, g3)
                        v.wait_ge(s_v.h, s_v.n)
                        v.tensor_tensor(dst_bf[0:np_, 0:F], ca[0:np_, :], sg[0:np_, :],
                                        op=ALU.mult).then_inc(s_v.h, 1); s_v.n += 1
                    sec("vector", fvA)

                if split is None:
                    _tail()
                else:
                    split.append(_tail)
                return s1

            def fv(v):
                v.wait_ge(s_v.h, s_v.n)
                v.wait_ge(s_s.h, s_s.n)
                if tguard:
                    v.wait_ge(s_t.h, tguard)
                src3 = bass.AP(src_f32.tensor, src_f32.offset,
                               [[src_f32.ap[0][0], np_], [16, blocks], [1, 16]])
                v.tensor_reduce(m_f[0:np_, :], src3, axis=AXX, op=ALU.max,
                                apply_absolute_value=True).then_inc(s_v.h, 1); s_v.n += 1
                v.wait_ge(s_v.h, s_v.n)
                v.tensor_scalar(sc_f[0:np_, :], m_f[0:np_, :], 1.0 / 6.0, 1e-12,
                                op0=ALU.mult, op1=ALU.max).then_inc(s_v.h, 1); s_v.n += 1
                v.wait_ge(s_v.h, s_v.n)
                v.reciprocal(r_f[0:np_, :], sc_f[0:np_, :]).then_inc(s_v.h, 1); s_v.n += 1
                v.tensor_copy(s_b[0:np_, :], sc_f[0:np_, :]).then_inc(s_v.h, 1); s_v.n += 1
                v.wait_ge(s_v.h, s_v.n)
                rb = bass.AP(r_f.tensor, r_f.offset,
                             [[r_f.ap[0][0], np_], [1, blocks], [0, 16]])
                v.tensor_tensor(t_f[0:np_, :], src_f32[0:np_, 0:F], rb,
                                op=ALU.mult).then_inc(s_v.h, 1); s_v.n += 1
            sec("vector", fv)
            v1 = s_v.n
            smid = s_s.n + 1

            def fsg(s, v1=v1, smid=smid):
                s.wait_ge(s_v.h, v1)
                s.activation(sg[0:np_, :], t_f[0:np_, :], AF.Sign).then_inc(s_s.h, 1)
                s.wait_ge(s_s.h, smid)
                s.activation(t_f[0:np_, :], t_f[0:np_, :], AF.Abs).then_inc(s_s.h, 1)
            ssec(fsg, 2)
            s1 = s_s.n

            if tmp is not None:
                def fv2(v):
                    v.wait_ge(s_s.h, s1)
                    a_f = t_f
                    v.wait_ge(s_v.h, s_v.n)
                    v.tensor_scalar(ca[0:np_, :], a_f[0:np_, :], 0.25, None,
                                    op0=ALU.is_gt).then_inc(s_v.h, 1); s_v.n += 1
                    v.wait_ge(s_v.h, s_v.n)
                    v.tensor_scalar(tmp[0:np_, :], a_f[0:np_, :], 0.75, None,
                                    op0=ALU.is_gt).then_inc(s_v.h, 1); s_v.n += 1
                    v.wait_ge(s_v.h, s_v.n)
                    v.tensor_tensor(ca[0:np_, :], ca[0:np_, :], tmp[0:np_, :],
                                    op=ALU.add).then_inc(s_v.h, 1); s_v.n += 1
                    v.wait_ge(s_v.h, s_v.n)
                    v.tensor_scalar(cbch[0:np_, :], a_f[0:np_, :], 1.25, None,
                                    op0=ALU.is_gt).then_inc(s_v.h, 1); s_v.n += 1
                    v.wait_ge(s_v.h, s_v.n)
                    v.tensor_scalar(tmp[0:np_, :], a_f[0:np_, :], 1.75, None,
                                    op0=ALU.is_gt).then_inc(s_v.h, 1); s_v.n += 1
                    v.wait_ge(s_v.h, s_v.n)
                    v.tensor_tensor(cbch[0:np_, :], cbch[0:np_, :], tmp[0:np_, :],
                                    op=ALU.add).then_inc(s_v.h, 1); s_v.n += 1
                    v.wait_ge(s_v.h, s_v.n)
                    v.tensor_tensor(ca[0:np_, :], ca[0:np_, :], cbch[0:np_, :],
                                    op=ALU.add).then_inc(s_v.h, 1); s_v.n += 1
                    v.wait_ge(s_v.h, s_v.n)
                    v.tensor_scalar(cbch[0:np_, :], a_f[0:np_, :], 2.5, None,
                                    op0=ALU.is_gt).then_inc(s_v.h, 1); s_v.n += 1
                    v.wait_ge(s_v.h, s_v.n)
                    v.tensor_scalar(tmp[0:np_, :], a_f[0:np_, :], 3.5, None,
                                    op0=ALU.is_gt).then_inc(s_v.h, 1); s_v.n += 1
                    v.wait_ge(s_v.h, s_v.n)
                    v.tensor_tensor(cbch[0:np_, :], cbch[0:np_, :], tmp[0:np_, :],
                                    op=ALU.add).then_inc(s_v.h, 1); s_v.n += 1
                    v.wait_ge(s_v.h, s_v.n)
                    v.tensor_scalar(tmp[0:np_, :], a_f[0:np_, :], 5.0, 2.0,
                                    op0=ALU.is_gt, op1=ALU.mult).then_inc(s_v.h, 1); s_v.n += 1
                    v.wait_ge(s_v.h, s_v.n)
                    v.tensor_tensor(cbch[0:np_, :], cbch[0:np_, :], tmp[0:np_, :],
                                    op=ALU.add).then_inc(s_v.h, 1); s_v.n += 1
                    v.wait_ge(s_v.h, s_v.n)
                    v.wait_ge(s_v.h, s_v.n)
                    v.scalar_tensor_tensor(ca[0:np_, :], ca[0:np_, :], 0.5, cbch[0:np_, :],
                                           op0=ALU.mult, op1=ALU.add).then_inc(s_v.h, 1); s_v.n += 1
                    sbb = bass.AP(s_b.tensor, s_b.offset,
                                  [[s_b.ap[0][0], np_], [1, blocks], [0, 16]])
                    v.wait_ge(s_v.h, s_v.n)
                    v.tensor_tensor(sg[0:np_, :], sg[0:np_, :], sbb, op=ALU.mult).then_inc(s_v.h, 1); s_v.n += 1
                    v.wait_ge(s_v.h, s_v.n)
                    v.tensor_tensor(dst_bf[0:np_, 0:F], ca[0:np_, :], sg[0:np_, :],
                                    op=ALU.mult).then_inc(s_v.h, 1); s_v.n += 1
                sec("vector", fv2)
            else:
                def fv2(v):
                    v.wait_ge(s_s.h, s1)
                    a_f = t_f
                    v.tensor_scalar(ca[0:np_, :], a_f[0:np_, :], FP4_TH_A[0], None,
                                    op0=ALU.is_gt).then_inc(s_v.h, 1); s_v.n += 1
                    for th in FP4_TH_A[1:]:
                        v.wait_ge(s_v.h, s_v.n)
                        v.scalar_tensor_tensor(ca[0:np_, :], a_f[0:np_, :], th, ca[0:np_, :],
                                               op0=ALU.is_gt, op1=ALU.add).then_inc(s_v.h, 1); s_v.n += 1
                    v.tensor_scalar(cbch[0:np_, :], a_f[0:np_, :], FP4_TH_B[0], None,
                                    op0=ALU.is_gt).then_inc(s_v.h, 1); s_v.n += 1
                    for th in FP4_TH_B[1:]:
                        v.wait_ge(s_v.h, s_v.n)
                        v.scalar_tensor_tensor(cbch[0:np_, :], a_f[0:np_, :], th, cbch[0:np_, :],
                                               op0=ALU.is_gt, op1=ALU.add).then_inc(s_v.h, 1); s_v.n += 1
                    v.wait_ge(s_v.h, s_v.n)
                    v.scalar_tensor_tensor(ca[0:np_, :], ca[0:np_, :], 0.5, cbch[0:np_, :],
                                           op0=ALU.mult, op1=ALU.add).then_inc(s_v.h, 1); s_v.n += 1
                    sbb = bass.AP(s_b.tensor, s_b.offset,
                                  [[s_b.ap[0][0], np_], [1, blocks], [0, 16]])
                    v.wait_ge(s_v.h, s_v.n)
                    v.tensor_tensor(sg[0:np_, :], sg[0:np_, :], sbb, op=ALU.mult).then_inc(s_v.h, 1); s_v.n += 1
                    v.wait_ge(s_v.h, s_v.n)
                    v.tensor_tensor(dst_bf[0:np_, 0:F], ca[0:np_, :], sg[0:np_, :],
                                    op=ALU.mult).then_inc(s_v.h, 1); s_v.n += 1
                sec("vector", fv2)

        def emit_elu1(bank, dst_bf, np_, n):
            t0 = s_t.n

            def fv(v):
                v.wait_ge(s_t.h, t0)
                v.wait_ge(s_s.h, s_s.n)
                v.wait_ge(s_v.h, s_v.n)
                v.tensor_scalar(elu1[0:np_, 0:n], PS[bank][0:np_, 0:n], 0.0, None,
                                op0=ALU.min).then_inc(s_v.h, 1); s_v.n += 1
            sec("vector", fv)
            v1 = s_v.n

            def fs(s, v1=v1):
                s.wait_ge(s_v.h, v1)
                s.activation(elu2[0:np_, 0:n], elu1[0:np_, 0:n], AF.Exp).then_inc(s_s.h, 1)
            ssec(fs, 1)
            s1 = s_s.n

            def fv2(v):
                v.wait_ge(s_s.h, s1)
                v.scalar_tensor_tensor(dst_bf[0:np_, 0:n], PS[bank][0:np_, 0:n], 0.0,
                                       elu2[0:np_, 0:n], op0=ALU.max,
                                       op1=ALU.add).then_inc(s_v.h, 1); s_v.n += 1
            sec("vector", fv2)
            return s_v.n

        XN_FM = O_R1

        emit_ln_stats()
        for i in range(17):
            emit_ln(i, save_stats=True)
            emit_tm_to_fm(xnbf, XN_FM, tile_colbase(i), tile_np(i), 6 + (i % 2))
        emit_wcoop()

        def xn_fm_view(k, a, b):
            return bfv(XN_FM + 2 * (XT * k + a), b - a)

        if STAGE < 2:
            early_finish()
            return nc

        sflush()
        Q_BF = O_R3
        bank_free = {}
        for gi, (a, b) in enumerate(TGROUPS):
            n = b - a
            for c in range(KC):
                bank = (gi * KC + c) % 4
                tw = tr_evac.get(6, 0)
                tw2 = tr_evac.get(7, 0)
                bf = bank_free.get(bank, 0)

                def fmm(t, c=c, a=a, b=b, n=n, bank=bank, tw=tw, tw2=tw2, bf=bf):
                    t.wait_ge(s_wg.h, wg_wq)
                    t.wait_ge(s_in.h, wcoop_done[0])
                    t.wait_ge(s_v.h, max(tw, tw2, bf))
                    for k in range(KC):
                        mi = t.matmul(PS[bank][:, 0:n],
                                 bfv(O_WQQ + 2048 * k + 256 * c, 128),
                                 xn_fm_view(k, a, b),
                                 start=(k == 0), stop=(k == KC - 1))
                    mi.then_inc(s_t.h, 1); s_t.n += 1
                sec("tensor", fmm)
                bank_free[bank] = emit_elu1(bank, bfv(Q_BF + 2 * (XT * c + a), n), 128, n)

        def q_bf_view(c, a, b):
            return bfv(Q_BF + 2 * (XT * c + a), b - a)

        if STAGE < 3:
            early_finish()
            return nc

        sflush()
        k_bf = bfv(O_KBF, D)
        v_ex = bfv(O_VEX, 1040)

        def fvo(v):
            v.wait_ge(s_v.h, s_v.n)
            v.memset(v_ex[:, :], 1.0).then_inc(s_v.h, 1); s_v.n += 1
        sec("vector", fvo)

        for i in range(NT):
            vfree = s_v.n

            def fkv(t, i=i, vfree=vfree):
                t.wait_ge(s_wg.h, wg_wkv)
                t.wait_ge(s_v.h, vfree)
                for half in range(2):
                    for k in range(KC):
                        mi = t.matmul(PS[half][:, :],
                                 xn_fm_view(k, 128 * i, 128 * (i + 1)),
                                 bfv(O_R2 + 4096 * k + 1024 * half, 512),
                                 start=(k == 0), stop=(k == KC - 1))
                for half in range(2):
                    for k in range(KC):
                        mi = t.matmul(PS[2 + half][:, :],
                                 xn_fm_view(k, 128 * i, 128 * (i + 1)),
                                 bfv(O_R2 + 4096 * k + 2048 + 1024 * half, 512),
                                 start=(k == 0), stop=(k == KC - 1))
                mi.then_inc(s_t.h, 1); s_t.n += 1
            sec("tensor", fkv)
            emit_elu1(0, bfv(O_KBF, 512), 128, 512)
            emit_elu1(1, bfv(O_KBF + 1024, 512), 128, 512)
            t_v = s_t.n

            def fvv(v, t_v=t_v):
                v.wait_ge(s_t.h, t_v)
                v.wait_ge(s_v.h, s_v.n)
                for half in range(2):
                    dst = bfap(O_VEX + 2 * 520 * half, [[65, 8], [1, 64]])
                    src = psap(2 + half, [[64, 8], [1, 64]])
                    v.tensor_copy(dst, src).then_inc(s_v.h, 1); s_v.n += 1
            sec("vector", fvv)
            v_kv = s_v.n

            te6 = tr_evac.get(6, 0)
            te7 = tr_evac.get(7, 0)

            def fkvmm(t, i=i, v_kv=v_kv, te6=te6, te7=te7):
                t.wait_ge(s_v.h, v_kv)
                if i == 0:
                    t.wait_ge(s_v.h, max(te6, te7))
                for p in range(8):
                    bank = 4 + p // 2
                    colb = 130 * (p % 2)
                    first = (i == 0 and p % 2 == 0)
                    last = (i == NT - 1 and p % 2 == 1)
                    mi = t.matmul(PS[bank][:, colb:colb + 130],
                                  k_bf[:, 128 * p: 128 * (p + 1)],
                                  v_ex[:, 130 * p: 130 * (p + 1)], start=first, stop=last)
                mi.then_inc(s_t.h, 1); s_t.n += 1
            sec("tensor", fkvmm)

        if STAGE < 4:
            early_finish()
            return nc

        t_kv = s_t.n

        def fkvout(v):
            v.wait_ge(s_t.h, t_kv)
            for h in range(H):
                p = h // 2
                bank = 4 + p // 2
                colb = 130 * (p % 2) + 65 * (h % 2)
                po = 64 * (h % 2)
                dst_c = 260 * (h // 8) + 65 * ((h // 2) % 4)
                v.tensor_copy(kvsb[po:po + 64, dst_c:dst_c + 65],
                              PS[bank][po:po + 64, colb:colb + 65]).then_inc(s_v.h, 1); s_v.n += 1
        sec("vector", fkvout)
        v_kvsb = s_v.n

        def fag(g):
            g.wait_ge(s_in.h, wcoop_done[0])
            g.collective_compute(
                "AllGather", ALU.bypass,
                replica_groups=[list(range(8))],
                ins=[wsh.ap().opt()],
                outs=[wsh_g.ap().opt()],
            ).then_inc(s_cc.h, 1); s_cc.n += 1
        sec("gpsimd", fag)
        cc_ag = s_cc.n

        def fcct(g):
            g.wait_ge(s_v.h, v_kvsb)
            g.dma_start(out=cc_in[:, :], in_=kvsb[:, :]).then_inc(s_g.h, 16); s_g.n += 16
            g.wait_ge(s_g.h, s_g.n)
            g.collective_compute(
                "AllReduce", ALU.add,
                replica_groups=[[0, 1, 2, 3], [4, 5, 6, 7]],
                ins=[cc_in.ap().opt()],
                outs=[cc_out.ap().opt()],
            ).then_inc(s_cc.h, 1); s_cc.n += 1
        sec("gpsimd", fcct)
        cc_kv = s_cc.n
        _gkv = [0]

        def emit_kv_readback():
            g0 = s_g.n

            def fccr(g, g0=g0):
                g.wait_ge(s_cc.h, cc_kv)
                g.wait_ge(s_g.h, g0)
                g.dma_start(out=kvsb[:, :], in_=cc_out[:, :]).then_inc(s_g.h, 16); s_g.n += 16
            sec("gpsimd", fccr)
            _gkv[0] = s_g.n
        g_kv = None

        wg0 = s_wg.n

        def fgwout(g):
            g.wait_ge(s_wg.h, wg0)
            g.wait_ge(s_t.h, t_kv)
            for c in range(KC):
                g.dma_start(out=bfv(O_R2 + 2048 * c, D),
                            in_=wout_d[128 * c: 128 * (c + 1), :]).then_inc(s_wg.h, 16); s_wg.n += 16
        sec("gpsimd", fgwout)
        wg_wout = s_wg.n



        sflush()
        XQ_FM = O_R1
        tr_evac.clear()
        def xq_fm_view(k, a, b):
            return bfv(XQ_FM + 2 * (XT * k + a), b - a)
        A_LO = O_R5 + 6272
        A_HI = O_R5 + 23616
        denr = f32v(O_DEN, 512, np_=16)
        recb = f32v(O_RCB, 512)
        v_bd_box = [0]
        c_ev = [0]

        def emit_c_prelude():
            def fbd(v):
                v.wait_ge(s_g.h, _gkv[0])
                v.wait_ge(s_v.h, v_consts)
                v.wait_ge(s_v.h, s_v.n)
                kv_all = f32_base[:, O_KV // 4: O_KV // 4 + 520]
                bd_all = bf_base[:, O_BD // 2: O_BD // 2 + KC * 128]
                zst_all = bf_base[:, O_ZST // 2: O_ZST // 2 + KC * 16]
                for c in range(KC):
                    for hh, po in ((2 * c, 0), (2 * c + 1, 64)):
                        base = 260 * (hh // 8) + 65 * ((hh // 2) % 4)
                        sp = 64 * (hh % 2)
                        v.tensor_copy(bd_all[po:po + 64, 128 * c + po: 128 * c + po + 64],
                                      kv_all[sp:sp + 64, base: base + 64]).then_inc(s_v.h, 1); s_v.n += 1
                        v.tensor_copy(zst_all[po:po + 64, 16 * c + hh: 16 * c + hh + 1],
                                      kv_all[sp:sp + 64, base + 64: base + 65]).then_inc(s_v.h, 1); s_v.n += 1
            sec("vector", fbd)
            v_bd_box[0] = s_v.n
            c_ev[0] = s_v.n

        def emit_c_group(gi, a, b):
            n = b - a
            v_bd = v_bd_box[0]
            cv0 = c_ev[0]

            def fden(t, a=a, b=b, n=n, cv0=cv0, v_bd=v_bd):
                t.wait_ge(s_v.h, max(v_bd, cv0))
                for c in range(KC):
                    mi = t.matmul(PS[2][0:16, 0:n], zst[:, 16 * c: 16 * (c + 1)],
                             q_bf_view(c, a, b), start=(c == 0), stop=(c == KC - 1))
                mi.then_inc(s_t.h, 1); s_t.n += 1
            sec("tensor", fden)
            t_den = s_t.n

            def fdr(v, n=n, t_den=t_den):
                v.wait_ge(s_t.h, t_den)
                v.wait_ge(s_v.h, s_v.n)
                v.tensor_scalar(denr[0:16, 0:n], PS[2][0:16, 0:n], 1e-6, None,
                                op0=ALU.add).then_inc(s_v.h, 1); s_v.n += 1
                v.wait_ge(s_v.h, s_v.n)
                v.reciprocal(denr[0:16, 0:n], denr[0:16, 0:n]).then_inc(s_v.h, 1); s_v.n += 1
            sec("vector", fdr)
            v_den = s_v.n

            for c in range(KC):
                nb = c % 2
                cv = c_ev[0]

                def fnum(t, c=c, a=a, b=b, n=n, nb=nb, v_den=v_den, cv=cv):
                    t.wait_ge(s_v.h, max(v_den, cv))
                    mi = t.matmul(PS[nb][:, 0:n], bd[:, 128 * c: 128 * (c + 1)],
                             q_bf_view(c, a, b), start=True, stop=True)
                    mi = t.matmul(PS[3][:, 0:n], sel[0:16, 128 * c: 128 * (c + 1)],
                             denr[0:16, 0:n], start=True, stop=True)
                    mi.then_inc(s_t.h, 1); s_t.n += 1
                sec("tensor", fnum)
                t_num = s_t.n

                def fnev(v, c=c, n=n, nb=nb, t_num=t_num):
                    v.wait_ge(s_t.h, t_num)
                    v.wait_ge(s_v.h, s_v.n)
                    v.tensor_copy(recb[:, 0:n], PS[3][:, 0:n]).then_inc(s_v.h, 1); s_v.n += 1
                    v.wait_ge(s_v.h, s_v.n)
                    ab = bfv(A_LO if c < 4 else A_HI, 4 * 512)
                    v.tensor_tensor(ab[:, 512 * (c % 4): 512 * (c % 4) + n], PS[nb][:, 0:n],
                                    recb[:, 0:n], op=ALU.mult).then_inc(s_v.h, 1); s_v.n += 1
                sec("vector", fnev)
                c_ev[0] = s_v.n
            v_attn = s_v.n

            for ti in range((n + 127) // 128):
                i = (a + 128 * ti) // 128 if a < TPC else NT
                np_ = tile_np(i)
                cv = c_ev[0]

                def fwo(t, ti=ti, np_=np_, v_attn=v_attn, cv=cv):
                    t.wait_ge(s_v.h, max(v_attn, cv, v_consts))
                    t.wait_ge(s_wg.h, wg_wout)
                    for half in range(2):
                        for c in range(KC):
                            off = (A_LO if c < 4 else A_HI) + 2 * (512 * (c % 4) + 128 * ti)
                            mi = t.matmul(PS[4 + half][0:np_, :],
                                     bfv(off, np_),
                                     bfv(O_R2 + 2048 * c + 1024 * half, 512),
                                     start=(c == 0), stop=False)
                        mi = t.matmul(PS[4 + half][0:np_, :], ones[0:1, 0:np_],
                                 boutb[0:1, 512 * half: 512 * (half + 1)],
                                 start=False, stop=True)
                    mi.then_inc(s_t.h, 1); s_t.n += 1
                sec("tensor", fwo)
                t_wo = s_t.n

                def fwe(v, i=i, np_=np_, t_wo=t_wo):
                    v.wait_ge(s_t.h, t_wo)
                    v.wait_ge(s_v.h, s_v.n)
                    dst = rtile(i)
                    for half in range(2):
                        v.tensor_tensor(dst[0:np_, 512 * half: 512 * (half + 1)],
                                        dst[0:np_, 512 * half: 512 * (half + 1)],
                                        PS[4 + half][0:np_, :], op=ALU.add).then_inc(s_v.h, 1); s_v.n += 1
                sec("vector", fwe)
                c_ev[0] = s_v.n

        XQ2 = O_R2 + 16384
        xsrc_of = {0: f32v(O_QA, D), 1: f32v(XQ2, D)}
        dst_of = {0: xnbf, 1: bfv(XQ2 + 13184, D)}
        vq = {0: s_v.n, 1: s_v.n}
        sq = {0: s_s.n, 1: s_s.n}
        tq = {0: 0, 1: 0}
        pend_tail = []
        for i in range(17):
            par = i % 2
            np_ = tile_np(i)
            in0 = s_in.n
            vguard = vq[par]

            def fxd(sy, i=i, np_=np_, vguard=vguard, in0=in0, par=par):
                sy.wait_ge(s_in.h, in0)
                sy.wait_ge(s_v.h, vguard)
                src = x_d[128 * i: 128 * (i + 1), :] if i < NT else xg_d[:, :]
                sy.dma_start(out=xsrc_of[par][0:np_, :],
                             in_=src).then_inc(s_in.h, 16); s_in.n += 16
            sec("sync", fxd)
            d_x = s_in.n

            def fx(v, i=i, np_=np_, d_x=d_x, par=par, vguard=vguard, sg_=sq[par]):
                v.wait_ge(s_in.h, d_x)
                v.wait_ge(s_v.h, vguard)
                v.wait_ge(s_s.h, sg_)
                xs = xsrc_of[par]
                v.tensor_scalar(xs[0:np_, :], xs[0:np_, :],
                                stats[0:np_, 2 * i: 2 * i + 1],
                                stats[0:np_, 2 * i + 1: 2 * i + 2],
                                op0=ALU.subtract, op1=ALU.mult).then_inc(s_v.h, 1); s_v.n += 1
            sec("vector", fx)
            tail = []
            s1_i = emit_quant(xsrc_of[par], dst_of[par], np_, D, 64, destructive=True,
                              geng=True, base=(XQ2 if par else None),
                              tmp_off=(XQ2 + 11136 if par else None),
                              tguard=tq[par], sguard=sq[par], split=tail)
            if i == 3:
                emit_kv_readback()
            if pend_tail:
                pend_tail.pop(0)()
            if i == 4:
                emit_c_prelude()
            elif i in (5, 7, 9, 11, 13):
                gi = (i - 5) // 2
                emit_c_group(gi, *TGROUPS[gi])

            def mk_tail(tail=tail, i=i, par=par, np_=np_, s1_i=s1_i):
                def run():
                    tail[0]()
                    emit_tm_to_fm(dst_of[par], XQ_FM, tile_colbase(i), np_, 6 + (i % 2))
                    vq[par] = s_v.n
                    sq[par] = s1_i
                    tq[par] = s_t.n
                return run
            pend_tail.append(mk_tail())
        pend_tail.pop(0)()
        vxq = s_v.n


        if STAGE < 5:
            early_finish()
            return nc

        sflush()


        hg_f = f32v(O_HG, 512)
        hf_f = f32v(O_HF, 512)
        hq_t = bfv(O_HQ, 512)
        hq_fm = bfv(O_HQF, 4 * 128)
        wT_ready = {}
        wT_free = {}
        b_free = [c_ev[0]]

        def wg_off(par): return O_R3 + par * 8192
        def wu_off(par): return O_R3 + 16384 + par * 8192
        def wd_off(par): return O_R2 + 16384 + par * 8192

        def emit_fld(si):
            par = si % 2
            n0, n1 = NSL[si]
            ns = n1 - n0
            in0 = s_in.n
            wfree = wT_free.get(par, s_t.n)

            def fld(sy, n0=n0, n1=n1, ns=ns, par=par, in0=in0, wfree=wfree):
                sy.wait_ge(s_in.h, in0)
                sy.wait_ge(s_cc.h, cc_ag)
                sy.wait_ge(s_t.h, wfree)
                sy.wait_ge(s_v.h, vxq)
                for c in range(KC):
                    sy.dma_start(out=bfv(wg_off(par) + 2 * 512 * c, ns),
                                 in_=bass.AP(wsh_g.ap().tensor, c * SHN + SH_GATE + n0,
                                             [[IF, 128], [1, ns]])).then_inc(s_in.h, 16); s_in.n += 16
                for c in range(KC):
                    sy.dma_start(out=bfv(wu_off(par) + 2 * 512 * c, ns),
                                 in_=bass.AP(wsh_g.ap().tensor, c * SHN + SH_UP + n0,
                                             [[IF, 128], [1, ns]])).then_inc(s_in.h, 16); s_in.n += 16
                for c in range(ns // 128):
                    srcap = bass.AP(wsh_g.ap().tensor, SH_DOWN + (n0 + 128 * c) * 128,
                                    [[128, 128], [SHN, 8], [1, 128]])
                    dstap = bfap(wd_off(par) + 2 * D * c, [[128, 8], [1, 128]])
                    sy.dma_start(out=dstap, in_=srcap).then_inc(s_in.h, 16); s_in.n += 16
            sec("sync", fld)
            wT_ready[si] = s_in.n

        items = [(si, p) for si in range(len(NSL)) for p in range(9)]
        hq_evac_s = {}
        bguard = [0, 0]
        t_gu_of = {}
        bhf = f32v(O_BHF, 1024)
        bhg = f32v(O_BHG, 1024)

        def emit_fgu(si, p):
            n0, n1 = NSL[si]
            ns = n1 - n0
            par = si % 2
            wr = wT_ready[si]
            gs, gv = bguard
            tiles = [2 * p, 2 * p + 1] if p < 8 else [16]

            def fgu(t, si=si, p=p, ns=ns, par=par, wr=wr, gs=gs, gv=gv, tiles=tiles):
                t.wait_ge(s_in.h, wr)
                if gs:
                    t.wait_ge(s_s.h, gs)
                if gv:
                    t.wait_ge(s_v.h, gv)
                for ti, i in enumerate(tiles):
                    np_ = tile_np(i)
                    colbase = tile_colbase(i)
                    bp = 2 * ti
                    for c in range(KC):
                        mi = t.matmul(PS[bp][0:np_, 0:ns],
                                      xq_fm_view(c, colbase, colbase + np_),
                                      bfv(wg_off(par) + 2 * 512 * c, ns),
                                      start=(c == 0), stop=(c == KC - 1))
                    for c in range(KC):
                        mi = t.matmul(PS[bp + 1][0:np_, 0:ns],
                                      xq_fm_view(c, colbase, colbase + np_),
                                      bfv(wu_off(par) + 2 * 512 * c, ns),
                                      start=(c == 0), stop=(c == KC - 1))
                mi.then_inc(s_t.h, 1); s_t.n += 1
            sec("tensor", fgu)
            t_gu_of[(si, p)] = s_t.n

        emit_fld(0)
        emit_fld(1)
        emit_fgu(0, 0)
        t_th_prev = [0]
        pend_fra = []
        for j, (si, p) in enumerate(items):
            par = si % 2
            n0, n1 = NSL[si]
            ns = n1 - n0
            tiles = [2 * p, 2 * p + 1] if p < 8 else [16]
            t_gu = t_gu_of[(si, p)]

            for ti, i in enumerate(tiles):
                np_ = tile_np(i)
                bp = 2 * ti
                vhg = s_v.n

                def fsl(s, np_=np_, ns=ns, bp=bp, ti=ti, t_gu=t_gu, vhg=vhg):
                    s.wait_ge(s_t.h, t_gu)
                    s.wait_ge(s_v.h, vhg)
                    s.activation(bhg[0:np_, 512 * ti: 512 * ti + ns], PS[bp][0:np_, 0:ns],
                                 AF.Silu).then_inc(s_s.h, 1)
                ssec(fsl, 1)
                s_si = s_s.n

                def fmu(v, np_=np_, ns=ns, bp=bp, ti=ti, s_si=s_si):
                    v.wait_ge(s_s.h, s_si)
                    v.wait_ge(s_v.h, s_v.n)
                    v.tensor_tensor(bhf[0:np_, 512 * ti: 512 * ti + ns],
                                    bhg[0:np_, 512 * ti: 512 * ti + ns],
                                    PS[bp + 1][0:np_, 0:ns], op=ALU.mult).then_inc(s_v.h, 1); s_v.n += 1
                sec("vector", fmu)
            bguard[0] = s_s.n
            bguard[1] = s_v.n

            if j + 1 < len(items):
                emit_fgu(*items[j + 1])

            npq = tile_np(tiles[0])
            Fq = ns * len(tiles)
            if len(tiles) == 2 and ns == 512:
                emit_quant(bhf, bfv(O_BHQ, 1024), 128, 1024, 64,
                           tguard=t_th_prev[0], destructive=True, geng=True)
                hq_views = [bfv(O_BHQ + 2 * 512 * ti, 512) for ti in range(2)]
            else:
                hq_views = []
                for ti, i in enumerate(tiles):
                    emit_quant(f32v(O_BHF + 4 * 512 * ti, ns), bfv(O_BHQ + 2 * 512 * ti, ns),
                               tile_np(i), ns, ns // 16, tguard=t_th_prev[0], destructive=True,
                               geng=True)
                    hq_views.append(bfv(O_BHQ + 2 * 512 * ti, ns))
            v_hq = s_v.n

            for fn in pend_fra:
                sec("vector", fn); s_v.n += 2
            del pend_fra[:]
            b_free[0] = s_v.n

            for ti, i in enumerate(tiles):
                np_ = tile_np(i)
                pb = 6 + (ti % 2)
                te_s = hq_evac_s.get(pb, 0)

                def fth(t, np_=np_, ns=ns, pb=pb, ti=ti, v_hq=v_hq, te_s=te_s):
                    t.wait_ge(s_v.h, v_hq)
                    if te_s:
                        t.wait_ge(s_s.h, te_s)
                    for c in range(ns // 128):
                        t.transpose(bass.AP(PB[pb].tensor, PB[pb].offset + 128 * c,
                                            [[PB[pb].ap[0][0], 128], [1, np_]]),
                                    bfv(O_BHQ + 2 * (512 * ti + 128 * c), 128, np_=np_),
                                    ident[0:np_, 0:np_]).then_inc(s_t.h, 1); s_t.n += 1
                sec("tensor", fth)
                t_th = s_t.n
                t_th_prev[0] = t_th

                def fhe(s, np_=np_, ns=ns, pb=pb, ti=ti, t_th=t_th):
                    s.wait_ge(s_t.h, t_th)
                    srcap = pbap(pb, [[128, ns // 128], [1, np_]])
                    dstap = bfap(O_HQF2 + 1024 * ti, [[128, ns // 128], [1, np_]])
                    s.copy(dstap, srcap).then_inc(s_s.h, 1)
                ssec(fhe, 1)
                hq_evac_s[pb] = s_s.n
                v_hf = s_s.n
                bfree = b_free[0]

                def fdn(t, np_=np_, ns=ns, par=par, ti=ti, v_hf=v_hf, bfree=bfree):
                    t.wait_ge(s_s.h, v_hf)
                    t.wait_ge(s_v.h, bfree)
                    for half in range(2):
                        for c in range(ns // 128):
                            mi = t.matmul(PS[4 + half][0:np_, :],
                                          bfv(O_HQF2 + 1024 * ti + 256 * c, np_),
                                          bfv(wd_off(par) + 2 * (D * c + 512 * half), 512),
                                          start=(c == 0), stop=(c == ns // 128 - 1))
                    mi.then_inc(s_t.h, 1); s_t.n += 1
                sec("tensor", fdn)
                t_dn = s_t.n

                def fra(v, i=i, np_=np_, t_dn=t_dn):
                    v.wait_ge(s_t.h, t_dn)
                    dst = rtile(i)
                    for half in range(2):
                        v.tensor_tensor(dst[0:np_, 512 * half: 512 * (half + 1)],
                                        dst[0:np_, 512 * half: 512 * (half + 1)],
                                        PS[4 + half][0:np_, :], op=ALU.add).then_inc(s_v.h, 1)
                pend_fra.append(fra)
                if ti == 0 and len(tiles) == 2:
                    for fn in pend_fra:
                        sec("vector", fn); s_v.n += 2
                    del pend_fra[:]
                    b_free[0] = s_v.n
            if p == 8:
                wT_free[par] = s_t.n
                if si + 2 < len(NSL):
                    emit_fld(si + 2)
        for fn in pend_fra:
            sec("vector", fn); s_v.n += 2
        del pend_fra[:]
        b_free[0] = s_v.n

        sflush()
        XN1_FM = O_R3
        tr_evac.clear()

        def xn1_view(k, a, b):
            return bfv(XN1_FM + 2 * (XT * k + a), b - a)

        emit_ln_stats()
        for i in range(17):
            emit_ln(i, save_stats=False)
            emit_tm_to_fm(xnbf, XN1_FM, tile_colbase(i), tile_np(i), 6 + (i % 2))
        v_xn1 = s_v.n

        O_W1A = O_R2
        O_W1B = O_R2 + 16384

        W1OFF = {"wq": SH_WQ, "wk": SH_WK, "wv": SH_WV, "wo": SH_WO}

        def emit_w1T(dst_off, src_s, key, guard_t):
            in0 = s_in.n

            def ftr(sy):
                sy.wait_ge(s_in.h, in0)
                sy.wait_ge(s_cc.h, cc_ag)
                sy.wait_ge(s_t.h, guard_t)
                for c in range(KC):
                    sy.dma_start(out=bfv(dst_off + 2 * D * c, D),
                                 in_=bass.AP(wsh_g.ap().tensor, c * SHN + W1OFF[key],
                                             [[1024, 128], [1, D]])).then_inc(s_in.h, 16); s_in.n += 16
            sec("sync", ftr)
            return s_in.n

        O_QH = O_R1
        O_KH = O_R1 + 16384
        probs = bfv(O_PROBS, 192)
        pT = bfv(O_PT, 256)
        av_f = bfv(O_AVF, KC * 128)
        zbv = f32v(O_ZB, 1)
        zrv = f32v(O_ZR, 1)
        l1b = {}
        l1_ev = [b_free[0]]
        probs_free = [0]

        for half in range(2):
            hb = TPC // 2 * half
            HG2 = [(hb, hb + 512), (hb + 512, hb + 1024)]
            tr_wq = emit_w1T(O_W1A, None, "wq", s_t.n)

            for gi, (a, b) in enumerate(HG2):
                for c in range(KC):
                    bank = (gi * KC + c) % 4
                    bg = l1b.get(("b", bank), 0)
                    lev = l1_ev[0]

                    def fq(t, c=c, a=a, b=b, bank=bank, bg=bg, lev=lev, tr_wq=tr_wq):
                        t.wait_ge(s_in.h, tr_wq)
                        t.wait_ge(s_v.h, max(lev, v_xn1))
                        if bg:
                            t.wait_ge(s_s.h, bg)
                        for k in range(KC):
                            mi = t.matmul(PS[bank][:, :],
                                     bfv(O_W1A + 2 * (D * k + 128 * c), 128),
                                     xn1_view(k, a, b),
                                     start=(k == 0), stop=(k == KC - 1))
                        mi.then_inc(s_t.h, 1); s_t.n += 1
                    sec("tensor", fq)
                    t_q = s_t.n

                    def fqe(s, c=c, gi=gi, bank=bank, t_q=t_q):
                        s.wait_ge(s_t.h, t_q)
                        s.copy(bfv(O_QH + 2 * (D * c + 512 * gi), 512)[:, :],
                               PS[bank][:, :]).then_inc(s_s.h, 1)
                    ssec(fqe, 1)
                    l1b[("b", bank)] = s_s.n

            tr_wk = emit_w1T(O_W1B, None, "wk", s_t.n)
            kg2 = list(HG2) + ([(TPC, XT)] if half == 0 else [])
            for gi, (a, b) in enumerate(kg2):
                n = b - a
                for c in range(KC):
                    bank = (gi * KC + c) % 4
                    bg = l1b.get(("b", bank), 0)

                    lev = l1_ev[0]

                    def fk(t, c=c, a=a, b=b, n=n, bank=bank, bg=bg, tr_wk=tr_wk, lev=lev):
                        t.wait_ge(s_in.h, tr_wk)
                        if lev:
                            t.wait_ge(s_v.h, lev)
                        if bg:
                            t.wait_ge(s_s.h, bg)
                        for k in range(KC):
                            mi = t.matmul(PS[bank][:, 0:n],
                                     bfv(O_W1B + 2 * (D * k + 128 * c), 128),
                                     xn1_view(k, a, b),
                                     start=(k == 0), stop=(k == KC - 1))
                        mi.then_inc(s_t.h, 1); s_t.n += 1
                    sec("tensor", fk)
                    t_k = s_t.n

                    def fke(s, c=c, gi=gi, a=a, n=n, bank=bank, t_k=t_k):
                        s.wait_ge(s_t.h, t_k)
                        if a < TPC:
                            dst = bfv(O_KH + 2 * (D * c + 512 * gi), n)
                        else:
                            dst = bfv(O_KG + 2 * (G * c), n)
                        s.copy(dst[:, :], PS[bank][0:128, 0:n]).then_inc(s_s.h, 1)
                    ssec(fke, 1)
                    l1b[("b", bank)] = s_s.n

            tr_wv = emit_w1T(O_W1A, None, "wv", s_t.n)
            vtl = list(range(8)) + ([16] if half == 0 else [])
            for vi in vtl:
                np_ = 128 if vi < 16 else G
                colbase = hb + 128 * vi if vi < 16 else TPC
                bank2 = 2 * ((vi % 2))
                bg = max(l1b.get(("b", bank2), 0), l1b.get(("b", bank2 + 1), 0))

                lev = l1_ev[0]

                def fv3(t, vi=vi, np_=np_, colbase=colbase, bank2=bank2, bg=bg, tr_wv=tr_wv, lev=lev):
                    t.wait_ge(s_in.h, tr_wv)
                    if lev:
                        t.wait_ge(s_v.h, lev)
                    if bg:
                        t.wait_ge(s_s.h, bg)
                    for hf in range(2):
                        for k in range(KC):
                            mi = t.matmul(PS[bank2 + hf][0:np_, :],
                                     xn1_view(k, colbase, colbase + np_),
                                     bfv(O_W1A + 2 * (D * k + 512 * hf), 512),
                                     start=(k == 0), stop=(k == KC - 1))
                    mi.then_inc(s_t.h, 1); s_t.n += 1
                sec("tensor", fv3)
                t_v3 = s_t.n

                def fve(s, vi=vi, np_=np_, bank2=bank2, t_v3=t_v3):
                    s.wait_ge(s_t.h, t_v3)
                    for hf in range(2):
                        if vi < 16:
                            dst = bfv(O_VH + 2 * (D * vi + 512 * hf), 512)
                        else:
                            dst = bfv(O_VG + 2 * 512 * hf, 512, np_=G)
                        s.copy(dst[0:np_, :], PS[bank2 + hf][0:np_, :]).then_inc(s_s.h, 1)
                ssec(fve, 2)
                l1b[("b", bank2)] = l1b[("b", bank2 + 1)] = s_s.n
            s_qkv = s_s.n

            tr_wo = emit_w1T(O_W1B, None, "wo", s_t.n)

            wseq = [(w, h) for w in range(8) for h in range(H)]
            t_sc_of = {}
            probs_par_free = {}

            def emit_fsc(j):
                w, h = wseq[j]
                scb = j % 2
                bg = l1b.get(("sc", scb), 0)

                def fsc(t, w=w, h=h, scb=scb, bg=bg):
                    t.wait_ge(s_s.h, max(s_qkv, bg))
                    po = 64 * (h % 2)
                    qb_ = O_QH // 2 + D * (h // 2) + 128 * w
                    kb_ = O_KH // 2 + D * (h // 2) + 128 * w
                    gb_ = O_KG // 2 + G * (h // 2)
                    qv = bf_base[po:po + 64, qb_:qb_ + 128]
                    kvw = bf_base[po:po + 64, kb_:kb_ + 128]
                    kgv = bf_base[po:po + 64, gb_:gb_ + G]
                    mi = t.matmul(PS[scb][:, 0:G], qv, kgv, start=True, stop=True)
                    mi = t.matmul(PS[scb][:, G:G + 128], qv, kvw, start=True, stop=True)
                    mi.then_inc(s_t.h, 1); s_t.n += 1
                sec("tensor", fsc)
                t_sc_of[j] = s_t.n

            emit_fsc(0)
            for j, (w, h) in enumerate(wseq):
                par = j % 2
                scb = par
                probsv = bfv(O_PROBS + 384 * par, 192)
                pTv = bfv(O_PT + 512 * par, 256)
                zbvp = f32v(O_ZB + 4 * par, 1)
                zrvp = f32v(O_ZR + 4 * par, 1)
                t_sc = t_sc_of[j]
                pf = probs_par_free.get(par, 0)

                def fex(s, scb=scb, t_sc=t_sc, pf=pf, probsv=probsv, zbvp=zbvp):
                    s.wait_ge(s_t.h, t_sc)
                    if pf:
                        s.wait_ge(s_v.h, pf)
                    s.activation(probsv[:, :], PS[scb][:, 0:192], AF.Exp,
                                 bias=cbc[:, 0:1], scale=0.125,
                                 accum_out=zbvp[:, 0:1]).then_inc(s_s.h, 1)
                ssec(fex, 1)
                l1b[("sc", scb)] = s_s.n
                l1b[("b", scb)] = s_s.n
                s_ex = s_s.n

                def fzr(v, s_ex=s_ex, probsv=probsv, zbvp=zbvp, zrvp=zrvp):
                    v.wait_ge(s_s.h, s_ex)
                    v.reciprocal(zrvp[:, 0:1], zbvp[:, 0:1]).then_inc(s_v.h, 1); s_v.n += 1
                    v.wait_ge(s_v.h, s_v.n)
                    v.tensor_scalar(probsv[:, :], probsv[:, :], zrvp[:, 0:1], None,
                                    op0=ALU.mult).then_inc(s_v.h, 1); s_v.n += 1
                sec("vector", fzr)
                v_pn = s_v.n

                if j + 1 < len(wseq):
                    emit_fsc(j + 1)

                ptb = 6 + par
                bg2 = l1b.get(("pt", ptb), 0)

                def fpt(t, ptb=ptb, v_pn=v_pn, bg2=bg2, probsv=probsv):
                    t.wait_ge(s_v.h, max(v_pn, bg2))
                    t.transpose(bass.AP(PB[ptb].tensor, PB[ptb].offset,
                                        [[PB[ptb].ap[0][0], 128], [1, 128]]),
                                probsv[:, G:192], ident[:, :]).then_inc(s_t.h, 1); s_t.n += 1
                    t.transpose(bass.AP(PB[ptb].tensor, PB[ptb].offset + 128,
                                        [[PB[ptb].ap[0][0], G], [1, 128]]),
                                probsv[:, 0:G], ident[:, :]).then_inc(s_t.h, 1); s_t.n += 1
                sec("tensor", fpt)
                t_pt = s_t.n

                def fpe(v, ptb=ptb, t_pt=t_pt, pTv=pTv):
                    v.wait_ge(s_t.h, t_pt)
                    v.tensor_copy(pTv[:, :], PB[ptb][:, 0:256]).then_inc(s_v.h, 1); s_v.n += 1
                sec("vector", fpe)
                l1b[("pt", ptb)] = s_v.n
                v_pt = s_v.n
                probs_par_free[par] = v_pt

                avg = l1b.get(("av",), 0)

                def fav(t, w=w, h=h, v_pt=v_pt, avg=avg, pTv=pTv):
                    t.wait_ge(s_v.h, v_pt)
                    if avg and h % 2 == 0:
                        t.wait_ge(s_v.h, avg)
                    po = 64 * (h % 2)
                    vv = bfv(O_VH + 2 * (D * w + 64 * h), 64)
                    vgv = bfv(O_VG + 2 * 64 * h, 64, np_=G)
                    outap = PS[4][po:po + 64, 0:128]
                    mi = t.matmul(outap, vv, pTv[:, 0:128], start=True, stop=False)
                    mi = t.matmul(outap, vgv[0:G, :], pTv[0:G, 128:256], start=False, stop=True)
                    mi.then_inc(s_t.h, 1); s_t.n += 1
                sec("tensor", fav)
                t_av = s_t.n

                if h % 2 == 1:
                    def fae(v, h=h, t_av=t_av):
                        v.wait_ge(s_t.h, t_av)
                        v.tensor_copy(av_f[:, 128 * (h // 2): 128 * (h // 2) + 128],
                                      PS[4][:, 0:128]).then_inc(s_v.h, 1); s_v.n += 1
                    sec("vector", fae)
                    l1b[("av",)] = s_v.n

                if h == H - 1:
                    s_av = s_v.n
                    lev = l1_ev[0]

                    def fw1(t, w=w, hb=hb, s_av=s_av, lev=lev, tr_wo=tr_wo):
                        t.wait_ge(s_v.h, max(s_av, lev))
                        t.wait_ge(s_in.h, tr_wo)
                        for hf in range(2):
                            for c in range(KC):
                                mi = t.matmul(PS[2 + hf][:, :],
                                              av_f[:, 128 * c: 128 * (c + 1)],
                                              bfv(O_W1B + 2 * (D * c + 512 * hf), 512),
                                              start=(c == 0), stop=(c == KC - 1))
                        mi.then_inc(s_t.h, 1); s_t.n += 1
                    sec("tensor", fw1)
                    t_w1 = s_t.n

                    def fwe1(v, w=w, hb=hb, t_w1=t_w1):
                        v.wait_ge(s_t.h, t_w1)
                        dst = rtile((hb + 128 * w) // 128)
                        for hf in range(2):
                            v.tensor_tensor(dst[:, 512 * hf: 512 * (hf + 1)],
                                            dst[:, 512 * hf: 512 * (hf + 1)],
                                            PS[2 + hf][:, :], op=ALU.add).then_inc(s_v.h, 1); s_v.n += 1
                    sec("vector", fwe1)
                    l1_ev[0] = s_v.n

        if STAGE < 7:
            early_finish()
            return nc

        sflush()
        O_YA = O_R1
        O_FCT = O_R2
        O_PJT = O_R2 + 16384
        h2v = bfv(O_H2, 4 * D)
        spa = bfv(O_SPA, D)
        l1f = {}
        h2_free = [l1_ev[0]]
        ya_guard = [0]

        for half in range(2):
            hb = TPC // 2 * half
            for si, (f0, f1) in enumerate(FSL):
                par = si % 2
                O_FT = O_FCT + par * 8192
                O_PT2 = O_PJT + par * 8192
                guard_t = l1f.get(("trg", par), s_t.n)

                in0 = s_in.n

                def ffw(sy, f0=f0, f1=f1, O_FT=O_FT, O_PT2=O_PT2, guard_t=guard_t, in0=in0):
                    sy.wait_ge(s_in.h, in0)
                    sy.wait_ge(s_cc.h, cc_ag)
                    if guard_t:
                        sy.wait_ge(s_t.h, guard_t)
                    for c in range(KC):
                        sy.dma_start(out=bfv(O_FT + 2 * 512 * c, 512),
                                     in_=bass.AP(wsh_g.ap().tensor, c * SHN + SH_FCT + f0,
                                                 [[FF, 128], [1, 512]])).then_inc(s_in.h, 16); s_in.n += 16
                    for c in range(4):
                        rank = f0 // 512
                        row = (f0 % 512) + 128 * c
                        sy.dma_start(out=bfv(O_PT2 + 2 * D * c, D),
                                     in_=bass.AP(wsh_g.ap().tensor, rank * SHN + SH_PJT + row * 1024,
                                                 [[1024, 128], [1, D]])).then_inc(s_in.h, 16); s_in.n += 16
                sec("sync", ffw)
                tr_f = s_in.n

                for fc_ in range(4):
                    for gi in range(2):
                        t0 = hb + 512 * gi
                        bank = (fc_ * 2 + gi) % 4
                        bg = l1f.get(("fc", bank), 0)
                        h2f = h2_free[0]

                        def ffc(t, fc_=fc_, t0=t0, bank=bank, tr_f=tr_f, bg=bg, h2f=h2f, O_FT=O_FT):
                            t.wait_ge(s_in.h, tr_f)
                            t.wait_ge(s_v.h, max(bg, h2f))
                            for k in range(KC):
                                mi = t.matmul(PS[bank][:, :],
                                         bfv(O_FT + 2 * (512 * k + 128 * fc_), 128),
                                         xn1_view(k, t0, t0 + 512),
                                         start=(k == 0), stop=(k == KC - 1))
                            mi.then_inc(s_t.h, 1); s_t.n += 1
                        sec("tensor", ffc)
                        t_fc = s_t.n

                        vhg = s_v.n

                        def frl(s, bank=bank, t_fc=t_fc, vhg=vhg):
                            s.wait_ge(s_t.h, t_fc)
                            s.wait_ge(s_v.h, vhg)
                            s.activation(hg_f[:, :], PS[bank][:, :], AF.Relu).then_inc(s_s.h, 1)
                        ssec(frl, 1)
                        s_rl = s_s.n

                        def fsq(v, fc_=fc_, gi=gi, s_rl=s_rl):
                            v.wait_ge(s_s.h, s_rl)
                            dst = bfv(O_H2 + 2 * (D * fc_ + 512 * gi), 512)
                            v.tensor_tensor(dst[:, :], hg_f[:, :], hg_f[:, :],
                                            op=ALU.mult).then_inc(s_v.h, 1); s_v.n += 1
                        sec("vector", fsq)
                        l1f[("fc", bank)] = s_v.n
                v_h2 = s_v.n
                l1f[("trg", par)] = s_t.n

                for ti in range(8):
                    pb2 = 4 + 2 * (ti % 2)
                    bg = l1f.get(("pj", pb2), 0)

                    def fpj(t, ti=ti, pb2=pb2, v_h2=v_h2, bg=bg, O_PT2=O_PT2):
                        t.wait_ge(s_v.h, max(v_h2, bg))
                        for hf in range(2):
                            for c in range(4):
                                mi = t.matmul(PS[pb2 + hf][:, :],
                                         bfv(O_H2 + 2 * (D * c + 128 * ti), 128),
                                         bfv(O_PT2 + 2 * (D * c + 512 * hf), 512),
                                         start=(c == 0), stop=(c == 3))
                        mi.then_inc(s_t.h, 1); s_t.n += 1
                    sec("tensor", fpj)
                    t_pj = s_t.n
                    l1f[("trg", par)] = s_t.n

                    def fya(v, ti=ti, si=si, pb2=pb2, t_pj=t_pj):
                        v.wait_ge(s_t.h, t_pj)
                        for hf in range(2):
                            dst = f32v(O_YA + 4 * (D * ti + 512 * hf), 512)
                            if si == 0:
                                v.tensor_copy(dst[:, :], PS[pb2 + hf][:, :]).then_inc(s_v.h, 1); s_v.n += 1
                            else:
                                v.tensor_tensor(dst[:, :], dst[:, :], PS[pb2 + hf][:, :],
                                                op=ALU.add).then_inc(s_v.h, 1); s_v.n += 1
                    sec("vector", fya)
                    l1f[("pj", pb2)] = s_v.n
                h2_free[0] = s_v.n

            for ti in range(8):
                i = (hb + 128 * ti) // 128

                def fsp(v, ti=ti, i=i):
                    v.wait_ge(s_v.h, s_v.n)
                    y = f32v(O_YA + 4 * D * ti, D)
                    v.tensor_scalar(spa[:, :], y[:, :], SPIKE_B[0], None,
                                    op0=ALU.is_gt).then_inc(s_v.h, 1); s_v.n += 1
                    for bth in SPIKE_B[1:]:
                        v.wait_ge(s_v.h, s_v.n)
                        v.scalar_tensor_tensor(spa[:, :], y[:, :], bth, spa[:, :],
                                               op0=ALU.is_gt, op1=ALU.add).then_inc(s_v.h, 1); s_v.n += 1
                    v.wait_ge(s_v.h, s_v.n)
                    dst = rtile(i)
                    v.scalar_tensor_tensor(dst[:, :], spa[:, :], 0.2, dst[:, :],
                                           op0=ALU.mult, op1=ALU.add).then_inc(s_v.h, 1); s_v.n += 1
                sec("vector", fsp)
                v_done = s_v.n

                st0 = s_st.n

                def fst(sy, i=i, v_done=v_done, st0=st0):
                    sy.wait_ge(s_st.h, st0)
                    sy.wait_ge(s_v.h, v_done)
                    sy.dma_start(out=out_d[128 * i: 128 * (i + 1), :],
                                 in_=f32v(O_R + 4096 * i, D)).then_inc(s_st.h, 16); s_st.n += 16
                sec("sync", fst)

        sflush()

        def ffin(sy):
            sy.wait_ge(s_st.h, s_st.n)
        sec("sync", ffin)

    return nc


class Sem:
    def __init__(self, h):
        self.h = h
        self.n = 0


_NC_CACHE = {}


def kernel(L0_attn_x, L0_attn_w_qkv, L0_attn_w_out, L0_attn_b_out,
           L0_ffn_w_gate, L0_ffn_w_up, L0_ffn_w_down,
           L1_attn_Wq, L1_attn_Wk, L1_attn_Wv, L1_attn_Wo,
           L1_ffn_W_fc, L1_ffn_W_proj):
    x = np.ascontiguousarray(np.asarray(L0_attn_x, dtype=np.float32))
    B, S, _ = x.shape
    CPS = 4
    ident = np.eye(128, dtype=np.float32)
    sel = np.zeros((16, KC * 128), dtype=np.float32)
    for c in range(KC):
        sel[2 * c, 128 * c: 128 * c + 64] = 1.0
        sel[2 * c + 1, 128 * c + 64: 128 * c + 128] = 1.0
    w = {
        "wqkv": np.ascontiguousarray(L0_attn_w_qkv, dtype=np.float32),
        "wout": np.ascontiguousarray(L0_attn_w_out, dtype=np.float32),
        "bout": np.ascontiguousarray(np.asarray(L0_attn_b_out, dtype=np.float32).reshape(1, D)),
        "wgate": np.ascontiguousarray(L0_ffn_w_gate, dtype=np.float32),
        "wup": np.ascontiguousarray(L0_ffn_w_up, dtype=np.float32),
        "wdown": np.ascontiguousarray(L0_ffn_w_down, dtype=np.float32),
        "wq": np.ascontiguousarray(L1_attn_Wq, dtype=np.float32),
        "wk": np.ascontiguousarray(L1_attn_Wk, dtype=np.float32),
        "wv": np.ascontiguousarray(L1_attn_Wv, dtype=np.float32),
        "wo": np.ascontiguousarray(L1_attn_Wo, dtype=np.float32),
        "wfc": np.ascontiguousarray(L1_ffn_W_fc, dtype=np.float32),
        "wproj": np.ascontiguousarray(L1_ffn_W_proj, dtype=np.float32),
        "ident": ident, "sel": sel,
    }
    if "nc" not in _NC_CACHE:
        _NC_CACHE["nc"] = build_program()
    nc = _NC_CACHE["nc"]
    in_maps = []
    for core in range(8):
        b = core // CPS
        t0 = (core % CPS) * TPC
        m = dict(w)
        m["x"] = np.ascontiguousarray(x[b, t0:t0 + TPC, :])
        m["xg"] = np.ascontiguousarray(x[b, 0:G, :])
        m["wg_sh"] = np.ascontiguousarray(w["wgate"][:, 128 * core: 128 * (core + 1)])
        m["wu_sh"] = np.ascontiguousarray(w["wup"][:, 128 * core: 128 * (core + 1)])
        m["wd_sh"] = np.ascontiguousarray(w["wdown"][128 * core: 128 * (core + 1), :])
        m["wfc_sh"] = np.ascontiguousarray(w["wfc"][:, 128 * core: 128 * (core + 1)])
        m["wpj_sh"] = np.ascontiguousarray(w["wproj"][:, 512 * core: 512 * (core + 1)])
        for k in ("wq", "wk", "wv", "wo"):
            m[k + "_sh"] = np.ascontiguousarray(w[k][:, 128 * core: 128 * (core + 1)])
        in_maps.append(m)
    res = run_bass_kernel_spmd(nc, in_maps, core_ids=list(range(8)))
    out = np.empty((B, S, D), dtype=np.float32)
    for core in range(8):
        b = core // CPS
        t0 = (core % CPS) * TPC
        out[b, t0:t0 + TPC, :] = res.results[core]["out"]
    return out
